# version 03
# baseline (speedup 1.0000x reference)
import numpy as np
import ml_dtypes
from contextlib import ExitStack

import concourse.bass as bass
import concourse.mybir as mybir
from concourse.bass_utils import run_bass_kernel_spmd

F32 = mybir.dt.float32
BF16 = mybir.dt.bfloat16
AF = mybir.ActivationFunctionType
ALU = mybir.AluOpType
BF = ml_dtypes.bfloat16

B = 2
T = 4096
D = 4096
TPC = 1024
HALF = 512
NKC = D // 128

NH, NKV, DH = 32, 8, 128
ROPE_THETA = 10000.0
NE, RR, KK = 8, 64, 128
MH, MP, MN, MG, MK = 64, 64, 128, 8, 4
MI = MH * MP
MGN = MG * MN
CONV_DIM = MI + 2 * MGN
PROJ = MI + CONV_DIM + MH
MEPS = 1e-5
LN_EPS = 1e-5

ENGINES = ("sync", "gpsimd", "tensor", "vector", "scalar")
_SELFSEM = {"vector": "ve", "scalar": "se", "gpsimd": "po"}


class Bld:

    def __init__(self, nc, ctx):
        self.nc = nc
        self.ctx = ctx
        self.ops = {e: [] for e in ENGINES}
        self.sems = {}
        self.cnt = {}
        for s in ("ve", "se", "po", "pe", "cld", "wld", "xld", "sto"):
            self.sem(s)

    def sem(self, name):
        if name not in self.sems:
            self.sems[name] = self.ctx.enter_context(self.nc.semaphore(name))
            self.cnt[name] = 0
        return self.sems[name]

    def sb(self, name, shape, dt):
        return self.ctx.enter_context(self.nc.sbuf_tensor(name, shape, dt))

    def psum(self, name, shape, dt=F32):
        return self.ctx.enter_context(self.nc.psum_tensor(name, shape, dt))

    def on(self, eng, f):
        self.ops[eng].append(f)

    def wait(self, eng, tok):
        toks = tok if isinstance(tok, list) else [tok]
        for t in toks:
            if t is None:
                continue
            (s, v) = t
            if v <= 0:
                continue
            sem = self.sem(s)
            self.on(eng, lambda e, sem=sem, vv=int(v): e.wait_ge(sem, vv))

    def mark(self, semname):
        return (semname, self.cnt[semname])

    def raw_emit(self, eng, fn, sem=None, inc=1):
        if sem is not None:
            s = self.sem(sem)
            self.cnt[sem] += inc
            self.on(eng, lambda e, fn=fn, s=s, inc=inc: fn(e).then_inc(s, inc))
            return (sem, self.cnt[sem])
        self.on(eng, lambda e, fn=fn: fn(e))
        return None

    def op(self, eng, fn, waits=()):
        ssem = _SELFSEM[eng]
        self.wait(eng, list(waits) + [self.mark(ssem)])
        return self.raw_emit(eng, fn, sem=ssem)

    def constcol(self, value):
        if not hasattr(self, "_cc"):
            self._cc = {}
            self._cc_sb = self.sb("constcols", [128, 8], F32)
        if value not in self._cc:
            i = len(self._cc)
            ap = self._cc_sb[:, i:i + 1]
            tok = self.op("vector", lambda e, ap=ap, v=float(value): e.memset(ap, v))
            self._cc[value] = (ap, tok)
        return self._cc[value]

    def dma(self, eng, out, in_, sem, waits=()):
        self.wait(eng, list(waits))
        return self.raw_emit(eng, lambda e, o=out, i=in_: e.dma_start(out=o, in_=i),
                             sem=sem, inc=16)

    def pe(self, fn, waits=(), sem=None):
        self.wait("tensor", list(waits))
        return self.raw_emit("tensor", fn, sem=sem)

    def finalize(self):
        block = self.ctx.enter_context(self.nc.Block())
        for name in ENGINES:
            ops = list(self.ops[name])
            if not ops:
                continue
            def make(ops):
                def run(handle):
                    for f in ops:
                        f(handle)
                return run
            getattr(block, name)(make(ops))


class Banks:

    def __init__(self, b, name, n, cols=512, dt=F32, handles=None):
        self.b = b
        if handles is None:
            handles = [b.psum(f"{name}{i}", [128, cols], dt) for i in range(n)]
        self.banks = handles
        self.free_tok = [None] * len(handles)
        self.i = 0

    def take(self):
        i = self.i
        self.i = (self.i + 1) % len(self.banks)
        tok = self.free_tok[i]
        return i, self.banks[i], tok

    def set_free(self, i, tok):
        self.free_tok[i] = tok


def _bfh(x):
    return np.ascontiguousarray(np.asarray(x)).astype(BF)


def _f32h(x):
    return np.ascontiguousarray(np.asarray(x), dtype=np.float32)



def ln_tile(b, xs, lt, stats, mv, rstd, ld_tok, width=D):
    nq = width // 512
    for q in range(nq):
        b.op("vector", lambda e, q=q: e.bn_stats(
            stats[:, q * 6:(q + 1) * 6], xs[:, q * 512:(q + 1) * 512]),
            waits=[ld_tok] if q == 0 else ())
    b.op("vector", lambda e: e.bn_aggr(mv[:, :], stats[:, 0:6 * nq]))
    eps_ap, eps_tok = b.constcol(LN_EPS)
    b.op("scalar", lambda e: e.activation(rstd[:, :], mv[:, 1:2], AF.Sqrt,
                                          bias=eps_ap),
         waits=[b.mark("ve"), eps_tok])
    b.op("vector", lambda e: e.reciprocal(rstd[:, :], rstd[:, :]),
         waits=[b.mark("se")])
    return b.op("vector", lambda e: e.tensor_scalar(
        lt, xs, mv[:, 0:1], rstd[:, :], ALU.subtract, ALU.mult))


def ln_stream_to_fm(b, x_dram, lnx_fm, n_tok, x_stage, ln_stage, stats, mv,
                    rstd, tbanks, ident, ident_tok, fm_tpc=None,
                    extra_tile_cb=None):
    if fm_tpc is None:
        fm_tpc = n_tok
    nt = n_tok // 128
    last_use = [None, None]
    for tj in range(nt):
        sl = tj % 2
        xs = x_stage[:, sl * D:(sl + 1) * D]
        lt = ln_stage[:, sl * D:(sl + 1) * D]
        ld = b.dma("sync", xs, x_dram[tj * 128:(tj + 1) * 128, :], f"xld{sl}",
                   waits=[last_use[sl]])
        aptok = ln_tile(b, xs, lt, stats, mv, rstd, ld)
        if extra_tile_cb is not None:
            extra_tile_cb(tj, xs, ld)
        for f0 in range(0, NKC, 4):
            bi, bank, war = tbanks.take()
            ptok = None
            for g in range(4):
                fi = f0 + g
                ptok = b.pe(lambda e, s=lt[:, fi * 128:(fi + 1) * 128],
                            o=bank[:, g * 128:(g + 1) * 128]:
                            e.transpose(o, s, ident),
                            waits=([war, aptok, ident_tok] if g == 0 else ()),
                            sem="pe" if g == 3 else None)
            vts = []
            for g in range(4):
                fi = f0 + g
                vts.append(b.op(
                    "vector",
                    lambda e, o=lnx_fm[:, fi * fm_tpc + tj * 128:
                                       fi * fm_tpc + (tj + 1) * 128],
                    i=bank[:, g * 128:(g + 1) * 128]:
                    e.tensor_copy(o, i),
                    waits=[ptok] if g == 0 else ()))
            tbanks.set_free(bi, vts[-1])
        last_use[sl] = b.mark("ve")
    return b.mark("ve")


def transpose_stream(b, banks, src_view, dst_view, n_groups, ident,
                     src_tok=None, grp=4):
    for gi in range(n_groups):
        bi, bank, war = banks.take()
        ptok = None
        for g in range(grp):
            ptok = b.pe(lambda e, s=src_view(gi, g), o=bank[:, g * 128:(g + 1) * 128]:
                        e.transpose(o, s, ident),
                        waits=([war, src_tok] if g == 0 else ()),
                        sem="pe" if g == grp - 1 else None)
        vt = b.op("vector", lambda e, o=dst_view(gi), i=bank[:, 0:grp * 128]:
                  e.tensor_copy(o, i), waits=[ptok])
        banks.set_free(bi, vt)
    return b.mark("ve")


def proj_fm(b, banks, w_dram, rhs_view, n_out, out_cb, w_sb, n_k=NKC,
            obs=4, halves=(0, 1), extra_k=None):
    slab_cols = obs * 128
    slab_tok = [None, None]
    use_tok = [None, None]
    n_ex = extra_k[0] if extra_k else 0
    for ob0 in range(0, n_out, obs):
        cur = min(obs, n_out - ob0)
        sl = (ob0 // obs) % 2
        base = sl * slab_cols * (n_k + n_ex)
        b.wait("gpsimd", [use_tok[sl]])
        for k in range(n_k):
            slab_tok[sl] = b.dma(
                "gpsimd", w_sb[:, base + k * slab_cols: base + k * slab_cols + cur * 128],
                w_dram[k * 128:(k + 1) * 128, ob0 * 128:(ob0 + cur) * 128], f"wld{sl}")
        if extra_k:
            _, w2, _ = extra_k
            for k in range(n_ex):
                slab_tok[sl] = b.dma(
                    "gpsimd",
                    w_sb[:, base + (n_k + k) * slab_cols: base + (n_k + k) * slab_cols + cur * 128],
                    w2[k * 128:(k + 1) * 128, ob0 * 128:(ob0 + cur) * 128], f"wld{sl}")
        for half in halves:
            for oi in range(cur):
                bi, bank, war = banks.take()
                ptok = None
                for k in range(n_k):
                    lhs = w_sb[:, base + k * slab_cols + oi * 128:
                               base + k * slab_cols + (oi + 1) * 128]
                    w_ = []
                    if k == 0:
                        w_ = [slab_tok[sl], war]
                    last = (k == n_k - 1) and not extra_k
                    r_ap = rhs_view(k, half)
                    rw = r_ap.shape[-1]
                    ptok = b.pe(lambda e, l=lhs, r=r_ap, o=bank[:, 0:rw],
                                s=(k == 0), st=last: e.matmul(o, l, r, start=s, stop=st),
                                waits=w_, sem="pe" if last else None)
                if extra_k:
                    _, _, rhs2 = extra_k
                    for k in range(n_ex):
                        lhs = w_sb[:, base + (n_k + k) * slab_cols + oi * 128:
                                   base + (n_k + k) * slab_cols + (oi + 1) * 128]
                        last = k == n_ex - 1
                        r_ap = rhs2(k, half)
                        rw = r_ap.shape[-1]
                        ptok = b.pe(lambda e, l=lhs, r=r_ap, o=bank[:, 0:rw],
                                    st=last: e.matmul(o, l, r, start=False, stop=st),
                                    sem="pe" if last else None)
                vt = out_cb(ob0 + oi, half, bank, ptok)
                banks.set_free(bi, vt)
        use_tok[sl] = b.mark("pe")



def host_prep(inputs):
    p = {}
    p["w_qkv"] = _f32h(np.concatenate(
        [inputs["L0_attn_Wq"], inputs["L0_attn_Wk"], inputs["L0_attn_Wv"]], axis=1))
    p["w_o"] = _f32h(inputs["L0_attn_Wo"])
    p["w_org_t"] = _f32h(np.asarray(inputs["L0_ffn_W_org"]).T)
    ld = np.asarray(inputs["L0_ffn_lora_down"])
    p["w_down"] = _f32h(np.concatenate([ld[e] for e in range(NE)], axis=1))
    lu = np.asarray(inputs["L0_ffn_lora_up"])
    p["w_up"] = _f32h(np.concatenate([lu[e] for e in range(NE)], axis=0))
    p["lora_keys"] = _f32h(inputs["L0_ffn_lora_keys"])
    p["wq_mole_t"] = _f32h(np.asarray(inputs["L0_ffn_Wq"]).T)
    p["w_inproj_t"] = _f32h(np.asarray(inputs["L1_attn_in_proj_w"]).T)
    p["conv_w"] = _f32h(inputs["L1_attn_conv_weight"])
    p["conv_b"] = _f32h(inputs["L1_attn_conv_bias"]).reshape(CONV_DIM // 128, 128).T.copy()
    p["dt_bias"] = _f32h(inputs["L1_attn_dt_bias"]).reshape(1, MH)
    p["a_log"] = _f32h(inputs["L1_attn_A_log"]).reshape(1, MH)
    p["d_vec"] = _f32h(inputs["L1_attn_D"]).reshape(1, MH)
    p["norm_w"] = _f32h(inputs["L1_attn_norm_weight"]).reshape(1, MI)
    p["w_outproj_t"] = _f32h(np.asarray(inputs["L1_attn_out_proj_w"]).T)
    p["w_ffn_t"] = _f32h(np.asarray(inputs["L1_ffn_W"]).T)
    p["ffn_bias"] = _f32h(inputs["L1_ffn_bias"]).reshape(NKC, 128).T.copy()
    A = np.asarray(inputs["L1_ffn_A"])
    p["w_a_t"] = _f32h(np.concatenate([A[e].T for e in range(16)], axis=1))
    Bf = np.asarray(inputs["L1_ffn_Bf"])
    p["w_bf"] = _f32h(np.concatenate([Bf[e].T for e in range(16)], axis=0))
    p["coeffs16"] = _f32h(inputs["L1_ffn_coeffs"]).reshape(1, 16)
    p["scales16"] = _f32h(inputs["L1_ffn_scales"]).reshape(1, 16)
    p["ident_bf"] = np.eye(128, dtype=BF)
    p["conv_w_cols"] = _f32h(inputs["L1_attn_conv_weight"]).reshape(
        CONV_DIM // 128, 128, MK)
    half = DH // 2
    inv_freq = 1.0 / ROPE_THETA ** (np.arange(0, half, dtype=np.float64) / half)
    pos = np.arange(T, dtype=np.float64)
    ang = pos[:, None] * inv_freq[None, :]
    p["cos_t"] = _f32h(np.cos(ang).T)
    p["sin_t"] = _f32h(np.sin(ang).T)
    m = np.triu(np.ones((HALF, HALF), dtype=np.float32), k=1) * (-3e38 / 4)
    p["mask_qk"] = _f32h(m)
    p["binm_kq"] = np.triu(np.ones((HALF, HALF), np.float32)).astype(BF)
    p["ident_f"] = np.eye(128, dtype=np.float32)
    p["ones_col"] = np.ones((128, 1), BF)
    p["ones_row_f"] = np.ones((1, 128), np.float32)
    p["ones_row_b"] = np.ones((1, 128), BF)
    p["cosq_t"] = _f32h(p["cos_t"] / np.sqrt(128.0))
    p["sinq_t"] = _f32h(p["sin_t"] / np.sqrt(128.0))
    p["keys_t"] = _f32h(np.asarray(inputs["L0_ffn_lora_keys"]).T)
    exp8 = np.zeros((8, 512), np.float32)
    for j in range(4):
        for q in range(128):
            exp8[2 * j + q // 64, j * 128 + q] = 1.0
    p["exp8"] = exp8
    exp16 = np.zeros((16, 256), np.float32)
    for j in range(2):
        for q in range(128):
            exp16[(j * 128 + q) // 16, j * 128 + q] = 1.0
    p["exp16"] = exp16
    wp = np.concatenate([np.asarray(inputs["L1_attn_in_proj_w"]),
                         np.zeros((NPC * 128 - PROJ, D), np.float32)], axis=0)
    p["w_inproj_p"] = _f32h(wp.T)
    cw = _f32h(inputs["L1_attn_conv_weight"])
    p["conv_w_c"] = np.ascontiguousarray(
        cw.reshape(48, 128, 4).transpose(1, 0, 2).reshape(128, 192))
    p["conv_b_c"] = np.ascontiguousarray(
        _f32h(inputs["L1_attn_conv_bias"]).reshape(48, 128).T)
    p["dt_bias_c"] = _f32h(inputs["L1_attn_dt_bias"]).reshape(64, 1)
    p["a_log_c"] = _f32h(inputs["L1_attn_A_log"]).reshape(64, 1)
    p["d_row"] = _f32h(np.repeat(np.asarray(inputs["L1_attn_D"]), 64)).reshape(1, 4096)
    rep = np.zeros((64, 4096), BF)
    for h in range(64):
        rep[h, h * 64:(h + 1) * 64] = 1
    p["rep"] = rep
    p["maskneg128"] = _f32h(np.tril(np.ones((128, 128), np.float32), -1) * (-3e38 / 4))
    p["norm_row"] = _f32h(inputs["L1_attn_norm_weight"]).reshape(1, 4096)
    p["bias_c"] = np.ascontiguousarray(
        _f32h(inputs["L1_ffn_bias"]).reshape(NKC, 128).T)
    return p



def build_l1():
    nc = bass.Bass("TRN2", target_bir_lowering=False, debug=False)
    ctx = ExitStack()
    b = Bld(nc, ctx)
    x_d = nc.dram_tensor("x", [TPC, D], F32, kind="ExternalInput").ap()
    wqkv_d = nc.dram_tensor("w_qkv", [D, 6144], F32, kind="ExternalInput").ap()
    cos_d = nc.dram_tensor("cos_c", [64, TPC], F32, kind="ExternalInput").ap()
    sin_d = nc.dram_tensor("sin_c", [64, TPC], F32, kind="ExternalInput").ap()
    cosq_d = nc.dram_tensor("cosq_c", [64, TPC], F32, kind="ExternalInput").ap()
    sinq_d = nc.dram_tensor("sinq_c", [64, TPC], F32, kind="ExternalInput").ap()
    ident_d = nc.dram_tensor("ident", [128, 128], BF16, kind="ExternalInput").ap()
    qkv_d = nc.dram_tensor("qkv_fm", [6144, TPC], BF16, kind="ExternalOutput").ap()
    vtm_d = nc.dram_tensor("v_tm", [TPC, 1024], BF16, kind="ExternalOutput").ap()

    NT = TPC // 128
    x_stage = b.sb("x_stage", [128, 2 * D], F32)
    ln_stage = b.sb("ln_stage", [128, 2 * D], BF16)
    stats = b.sb("stats", [128, 48], F32)
    mv = b.sb("mv", [128, 2], F32)
    rstd = b.sb("rstd", [128, 1], F32)
    lnx_fm = b.sb("lnx_fm", [128, NKC * TPC], BF16)
    ident = b.sb("ident_sb", [128, 128], BF16)
    cos_sb = b.sb("cos_sb", [64, TPC], BF16)
    sin_sb = b.sb("sin_sb", [64, TPC], BF16)
    cosq_sb = b.sb("cosq_sb", [64, TPC], BF16)
    sinq_sb = b.sb("sinq_sb", [64, TPC], BF16)
    w_sb = b.sb("w_sb", [128, 2 * 4 * 128 * NKC], BF16)
    stage = b.sb("stage", [128, 4 * HALF], BF16)
    ra = b.sb("ra", [64, HALF], F32)
    rb = b.sb("rb", [64, HALF], F32)
    rc = b.sb("rc", [64, HALF], F32)
    rd_ = b.sb("rd", [64, HALF], F32)
    vstage = b.sb("vstage", [128, 4 * HALF], BF16)

    b.dma("sync", ident[:, :], ident_d[:, :], "cld")
    b.dma("gpsimd", cos_sb[:, :], cos_d[:, :], "cldg")
    b.dma("gpsimd", sin_sb[:, :], sin_d[:, :], "cldg")
    b.dma("gpsimd", cosq_sb[:, :], cosq_d[:, :], "cldg")
    b.dma("gpsimd", sinq_sb[:, :], sinq_d[:, :], "cldg")
    c0 = b.mark("cld")
    c1 = b.mark("cldg")
    b.wait("vector", [c1])

    banks = Banks(b, "bk", 6)
    tbanks = Banks(b, "tb", 2, dt=BF16)

    ident_ap = ident[:, :]
    lnfm_done = ln_stream_to_fm(b, x_d, lnx_fm, TPC, x_stage, ln_stage, stats,
                                mv, rstd, tbanks, ident_ap, c0)
    b.wait("tensor", [lnfm_done])

    stage_i = [0]
    stage_free = [None] * 4
    vstage_free = [None] * 4

    def rhs_view(k, half):
        return lnx_fm[:, k * TPC + half * HALF: k * TPC + (half + 1) * HALF]

    def out_cb(oi, half, bank, ptok):
        si = stage_i[0] % 4
        stage_i[0] += 1
        stg = stage[:, si * HALF:(si + 1) * HALF]
        if oi < NH + NKV:
            csrc = cosq_sb if oi < NH else cos_sb
            ssrc = sinq_sb if oi < NH else sin_sb
            cs = csrc[:, half * HALF:(half + 1) * HALF]
            sn = ssrc[:, half * HALF:(half + 1) * HALF]
            b.op("vector", lambda e: e.tensor_tensor(ra[:, :], bank[0:64, :], cs, ALU.mult),
                 waits=[ptok, c1])
            b.op("vector", lambda e: e.tensor_tensor(rb[:, :], bank[64:128, :], sn, ALU.mult))
            b.op("vector", lambda e: e.tensor_tensor(rc[:, :], bank[0:64, :], sn, ALU.mult))
            last_ps = b.op("vector", lambda e: e.tensor_tensor(rd_[:, :], bank[64:128, :], cs, ALU.mult))
            b.op("vector", lambda e, stg=stg: e.tensor_tensor(stg[0:64, :], ra[:, :], rb[:, :], ALU.subtract),
                 waits=[stage_free[si]])
            vt = b.op("vector", lambda e, stg=stg: e.tensor_tensor(stg[64:128, :], rc[:, :], rd_[:, :], ALU.add))
            bank_free = last_ps
        else:
            vt = b.op("vector", lambda e, stg=stg: e.tensor_copy(stg, bank[:, :]),
                      waits=[ptok, stage_free[si]])
            bank_free = vt
        dtok = b.dma("sync", qkv_d[oi * 128:(oi + 1) * 128, half * HALF:(half + 1) * HALF],
                     stg, f"so{si}", waits=[vt])
        stage_free[si] = dtok
        if oi >= NH + NKV:
            vi = oi - (NH + NKV)
            bi2, bank2, war2 = tbanks.take()
            ptok2 = None
            for g in range(4):
                ptok2 = b.pe(lambda e, s=stg[:, g * 128:(g + 1) * 128],
                             o=bank2[:, g * 128:(g + 1) * 128]:
                             e.transpose(o, s, ident_ap),
                             waits=([war2, vt] if g == 0 else ()),
                             sem="pe" if g == 3 else None)
            sj = stage_i[0] % 4
            stage_i[0] += 1
            vstg = vstage[:, sj * HALF:(sj + 1) * HALF]
            vt2 = b.op("vector", lambda e, vstg=vstg: e.tensor_copy(vstg, bank2[:, :]),
                       waits=[ptok2, vstage_free[sj]])
            tbanks.set_free(bi2, vt2)
            d2 = None
            for g in range(4):
                d2 = b.dma("sync",
                           vtm_d[half * HALF + g * 128: half * HALF + (g + 1) * 128,
                                 vi * 128:(vi + 1) * 128],
                           vstg[:, g * 128:(g + 1) * 128], f"sv{sj}", waits=[vt2] if g == 0 else ())
            vstage_free[sj] = d2
        return bank_free

    proj_fm(b, banks, wqkv_d, rhs_view, 48, out_cb, w_sb)

    b.wait("sync", [b.mark(s) for s in list(b.cnt) if s.startswith(("so", "sv"))])
    b.finalize()
    return nc, ctx


def run_l1(inputs, prep):
    nc, ctx = build_l1()
    in_maps = []
    X = np.asarray(inputs["L0_attn_X"])
    for core in range(8):
        bb, cc = core // 4, core % 4
        sl = slice(cc * TPC, (cc + 1) * TPC)
        in_maps.append({
            "x": _f32h(X[bb, sl, :]),
            "w_qkv": prep["w_qkv"],
            "cos_c": prep["cos_t"][:, sl].copy(),
            "sin_c": prep["sin_t"][:, sl].copy(),
            "ident": prep["ident_bf"],
        })
    res = run_bass_kernel_spmd(nc, in_maps, core_ids=list(range(8)))
    ctx.close()
    return res.results






def proj_to_scr(b, banks, w_dram, rhs_view, n_out, scr_d, w_sb, wstage,
                sem_pfx="wo", obs=4, extra_k=None, evac_fn=None,
                halves=(0, 1), hw=HALF):
    stage_free = [None] * 4
    wi = [0]

    def cb(oi, half, bank, ptok):
        si = wi[0] % 4
        wi[0] += 1
        stg = wstage[:, si * hw:(si + 1) * hw]
        if evac_fn is None:
            vt = b.op("vector", lambda e, stg=stg, w=hw: e.tensor_copy(
                stg, bank[:, 0:w]), waits=[ptok, stage_free[si]])
        else:
            vt = evac_fn(oi, half, bank, stg, [ptok, stage_free[si]])
        dt_ = b.dma("sync", scr_d[oi * 128:(oi + 1) * 128,
                                  half * hw:(half + 1) * hw],
                    stg, f"{sem_pfx}{si}", waits=[vt])
        stage_free[si] = dt_
        return vt

    proj_fm(b, banks, w_dram, rhs_view, n_out, cb, w_sb, obs=obs,
            extra_k=extra_k, halves=halves)
    return [b.mark(s) for s in list(b.cnt) if s.startswith(sem_pfx)]


def resid_tile_loop(b, scr_done, scr_d, x_in_d, xout_d, ident, ttb, scr_tile,
                    x_stage, ln_extras=None):
    NT = TPC // 128
    d1 = None
    pm_started = [False]
    for tt in range(NT):
        wl = None
        for o in range(NKC):
            wl = b.dma("sync", scr_tile[:, o * 128:(o + 1) * 128],
                       scr_d[o * 128:(o + 1) * 128, tt * 128:(tt + 1) * 128],
                       "wtl", waits=(scr_done + [b.mark("pe")]) if o == 0 else ())
        xl = b.dma("sync", x_stage[:, :], x_in_d[tt * 128:(tt + 1) * 128, :],
                   "xl", waits=[b.mark("ve"), d1])
        for f0 in range(0, NKC, 4):
            bi, bank, war = ttb.take()
            ptk = None
            for g in range(4):
                fi = f0 + g
                ptk = b.pe(lambda e, s=scr_tile[:, fi * 128:(fi + 1) * 128],
                           o=bank[:, g * 128:(g + 1) * 128], i=ident[:, :]:
                           e.transpose(o, s, i),
                           waits=([war, wl] if g == 0 else ()), sem="pe")
            vt = b.op("vector", lambda e, o=x_stage[:, f0 * 128:(f0 + 4) * 128],
                      i=bank[:, 0:512]: e.tensor_tensor(o, o, i, ALU.add),
                      waits=[ptk, xl])
            ttb.set_free(bi, vt)
        x1t = b.mark("ve")
        d1 = b.dma("sync", xout_d[tt * 128:(tt + 1) * 128, :], x_stage[:, :],
                   "x1s", waits=[x1t])
        if ln_extras is not None:
            (stats, mv, rstd, ln_t, ones_sb, pmean_ps, pm_stage, pm_d) = ln_extras
            b.wait("vector", [b.mark("pe")])
            ap2 = ln_tile(b, x_stage[:, :], ln_t[:, :], stats, mv, rstd, None)
            b.wait("tensor", [ap2])
            for fi in range(NKC):
                b.pe(lambda e, l=ln_t[:, fi * 128:(fi + 1) * 128],
                     o=pmean_ps[:, fi:fi + 1], s=(not pm_started[0]),
                     st=(tt == NT - 1 and fi == NKC - 1):
                     e.matmul(o, l, ones_sb[:, :], start=s, stop=st,
                              skip_group_check=True),
                     sem="pe" if fi == NKC - 1 else None)
                pm_started[0] = True
    if ln_extras is not None:
        (stats, mv, rstd, ln_t, ones_sb, pmean_ps, pm_stage, pm_d) = ln_extras
        pmt = b.op("vector", lambda e: e.tensor_copy(pm_stage[:, :],
                                                     pmean_ps[:, :]),
                   waits=[b.mark("pe")])
        b.dma("sync", pm_d[:, :], pm_stage[:, :], "pms", waits=[pmt])



def attn_unit(b, env, q_ap_full, k_sb, v_sb, out_write, qb):
    (mm, acc_bank, pb3, mask_sb, binm_sb, s_buf, pj, mq_sb, dq_sb, negm,
     mcol, nmcol, ccol, crow, mrow, msc, cbc, pt_sb, identf, onesr_sb,
     neginf_ap, neginf_tok) = env
    g0 = qb * HALF
    nkb = qb + 1
    nkc = nkb * 4
    dkb = qb
    b.wait("vector", [b.mark("se")])
    for qt in range(4):
        mtile = mask_sb[:, qt * HALF:(qt + 1) * HALF]
        qa = q_ap_full[:, qb * HALF + qt * 128: qb * HALF + (qt + 1) * 128]
        for kb in range(nkb):
            bi, bank, war = mm.take()
            pt = b.pe(lambda e, qa=qa, rr=k_sb[:, kb * HALF:(kb + 1) * HALF],
                      o=bank[:, :]: e.matmul(o, qa, rr, start=True, stop=True),
                      waits=[war], sem="pe")
            sb_sl = s_buf[:, kb * HALF:(kb + 1) * HALF]
            w0 = [pt, neginf_tok] + ([b.mark("se")] if kb == 0 else [])
            seed = mq_sb[:, qt:qt + 1] if kb > 0 else neginf_ap
            if kb == dkb:
                b.op("vector", lambda e, o=sb_sl, i=bank[:, :], m=mtile:
                     e.tensor_tensor(o, i, m, ALU.add), waits=w0)
                vt = b.op("vector", lambda e, o=sb_sl, seed=seed, qt=qt:
                          e.tensor_scalar(o, o, 0.0, seed, ALU.add, ALU.max,
                                          accum_out=mq_sb[:, qt:qt + 1]))
            else:
                vt = b.op("vector", lambda e, o=sb_sl, i=bank[:, :], seed=seed,
                          qt=qt: e.tensor_scalar(o, i, 0.0, seed, ALU.add,
                                                 ALU.max,
                                                 accum_out=mq_sb[:, qt:qt + 1]),
                          waits=w0)
            mm.set_free(bi, vt)
        b.op("vector", lambda e, qt=qt: e.tensor_scalar(
            negm[:, :], mq_sb[:, qt:qt + 1], -1.0, None, ALU.mult, ALU.bypass),
            waits=[b.mark("se")])
        b.op("scalar", lambda e, qt=qt, n=nkb * HALF: e.activation(
            pj[:, 0:n], s_buf[:, 0:n], AF.Exp, bias=negm[:, :],
            accum_out=dq_sb[:, qt:qt + 1]), waits=[b.mark("ve")])
    for qt in range(4):
        ptc = b.pe(lambda e, qt=qt: e.transpose(
            pb3[0:1, 0:128], mq_sb[:, qt:qt + 1], identf[:, :]),
            waits=[b.mark("ve"), b.mark("se")], sem="pe")
        b.op("vector", lambda e, qt=qt: e.reduce_max(
            mrow[:, qt:qt + 1], pb3[0:1, 0:128],
            axis=mybir.AxisListType.X), waits=[ptc])
    b.op("vector", lambda e: e.reduce_max(msc[:, 0:1], mrow[:, 0:4],
                                          axis=mybir.AxisListType.X))
    pmb = b.pe(lambda e: e.matmul(pb3[:, 128:129], onesr_sb[:, :],
                                  msc[:, 0:1], start=True, stop=True),
               waits=[b.mark("ve")], sem="pe")
    b.op("vector", lambda e: e.tensor_copy(mcol[:, :], pb3[:, 128:129]),
         waits=[pmb, b.mark("se")])
    b.op("vector", lambda e: e.tensor_scalar(
        nmcol[:, :], mcol[:, :], -1.0, None, ALU.mult, ALU.bypass))
    mcol_tok = b.mark("ve")
    for qt in range(4):
        b.op("scalar", lambda e, qt=qt: e.activation(
            mq_sb[:, 4 + qt:5 + qt], mq_sb[:, qt:qt + 1], AF.Exp,
            bias=mcol[:, :], scale=-1.0), waits=[mcol_tok])
        b.op("vector", lambda e, qt=qt: e.reciprocal(
            dq_sb[:, 4 + qt:5 + qt], dq_sb[:, qt:qt + 1]),
            waits=[b.mark("se")])
        b.op("vector", lambda e, qt=qt: e.tensor_tensor(
            ccol[:, 0:1], mq_sb[:, 4 + qt:5 + qt], dq_sb[:, 4 + qt:5 + qt],
            ALU.mult))
        ptc2 = b.pe(lambda e: e.transpose(pb3[0:1, 256:384], ccol[:, 0:1],
                                          identf[:, :]),
                    waits=[b.mark("ve")], sem="pe")
        b.op("vector", lambda e, qt=qt: e.tensor_copy(
            crow[:, qt * 128:(qt + 1) * 128], pb3[0:1, 256:384]),
            waits=[ptc2])
    cbb = b.pe(lambda e: e.matmul(pb3[:, :], onesr_sb[:, :], crow[:, :],
                                  start=True, stop=True),
               waits=[b.mark("ve")], sem="pe")
    cbt = b.op("vector", lambda e: e.tensor_copy(cbc[:, :], pb3[:, :]),
               waits=[cbb])
    pt_free = [None, None]
    av_tok = None
    for kc in range(nkc):
        bi, bank, war = mm.take()
        stok = b.pe(lambda e, l=k_sb[:, kc * 128:(kc + 1) * 128],
                    r=q_ap_full[:, qb * HALF:(qb + 1) * HALF],
                    o=bank[:, :]: e.matmul(o, l, r, start=True, stop=True),
                    waits=[war], sem="pe")
        sl = kc % 2
        pt_ap = pt_sb[:, sl * HALF:(sl + 1) * HALF]
        et = b.op("scalar", lambda e, o=pt_ap, i=bank[:, :]: e.activation(
            o, i, AF.Exp, bias=nmcol[:, :]),
            waits=[stok, pt_free[sl], mcol_tok])
        bank_reader = et
        if kc >= dkb * 4:
            r2i = kc - dkb * 4
            et = b.op("vector", lambda e, o=pt_ap,
                      m=binm_sb[:, r2i * HALF:(r2i + 1) * HALF]:
                      e.tensor_tensor(o, o, m, ALU.mult), waits=[et])
        av_tok = b.pe(lambda e, l=v_sb[:, kc * 128:(kc + 1) * 128],
                      r=pt_ap, o=acc_bank[:, :], s=(kc == 0),
                      st=(kc == nkc - 1):
                      e.matmul(o, l, r, start=s, stop=st,
                               skip_group_check=True),
                      waits=[et], sem="pe")
        pt_free[sl] = av_tok
        mm.set_free(bi, bank_reader)
    return out_write(acc_bank, cbc, av_tok, cbt)


def build_l2a():
    nc = bass.Bass("TRN2", target_bir_lowering=False, debug=False)
    ctx = ExitStack()
    b = Bld(nc, ctx)
    q_d = nc.dram_tensor("q_fm", [512, 2 * T], BF16, kind="ExternalInput").ap()
    k_d = nc.dram_tensor("k_fm", [128, 2 * T], BF16, kind="ExternalInput").ap()
    v_d = nc.dram_tensor("v_tm", [2 * T, 128], BF16, kind="ExternalInput").ap()
    mask_d = nc.dram_tensor("mask_qk", [HALF, HALF], F32, kind="ExternalInput").ap()
    binm_d = nc.dram_tensor("binm_kq", [HALF, HALF], BF16, kind="ExternalInput").ap()
    identf_d = nc.dram_tensor("ident_f", [128, 128], F32, kind="ExternalInput").ap()
    onesr_d = nc.dram_tensor("ones_row_f", [1, 128], F32, kind="ExternalInput").ap()
    attn_d = nc.dram_tensor("attn_fm", [512, 2 * T], BF16, kind="ExternalOutput").ap()

    q_sb = b.sb("q_sb", [128, 8 * T], BF16)
    k_sb = b.sb("k_sb", [128, 2 * T], BF16)
    v_sb = b.sb("v_sb", [128, 2 * T], BF16)
    s_buf = b.sb("s_buf", [128, T], BF16)
    pj = b.sb("pj", [128, T], BF16)
    mq_sb = b.sb("mq_sb", [128, 8], F32)
    dq_sb = b.sb("dq_sb", [128, 8], F32)
    negm = b.sb("negm", [128, 1], F32)
    mask_sb = b.sb("mask_sb", [128, 4 * HALF], F32)
    binm_sb = b.sb("binm_sb", [128, 4 * HALF], BF16)
    identf = b.sb("identf_sb", [128, 128], F32)
    onesr_sb = b.sb("onesr_sb", [1, 128], F32)
    ccol = b.sb("ccol", [128, 1], F32)
    crow = b.sb("crow", [1, HALF], F32)
    mrow = b.sb("mrow", [1, 8], F32)
    msc = b.sb("msc", [1, 2], F32)
    mcol = b.sb("mcol", [128, 1], F32)
    nmcol = b.sb("nmcol", [128, 1], F32)
    cbc = b.sb("cbc", [128, HALF], F32)
    pt_sb = b.sb("pt_sb", [128, 2 * HALF], BF16)

    pb = [b.psum(f"pb{i}", [128, 512], F32) for i in range(4)]
    mm = Banks(b, "mm", 0, handles=pb[0:2])

    b.dma("sync", identf[:, :], identf_d[:, :], "cld")
    b.dma("sync", onesr_sb[:, :], onesr_d[:, :], "cld")
    for r in range(4):
        b.dma("sync", mask_sb[:, r * HALF:(r + 1) * HALF],
              mask_d[r * 128:(r + 1) * 128, :], "cld")
        b.dma("sync", binm_sb[:, r * HALF:(r + 1) * HALF],
              binm_d[r * 128:(r + 1) * 128, :], "cld")
    b.dma("sync", k_sb[:, :], k_d[:, :], "cld")
    b.dma("sync", v_sb[:, :].rearrange("p (t d) -> p t d", d=128),
          v_d[:, :].rearrange("(t p) d -> p t d", p=128), "cld")
    for ch in range(8):
        b.dma("sync", q_sb[:, ch * T:(ch + 1) * T],
              q_d[(ch // 2) * 128:(ch // 2 + 1) * 128,
                  (ch % 2) * T:(ch % 2 + 1) * T], "cld")
    consts = b.mark("cld")
    b.wait("tensor", [consts])
    b.wait("vector", [consts])
    b.wait("scalar", [consts])
    neginf_ap, neginf_tok = b.constcol(-3.0e38)

    env = (mm, pb[2], pb[3], mask_sb, binm_sb, s_buf, pj, mq_sb, dq_sb, negm,
           mcol, nmcol, ccol, crow, mrow, msc, cbc, pt_sb, identf, onesr_sb,
           neginf_ap, neginf_tok)

    for ch in range(8):
        qh, bt = ch // 2, ch % 2
        q_full = q_sb[:, ch * T:(ch + 1) * T]
        kk = k_sb[:, bt * T:(bt + 1) * T]
        vv = v_sb[:, bt * T:(bt + 1) * T]
        for qb in range(8):
            def out_write(acc_bank, cbc_, av_tok, cbt, ch=ch, qb=qb):
                wt = b.op("vector", lambda e,
                          o=q_sb[:, ch * T + qb * HALF: ch * T + (qb + 1) * HALF],
                          i=acc_bank[:, :]: e.tensor_tensor(o, i, cbc_[:, :],
                                                            ALU.mult),
                          waits=[av_tok, cbt])
                b.dma("sync", attn_d[(ch // 2) * 128:(ch // 2 + 1) * 128,
                                     (ch % 2) * T + qb * HALF:
                                     (ch % 2) * T + (qb + 1) * HALF],
                      q_sb[:, ch * T + qb * HALF: ch * T + (qb + 1) * HALF],
                      "aso", waits=[wt])
                return wt
            attn_unit(b, env, q_full, kk, vv, out_write, qb)

    b.wait("sync", [b.mark("aso")])
    b.finalize()
    return nc, ctx



def build_l2b():
    nc = bass.Bass("TRN2", target_bir_lowering=False, debug=False)
    ctx = ExitStack()
    b = Bld(nc, ctx)
    a_d = nc.dram_tensor("attn_fm", [4096, TPC], BF16, kind="ExternalInput").ap()
    x_d = nc.dram_tensor("x", [TPC, D], F32, kind="ExternalInput").ap()
    wo_d = nc.dram_tensor("w_o", [D, D], F32, kind="ExternalInput").ap()
    ident_d = nc.dram_tensor("ident", [128, 128], BF16, kind="ExternalInput").ap()
    ones_d = nc.dram_tensor("ones_col", [128, 1], BF16, kind="ExternalInput").ap()
    x1_d = nc.dram_tensor("x1", [TPC, D], F32, kind="ExternalOutput").ap()
    pm_d = nc.dram_tensor("pmean", [128, NKC], F32, kind="ExternalOutput").ap()
    wo_scr = nc.dram_tensor("wo_scr", [D, TPC], BF16)

    a_sb = b.sb("a_sb", [128, NKC * TPC], BF16)
    ident = b.sb("ident_sb", [128, 128], BF16)
    ones_sb = b.sb("ones_sb", [128, 1], BF16)
    w_sb = b.sb("w_sb", [128, 2 * 4 * 128 * NKC], BF16)
    wstage = b.sb("wstage", [128, 4 * HALF], BF16)
    wo_tile = b.sb("wo_tile", [128, D], BF16)
    x_stage = b.sb("x_stage", [128, D], F32)
    stats = b.sb("stats", [128, 48], F32)
    mv = b.sb("mv", [128, 2], F32)
    rstd = b.sb("rstd", [128, 1], F32)
    ln_t = b.sb("ln_t", [128, D], BF16)
    pm_stage = b.sb("pm_stage", [128, NKC], F32)

    pb = [b.psum(f"pb{i}", [128, 512], F32) for i in range(4)]
    tbp = [b.psum(f"tbp{i}", [128, 512], BF16) for i in range(2)]
    pmean_ps = b.psum("pmean_ps", [128, NKC], F32)

    b.dma("sync", ident[:, :], ident_d[:, :], "cld")
    b.dma("sync", ones_sb[:, :], ones_d[:, :], "cld")
    for ch in range(NKC):
        b.dma("sync", a_sb[:, ch * TPC:(ch + 1) * TPC],
              a_d[ch * 128:(ch + 1) * 128, :], "cld")
    consts = b.mark("cld")
    b.wait("tensor", [consts])
    b.wait("vector", [consts])
    b.wait("scalar", [consts])

    wo_banks = Banks(b, "wob", 0, handles=pb[0:4])

    def wo_rhs(k, half):
        return a_sb[:, k * TPC + half * HALF: k * TPC + half * HALF + HALF]

    wo_done = proj_to_scr(b, wo_banks, wo_d, wo_rhs, NKC, wo_scr, w_sb, wstage)
    ttb = Banks(b, "ttb", 0, handles=tbp)
    resid_tile_loop(b, wo_done, wo_scr, x_d, x1_d, ident, ttb, wo_tile, x_stage,
                    ln_extras=(stats, mv, rstd, ln_t, ones_sb, pmean_ps,
                               pm_stage, pm_d))
    b.wait("sync", [b.mark(s) for s in list(b.cnt)
                    if s.startswith(("x1s", "pms"))])
    b.finalize()
    return nc, ctx


def build_l3():
    nc = bass.Bass("TRN2", target_bir_lowering=False, debug=False)
    ctx = ExitStack()
    b = Bld(nc, ctx)
    x1_d = nc.dram_tensor("x1", [TPC, D], F32, kind="ExternalInput").ap()
    pms_d = nc.dram_tensor("pmeans", [128, 4 * NKC], F32, kind="ExternalInput").ap()
    worg_d = nc.dram_tensor("w_org_t", [D, D], F32, kind="ExternalInput").ap()
    wdown_d = nc.dram_tensor("w_down", [D, 512], F32, kind="ExternalInput").ap()
    wup_d = nc.dram_tensor("w_up", [512, D], F32, kind="ExternalInput").ap()
    keys_d = nc.dram_tensor("keys_t", [128, 8], F32, kind="ExternalInput").ap()
    wqm_d = nc.dram_tensor("wq_mole_t", [D, 128], F32, kind="ExternalInput").ap()
    exp8_d = nc.dram_tensor("exp8", [8, 512], F32, kind="ExternalInput").ap()
    ident_d = nc.dram_tensor("ident", [128, 128], BF16, kind="ExternalInput").ap()
    identf_d = nc.dram_tensor("ident_f", [128, 128], F32, kind="ExternalInput").ap()
    x2_d = nc.dram_tensor("x2", [TPC, D], F32, kind="ExternalOutput").ap()
    scr_d = nc.dram_tensor("ffn_scr", [D, TPC], BF16)

    lnx_fm = b.sb("lnx_fm", [128, NKC * TPC], BF16)
    h_fm = b.sb("h_fm", [128, 4 * TPC], BF16)
    x_stage = b.sb("x_stage", [128, 2 * D], F32)
    ln_stage = b.sb("ln_stage", [128, 2 * D], BF16)
    stats = b.sb("stats", [128, 48], F32)
    mv = b.sb("mv", [128, 2], F32)
    rstd = b.sb("rstd", [128, 1], F32)
    ident = b.sb("ident_sb", [128, 128], BF16)
    identf = b.sb("identf_sb", [128, 128], F32)
    keys_sb = b.sb("keys_sb", [128, 8], BF16)
    wqm_sb = b.sb("wqm_sb", [128, NKC * 128], BF16)
    exp8_sb = b.sb("exp8_sb", [8, 512], F32)
    pms_sb = b.sb("pms_sb", [128, 4 * NKC], F32)
    mean_sb = b.sb("mean_sb", [128, NKC], F32)
    meanb_sb = b.sb("meanb_sb", [128, NKC], BF16)
    qv_sb = b.sb("qv_sb", [128, 1], BF16)
    scol = b.sb("scol", [8, 1], F32)
    srow = b.sb("srow", [1, 8], F32)
    urow = b.sb("urow", [1, 8], F32)
    smax = b.sb("smax", [1, 2], F32)
    sexp = b.sb("sexp", [1, 8], F32)
    ssum = b.sb("ssum", [1, 2], F32)
    crow8 = b.sb("crow8", [1, 8], F32)
    ccolv = b.sb("ccolv", [8, 1], F32)
    ecol = b.sb("ecol", [128, 4], F32)
    w_sb = b.sb("w_sb", [128, 2 * 2 * 128 * (NKC + 4)], BF16)
    wstage = b.sb("wstage", [128, 4 * HALF], BF16)
    scr_tile = b.sb("scr_tile", [128, D], BF16)

    pb = [b.psum(f"pb{i}", [128, 512], F32) for i in range(4)]
    tbp = [b.psum(f"tbp{i}", [128, 512], BF16) for i in range(2)]
    sm_ps = b.psum("sm_ps", [128, 16], F32)

    b.dma("sync", ident[:, :], ident_d[:, :], "cld")
    b.dma("sync", identf[:, :], identf_d[:, :], "cld")
    b.dma("gpsimd", keys_sb[:, :], keys_d[:, :], "cldg")
    b.dma("gpsimd", wqm_sb[:, :].rearrange("p (k q) -> p k q", q=128),
          wqm_d[:, :].rearrange("(k p) q -> p k q", p=128), "cldg")
    b.dma("sync", exp8_sb[:, :], exp8_d[:, :], "cld")
    b.dma("sync", pms_sb[:, :], pms_d[:, :], "cld")
    consts = b.mark("cld")
    constsg = b.mark("cldg")
    b.wait("tensor", [consts, constsg])
    b.wait("vector", [consts, constsg])
    b.wait("scalar", [consts, constsg])

    tbanks = Banks(b, "tbk", 0, handles=tbp)
    lnfm_done = ln_stream_to_fm(b, x1_d, lnx_fm, TPC, x_stage, ln_stage, stats,
                                mv, rstd, tbanks, ident[:, :], consts)
    b.wait("tensor", [lnfm_done])

    b.op("vector", lambda e: e.tensor_tensor(
        mean_sb[:, :], pms_sb[:, 0:NKC], pms_sb[:, NKC:2 * NKC], ALU.add))
    b.op("vector", lambda e: e.tensor_tensor(
        mean_sb[:, :], mean_sb[:, :], pms_sb[:, 2 * NKC:3 * NKC], ALU.add))
    b.op("vector", lambda e: e.tensor_tensor(
        mean_sb[:, :], mean_sb[:, :], pms_sb[:, 3 * NKC:4 * NKC], ALU.add))
    b.op("vector", lambda e: e.tensor_scalar(
        meanb_sb[:, :], mean_sb[:, :], 1.0 / T, None, ALU.mult, ALU.bypass))
    mtok = b.mark("ve")
    qtok = None
    for k in range(NKC):
        qtok = b.pe(lambda e, l=wqm_sb[:, k * 128:(k + 1) * 128],
                    r=meanb_sb[:, k:k + 1], o=sm_ps[:, 0:1],
                    s=(k == 0), st=(k == NKC - 1):
                    e.matmul(o, l, r, start=s, stop=st, skip_group_check=True),
                    waits=[mtok] if k == 0 else (),
                    sem="pe" if k == NKC - 1 else None)
    b.op("vector", lambda e: e.tensor_copy(qv_sb[:, :], sm_ps[:, 0:1]),
         waits=[qtok])
    stok = b.pe(lambda e: e.matmul(sm_ps[0:8, 1:2], keys_sb[:, :], qv_sb[:, :],
                                   start=True, stop=True, skip_group_check=True),
                waits=[b.mark("ve")], sem="pe")
    b.op("vector", lambda e: e.tensor_copy(scol[:, :], sm_ps[0:8, 1:2]),
         waits=[stok])
    t1 = b.pe(lambda e: e.transpose(sm_ps[0:1, 2:10], scol[:, :],
                                    identf[0:8, 0:8]),
              waits=[b.mark("ve")], sem="pe")
    b.op("vector", lambda e: e.tensor_copy(srow[:, :], sm_ps[0:1, 2:10]),
         waits=[t1])
    b.op("vector", lambda e: e.tensor_scalar(
        urow[:, :], srow[:, :], 1.0 / np.sqrt(128.0), None, ALU.mult, ALU.bypass))
    b.op("vector", lambda e: e.reduce_max(smax[:, 0:1], urow[:, :],
                                          axis=mybir.AxisListType.X))
    b.op("vector", lambda e: e.tensor_scalar(
        smax[:, 1:2], smax[:, 0:1], -1.0, None, ALU.mult, ALU.bypass))
    b.op("scalar", lambda e: e.activation(sexp[:, :], urow[:, :], AF.Exp,
                                          bias=smax[:, 1:2],
                                          accum_out=ssum[:, 0:1]),
         waits=[b.mark("ve")])
    b.op("vector", lambda e: e.reciprocal(ssum[:, 1:2], ssum[:, 0:1]),
         waits=[b.mark("se")])
    b.op("vector", lambda e: e.tensor_scalar(
        crow8[:, :], sexp[:, :], ssum[:, 1:2], None, ALU.mult, ALU.bypass))
    t2 = b.pe(lambda e: e.transpose(sm_ps[0:8, 10:11], crow8[:, :],
                                    identf[0:1, 0:1]),
              waits=[b.mark("ve")], sem="pe")
    b.op("vector", lambda e: e.tensor_copy(ccolv[:, :], sm_ps[0:8, 10:11]),
         waits=[t2])
    ec_tok = None
    for j in range(4):
        ej = b.pe(lambda e, j=j: e.matmul(sm_ps[:, 11 + j:12 + j],
                                          exp8_sb[:, j * 128:(j + 1) * 128],
                                          ccolv[:, :], start=True, stop=True,
                                          skip_group_check=True),
                  waits=[b.mark("ve")] if j == 0 else (), sem="pe")
        ec_tok = b.op("vector", lambda e, j=j: e.tensor_copy(
            ecol[:, j:j + 1], sm_ps[:, 11 + j:12 + j]), waits=[ej])

    hbanks = Banks(b, "hbk", 0, handles=pb[0:4])

    def lnx_rhs(k, half):
        return lnx_fm[:, k * TPC + half * HALF: k * TPC + half * HALF + HALF]

    def h_cb(oi, half, bank, ptok):
        vt = b.op("vector", lambda e, o=h_fm[:, oi * TPC + half * HALF:
                                             oi * TPC + half * HALF + HALF],
                  i=bank[:, :], j=oi: e.tensor_scalar(
                      o, i, ecol[:, j:j + 1], None, ALU.mult, ALU.bypass),
                  waits=[ptok, ec_tok])
        return vt

    proj_fm(b, hbanks, wdown_d, lnx_rhs, 4, h_cb, w_sb, obs=2)
    h_done = b.mark("ve")
    b.wait("tensor", [h_done])

    def h_rhs(k, half):
        return h_fm[:, k * TPC + half * HALF: k * TPC + half * HALF + HALF]

    obanks = Banks(b, "obk", 0, handles=pb[0:4])
    b.wait("gpsimd", [b.mark("pe")])
    scr_done = proj_to_scr(b, obanks, worg_d, lnx_rhs, NKC, scr_d, w_sb,
                           wstage, obs=2, extra_k=(4, wup_d, h_rhs))
    ttb = Banks(b, "ttb", 0, handles=tbp)
    resid_tile_loop(b, scr_done, scr_d, x1_d, x2_d, ident, ttb, scr_tile,
                    x_stage[:, 0:D])

    b.wait("sync", [b.mark(s) for s in list(b.cnt) if s.startswith("x1s")])
    b.finalize()
    return nc, ctx


L4W = TPC + 128
NPC = 81


def build_l4a():
    nc = bass.Bass("TRN2", target_bir_lowering=False, debug=False)
    ctx = ExitStack()
    b = Bld(nc, ctx)
    x2_d = nc.dram_tensor("x2", [TPC, D], F32, kind="ExternalInput").ap()
    halo_d = nc.dram_tensor("halo", [3, D], F32, kind="ExternalInput").ap()
    w_d = nc.dram_tensor("w_inproj_p", [D, NPC * 128], F32, kind="ExternalInput").ap()
    ident_d = nc.dram_tensor("ident", [128, 128], BF16, kind="ExternalInput").ap()
    p_d = nc.dram_tensor("pout", [NPC * 128, L4W], BF16, kind="ExternalOutput").ap()

    lnx_fm = b.sb("lnx_fm", [128, NKC * L4W], BF16)
    x_stage = b.sb("x_stage", [128, 2 * D], F32)
    ln_stage = b.sb("ln_stage", [128, 2 * D], BF16)
    stats = b.sb("stats", [128, 48], F32)
    mv = b.sb("mv", [128, 2], F32)
    rstd = b.sb("rstd", [128, 1], F32)
    ident = b.sb("ident_sb", [128, 128], BF16)
    w_sb = b.sb("w_sb", [128, 2 * 2 * 128 * NKC], BF16)
    wstage = b.sb("wstage", [128, 4 * 384], BF16)

    pb = [b.psum(f"pb{i}", [128, 512], F32) for i in range(4)]
    tbp = [b.psum(f"tbp{i}", [128, 512], BF16) for i in range(2)]

    b.dma("sync", ident[:, :], ident_d[:, :], "cld")
    consts = b.mark("cld")
    b.wait("tensor", [consts])
    b.wait("vector", [consts])
    b.wait("scalar", [consts])

    tbanks = Banks(b, "tbk", 0, handles=tbp)
    z0 = b.op("vector", lambda e: e.memset(x_stage[:, 0:D], 0.0))
    hl = b.dma("sync", x_stage[125:128, 0:D], halo_d[:, :], "hld", waits=[z0])
    htok = ln_tile(b, x_stage[:, 0:D], ln_stage[:, 0:D], stats, mv, rstd, hl)
    for f0 in range(0, NKC, 4):
        bi, bank, war = tbanks.take()
        ptok = None
        for g in range(4):
            fi = f0 + g
            ptok = b.pe(lambda e, s=ln_stage[:, fi * 128:(fi + 1) * 128],
                        o=bank[:, g * 128:(g + 1) * 128]:
                        e.transpose(o, s, ident[:, :]),
                        waits=([war, htok, consts] if g == 0 else ()),
                        sem="pe" if g == 3 else None)
        for g in range(4):
            fi = f0 + g
            vt = b.op("vector", lambda e, o=lnx_fm[:, fi * L4W: fi * L4W + 128],
                      i=bank[:, g * 128:(g + 1) * 128]: e.tensor_copy(o, i),
                      waits=[ptok] if g == 0 else ())
        tbanks.set_free(bi, vt)
    def fmoff(fi, tj):
        return fi * L4W + 128 + tj * 128
    nt = TPC // 128
    last_use = [b.mark("ve"), None]
    for tj in range(nt):
        sl = tj % 2
        xs = x_stage[:, sl * D:(sl + 1) * D]
        lt = ln_stage[:, sl * D:(sl + 1) * D]
        ld = b.dma("sync", xs, x2_d[tj * 128:(tj + 1) * 128, :], f"xld{sl}",
                   waits=[last_use[sl]])
        aptok = ln_tile(b, xs, lt, stats, mv, rstd, ld)
        for f0 in range(0, NKC, 4):
            bi, bank, war = tbanks.take()
            ptok = None
            for g in range(4):
                fi = f0 + g
                ptok = b.pe(lambda e, s=lt[:, fi * 128:(fi + 1) * 128],
                            o=bank[:, g * 128:(g + 1) * 128]:
                            e.transpose(o, s, ident[:, :]),
                            waits=([war, aptok] if g == 0 else ()),
                            sem="pe" if g == 3 else None)
            for g in range(4):
                fi = f0 + g
                vt = b.op("vector", lambda e, o=lnx_fm[:, fmoff(fi, tj):
                                                       fmoff(fi, tj) + 128],
                          i=bank[:, g * 128:(g + 1) * 128]: e.tensor_copy(o, i),
                          waits=[ptok] if g == 0 else ())
            tbanks.set_free(bi, vt)
        last_use[sl] = b.mark("ve")
    b.wait("tensor", [b.mark("ve")])

    banks = Banks(b, "bk", 0, handles=pb)

    def rhs_view(k, third):
        return lnx_fm[:, k * L4W + third * 384: k * L4W + (third + 1) * 384]

    proj_to_scr(b, banks, w_d, rhs_view, NPC, p_d, w_sb, wstage, obs=2,
                halves=(0, 1, 2), hw=384)

    b.wait("sync", [b.mark(s) for s in list(b.cnt) if s.startswith("wo")])
    b.finalize()
    return nc, ctx


def build_l4b():
    nc = bass.Bass("TRN2", target_bir_lowering=False, debug=False)
    ctx = ExitStack()
    b = Bld(nc, ctx)
    xbc_d = nc.dram_tensor("xbc_fm", [CONV_DIM, L4W], BF16, kind="ExternalInput").ap()
    dt_d = nc.dram_tensor("dt_fm", [64, L4W], BF16, kind="ExternalInput").ap()
    cw_d = nc.dram_tensor("conv_w_c", [128, 192], F32, kind="ExternalInput").ap()
    cb_d = nc.dram_tensor("conv_b_c", [128, 48], F32, kind="ExternalInput").ap()
    dtb_d = nc.dram_tensor("dt_bias_c", [64, 1], F32, kind="ExternalInput").ap()
    al_d = nc.dram_tensor("a_log_c", [64, 1], F32, kind="ExternalInput").ap()
    dr_d = nc.dram_tensor("d_row", [1, 4096], F32, kind="ExternalInput").ap()
    rep_d = nc.dram_tensor("rep", [64, 4096], BF16, kind="ExternalInput").ap()
    bm_d = nc.dram_tensor("maskneg128", [128, 128], F32, kind="ExternalInput").ap()
    ident_d = nc.dram_tensor("ident", [128, 128], BF16, kind="ExternalInput").ap()
    identf_d = nc.dram_tensor("ident_f", [128, 128], F32, kind="ExternalInput").ap()
    onesr_d = nc.dram_tensor("ones_row_f", [1, 128], F32, kind="ExternalInput").ap()
    onesb_d = nc.dram_tensor("ones_row_b", [1, 128], BF16, kind="ExternalInput").ap()
    y_d = nc.dram_tensor("y_tm", [TPC, 4096], BF16, kind="ExternalOutput").ap()
    cfm_d = nc.dram_tensor("c_fm", [1024, TPC], BF16, kind="ExternalOutput").ap()
    ccl_d = nc.dram_tensor("coreclg", [64, TPC], F32, kind="ExternalOutput").ap()
    sc_d = nc.dram_tensor("s_core", [128, 4096], BF16, kind="ExternalOutput").ap()

    x_tm = b.sb("x_tm", [128, 8 * 4096], BF16)
    bc_sb = b.sb("bc_sb", [128, 16 * 1024], BF16)
    b_tm = b.sb("b_tm", [128, 16 * 128], BF16)
    dtx_sb = b.sb("dtx_sb", [128, 2 * 4096], BF16)
    r_sb = b.sb("r_sb", [128, 4096], BF16)
    lam_sb = b.sb("lam_sb", [128, 4096], BF16)
    drep_sb = b.sb("drep_sb", [128, 4096], BF16)
    cum_sb = b.sb("cum_sb", [128, 512], BF16)
    xstg = b.sb("xstg", [128, 2 * L4W], BF16)
    conv_sb = b.sb("conv_sb", [128, 1024], F32)
    cw_sb = b.sb("cw_sb", [128, 192], F32)
    cbb_sb = b.sb("cbb_sb", [128, 48], F32)
    dt_sb = b.sb("dt_sb", [64, L4W], BF16)
    u_sb = b.sb("u_sb", [64, 1024], F32)

    dtp_sb = b.sb("dtp_sb", [64, 1024], F32)
    dtpb_sb = b.sb("dtpb_sb", [64, 1024], BF16)
    logda = b.sb("logda", [64, 1024], F32)
    u2_sb = logda
    clg_sb = b.sb("clg_sb", [64, 256], F32)
    clgb_sb = b.sb("clgb_sb", [64, 256], BF16)
    ccl_sb = b.sb("ccl_sb", [64, 256], F32)
    off_sb = b.sb("off_sb", [64, 2], F32)
    es_sb = b.sb("es_sb", [64, 256], F32)
    esb_sb = b.sb("esb_sb", [64, 256], BF16)
    clgq_sb = b.sb("clgq_sb", [64, 2], F32)
    clgqb_sb = b.sb("clgqb_sb", [64, 1], BF16)
    clgtm_sb = b.sb("clgtm_sb", [128, 2 * 64], F32)
    nclgtm_sb = b.sb("nclgtm_sb", [128, 2 * 64], F32)
    rl_sb = b.sb("rl_sb", [1, 512], F32)
    trow_sb = b.sb("trow_sb", [1, 128], F32)
    dtt_sb = b.sb("dtt_sb", [128, 128], F32)
    dec_sb = b.sb("dec_sb", [128, 128], BF16)
    g_sb = b.sb("g_sb", [128, 128], BF16)
    corr_sb = b.sb("corr_sb", [128, 512], BF16)
    ya_sb = b.sb("ya_sb", [128, 512], BF16)
    yb_sb = b.sb("yb_sb", [128, 512], BF16)
    t1_sb = b.sb("t1_sb", [128, 512], BF16)
    bias_a = b.sb("bias_a", [64, 2], F32)
    binm_sb = b.sb("maskneg_sb", [128, 128], F32)
    ident = b.sb("ident_sb", [128, 128], BF16)
    identf = b.sb("identf_sb", [128, 128], F32)
    onesr = b.sb("onesr_sb", [1, 128], F32)
    drow_sb = b.sb("drow_sb", [1, 4096], BF16)
    onesb = b.sb("onesb_sb", [1, 128], BF16)
    rep_sb = b.sb("rep_sb", [64, 4096], BF16)

    psY = [b.psum(f"psY{i}", [128, 512], F32) for i in range(2)]
    psC = b.psum("psC", [128, 512], F32)
    psS = b.psum("psS", [128, 512], F32)
    psM = [b.psum(f"psM{i}", [128, 512], F32) for i in range(2)]
    psT = [b.psum(f"psT{i}", [128, 512], BF16) for i in range(2)]

    for ap_, src_ in ((cw_sb, cw_d), (cbb_sb, cb_d), (dtb_d, None)):
        pass
    b.dma("sync", cw_sb[:, :], cw_d[:, :], "cld")
    b.dma("sync", cbb_sb[:, :], cb_d[:, :], "cld")
    b.dma("sync", bias_a[:, 0:1], dtb_d[:, :], "cld")
    b.dma("sync", bias_a[:, 1:2], al_d[:, :], "cld")
    b.dma("gpsimd", drow_sb[:, :], dr_d[:, :], "cldg")
    b.dma("sync", onesb[:, :], onesb_d[:, :], "cld")
    b.dma("sync", rep_sb[:, :], rep_d[:, :], "cld")
    b.dma("sync", binm_sb[:, :], bm_d[:, :], "cld")
    b.dma("sync", ident[:, :], ident_d[:, :], "cld")
    b.dma("sync", identf[:, :], identf_d[:, :], "cld")
    b.dma("sync", onesr[:, :], onesr_d[:, :], "cld")
    b.dma("sync", dt_sb[:, :], dt_d[:, :], "cld")
    consts = b.mark("cld")
    constsg = b.mark("cldg")
    b.wait("tensor", [consts, constsg])
    b.wait("vector", [consts, constsg])
    b.wait("scalar", [consts, constsg])

    tb = Banks(b, "tb", 0, handles=psT)

    dtok = None
    for rg in range(8):
        pm = b.pe(lambda e, rg=rg: e.matmul(psM[0][:, :], onesb[:, :],
                                            drow_sb[:, rg * 512:(rg + 1) * 512],
                                            start=True, stop=True),
                  waits=[b.mark("ve")] if rg else [consts], sem="pe")
        dtok = b.op("vector", lambda e, rg=rg: e.tensor_copy(
            drep_sb[:, rg * 512:(rg + 1) * 512], psM[0][:, :]), waits=[pm])

    stg_free = [None, None]
    for ch in range(48):
        sl = ch % 2
        stg = xstg[:, sl * L4W:(sl + 1) * L4W]
        ld = b.dma("sync", stg, xbc_d[ch * 128:(ch + 1) * 128, :], f"xb{sl}",
                   waits=[stg_free[sl]])
        cv = b.op("vector", lambda e, stg=stg, ch=ch: e.tensor_scalar(
            conv_sb[:, :], stg[:, 125:1149], cw_sb[:, ch * 4:ch * 4 + 1],
            cbb_sb[:, ch:ch + 1], ALU.mult, ALU.add),
            waits=[ld, b.mark("se")])
        for tap in range(1, 4):
            cv = b.op("vector", lambda e, stg=stg, ch=ch, tap=tap:
                      e.scalar_tensor_tensor(
                          conv_sb[:, :], stg[:, 125 + tap:1149 + tap],
                          cw_sb[:, ch * 4 + tap:ch * 4 + tap + 1],
                          conv_sb[:, :], ALU.mult, ALU.add))
        stg_free[sl] = cv
        if ch < 32:
            sx = b.op("scalar", lambda e: e.activation(
                xstg[:, sl * L4W:sl * L4W + 1024] if False else conv_sb[:, :],
                conv_sb[:, :], AF.Silu), waits=[b.mark("ve")])
            cbf = b.op("vector", lambda e: e.tensor_copy(
                dtx_sb[:, 0:1024], conv_sb[:, :]), waits=[b.mark("se")])
            for tj0 in range(0, 8, 4):
                bi, bank, war = tb.take()
                ptk = None
                for g2 in range(4):
                    tj = tj0 + g2
                    ptk = b.pe(lambda e, s=dtx_sb[:, tj * 128:(tj + 1) * 128],
                               o=bank[:, g2 * 128:(g2 + 1) * 128]:
                               e.transpose(o, s, ident[:, :]),
                               waits=([war, cbf] if g2 == 0 else ()),
                               sem="pe" if g2 == 3 else None)
                for g2 in range(4):
                    tj = tj0 + g2
                    vt = b.op("vector", lambda e, o=x_tm[:, (tj0 + g2) * 4096 + ch * 128:
                                                         (tj0 + g2) * 4096 + (ch + 1) * 128],
                              i=bank[:, g2 * 128:(g2 + 1) * 128]:
                              e.tensor_copy(o, i),
                              waits=[ptk] if g2 == 0 else ())
                tb.set_free(bi, vt)
        else:
            bci = ch - 32
            sx = b.op("scalar", lambda e, bci=bci: e.activation(
                bc_sb[:, bci * 1024:(bci + 1) * 1024], conv_sb[:, :], AF.Silu),
                waits=[b.mark("ve")])
            stg_free[sl] = sx
            if ch >= 40:
                b.dma("sync", cfm_d[(ch - 40) * 128:(ch - 39) * 128, :],
                      bc_sb[:, bci * 1024:(bci + 1) * 1024], "cso",
                      waits=[sx])
    conv_done_v = b.mark("ve")
    conv_done_s = b.mark("se")

    b.op("vector", lambda e: e.tensor_scalar(
        u_sb[:, :], dt_sb[:, 128:1152], bias_a[:, 0:1], None, ALU.add,
        ALU.bypass), waits=[consts])
    b.op("scalar", lambda e: e.activation(u2_sb[:, :], u_sb[:, :], AF.Abs),
         waits=[b.mark("ve")])
    b.op("scalar", lambda e: e.activation(u2_sb[:, :], u2_sb[:, :], AF.Exp,
                                          scale=-1.0))
    one_ap, one_tok = b.constcol(1.0)
    b.op("scalar", lambda e: e.activation(u2_sb[:, :], u2_sb[:, :], AF.Ln,
                                          bias=one_ap[0:64, :]),
         waits=[one_tok])
    b.op("vector", lambda e: e.tensor_scalar(
        dtp_sb[:, :], u_sb[:, :], 0.0, None, ALU.max, ALU.bypass))
    b.op("vector", lambda e: e.tensor_tensor(
        dtp_sb[:, :], dtp_sb[:, :], u2_sb[:, :], ALU.add),
        waits=[b.mark("se")])
    b.op("vector", lambda e: e.tensor_copy(dtpb_sb[:, :], dtp_sb[:, :]))
    b.op("scalar", lambda e: e.activation(bias_a[:, 1:2] if False else
                                          off_sb[:, 1:2], bias_a[:, 1:2],
                                          AF.Exp), waits=[b.mark("ve")])
    b.op("vector", lambda e: e.tensor_scalar(
        off_sb[:, 1:2], off_sb[:, 1:2], -1.0, None, ALU.mult, ALU.bypass),
        waits=[b.mark("se")])
    b.op("vector", lambda e: e.tensor_scalar(
        logda[:, :], dtp_sb[:, :], off_sb[:, 1:2], None, ALU.mult, ALU.bypass))
    z_off = b.op("vector", lambda e: e.memset(off_sb[:, 0:1], 0.0))

    r_z = b.op("vector", lambda e: e.memset(r_sb[:, :], 0.0))
    last_yd = [None]
    for sc in range(4):
        b.wait("vector", [b.mark("pe")])
        b.op("vector", lambda e, sc=sc: e.tensor_tensor_scan(
            clg_sb[:, :], logda[:, sc * 256:(sc + 1) * 256],
            logda[:, sc * 256:(sc + 1) * 256], 0.0, ALU.add, ALU.bypass))
        b.op("vector", lambda e: e.tensor_scalar(
            ccl_sb[:, :], clg_sb[:, :], off_sb[:, 0:1], None, ALU.add,
            ALU.bypass))
        ccd = b.dma("sync", ccl_d[:, sc * 256:(sc + 1) * 256], ccl_sb[:, :],
                    f"ccs{sc % 2}", waits=[b.mark("ve")])
        b.op("vector", lambda e: e.tensor_copy(off_sb[:, 0:1],
                                               ccl_sb[:, 255:256]),
             waits=[ccd])
        b.op("vector", lambda e: e.tensor_copy(clgq_sb[:, 0:1],
                                               clg_sb[:, 255:256]))
        b.op("vector", lambda e: e.tensor_copy(clgqb_sb[:, :],
                                               clgq_sb[:, 0:1]))
        b.op("vector", lambda e: e.tensor_copy(clgb_sb[:, :], clg_sb[:, :]))
        b.op("scalar", lambda e: e.activation(es_sb[:, :], clg_sb[:, :],
                                              AF.Exp, bias=clgq_sb[:, 0:1],
                                              scale=-1.0),
             waits=[b.mark("ve")])
        b.op("vector", lambda e: e.tensor_copy(esb_sb[:, :], es_sb[:, :]),
             waits=[b.mark("se")])
        for tc in range(2):
            pt = b.pe(lambda e, tc=tc: e.transpose(
                psM[0][:, 0:64], clg_sb[:, tc * 128:(tc + 1) * 128],
                identf[0:64, 0:64]), waits=[b.mark("ve")], sem="pe")
            b.op("vector", lambda e, tc=tc: e.tensor_copy(
                clgtm_sb[:, tc * 64:(tc + 1) * 64], psM[0][:, 0:64]),
                waits=[pt])
            b.op("vector", lambda e, tc=tc: e.tensor_scalar(
                nclgtm_sb[:, tc * 64:(tc + 1) * 64], psM[0][:, 0:64], -1.0,
                None, ALU.mult, ALU.bypass))
        for g in range(8):
            bi, bank, war = tb.take()
            ptk = None
            for a in range(2):
                ptk = b.pe(lambda e, o=bank[:, a * 128:(a + 1) * 128],
                           s=bc_sb[:, g * 1024 + sc * 256 + a * 128:
                                   g * 1024 + sc * 256 + (a + 1) * 128]:
                           e.transpose(o, s, ident[:, :]),
                    waits=([war, conv_done_s, conv_done_v] if a == 0 else ()),
                    sem="pe" if a == 1 else None)
            vt = b.op("vector", lambda e, o=b_tm[:, g * 256:(g + 1) * 256],
                      i=bank[:, 0:256]: e.tensor_copy(o, i),
                      waits=[ptk, b.mark("pe")])
            tb.set_free(bi, vt)
        for rg in range(8):
            pm = b.pe(lambda e, rg=rg: e.matmul(
                psM[1][0:1, 0:512], clgqb_sb[:, :],
                rep_sb[:, rg * 512:(rg + 1) * 512], start=True, stop=True,
                skip_group_check=True),
                waits=[b.mark("ve"), b.mark("se")], sem="pe")
            b.op("vector", lambda e: e.tensor_copy(
                rl_sb[:, :], psM[1][0:1, 0:512]), waits=[pm])
            pm2_ = b.pe(lambda e: e.matmul(
                psM[1][:, :], onesr[:, :], rl_sb[:, :],
                start=True, stop=True, skip_group_check=True),
                waits=[b.mark("ve"), b.mark("se")], sem="pe")
            b.op("scalar", lambda e, rg=rg: e.activation(
                lam_sb[:, rg * 512:(rg + 1) * 512], psM[1][:, :], AF.Exp),
                waits=[pm2_, b.mark("ve")])
        lam_tok = b.mark("se")
        for a in range(2):
            for g in range(8):
                pm = b.pe(lambda e, a=a, g=g, sc=sc: e.matmul(
                    psM[0][:, :], dtpb_sb[:, sc * 256 + a * 128:
                                          sc * 256 + (a + 1) * 128],
                    rep_sb[:, g * 512:(g + 1) * 512], start=True, stop=True,
                    skip_group_check=True),
                    waits=[b.mark("ve"), b.mark("se")], sem="pe")
                b.op("vector", lambda e, a=a, g=g, sc=sc: e.tensor_tensor(
                    dtx_sb[:, a * 4096 + g * 512:a * 4096 + (g + 1) * 512],
                    x_tm[:, (sc * 2 + a) * 4096 + g * 512:
                         (sc * 2 + a) * 4096 + (g + 1) * 512],
                    psM[0][:, :], ALU.mult), waits=[pm])
        dtx_tok = b.mark("ve")
        for tc in range(2):
            for g in range(8):
                yb = psY[tc % 2]
                pc = b.pe(lambda e, g=g, tc=tc, sc=sc: e.matmul(
                    psC[:, :],
                    bc_sb[:, (8 + g) * 1024 + sc * 256 + tc * 128:
                          (8 + g) * 1024 + sc * 256 + (tc + 1) * 128],
                    r_sb[:, g * 512:(g + 1) * 512], start=True, stop=True,
                    skip_group_check=True),
                    waits=[b.mark("ve"), conv_done_s], sem="pe")
                cevac = b.op("vector", lambda e: e.tensor_copy(
                    corr_sb[:, :], psC[:, :]), waits=[pc])
                pcm = b.pe(lambda e, g=g, tc=tc: e.matmul(
                    psM[0][:, :], clgb_sb[:, tc * 128:(tc + 1) * 128],
                    rep_sb[:, g * 512:(g + 1) * 512], start=True, stop=True,
                    skip_group_check=True),
                    waits=[b.mark("ve"), b.mark("se")], sem="pe")
                cume = b.op("scalar", lambda e: e.activation(
                    cum_sb[:, 0:512], psM[0][:, :], AF.Exp),
                    waits=[pcm, b.mark("ve")])
                ystarted = False
                ylast = None
                for hl in range(8):
                    h = g * 8 + hl
                    ptr = b.pe(lambda e, h=h, tc=tc: e.transpose(
                        psM[1][0:1, 0:128],
                        clgtm_sb[:, tc * 64 + h:tc * 64 + h + 1],
                        identf[:, :]), waits=[b.mark("ve"), b.mark("se")],
                        sem="pe")
                    b.op("vector", lambda e: e.tensor_copy(
                        trow_sb[:, :], psM[1][0:1, 0:128]), waits=[ptr])
                    pt2 = b.pe(lambda e: e.matmul(
                        psM[1][:, 128:256], onesr[:, :], trow_sb[:, :],
                        start=True, stop=True, skip_group_check=True),
                        waits=[b.mark("ve")], sem="pe")
                    for a in range(tc + 1):
                        pcb = b.pe(lambda e, g=g, a=a, tc=tc, sc=sc: e.matmul(
                            psM[0][:, 0:128],
                            bc_sb[:, g * 1024 + sc * 256 + a * 128:
                                  g * 1024 + sc * 256 + (a + 1) * 128],
                            bc_sb[:, (8 + g) * 1024 + sc * 256 + tc * 128:
                                  (8 + g) * 1024 + sc * 256 + (tc + 1) * 128],
                            start=True, stop=True, skip_group_check=True),
                            waits=[b.mark("se"), b.mark("ve")], sem="pe")
                        dtt = b.op("vector", lambda e, a=a, tc=tc, h=h:
                                   e.tensor_scalar(
                                       dtt_sb[:, :], psM[1][:, 128:256],
                                       nclgtm_sb[:, a * 64 + h:a * 64 + h + 1],
                                       None, ALU.add, ALU.bypass),
                                   waits=[pt2, pcb, b.mark("se")])
                        if a == tc:
                            dtt = b.op("vector", lambda e: e.tensor_tensor(
                                dtt_sb[:, :], dtt_sb[:, :], binm_sb[:, :],
                                ALU.add))
                        dce = b.op("scalar", lambda e: e.activation(
                            dec_sb[:, :], dtt_sb[:, :], AF.Exp),
                            waits=[b.mark("ve")])
                        gm = b.op("vector", lambda e: e.tensor_tensor(
                            g_sb[:, :], psM[0][:, 0:128], dec_sb[:, :],
                            ALU.mult), waits=[dce, b.mark("pe")])
                        ym = b.pe(lambda e, yb=yb, hl=hl, a=a, h=h,
                                  s=(not ystarted),
                                  st=(hl == 7 and a == tc): e.matmul(
                                      yb[:, hl * 64:(hl + 1) * 64], g_sb[:, :],
                                      dtx_sb[:, a * 4096 + h * 64:
                                             a * 4096 + (h + 1) * 64],
                                      start=s, stop=st,
                                      skip_group_check=True),
                                  waits=([gm, dtx_tok, b.mark("ve")]
                                         if not ystarted else [gm, dtx_tok]),
                                  sem="pe")
                        ystarted = True
                        ylast = ym
                a1 = b.op("vector", lambda e: e.tensor_tensor(
                    t1_sb[:, :], corr_sb[:, :], cum_sb[:, 0:512], ALU.mult),
                    waits=[cume, cevac])
                a2 = b.op("vector", lambda e, yb=yb: e.tensor_tensor(
                    ya_sb[:, :], t1_sb[:, :], yb[:, :], ALU.add),
                    waits=[ylast])
                a3 = b.op("vector", lambda e, sc=sc, tc=tc, g=g:
                          e.tensor_tensor(
                              t1_sb[:, :],
                              x_tm[:, (sc * 2 + tc) * 4096 + g * 512:
                                   (sc * 2 + tc) * 4096 + (g + 1) * 512],
                              drep_sb[:, g * 512:(g + 1) * 512], ALU.mult))
                a4 = b.op("vector", lambda e: e.tensor_tensor(
                    yb_sb[:, :], ya_sb[:, :], t1_sb[:, :], ALU.add),
                    waits=[last_yd[0]])
                yd = b.dma("sync",
                           y_d[(sc * 2 + tc) * 128:(sc * 2 + tc + 1) * 128,
                               g * 512:(g + 1) * 512],
                           yb_sb[:, :], f"ys{g % 4}", waits=[a4])
                last_yd[0] = yd
        for g in range(8):
            ps1 = None
            for a in range(2):
                pm2 = b.pe(lambda e, a=a, g=g: e.matmul(
                    psM[1][:, :], esb_sb[:, a * 128:(a + 1) * 128],
                    rep_sb[:, g * 512:(g + 1) * 512], start=True, stop=True,
                    skip_group_check=True),
                    waits=[b.mark("ve"), b.mark("se")], sem="pe")
                dxp = b.op("vector", lambda e, a=a, g=g: e.tensor_tensor(
                    t1_sb[:, :],
                    dtx_sb[:, a * 4096 + g * 512:a * 4096 + (g + 1) * 512],
                    psM[1][:, :], ALU.mult), waits=[pm2])
                ps1 = b.pe(lambda e, g=g, a=a, s=(a == 0): e.matmul(
                    psS[:, :], b_tm[:, g * 256 + a * 128:g * 256 + (a + 1) * 128],
                    t1_sb[:, :],
                    start=s, stop=(a == 1), skip_group_check=True),
                    waits=[dxp], sem="pe")
            b.op("vector", lambda e, g=g: e.tensor_tensor(
                ya_sb[:, :], lam_sb[:, g * 512:(g + 1) * 512],
                r_sb[:, g * 512:(g + 1) * 512], ALU.mult),
                waits=[lam_tok, ps1])
            b.op("vector", lambda e, g=g: e.tensor_tensor(
                r_sb[:, g * 512:(g + 1) * 512], ya_sb[:, :], psS[:, :],
                ALU.add))
    std = b.dma("sync", sc_d[:, :], r_sb[:, :], "sts", waits=[b.mark("ve")])

    b.wait("sync", [b.mark(s) for s in list(b.cnt)
                    if s.startswith(("ys", "cso", "ccs", "sts"))])
    b.finalize()
    return nc, ctx


def build_l5():
    nc = bass.Bass("TRN2", target_bir_lowering=False, debug=False)
    ctx = ExitStack()
    b = Bld(nc, ctx)
    y_di = nc.dram_tensor("y_tm", [TPC, 4096], BF16, kind="ExternalInput").ap()
    c_di = nc.dram_tensor("c_fm", [1024, TPC], BF16, kind="ExternalInput").ap()
    ccl_di = nc.dram_tensor("coreclg", [64, TPC], F32, kind="ExternalInput").ap()
    sp_di = nc.dram_tensor("s_prev", [3 * 128, 4096], BF16, kind="ExternalInput").ap()
    ll_di = nc.dram_tensor("lamlog_prev", [64, 3], F32, kind="ExternalInput").ap()
    gate_di = nc.dram_tensor("gate_fm", [4096, L4W], BF16, kind="ExternalInput").ap()
    x2_di = nc.dram_tensor("x2", [TPC, D], F32, kind="ExternalInput").ap()
    wop_di = nc.dram_tensor("w_outproj_t", [MI, D], F32, kind="ExternalInput").ap()
    nw_di = nc.dram_tensor("norm_row", [1, 4096], F32, kind="ExternalInput").ap()
    rep_di = nc.dram_tensor("rep", [64, 4096], BF16, kind="ExternalInput").ap()
    ident_d = nc.dram_tensor("ident", [128, 128], BF16, kind="ExternalInput").ap()
    onesr_d = nc.dram_tensor("ones_row_f", [1, 128], F32, kind="ExternalInput").ap()
    x3_d = nc.dram_tensor("x3", [TPC, D], F32, kind="ExternalOutput").ap()
    scr_d = nc.dram_tensor("op_scr", [D, TPC], BF16)

    yn_fm = b.sb("yn_fm", [128, NKC * TPC], BF16)
    r5_sb = b.sb("r5_sb", [128, 4096], BF16)
    lam_sb = b.sb("lam_sb", [128, 4096], BF16)
    llb_sb = b.sb("llb_sb", [64, 3], BF16)
    rl_sb = b.sb("rl_sb", [1, 512], F32)
    clgb_sb = b.sb("clgb_sb", [64, TPC], BF16)
    cum_sb = b.sb("cum_sb", [128, 512], BF16)
    corr_sb = b.sb("corr_sb", [128, 512], BF16)
    y_sb = b.sb("y_sb", [128, 4096], BF16)
    gate_tm = b.sb("gate_tm", [128, 4096], BF16)
    gstage = b.sb("gstage", [128, 4096], BF16)
    cstage = b.sb("cstage", [128, 8 * 128], BF16)
    normrep = b.sb("normrep", [128, 4096], BF16)
    ss_sb = b.sb("ss_sb", [128, 4], F32)
    junk_sb = b.sb("junk_sb", [128, 512], BF16)
    ident = b.sb("ident_sb", [128, 128], BF16)
    onesr = b.sb("onesr_sb", [1, 128], F32)
    nwrow = b.sb("nwrow", [1, 4096], F32)
    rep_sb = b.sb("rep_sb", [64, 4096], BF16)
    w_sb = b.sb("w_sb", [128, 2 * 2 * 128 * NKC], BF16)
    wstage = b.sb("wstage", [128, 4 * HALF], BF16)
    scr_tile = b.sb("scr_tile", [128, D], BF16)
    sprev_sb = scr_tile
    x_stage = b.sb("x_stage", [128, D], F32)

    pb = [b.psum(f"pb{i}", [128, 512], F32) for i in range(4)]
    tbp = [b.psum(f"tbp{i}", [128, 512], BF16) for i in range(2)]

    b.dma("sync", ident[:, :], ident_d[:, :], "cld")
    b.dma("sync", onesr[:, :], onesr_d[:, :], "cld")
    b.dma("gpsimd", llb_sb[:, :], ll_di[:, :], "cldg")
    b.dma("gpsimd", clgb_sb[:, :], ccl_di[:, :], "cldg")
    b.dma("sync", rep_sb[:, :], rep_di[:, :], "cld")
    b.dma("sync", nwrow[:, :], nw_di[:, :], "cld")
    consts = b.mark("cld")
    constsg = b.mark("cldg")
    b.wait("tensor", [consts, constsg])
    b.wait("vector", [consts, constsg])
    b.wait("scalar", [consts, constsg])

    for rg in range(8):
        pm = b.pe(lambda e, rg=rg: e.matmul(
            pb[1][:, :], onesr[:, :], nwrow[:, rg * 512:(rg + 1) * 512],
            start=True, stop=True, skip_group_check=True),
            waits=[b.mark("ve")], sem="pe")
        b.op("vector", lambda e, rg=rg: e.tensor_copy(
            normrep[:, rg * 512:(rg + 1) * 512], pb[1][:, :]), waits=[pm])

    ld3 = b.dma("sync", r5_sb[:, :], sp_di[256:384, :], "sl3")
    for j in (1, 0):
        lds = b.dma("sync", sprev_sb[:, :], sp_di[j * 128:(j + 1) * 128, :],
                    f"sl{j}", waits=[b.mark("ve")])
        for rg in range(8):
            pm = b.pe(lambda e, j=j, rg=rg: e.matmul(
                pb[1][0:1, 0:512], llb_sb[:, j:j + 1],
                rep_sb[:, rg * 512:(rg + 1) * 512], start=True, stop=True,
                skip_group_check=True),
                waits=[b.mark("ve"), b.mark("se")], sem="pe")
            b.op("vector", lambda e: e.tensor_copy(rl_sb[:, :],
                                                   pb[1][0:1, 0:512]),
                 waits=[pm])
            pm2 = b.pe(lambda e: e.matmul(
                pb[1][:, :], onesr[:, :], rl_sb[:, :], start=True, stop=True,
                skip_group_check=True), waits=[b.mark("ve"), b.mark("se")],
                sem="pe")
            b.op("scalar", lambda e, rg=rg: e.activation(
                lam_sb[:, rg * 512:(rg + 1) * 512], pb[1][:, :], AF.Exp),
                waits=[pm2, b.mark("ve")])
        for rg in range(8):
            b.op("vector", lambda e, rg=rg: e.tensor_tensor(
                r5_sb[:, rg * 512:(rg + 1) * 512],
                lam_sb[:, rg * 512:(rg + 1) * 512],
                r5_sb[:, rg * 512:(rg + 1) * 512], ALU.mult),
                waits=[b.mark("se"), ld3] if rg == 0 else ())
        for rg in range(8):
            b.op("vector", lambda e, rg=rg: e.tensor_tensor(
                r5_sb[:, rg * 512:(rg + 1) * 512],
                r5_sb[:, rg * 512:(rg + 1) * 512],
                sprev_sb[:, rg * 512:(rg + 1) * 512], ALU.add),
                waits=[lds] if rg == 0 else ())
    rin_tok = b.mark("ve")
    b.wait("tensor", [rin_tok])

    eps_ap, eps_tok = b.constcol(MEPS)
    tb5 = Banks(b, "tb5", 0, handles=tbp)
    yld_prev = None
    gld_prev = None
    cld_prev = None
    for tt in range(8):
        yld = b.dma("sync", y_sb[:, :], y_di[tt * 128:(tt + 1) * 128, :],
                    "yld", waits=[yld_prev])
        gld = b.dma("sync", gstage[:, :].rearrange("p (f t) -> p f t", t=128),
                    gate_di[:, 128 + tt * 128:128 + (tt + 1) * 128].rearrange(
                        "(f p) t -> p f t", p=128), "gld",
                    waits=[gld_prev, b.mark("pe")])
        cld = b.dma("sync", cstage[:, :].rearrange("p (g t) -> p g t", t=128),
                    c_di[:, tt * 128:(tt + 1) * 128].rearrange(
                        "(g p) t -> p g t", p=128), "clD",
                    waits=[cld_prev, b.mark("pe")])
        for f0 in range(0, NKC, 4):
            bi, bank, war = tb5.take()
            ptk = None
            for g2 in range(4):
                fi = f0 + g2
                ptk = b.pe(lambda e, o=bank[:, g2 * 128:(g2 + 1) * 128],
                           s=gstage[:, fi * 128:(fi + 1) * 128]:
                           e.transpose(o, s, ident[:, :]),
                           waits=([war, gld] if g2 == 0 else ()),
                           sem="pe" if g2 == 3 else None)
            sv = b.op("scalar", lambda e, f0=f0, i=bank[:, :]: e.activation(
                gate_tm[:, f0 * 128:(f0 + 4) * 128], i, AF.Silu),
                waits=[ptk])
            tb5.set_free(bi, sv)
        gate_done = b.mark("se")
        for g in range(8):
            pc = b.pe(lambda e, g=g: e.matmul(
                pb[0][:, :], cstage[:, g * 128:(g + 1) * 128],
                r5_sb[:, g * 512:(g + 1) * 512], start=True, stop=True,
                skip_group_check=True),
                waits=[cld, b.mark("ve"), b.mark("se")], sem="pe")
            cevac = b.op("vector", lambda e: e.tensor_copy(corr_sb[:, :],
                                                           pb[0][:, :]),
                         waits=[pc])
            pcm = b.pe(lambda e, g=g, tt=tt: e.matmul(
                pb[1][:, :], clgb_sb[:, tt * 128:(tt + 1) * 128],
                rep_sb[:, g * 512:(g + 1) * 512], start=True, stop=True,
                skip_group_check=True),
                waits=[b.mark("ve"), b.mark("se")], sem="pe")
            cume = b.op("scalar", lambda e: e.activation(
                cum_sb[:, :], pb[1][:, :], AF.Exp),
                waits=[pcm, b.mark("ve")])
            b.op("vector", lambda e, g=g: e.scalar_tensor_tensor(
                corr_sb[:, :], corr_sb[:, :], 1.0, cum_sb[:, :], ALU.mult,
                ALU.mult) if False else e.tensor_tensor(
                corr_sb[:, :], corr_sb[:, :], cum_sb[:, :], ALU.mult),
                waits=[cume, cevac, yld])
            yn_tok = b.op("vector", lambda e, g=g: e.tensor_tensor(
                y_sb[:, g * 512:(g + 1) * 512],
                y_sb[:, g * 512:(g + 1) * 512], corr_sb[:, :], ALU.add))
            b.op("vector", lambda e, g=g: e.tensor_tensor(
                junk_sb[:, :], y_sb[:, g * 512:(g + 1) * 512],
                y_sb[:, g * 512:(g + 1) * 512], ALU.mult))
            b.op("vector", lambda e: e.tensor_scalar(
                junk_sb[:, :], junk_sb[:, :], 0.0, 0.0, ALU.add, ALU.add,
                accum_out=ss_sb[:, 0:1]))
            b.op("vector", lambda e: e.tensor_scalar(
                ss_sb[:, 1:2], ss_sb[:, 0:1], 1.0 / 512, None, ALU.mult,
                ALU.bypass))
            b.op("scalar", lambda e: e.activation(
                ss_sb[:, 2:3], ss_sb[:, 1:2], AF.Sqrt, bias=eps_ap),
                waits=[b.mark("ve"), eps_tok])
            b.op("vector", lambda e: e.reciprocal(ss_sb[:, 3:4], ss_sb[:, 2:3]),
                 waits=[b.mark("se")])
            b.op("vector", lambda e, g=g: e.tensor_scalar(
                y_sb[:, g * 512:(g + 1) * 512],
                y_sb[:, g * 512:(g + 1) * 512], ss_sb[:, 3:4], None,
                ALU.mult, ALU.bypass))
            b.op("vector", lambda e, g=g: e.tensor_tensor(
                y_sb[:, g * 512:(g + 1) * 512],
                y_sb[:, g * 512:(g + 1) * 512],
                normrep[:, g * 512:(g + 1) * 512], ALU.mult))
            b.op("vector", lambda e, g=g: e.tensor_tensor(
                y_sb[:, g * 512:(g + 1) * 512],
                y_sb[:, g * 512:(g + 1) * 512],
                gate_tm[:, g * 512:(g + 1) * 512], ALU.mult),
                waits=[gate_done])
        ynt = b.mark("ve")
        for f0 in range(0, NKC, 4):
            bi, bank, war = tb5.take()
            ptk = None
            for g2 in range(4):
                fi = f0 + g2
                ptk = b.pe(lambda e, o=bank[:, g2 * 128:(g2 + 1) * 128],
                           s=y_sb[:, fi * 128:(fi + 1) * 128]:
                           e.transpose(o, s, ident[:, :]),
                           waits=([war, ynt] if g2 == 0 else ()),
                           sem="pe" if g2 == 3 else None)
            for g2 in range(4):
                fi = f0 + g2
                vt = b.op("vector", lambda e,
                          o=yn_fm[:, fi * TPC + tt * 128:fi * TPC + (tt + 1) * 128],
                          i=bank[:, g2 * 128:(g2 + 1) * 128]:
                          e.tensor_copy(o, i), waits=[ptk] if g2 == 0 else ())
            tb5.set_free(bi, vt)
        yld_prev = b.mark("ve")
        gld_prev = b.mark("se")
        cld_prev = b.mark("pe")
    b.wait("tensor", [b.mark("ve")])

    obanks = Banks(b, "ob", 0, handles=pb)

    def yn_rhs(k, half):
        return yn_fm[:, k * TPC + half * HALF: k * TPC + half * HALF + HALF]

    scr_done = proj_to_scr(b, obanks, wop_di, yn_rhs, NKC, scr_d, w_sb,
                           wstage, obs=2)
    ttb = Banks(b, "ttb", 0, handles=tbp)
    resid_tile_loop(b, scr_done, scr_d, x2_di, x3_d, ident, ttb, scr_tile,
                    x_stage)
    b.wait("sync", [b.mark(s) for s in list(b.cnt) if s.startswith("x1s")])
    b.finalize()
    return nc, ctx


def build_l6():
    nc = bass.Bass("TRN2", target_bir_lowering=False, debug=False)
    ctx = ExitStack()
    b = Bld(nc, ctx)
    x3_d = nc.dram_tensor("x3", [TPC, D], F32, kind="ExternalInput").ap()
    wf_d = nc.dram_tensor("w_ffn_t", [D, D], F32, kind="ExternalInput").ap()
    bias_d = nc.dram_tensor("bias_c", [128, NKC], F32, kind="ExternalInput").ap()
    wa_d = nc.dram_tensor("w_a_t", [D, 256], F32, kind="ExternalInput").ap()
    wbf_d = nc.dram_tensor("w_bf", [256, D], F32, kind="ExternalInput").ap()
    cf_d = nc.dram_tensor("coeffs16", [1, 16], F32, kind="ExternalInput").ap()
    sc_d = nc.dram_tensor("scales16", [1, 16], F32, kind="ExternalInput").ap()
    e16_d = nc.dram_tensor("exp16", [16, 256], F32, kind="ExternalInput").ap()
    ident_d = nc.dram_tensor("ident", [128, 128], BF16, kind="ExternalInput").ap()
    identf_d = nc.dram_tensor("ident_f", [128, 128], F32, kind="ExternalInput").ap()
    x4_d = nc.dram_tensor("x4", [TPC, D], F32, kind="ExternalOutput").ap()
    scr_d = nc.dram_tensor("f2_scr", [D, TPC], BF16)

    lnx_fm = b.sb("lnx_fm", [128, NKC * TPC], BF16)
    h2_fm = b.sb("h2_fm", [128, 2 * TPC], BF16)
    x_stage = b.sb("x_stage", [128, 2 * D], F32)
    ln_stage = b.sb("ln_stage", [128, 2 * D], BF16)
    stats = b.sb("stats", [128, 48], F32)
    mv = b.sb("mv", [128, 2], F32)
    rstd = b.sb("rstd", [128, 1], F32)
    ident = b.sb("ident_sb", [128, 128], BF16)
    identf = b.sb("identf_sb", [128, 128], F32)
    bias_sb = b.sb("bias_sb", [128, NKC], F32)
    wrow = b.sb("wrow", [1, 16], F32)
    wcol = b.sb("wcol", [16, 1], F32)
    cf_sb = b.sb("cf_sb", [1, 16], F32)
    scs_sb = b.sb("scs_sb", [1, 16], F32)
    e16_sb = b.sb("e16_sb", [16, 256], F32)
    ecol2 = b.sb("ecol2", [128, 2], F32)
    w_sb = b.sb("w_sb", [128, 2 * 2 * 128 * (NKC + 2)], BF16)
    wstage = b.sb("wstage", [128, 4 * HALF], BF16)
    scr_tile = b.sb("scr_tile", [128, D], BF16)

    pb = [b.psum(f"pb{i}", [128, 512], F32) for i in range(4)]
    tbp = [b.psum(f"tbp{i}", [128, 512], BF16) for i in range(2)]
    sm_ps = b.psum("sm_ps", [128, 16], F32)

    b.dma("sync", ident[:, :], ident_d[:, :], "cld")
    b.dma("sync", identf[:, :], identf_d[:, :], "cld")
    b.dma("sync", bias_sb[:, :], bias_d[:, :], "cld")
    b.dma("sync", cf_sb[:, :], cf_d[:, :], "cld")
    b.dma("sync", scs_sb[:, :], sc_d[:, :], "cld")
    b.dma("sync", e16_sb[:, :], e16_d[:, :], "cld")
    consts = b.mark("cld")
    b.wait("tensor", [consts])
    b.wait("vector", [consts])
    b.wait("scalar", [consts])

    tbanks = Banks(b, "tbk", 0, handles=tbp)
    lnfm_done = ln_stream_to_fm(b, x3_d, lnx_fm, TPC, x_stage, ln_stage, stats,
                                mv, rstd, tbanks, ident[:, :], consts)
    b.wait("tensor", [lnfm_done])

    b.op("vector", lambda e: e.tensor_tensor(wrow[:, :], cf_sb[:, :],
                                             scs_sb[:, :], ALU.mult))
    t2 = b.pe(lambda e: e.transpose(sm_ps[0:16, 0:1], wrow[:, :],
                                    identf[0:1, 0:1]),
              waits=[b.mark("ve")], sem="pe")
    b.op("vector", lambda e: e.tensor_copy(wcol[:, :], sm_ps[0:16, 0:1]),
         waits=[t2])
    for j in range(2):
        ej = b.pe(lambda e, j=j: e.matmul(
            sm_ps[:, 1 + j:2 + j], e16_sb[:, j * 128:(j + 1) * 128],
            wcol[:, :], start=True, stop=True, skip_group_check=True),
            waits=[b.mark("ve")], sem="pe")
        ec_tok = b.op("vector", lambda e, j=j: e.tensor_copy(
            ecol2[:, j:j + 1], sm_ps[:, 1 + j:2 + j]), waits=[ej])

    def lnx_rhs(k, half):
        return lnx_fm[:, k * TPC + half * HALF: k * TPC + half * HALF + HALF]

    hbanks = Banks(b, "hbk", 0, handles=pb)

    def h_cb(oi, half, bank, ptok):
        return b.op("vector", lambda e, o=h2_fm[:, oi * TPC + half * HALF:
                                                oi * TPC + half * HALF + HALF],
                    i=bank[:, :], j=oi: e.tensor_scalar(
                        o, i, ecol2[:, j:j + 1], None, ALU.mult, ALU.bypass),
                    waits=[ptok, ec_tok])

    proj_fm(b, hbanks, wa_d, lnx_rhs, 2, h_cb, w_sb, obs=2)
    h_done = b.mark("ve")
    b.wait("tensor", [h_done])
    b.wait("gpsimd", [b.mark("pe")])

    def h_rhs(k, half):
        return h2_fm[:, k * TPC + half * HALF: k * TPC + half * HALF + HALF]

    def bias_evac(oi, half, bank, stg, waits):
        return b.op("vector", lambda e, stg=stg, oi=oi, bank=bank:
                    e.tensor_scalar(stg, bank[:, :], bias_sb[:, oi:oi + 1],
                                    None, ALU.add, ALU.bypass), waits=waits)

    obanks = Banks(b, "obk", 0, handles=pb)
    scr_done = proj_to_scr(b, obanks, wf_d, lnx_rhs, NKC, scr_d, w_sb,
                           wstage, obs=2, extra_k=(2, wbf_d, h_rhs),
                           evac_fn=bias_evac)
    ttb = Banks(b, "ttb", 0, handles=tbp)
    resid_tile_loop(b, scr_done, scr_d, x3_d, x4_d, ident, ttb, scr_tile,
                    x_stage[:, 0:D])
    b.wait("sync", [b.mark(s) for s in list(b.cnt) if s.startswith("x1s")])
    b.finalize()
    return nc, ctx



_NC_CACHE = {}


def _get_nc(name, builder):
    if name not in _NC_CACHE:
        _NC_CACHE[name] = builder()
    return _NC_CACHE[name][0]


def _run(name, builder, in_maps):
    nc = _get_nc(name, builder)
    res = run_bass_kernel_spmd(nc, in_maps, core_ids=list(range(8)))
    return res.results


def kernel(**inputs):
    p = host_prep(inputs)
    X = np.asarray(inputs["L0_attn_X"], dtype=np.float32)
    eye_b = p["ident_bf"]

    def tok(core):
        bb, cc = core // 4, core % 4
        return bb, slice(cc * TPC, (cc + 1) * TPC)

    im = []
    for core in range(8):
        bb, sl = tok(core)
        im.append({
            "x": _f32h(X[bb, sl, :]),
            "w_qkv": p["w_qkv"],
            "cos_c": np.ascontiguousarray(p["cos_t"][:, sl]),
            "sin_c": np.ascontiguousarray(p["sin_t"][:, sl]),
            "cosq_c": np.ascontiguousarray(p["cosq_t"][:, sl]),
            "sinq_c": np.ascontiguousarray(p["sinq_t"][:, sl]),
            "ident": eye_b,
        })
    r1 = _run("l1", build_l1, im)

    im = []
    for k in range(8):
        qfm = np.concatenate(
            [np.concatenate([np.asarray(r1[bt * 4 + cc]["qkv_fm"])
                             [4 * k * 128:(4 * k + 4) * 128, :]
                             for cc in range(4)], axis=1)
             for bt in range(2)], axis=1)
        kfm = np.concatenate(
            [np.concatenate([np.asarray(r1[bt * 4 + cc]["qkv_fm"])
                             [4096 + k * 128:4096 + (k + 1) * 128, :]
                             for cc in range(4)], axis=1)
             for bt in range(2)], axis=1)
        vtm = np.concatenate(
            [np.concatenate([np.asarray(r1[bt * 4 + cc]["v_tm"])
                             [:, k * 128:(k + 1) * 128]
                             for cc in range(4)], axis=0)
             for bt in range(2)], axis=0)
        im.append({
            "q_fm": np.ascontiguousarray(qfm),
            "k_fm": np.ascontiguousarray(kfm),
            "v_tm": np.ascontiguousarray(vtm),
            "mask_qk": p["mask_qk"], "binm_kq": p["binm_kq"],
            "ident_f": p["ident_f"], "ones_row_f": p["ones_row_f"],
        })
    r2a = _run("l2a", build_l2a, im)

    im = []
    for core in range(8):
        bb, sl = tok(core)
        cc = core % 4
        afm = np.concatenate(
            [np.asarray(r2a[k]["attn_fm"])[:, bb * T + cc * TPC:
                                           bb * T + (cc + 1) * TPC]
             for k in range(8)], axis=0)
        im.append({
            "attn_fm": np.ascontiguousarray(afm),
            "x": _f32h(X[bb, sl, :]),
            "w_o": p["w_o"],
            "ident": eye_b, "ones_col": p["ones_col"],
        })
    r2b = _run("l2b", build_l2b, im)

    im = []
    for core in range(8):
        bb, sl = tok(core)
        pms = np.concatenate([np.asarray(r2b[bb * 4 + j]["pmean"])
                              for j in range(4)], axis=1)
        im.append({
            "x1": np.asarray(r2b[core]["x1"]),
            "pmeans": np.ascontiguousarray(pms),
            "w_org_t": p["w_org_t"], "w_down": p["w_down"], "w_up": p["w_up"],
            "keys_t": p["keys_t"], "wq_mole_t": p["wq_mole_t"],
            "exp8": p["exp8"], "ident": eye_b, "ident_f": p["ident_f"],
        })
    r3 = _run("l3", build_l3, im)

    im = []
    for core in range(8):
        bb, sl = tok(core)
        cc = core % 4
        halo = (np.zeros((3, D), np.float32) if cc == 0 else
                np.asarray(r3[core - 1]["x2"])[-3:, :])
        im.append({
            "x2": np.asarray(r3[core]["x2"]),
            "halo": _f32h(halo),
            "w_inproj_p": p["w_inproj_p"],
            "ident": eye_b,
        })
    r4a = _run("l4a", build_l4a, im)

    im = []
    for core in range(8):
        pout = np.asarray(r4a[core]["pout"])
        im.append({
            "xbc_fm": np.ascontiguousarray(pout[4096:4096 + CONV_DIM, :]),
            "dt_fm": np.ascontiguousarray(pout[10240:10304, :]),
            "conv_w_c": p["conv_w_c"], "conv_b_c": p["conv_b_c"],
            "dt_bias_c": p["dt_bias_c"], "a_log_c": p["a_log_c"],
            "d_row": p["d_row"], "rep": p["rep"],
            "maskneg128": p["maskneg128"],
            "ident": eye_b, "ident_f": p["ident_f"],
            "ones_row_f": p["ones_row_f"], "ones_row_b": p["ones_row_b"],
        })
    r4b = _run("l4b", build_l4b, im)

    im = []
    for core in range(8):
        bb, sl = tok(core)
        cc = core % 4
        sprev = np.zeros((3, 128, 4096), BF)
        llog = np.zeros((64, 3), np.float32)
        for j in range(3):
            pc = cc - 1 - j
            if pc >= 0:
                pcore = bb * 4 + pc
                sprev[j] = np.asarray(r4b[pcore]["s_core"])
                llog[:, j] = np.asarray(r4b[pcore]["coreclg"])[:, -1]
        pout = np.asarray(r4a[core]["pout"])
        im.append({
            "y_tm": np.asarray(r4b[core]["y_tm"]),
            "c_fm": np.asarray(r4b[core]["c_fm"]),
            "coreclg": np.asarray(r4b[core]["coreclg"]),
            "s_prev": sprev.reshape(384, 4096),
            "lamlog_prev": llog,
            "gate_fm": np.ascontiguousarray(pout[0:4096, :]),
            "x2": np.asarray(r3[core]["x2"]),
            "w_outproj_t": p["w_outproj_t"],
            "norm_row": p["norm_row"], "rep": p["rep"],
            "ident": eye_b, "ones_row_f": p["ones_row_f"],
        })
    r5 = _run("l5", build_l5, im)

    im = []
    for core in range(8):
        im.append({
            "x3": np.asarray(r5[core]["x3"]),
            "w_ffn_t": p["w_ffn_t"], "bias_c": p["bias_c"],
            "w_a_t": p["w_a_t"], "w_bf": p["w_bf"],
            "coeffs16": p["coeffs16"], "scales16": p["scales16"],
            "exp16": p["exp16"],
            "ident": eye_b, "ident_f": p["ident_f"],
        })
    r6 = _run("l6", build_l6, im)

    out = np.empty((B, T, D), np.float32)
    for core in range(8):
        bb, sl = tok(core)
        out[bb, sl, :] = np.asarray(r6[core]["x4"])
    return out


# version 04
# speedup vs baseline: 1.0530x; 1.0530x over previous
import numpy as np
import ml_dtypes
from contextlib import ExitStack

import concourse.bass as bass
import concourse.mybir as mybir
from concourse.bass_utils import run_bass_kernel_spmd

F32 = mybir.dt.float32
BF16 = mybir.dt.bfloat16
AF = mybir.ActivationFunctionType
ALU = mybir.AluOpType
BF = ml_dtypes.bfloat16

B = 2
T = 4096
D = 4096
TPC = 1024
HALF = 512
NKC = D // 128

NH, NKV, DH = 32, 8, 128
ROPE_THETA = 10000.0
NE, RR, KK = 8, 64, 128
MH, MP, MN, MG, MK = 64, 64, 128, 8, 4
MI = MH * MP
MGN = MG * MN
CONV_DIM = MI + 2 * MGN
PROJ = MI + CONV_DIM + MH
MEPS = 1e-5
LN_EPS = 1e-5

ENGINES = ("sync", "gpsimd", "tensor", "vector", "scalar")
_SELFSEM = {"vector": "ve", "scalar": "se", "gpsimd": "po"}


class Bld:

    def __init__(self, nc, ctx):
        self.nc = nc
        self.ctx = ctx
        self.ops = {e: [] for e in ENGINES}
        self.sems = {}
        self.cnt = {}
        for s in ("ve", "se", "po", "pe", "cld", "wld", "xld", "sto"):
            self.sem(s)

    def sem(self, name):
        if name not in self.sems:
            self.sems[name] = self.ctx.enter_context(self.nc.semaphore(name))
            self.cnt[name] = 0
        return self.sems[name]

    def sb(self, name, shape, dt):
        return self.ctx.enter_context(self.nc.sbuf_tensor(name, shape, dt))

    def psum(self, name, shape, dt=F32):
        return self.ctx.enter_context(self.nc.psum_tensor(name, shape, dt))

    def on(self, eng, f):
        self.ops[eng].append(f)

    def wait(self, eng, tok):
        toks = tok if isinstance(tok, list) else [tok]
        for t in toks:
            if t is None:
                continue
            (s, v) = t
            if v <= 0:
                continue
            sem = self.sem(s)
            self.on(eng, lambda e, sem=sem, vv=int(v): e.wait_ge(sem, vv))

    def mark(self, semname):
        return (semname, self.cnt[semname])

    def raw_emit(self, eng, fn, sem=None, inc=1):
        if sem is not None:
            s = self.sem(sem)
            self.cnt[sem] += inc
            self.on(eng, lambda e, fn=fn, s=s, inc=inc: fn(e).then_inc(s, inc))
            return (sem, self.cnt[sem])
        self.on(eng, lambda e, fn=fn: fn(e))
        return None

    def op(self, eng, fn, waits=(), chain=True):
        ssem = _SELFSEM[eng]
        w = list(waits) + ([self.mark(ssem)] if chain else [])
        self.wait(eng, w)
        return self.raw_emit(eng, fn, sem=ssem)

    def constcol(self, value):
        if not hasattr(self, "_cc"):
            self._cc = {}
            self._cc_sb = self.sb("constcols", [128, 8], F32)
        if value not in self._cc:
            i = len(self._cc)
            ap = self._cc_sb[:, i:i + 1]
            tok = self.op("vector", lambda e, ap=ap, v=float(value): e.memset(ap, v))
            self._cc[value] = (ap, tok)
        return self._cc[value]

    def dma(self, eng, out, in_, sem, waits=()):
        self.wait(eng, list(waits))
        return self.raw_emit(eng, lambda e, o=out, i=in_: e.dma_start(out=o, in_=i),
                             sem=sem, inc=16)

    def pe(self, fn, waits=(), sem=None):
        self.wait("tensor", list(waits))
        return self.raw_emit("tensor", fn, sem=sem)

    def finalize(self):
        block = self.ctx.enter_context(self.nc.Block())
        for name in ENGINES:
            ops = list(self.ops[name])
            if not ops:
                continue
            def make(ops):
                def run(handle):
                    for f in ops:
                        f(handle)
                return run
            getattr(block, name)(make(ops))


class Banks:

    def __init__(self, b, name, n, cols=512, dt=F32, handles=None):
        self.b = b
        if handles is None:
            handles = [b.psum(f"{name}{i}", [128, cols], dt) for i in range(n)]
        self.banks = handles
        self.free_tok = [None] * len(handles)
        self.i = 0

    def take(self):
        i = self.i
        self.i = (self.i + 1) % len(self.banks)
        tok = self.free_tok[i]
        return i, self.banks[i], tok

    def set_free(self, i, tok):
        self.free_tok[i] = tok


def _bfh(x):
    return np.ascontiguousarray(np.asarray(x)).astype(BF)


def _f32h(x):
    return np.ascontiguousarray(np.asarray(x), dtype=np.float32)



def ln_tile(b, xs, lt, stats, mv, rstd, ld_tok, width=D):
    nq = width // 512
    for q in range(nq):
        b.op("vector", lambda e, q=q: e.bn_stats(
            stats[:, q * 6:(q + 1) * 6], xs[:, q * 512:(q + 1) * 512]),
            waits=[ld_tok] if q == 0 else ())
    b.op("vector", lambda e: e.bn_aggr(mv[:, :], stats[:, 0:6 * nq]))
    eps_ap, eps_tok = b.constcol(LN_EPS)
    b.op("scalar", lambda e: e.activation(rstd[:, :], mv[:, 1:2], AF.Sqrt,
                                          bias=eps_ap),
         waits=[b.mark("ve"), eps_tok])
    b.op("vector", lambda e: e.reciprocal(rstd[:, :], rstd[:, :]),
         waits=[b.mark("se")])
    return b.op("vector", lambda e: e.tensor_scalar(
        lt, xs, mv[:, 0:1], rstd[:, :], ALU.subtract, ALU.mult))


def ln_stream_to_fm(b, x_dram, lnx_fm, n_tok, x_stage, ln_stage, stats, mv,
                    rstd, tbanks, ident, ident_tok, fm_tpc=None,
                    extra_tile_cb=None):
    if fm_tpc is None:
        fm_tpc = n_tok
    nt = n_tok // 128
    last_use = [None, None]
    for tj in range(nt):
        sl = tj % 2
        xs = x_stage[:, sl * D:(sl + 1) * D]
        lt = ln_stage[:, sl * D:(sl + 1) * D]
        ld = b.dma("sync", xs, x_dram[tj * 128:(tj + 1) * 128, :], f"xld{sl}",
                   waits=[last_use[sl]])
        aptok = ln_tile(b, xs, lt, stats, mv, rstd, ld)
        if extra_tile_cb is not None:
            extra_tile_cb(tj, xs, ld)
        for f0 in range(0, NKC, 4):
            bi, bank, war = tbanks.take()
            ptok = None
            for g in range(4):
                fi = f0 + g
                ptok = b.pe(lambda e, s=lt[:, fi * 128:(fi + 1) * 128],
                            o=bank[:, g * 128:(g + 1) * 128]:
                            e.transpose(o, s, ident),
                            waits=([war, aptok, ident_tok] if g == 0 else ()),
                            sem="pe" if g == 3 else None)
            vts = []
            for g in range(4):
                fi = f0 + g
                vts.append(b.op(
                    "vector",
                    lambda e, o=lnx_fm[:, fi * fm_tpc + tj * 128:
                                       fi * fm_tpc + (tj + 1) * 128],
                    i=bank[:, g * 128:(g + 1) * 128]:
                    e.tensor_copy(o, i),
                    waits=[ptok] if g == 0 else ()))
            tbanks.set_free(bi, vts[-1])
        last_use[sl] = b.mark("ve")
    return b.mark("ve")


def transpose_stream(b, banks, src_view, dst_view, n_groups, ident,
                     src_tok=None, grp=4):
    for gi in range(n_groups):
        bi, bank, war = banks.take()
        ptok = None
        for g in range(grp):
            ptok = b.pe(lambda e, s=src_view(gi, g), o=bank[:, g * 128:(g + 1) * 128]:
                        e.transpose(o, s, ident),
                        waits=([war, src_tok] if g == 0 else ()),
                        sem="pe" if g == grp - 1 else None)
        vt = b.op("vector", lambda e, o=dst_view(gi), i=bank[:, 0:grp * 128]:
                  e.tensor_copy(o, i), waits=[ptok])
        banks.set_free(bi, vt)
    return b.mark("ve")


def proj_fm(b, banks, w_dram, rhs_view, n_out, out_cb, w_sb, n_k=NKC,
            obs=4, halves=(0, 1), extra_k=None):
    slab_cols = obs * 128
    slab_tok = [None, None]
    use_tok = [None, None]
    n_ex = extra_k[0] if extra_k else 0
    for ob0 in range(0, n_out, obs):
        cur = min(obs, n_out - ob0)
        sl = (ob0 // obs) % 2
        base = sl * slab_cols * (n_k + n_ex)
        b.wait("gpsimd", [use_tok[sl]])
        for k in range(n_k):
            slab_tok[sl] = b.dma(
                "gpsimd", w_sb[:, base + k * slab_cols: base + k * slab_cols + cur * 128],
                w_dram[k * 128:(k + 1) * 128, ob0 * 128:(ob0 + cur) * 128], f"wld{sl}")
        if extra_k:
            _, w2, _ = extra_k
            for k in range(n_ex):
                slab_tok[sl] = b.dma(
                    "gpsimd",
                    w_sb[:, base + (n_k + k) * slab_cols: base + (n_k + k) * slab_cols + cur * 128],
                    w2[k * 128:(k + 1) * 128, ob0 * 128:(ob0 + cur) * 128], f"wld{sl}")
        for half in halves:
            for oi in range(cur):
                bi, bank, war = banks.take()
                ptok = None
                for k in range(n_k):
                    lhs = w_sb[:, base + k * slab_cols + oi * 128:
                               base + k * slab_cols + (oi + 1) * 128]
                    w_ = []
                    if k == 0:
                        w_ = [slab_tok[sl], war]
                    last = (k == n_k - 1) and not extra_k
                    r_ap = rhs_view(k, half)
                    rw = r_ap.shape[-1]
                    ptok = b.pe(lambda e, l=lhs, r=r_ap, o=bank[:, 0:rw],
                                s=(k == 0), st=last: e.matmul(o, l, r, start=s, stop=st),
                                waits=w_, sem="pe" if last else None)
                if extra_k:
                    _, _, rhs2 = extra_k
                    for k in range(n_ex):
                        lhs = w_sb[:, base + (n_k + k) * slab_cols + oi * 128:
                                   base + (n_k + k) * slab_cols + (oi + 1) * 128]
                        last = k == n_ex - 1
                        r_ap = rhs2(k, half)
                        rw = r_ap.shape[-1]
                        ptok = b.pe(lambda e, l=lhs, r=r_ap, o=bank[:, 0:rw],
                                    st=last: e.matmul(o, l, r, start=False, stop=st),
                                    sem="pe" if last else None)
                vt = out_cb(ob0 + oi, half, bank, ptok)
                banks.set_free(bi, vt)
        use_tok[sl] = b.mark("pe")



def host_prep(inputs):
    p = {}
    p["w_qkv"] = _f32h(np.concatenate(
        [inputs["L0_attn_Wq"], inputs["L0_attn_Wk"], inputs["L0_attn_Wv"]], axis=1))
    p["w_o"] = _f32h(inputs["L0_attn_Wo"])
    p["w_org_t"] = _f32h(np.asarray(inputs["L0_ffn_W_org"]).T)
    ld = np.asarray(inputs["L0_ffn_lora_down"])
    p["w_down"] = _f32h(np.concatenate([ld[e] for e in range(NE)], axis=1))
    lu = np.asarray(inputs["L0_ffn_lora_up"])
    p["w_up"] = _f32h(np.concatenate([lu[e] for e in range(NE)], axis=0))
    p["lora_keys"] = _f32h(inputs["L0_ffn_lora_keys"])
    p["wq_mole_t"] = _f32h(np.asarray(inputs["L0_ffn_Wq"]).T)
    p["w_inproj_t"] = _f32h(np.asarray(inputs["L1_attn_in_proj_w"]).T)
    p["conv_w"] = _f32h(inputs["L1_attn_conv_weight"])
    p["conv_b"] = _f32h(inputs["L1_attn_conv_bias"]).reshape(CONV_DIM // 128, 128).T.copy()
    p["dt_bias"] = _f32h(inputs["L1_attn_dt_bias"]).reshape(1, MH)
    p["a_log"] = _f32h(inputs["L1_attn_A_log"]).reshape(1, MH)
    p["d_vec"] = _f32h(inputs["L1_attn_D"]).reshape(1, MH)
    p["norm_w"] = _f32h(inputs["L1_attn_norm_weight"]).reshape(1, MI)
    p["w_outproj_t"] = _f32h(np.asarray(inputs["L1_attn_out_proj_w"]).T)
    p["w_ffn_t"] = _f32h(np.asarray(inputs["L1_ffn_W"]).T)
    p["ffn_bias"] = _f32h(inputs["L1_ffn_bias"]).reshape(NKC, 128).T.copy()
    A = np.asarray(inputs["L1_ffn_A"])
    p["w_a_t"] = _f32h(np.concatenate([A[e].T for e in range(16)], axis=1))
    Bf = np.asarray(inputs["L1_ffn_Bf"])
    p["w_bf"] = _f32h(np.concatenate([Bf[e].T for e in range(16)], axis=0))
    p["coeffs16"] = _f32h(inputs["L1_ffn_coeffs"]).reshape(1, 16)
    p["scales16"] = _f32h(inputs["L1_ffn_scales"]).reshape(1, 16)
    p["ident_bf"] = np.eye(128, dtype=BF)
    p["conv_w_cols"] = _f32h(inputs["L1_attn_conv_weight"]).reshape(
        CONV_DIM // 128, 128, MK)
    half = DH // 2
    inv_freq = 1.0 / ROPE_THETA ** (np.arange(0, half, dtype=np.float64) / half)
    pos = np.arange(T, dtype=np.float64)
    ang = pos[:, None] * inv_freq[None, :]
    p["cos_t"] = _f32h(np.cos(ang).T)
    p["sin_t"] = _f32h(np.sin(ang).T)
    m = np.triu(np.ones((HALF, HALF), dtype=np.float32), k=1) * (-3e38 / 4)
    p["mask_qk"] = _f32h(m)
    p["mask_qk_b"] = m.astype(BF)
    p["maskt_kq_b"] = (np.tril(np.ones((HALF, HALF), np.float32), -1)
                       * (-3e38 / 4)).astype(BF)
    p["ident_f"] = np.eye(128, dtype=np.float32)
    p["ones_col"] = np.ones((128, 1), BF)
    p["ones_row_f"] = np.ones((1, 128), np.float32)
    p["ones_row_b"] = np.ones((1, 128), BF)
    p["cosq_t"] = _f32h(p["cos_t"] / np.sqrt(128.0))
    p["sinq_t"] = _f32h(p["sin_t"] / np.sqrt(128.0))
    p["keys_t"] = _f32h(np.asarray(inputs["L0_ffn_lora_keys"]).T)
    exp8 = np.zeros((8, 512), np.float32)
    for j in range(4):
        for q in range(128):
            exp8[2 * j + q // 64, j * 128 + q] = 1.0
    p["exp8"] = exp8
    exp16 = np.zeros((16, 256), np.float32)
    for j in range(2):
        for q in range(128):
            exp16[(j * 128 + q) // 16, j * 128 + q] = 1.0
    p["exp16"] = exp16
    wp = np.concatenate([np.asarray(inputs["L1_attn_in_proj_w"]),
                         np.zeros((NPC * 128 - PROJ, D), np.float32)], axis=0)
    p["w_inproj_p"] = _f32h(wp.T)
    cw = _f32h(inputs["L1_attn_conv_weight"])
    p["conv_w_c"] = np.ascontiguousarray(
        cw.reshape(48, 128, 4).transpose(1, 0, 2).reshape(128, 192))
    p["conv_b_c"] = np.ascontiguousarray(
        _f32h(inputs["L1_attn_conv_bias"]).reshape(48, 128).T)
    p["dt_bias_c"] = _f32h(inputs["L1_attn_dt_bias"]).reshape(64, 1)
    p["a_log_c"] = _f32h(inputs["L1_attn_A_log"]).reshape(64, 1)
    p["d_row"] = _f32h(np.repeat(np.asarray(inputs["L1_attn_D"]), 64)).reshape(1, 4096)
    rep = np.zeros((64, 4096), BF)
    for h in range(64):
        rep[h, h * 64:(h + 1) * 64] = 1
    p["rep"] = rep
    p["maskneg128"] = _f32h(np.tril(np.ones((128, 128), np.float32), -1) * (-3e38 / 4))
    p["norm_row"] = _f32h(inputs["L1_attn_norm_weight"]).reshape(1, 4096)
    p["bias_c"] = np.ascontiguousarray(
        _f32h(inputs["L1_ffn_bias"]).reshape(NKC, 128).T)
    return p



def build_l1():
    nc = bass.Bass("TRN2", target_bir_lowering=False, debug=False)
    ctx = ExitStack()
    b = Bld(nc, ctx)
    x_d = nc.dram_tensor("x", [TPC, D], F32, kind="ExternalInput").ap()
    wqkv_d = nc.dram_tensor("w_qkv", [D, 6144], F32, kind="ExternalInput").ap()
    cos_d = nc.dram_tensor("cos_c", [64, TPC], F32, kind="ExternalInput").ap()
    sin_d = nc.dram_tensor("sin_c", [64, TPC], F32, kind="ExternalInput").ap()
    cosq_d = nc.dram_tensor("cosq_c", [64, TPC], F32, kind="ExternalInput").ap()
    sinq_d = nc.dram_tensor("sinq_c", [64, TPC], F32, kind="ExternalInput").ap()
    ident_d = nc.dram_tensor("ident", [128, 128], BF16, kind="ExternalInput").ap()
    qkv_d = nc.dram_tensor("qkv_fm", [6144, TPC], BF16, kind="ExternalOutput").ap()
    vtm_d = nc.dram_tensor("v_tm", [TPC, 1024], BF16, kind="ExternalOutput").ap()

    NT = TPC // 128
    x_stage = b.sb("x_stage", [128, 2 * D], F32)
    ln_stage = b.sb("ln_stage", [128, 2 * D], BF16)
    stats = b.sb("stats", [128, 48], F32)
    mv = b.sb("mv", [128, 2], F32)
    rstd = b.sb("rstd", [128, 1], F32)
    lnx_fm = b.sb("lnx_fm", [128, NKC * TPC], BF16)
    ident = b.sb("ident_sb", [128, 128], BF16)
    cos_sb = b.sb("cos_sb", [64, TPC], BF16)
    sin_sb = b.sb("sin_sb", [64, TPC], BF16)
    cosq_sb = b.sb("cosq_sb", [64, TPC], BF16)
    sinq_sb = b.sb("sinq_sb", [64, TPC], BF16)
    w_sb = b.sb("w_sb", [128, 2 * 4 * 128 * NKC], BF16)
    stage = b.sb("stage", [128, 4 * HALF], BF16)
    ra = b.sb("ra", [64, HALF], F32)
    rb = b.sb("rb", [64, HALF], F32)
    rc = b.sb("rc", [64, HALF], F32)
    rd_ = b.sb("rd", [64, HALF], F32)
    vstage = b.sb("vstage", [128, 4 * HALF], BF16)

    b.dma("sync", ident[:, :], ident_d[:, :], "cld")
    b.dma("gpsimd", cos_sb[:, :], cos_d[:, :], "cldg")
    b.dma("gpsimd", sin_sb[:, :], sin_d[:, :], "cldg")
    b.dma("gpsimd", cosq_sb[:, :], cosq_d[:, :], "cldg")
    b.dma("gpsimd", sinq_sb[:, :], sinq_d[:, :], "cldg")
    c0 = b.mark("cld")
    c1 = b.mark("cldg")
    b.wait("vector", [c1])

    banks = Banks(b, "bk", 6)
    tbanks = Banks(b, "tb", 2, dt=BF16)

    ident_ap = ident[:, :]
    lnfm_done = ln_stream_to_fm(b, x_d, lnx_fm, TPC, x_stage, ln_stage, stats,
                                mv, rstd, tbanks, ident_ap, c0)
    b.wait("tensor", [lnfm_done])

    stage_i = [0]
    stage_free = [None] * 4
    vstage_free = [None] * 4

    def rhs_view(k, half):
        return lnx_fm[:, k * TPC + half * HALF: k * TPC + (half + 1) * HALF]

    def out_cb(oi, half, bank, ptok):
        si = stage_i[0] % 4
        stage_i[0] += 1
        stg = stage[:, si * HALF:(si + 1) * HALF]
        if oi < NH + NKV:
            csrc = cosq_sb if oi < NH else cos_sb
            ssrc = sinq_sb if oi < NH else sin_sb
            cs = csrc[:, half * HALF:(half + 1) * HALF]
            sn = ssrc[:, half * HALF:(half + 1) * HALF]
            b.op("vector", lambda e: e.tensor_tensor(ra[:, :], bank[0:64, :], cs, ALU.mult),
                 waits=[ptok, c1])
            b.op("vector", lambda e: e.tensor_tensor(rb[:, :], bank[64:128, :], sn, ALU.mult))
            b.op("vector", lambda e: e.tensor_tensor(rc[:, :], bank[0:64, :], sn, ALU.mult))
            last_ps = b.op("vector", lambda e: e.tensor_tensor(rd_[:, :], bank[64:128, :], cs, ALU.mult))
            b.op("vector", lambda e, stg=stg: e.tensor_tensor(stg[0:64, :], ra[:, :], rb[:, :], ALU.subtract),
                 waits=[stage_free[si]])
            vt = b.op("vector", lambda e, stg=stg: e.tensor_tensor(stg[64:128, :], rc[:, :], rd_[:, :], ALU.add))
            bank_free = last_ps
        else:
            vt = b.op("vector", lambda e, stg=stg: e.tensor_copy(stg, bank[:, :]),
                      waits=[ptok, stage_free[si]])
            bank_free = vt
        dtok = b.dma("sync", qkv_d[oi * 128:(oi + 1) * 128, half * HALF:(half + 1) * HALF],
                     stg, f"so{si}", waits=[vt])
        stage_free[si] = dtok
        if oi >= NH + NKV:
            vi = oi - (NH + NKV)
            bi2, bank2, war2 = tbanks.take()
            ptok2 = None
            for g in range(4):
                ptok2 = b.pe(lambda e, s=stg[:, g * 128:(g + 1) * 128],
                             o=bank2[:, g * 128:(g + 1) * 128]:
                             e.transpose(o, s, ident_ap),
                             waits=([war2, vt] if g == 0 else ()),
                             sem="pe" if g == 3 else None)
            sj = stage_i[0] % 4
            stage_i[0] += 1
            vstg = vstage[:, sj * HALF:(sj + 1) * HALF]
            vt2 = b.op("vector", lambda e, vstg=vstg: e.tensor_copy(vstg, bank2[:, :]),
                       waits=[ptok2, vstage_free[sj]])
            tbanks.set_free(bi2, vt2)
            d2 = None
            for g in range(4):
                d2 = b.dma("sync",
                           vtm_d[half * HALF + g * 128: half * HALF + (g + 1) * 128,
                                 vi * 128:(vi + 1) * 128],
                           vstg[:, g * 128:(g + 1) * 128], f"sv{sj}", waits=[vt2] if g == 0 else ())
            vstage_free[sj] = d2
        return bank_free

    proj_fm(b, banks, wqkv_d, rhs_view, 48, out_cb, w_sb)

    b.wait("sync", [b.mark(s) for s in list(b.cnt) if s.startswith(("so", "sv"))])
    b.finalize()
    return nc, ctx


def run_l1(inputs, prep):
    nc, ctx = build_l1()
    in_maps = []
    X = np.asarray(inputs["L0_attn_X"])
    for core in range(8):
        bb, cc = core // 4, core % 4
        sl = slice(cc * TPC, (cc + 1) * TPC)
        in_maps.append({
            "x": _f32h(X[bb, sl, :]),
            "w_qkv": prep["w_qkv"],
            "cos_c": prep["cos_t"][:, sl].copy(),
            "sin_c": prep["sin_t"][:, sl].copy(),
            "ident": prep["ident_bf"],
        })
    res = run_bass_kernel_spmd(nc, in_maps, core_ids=list(range(8)))
    ctx.close()
    return res.results






def proj_to_scr(b, banks, w_dram, rhs_view, n_out, scr_d, w_sb, wstage,
                sem_pfx="wo", obs=4, extra_k=None, evac_fn=None,
                halves=(0, 1), hw=HALF):
    stage_free = [None] * 4
    wi = [0]

    def cb(oi, half, bank, ptok):
        si = wi[0] % 4
        wi[0] += 1
        stg = wstage[:, si * hw:(si + 1) * hw]
        if evac_fn is None:
            vt = b.op("vector", lambda e, stg=stg, w=hw: e.tensor_copy(
                stg, bank[:, 0:w]), waits=[ptok, stage_free[si]])
        else:
            vt = evac_fn(oi, half, bank, stg, [ptok, stage_free[si]])
        dt_ = b.dma("sync", scr_d[oi * 128:(oi + 1) * 128,
                                  half * hw:(half + 1) * hw],
                    stg, f"{sem_pfx}{si}", waits=[vt])
        stage_free[si] = dt_
        return vt

    proj_fm(b, banks, w_dram, rhs_view, n_out, cb, w_sb, obs=obs,
            extra_k=extra_k, halves=halves)
    return [b.mark(s) for s in list(b.cnt) if s.startswith(sem_pfx)]


def resid_tile_loop(b, scr_done, scr_d, x_in_d, xout_d, ident, ttb, scr_tile,
                    x_stage, ln_extras=None):
    NT = TPC // 128
    d1 = None
    pm_started = [False]
    for tt in range(NT):
        wl = None
        for o in range(NKC):
            wl = b.dma("sync", scr_tile[:, o * 128:(o + 1) * 128],
                       scr_d[o * 128:(o + 1) * 128, tt * 128:(tt + 1) * 128],
                       "wtl", waits=(scr_done + [b.mark("pe")]) if o == 0 else ())
        xl = b.dma("sync", x_stage[:, :], x_in_d[tt * 128:(tt + 1) * 128, :],
                   "xl", waits=[b.mark("ve"), d1])
        for f0 in range(0, NKC, 4):
            bi, bank, war = ttb.take()
            ptk = None
            for g in range(4):
                fi = f0 + g
                ptk = b.pe(lambda e, s=scr_tile[:, fi * 128:(fi + 1) * 128],
                           o=bank[:, g * 128:(g + 1) * 128], i=ident[:, :]:
                           e.transpose(o, s, i),
                           waits=([war, wl] if g == 0 else ()), sem="pe")
            vt = b.op("vector", lambda e, o=x_stage[:, f0 * 128:(f0 + 4) * 128],
                      i=bank[:, 0:512]: e.tensor_tensor(o, o, i, ALU.add),
                      waits=[ptk, xl])
            ttb.set_free(bi, vt)
        x1t = b.mark("ve")
        d1 = b.dma("sync", xout_d[tt * 128:(tt + 1) * 128, :], x_stage[:, :],
                   "x1s", waits=[x1t])
        if ln_extras is not None:
            (stats, mv, rstd, ln_t, ones_sb, pmean_ps, pm_stage, pm_d) = ln_extras
            b.wait("vector", [b.mark("pe")])
            ap2 = ln_tile(b, x_stage[:, :], ln_t[:, :], stats, mv, rstd, None)
            b.wait("tensor", [ap2])
            for fi in range(NKC):
                b.pe(lambda e, l=ln_t[:, fi * 128:(fi + 1) * 128],
                     o=pmean_ps[:, fi:fi + 1], s=(not pm_started[0]),
                     st=(tt == NT - 1 and fi == NKC - 1):
                     e.matmul(o, l, ones_sb[:, :], start=s, stop=st,
                              skip_group_check=True),
                     sem="pe" if fi == NKC - 1 else None)
                pm_started[0] = True
    if ln_extras is not None:
        (stats, mv, rstd, ln_t, ones_sb, pmean_ps, pm_stage, pm_d) = ln_extras
        pmt = b.op("vector", lambda e: e.tensor_copy(pm_stage[:, :],
                                                     pmean_ps[:, :]),
                   waits=[b.mark("pe")])
        b.dma("sync", pm_d[:, :], pm_stage[:, :], "pms", waits=[pmt])



def attn_unit(b, env, q_ap_full, k_sb, v_sb, out_write, qb):
    (acc_bank, pb3, mask_sb, maskt_sb, sbufs, pj, mq_sb, dq_sb,
     negm, ccol, crow, cbc, pt_sb, identf, ident, onesr_sb, neginf_ap,
     neginf_tok, c20_ap, nc20_ap, cc_tok, psA, slot_dq) = env
    nkb = qb + 1
    nkc = nkb * 4
    dkb = qb
    for qt in range(4):
        sb_cur = sbufs[qt % 2]
        qa = q_ap_full[:, qb * HALF + qt * 128: qb * HALF + (qt + 1) * 128]
        nw = (nkb + 3) // 4
        ev_toks = []
        for w in range(nw):
            kbs = list(range(w * 4, min(w * 4 + 4, nkb)))
            ptok = None
            for kb in kbs:
                last = kb == kbs[-1]
                ptok = b.pe(lambda e, qa=qa,
                            rr=k_sb[:, kb * HALF:(kb + 1) * HALF],
                            o=psA[:, (kb % 4) * HALF:((kb % 4) + 1) * HALF],
                            st=(kb != dkb):
                            e.matmul(o, qa, rr, start=True, stop=st,
                                     skip_group_check=True),
                            waits=([b.mark("ve")] if kb == kbs[0] else ()),
                            sem="pe" if (last and kb != dkb) else None)
                if kb == dkb:
                    ptok = b.pe(lambda e, m=mask_sb[:, qt * HALF:(qt + 1) * HALF],
                                o=psA[:, (kb % 4) * HALF:((kb % 4) + 1) * HALF]:
                                e.matmul(o, ident[:, :], m, start=False,
                                         stop=True, skip_group_check=True),
                                sem="pe" if last else None)
            nk_w = (kbs[-1] + 1 - kbs[0]) * HALF
            ev = b.op("vector", lambda e, o=sb_cur[:, w * 2048:w * 2048 + nk_w],
                      i=psA[:, 0:nk_w], col=mq_sb[:, 4 + w:5 + w]:
                      e.tensor_scalar(o, i, 0.0, neginf_ap, ALU.add, ALU.max,
                                      accum_out=col),
                      waits=[ptok, neginf_tok, slot_dq[qt % 2]], chain=False)
            ev_toks.append(ev)
        b.op("vector", lambda e, qt=qt, nw=nw: e.reduce_max(
            mq_sb[:, qt:qt + 1], mq_sb[:, 4:4 + nw], axis=mybir.AxisListType.X),
            waits=ev_toks, chain=False)
        b.op("vector", lambda e, qt=qt: e.tensor_scalar(
            negm[:, :], mq_sb[:, qt:qt + 1], -1.0, None, ALU.mult, ALU.bypass),
            waits=[b.mark("se")])
        dq = b.op("scalar", lambda e, qt=qt, n=nkb * HALF, sb_cur=sb_cur:
                  e.activation(pj[:, 0:n], sb_cur[:, 0:n], AF.Exp,
                               bias=negm[:, :], accum_out=dq_sb[:, qt:qt + 1]),
                  waits=[b.mark("ve")])
        slot_dq[qt % 2] = dq
        b.op("scalar", lambda e, qt=qt: e.activation(
            dq_sb[:, 4:5], mq_sb[:, qt:qt + 1], AF.Exp, bias=c20_ap,
            scale=-1.0), waits=[cc_tok])
        b.op("vector", lambda e, qt=qt: e.reciprocal(dq_sb[:, 5:6],
                                                     dq_sb[:, qt:qt + 1]),
             waits=[b.mark("se")])
        b.op("vector", lambda e: e.tensor_tensor(
            ccol[:, 0:1], dq_sb[:, 4:5], dq_sb[:, 5:6], ALU.mult))
        ptc2 = b.pe(lambda e: e.transpose(pb3[0:1, 256:384], ccol[:, 0:1],
                                          identf[:, :]),
                    waits=[b.mark("ve")], sem="pe")
        b.op("vector", lambda e, qt=qt: e.tensor_copy(
            crow[:, qt * 128:(qt + 1) * 128], pb3[0:1, 256:384]),
            waits=[ptc2])
    cbb = b.pe(lambda e: e.matmul(pb3[:, :], onesr_sb[:, :], crow[:, :],
                                  start=True, stop=True,
                                  skip_group_check=True),
               waits=[b.mark("ve")], sem="pe")
    cbt = b.op("vector", lambda e: e.tensor_copy(cbc[:, :], pb3[:, :]),
               waits=[cbb])
    av_tok = None
    started = False
    for k0 in range(0, nkc, 4):
        kn = min(4, nkc - k0)
        ptok = None
        for i in range(kn):
            kc = k0 + i
            dmask = kc >= dkb * 4
            last = i == kn - 1
            ptok = b.pe(lambda e, l=k_sb[:, kc * 128:(kc + 1) * 128],
                        r=q_ap_full[:, qb * HALF:(qb + 1) * HALF],
                        o=psA[:, i * HALF:(i + 1) * HALF], st=(not dmask):
                        e.matmul(o, l, r, start=True, stop=st,
                                 skip_group_check=True),
                        waits=([b.mark("se"), b.mark("ve")] if i == 0 else ()),
                        sem="pe" if (last and not dmask) else None)
            if dmask:
                r2i = kc - dkb * 4
                ptok = b.pe(lambda e, m=maskt_sb[:, r2i * HALF:(r2i + 1) * HALF],
                            o=psA[:, i * HALF:(i + 1) * HALF]:
                            e.matmul(o, ident[:, :], m, start=False, stop=True,
                                     skip_group_check=True),
                            sem="pe" if last else None)
        et = b.op("scalar", lambda e, o=pt_sb[:, 0:kn * HALF],
                  i=psA[:, 0:kn * HALF]: e.activation(o, i, AF.Exp,
                                                      bias=nc20_ap),
                  waits=[ptok, b.mark("pe")], chain=False)
        for i in range(kn):
            kc = k0 + i
            av_tok = b.pe(lambda e, l=v_sb[:, kc * 128:(kc + 1) * 128],
                          r=pt_sb[:, i * HALF:(i + 1) * HALF],
                          o=acc_bank[:, :], s=(not started),
                          st=(kc == nkc - 1):
                          e.matmul(o, l, r, start=s, stop=st,
                                   skip_group_check=True),
                          waits=([et] if i == 0 else ()),
                          sem="pe" if kc == nkc - 1 else None)
            started = True
    return out_write(acc_bank, cbc, av_tok, cbt)


def build_l2a():
    nc = bass.Bass("TRN2", target_bir_lowering=False, debug=False)
    ctx = ExitStack()
    b = Bld(nc, ctx)
    q_d = nc.dram_tensor("q_fm", [512, 2 * T], BF16, kind="ExternalInput").ap()
    k_d = nc.dram_tensor("k_fm", [128, 2 * T], BF16, kind="ExternalInput").ap()
    v_d = nc.dram_tensor("v_tm", [2 * T, 128], BF16, kind="ExternalInput").ap()
    mask_d = nc.dram_tensor("mask_qk_b", [HALF, HALF], BF16, kind="ExternalInput").ap()
    maskt_d = nc.dram_tensor("maskt_kq_b", [HALF, HALF], BF16, kind="ExternalInput").ap()
    ident_d = nc.dram_tensor("ident", [128, 128], BF16, kind="ExternalInput").ap()
    identf_d = nc.dram_tensor("ident_f", [128, 128], F32, kind="ExternalInput").ap()
    onesr_d = nc.dram_tensor("ones_row_f", [1, 128], F32, kind="ExternalInput").ap()
    attn_d = nc.dram_tensor("attn_fm", [512, 2 * T], BF16, kind="ExternalOutput").ap()

    q_sb = b.sb("q_sb", [128, 8 * T], BF16)
    k_sb = b.sb("k_sb", [128, 2 * T], BF16)
    v_sb = b.sb("v_sb", [128, 2 * T], BF16)
    sbufs = [b.sb("s_buf0", [128, T], BF16), b.sb("s_buf1", [128, T], BF16)]
    pj = b.sb("pj", [128, T], BF16)
    mq_sb = b.sb("mq_sb", [128, 8], F32)
    dq_sb = b.sb("dq_sb", [128, 8], F32)
    negm = b.sb("negm", [128, 1], F32)
    mask_sb = b.sb("mask_sb", [128, 4 * HALF], BF16)
    maskt_sb = b.sb("maskt_sb", [128, 4 * HALF], BF16)
    identf = b.sb("identf_sb", [128, 128], F32)
    ident = b.sb("ident_sb", [128, 128], BF16)
    onesr_sb = b.sb("onesr_sb", [1, 128], F32)
    ccol = b.sb("ccol", [128, 1], F32)
    crow = b.sb("crow", [1, HALF], F32)
    cbc = b.sb("cbc", [128, HALF], F32)
    pt_sb = b.sb("pt_sb", [128, 4 * HALF], BF16)

    psA = b.psum("psA", [128, 2048], F32)
    pb2 = b.psum("pb2", [128, 512], F32)
    pb3 = b.psum("pb3", [128, 512], F32)

    b.dma("sync", identf[:, :], identf_d[:, :], "cld")
    b.dma("sync", ident[:, :], ident_d[:, :], "cld")
    b.dma("sync", onesr_sb[:, :], onesr_d[:, :], "cld")
    for r in range(4):
        b.dma("sync", mask_sb[:, r * HALF:(r + 1) * HALF],
              mask_d[r * 128:(r + 1) * 128, :], "cld")
        b.dma("sync", maskt_sb[:, r * HALF:(r + 1) * HALF],
              maskt_d[r * 128:(r + 1) * 128, :], "cld")
    b.dma("sync", k_sb[:, :], k_d[:, :], "cld")
    b.dma("sync", v_sb[:, :].rearrange("p (t d) -> p t d", d=128),
          v_d[:, :].rearrange("(t p) d -> p t d", p=128), "cld")
    for ch in range(8):
        b.dma("sync", q_sb[:, ch * T:(ch + 1) * T],
              q_d[(ch // 2) * 128:(ch // 2 + 1) * 128,
                  (ch % 2) * T:(ch % 2 + 1) * T], "cld")
    consts = b.mark("cld")
    b.wait("tensor", [consts])
    b.wait("vector", [consts])
    b.wait("scalar", [consts])
    neginf_ap, neginf_tok = b.constcol(-3.0e38)
    c20_ap, t1 = b.constcol(20.0)
    nc20_ap, t2 = b.constcol(-20.0)
    cc_tok = b.mark("ve")
    slot_dq = [None, None]

    env = (pb2, pb3, mask_sb, maskt_sb, sbufs, pj, mq_sb, dq_sb,
           negm, ccol, crow, cbc, pt_sb, identf, ident, onesr_sb, neginf_ap,
           neginf_tok, c20_ap, nc20_ap, cc_tok, psA, slot_dq)

    for ch in range(8):
        qh, bt = ch // 2, ch % 2
        q_full = q_sb[:, ch * T:(ch + 1) * T]
        kk = k_sb[:, bt * T:(bt + 1) * T]
        vv = v_sb[:, bt * T:(bt + 1) * T]
        for qb in range(8):
            def out_write(acc_bank, cbc_, av_tok, cbt, ch=ch, qb=qb):
                wt = b.op("vector", lambda e,
                          o=q_sb[:, ch * T + qb * HALF: ch * T + (qb + 1) * HALF],
                          i=acc_bank[:, :]: e.tensor_tensor(o, i, cbc_[:, :],
                                                            ALU.mult),
                          waits=[av_tok, cbt])
                b.dma("sync", attn_d[(ch // 2) * 128:(ch // 2 + 1) * 128,
                                     (ch % 2) * T + qb * HALF:
                                     (ch % 2) * T + (qb + 1) * HALF],
                      q_sb[:, ch * T + qb * HALF: ch * T + (qb + 1) * HALF],
                      "aso", waits=[wt])
                return wt
            attn_unit(b, env, q_full, kk, vv, out_write, qb)

    b.wait("sync", [b.mark("aso")])
    b.finalize()
    return nc, ctx



def build_l2b():
    nc = bass.Bass("TRN2", target_bir_lowering=False, debug=False)
    ctx = ExitStack()
    b = Bld(nc, ctx)
    a_d = nc.dram_tensor("attn_fm", [4096, TPC], BF16, kind="ExternalInput").ap()
    x_d = nc.dram_tensor("x", [TPC, D], F32, kind="ExternalInput").ap()
    wo_d = nc.dram_tensor("w_o", [D, D], F32, kind="ExternalInput").ap()
    ident_d = nc.dram_tensor("ident", [128, 128], BF16, kind="ExternalInput").ap()
    ones_d = nc.dram_tensor("ones_col", [128, 1], BF16, kind="ExternalInput").ap()
    x1_d = nc.dram_tensor("x1", [TPC, D], F32, kind="ExternalOutput").ap()
    pm_d = nc.dram_tensor("pmean", [128, NKC], F32, kind="ExternalOutput").ap()
    wo_scr = nc.dram_tensor("wo_scr", [D, TPC], BF16)

    a_sb = b.sb("a_sb", [128, NKC * TPC], BF16)
    ident = b.sb("ident_sb", [128, 128], BF16)
    ones_sb = b.sb("ones_sb", [128, 1], BF16)
    w_sb = b.sb("w_sb", [128, 2 * 4 * 128 * NKC], BF16)
    wstage = b.sb("wstage", [128, 4 * HALF], BF16)
    wo_tile = b.sb("wo_tile", [128, D], BF16)
    x_stage = b.sb("x_stage", [128, D], F32)
    stats = b.sb("stats", [128, 48], F32)
    mv = b.sb("mv", [128, 2], F32)
    rstd = b.sb("rstd", [128, 1], F32)
    ln_t = b.sb("ln_t", [128, D], BF16)
    pm_stage = b.sb("pm_stage", [128, NKC], F32)

    pb = [b.psum(f"pb{i}", [128, 512], F32) for i in range(4)]
    tbp = [b.psum(f"tbp{i}", [128, 512], BF16) for i in range(2)]
    pmean_ps = b.psum("pmean_ps", [128, NKC], F32)

    b.dma("sync", ident[:, :], ident_d[:, :], "cld")
    b.dma("sync", ones_sb[:, :], ones_d[:, :], "cld")
    for ch in range(NKC):
        b.dma("sync", a_sb[:, ch * TPC:(ch + 1) * TPC],
              a_d[ch * 128:(ch + 1) * 128, :], "cld")
    consts = b.mark("cld")
    b.wait("tensor", [consts])
    b.wait("vector", [consts])
    b.wait("scalar", [consts])

    wo_banks = Banks(b, "wob", 0, handles=pb[0:4])

    def wo_rhs(k, half):
        return a_sb[:, k * TPC + half * HALF: k * TPC + half * HALF + HALF]

    wo_done = proj_to_scr(b, wo_banks, wo_d, wo_rhs, NKC, wo_scr, w_sb, wstage)
    ttb = Banks(b, "ttb", 0, handles=tbp)
    resid_tile_loop(b, wo_done, wo_scr, x_d, x1_d, ident, ttb, wo_tile, x_stage,
                    ln_extras=(stats, mv, rstd, ln_t, ones_sb, pmean_ps,
                               pm_stage, pm_d))
    b.wait("sync", [b.mark(s) for s in list(b.cnt)
                    if s.startswith(("x1s", "pms"))])
    b.finalize()
    return nc, ctx


def build_l3():
    nc = bass.Bass("TRN2", target_bir_lowering=False, debug=False)
    ctx = ExitStack()
    b = Bld(nc, ctx)
    x1_d = nc.dram_tensor("x1", [TPC, D], F32, kind="ExternalInput").ap()
    pms_d = nc.dram_tensor("pmeans", [128, 4 * NKC], F32, kind="ExternalInput").ap()
    worg_d = nc.dram_tensor("w_org_t", [D, D], F32, kind="ExternalInput").ap()
    wdown_d = nc.dram_tensor("w_down", [D, 512], F32, kind="ExternalInput").ap()
    wup_d = nc.dram_tensor("w_up", [512, D], F32, kind="ExternalInput").ap()
    keys_d = nc.dram_tensor("keys_t", [128, 8], F32, kind="ExternalInput").ap()
    wqm_d = nc.dram_tensor("wq_mole_t", [D, 128], F32, kind="ExternalInput").ap()
    exp8_d = nc.dram_tensor("exp8", [8, 512], F32, kind="ExternalInput").ap()
    ident_d = nc.dram_tensor("ident", [128, 128], BF16, kind="ExternalInput").ap()
    identf_d = nc.dram_tensor("ident_f", [128, 128], F32, kind="ExternalInput").ap()
    x2_d = nc.dram_tensor("x2", [TPC, D], F32, kind="ExternalOutput").ap()
    scr_d = nc.dram_tensor("ffn_scr", [D, TPC], BF16)

    lnx_fm = b.sb("lnx_fm", [128, NKC * TPC], BF16)
    h_fm = b.sb("h_fm", [128, 4 * TPC], BF16)
    x_stage = b.sb("x_stage", [128, 2 * D], F32)
    ln_stage = b.sb("ln_stage", [128, 2 * D], BF16)
    stats = b.sb("stats", [128, 48], F32)
    mv = b.sb("mv", [128, 2], F32)
    rstd = b.sb("rstd", [128, 1], F32)
    ident = b.sb("ident_sb", [128, 128], BF16)
    identf = b.sb("identf_sb", [128, 128], F32)
    keys_sb = b.sb("keys_sb", [128, 8], BF16)
    wqm_sb = b.sb("wqm_sb", [128, NKC * 128], BF16)
    exp8_sb = b.sb("exp8_sb", [8, 512], F32)
    pms_sb = b.sb("pms_sb", [128, 4 * NKC], F32)
    mean_sb = b.sb("mean_sb", [128, NKC], F32)
    meanb_sb = b.sb("meanb_sb", [128, NKC], BF16)
    qv_sb = b.sb("qv_sb", [128, 1], BF16)
    scol = b.sb("scol", [8, 1], F32)
    srow = b.sb("srow", [1, 8], F32)
    urow = b.sb("urow", [1, 8], F32)
    smax = b.sb("smax", [1, 2], F32)
    sexp = b.sb("sexp", [1, 8], F32)
    ssum = b.sb("ssum", [1, 2], F32)
    crow8 = b.sb("crow8", [1, 8], F32)
    ccolv = b.sb("ccolv", [8, 1], F32)
    ecol = b.sb("ecol", [128, 4], F32)
    w_sb = b.sb("w_sb", [128, 2 * 2 * 128 * (NKC + 4)], BF16)
    wstage = b.sb("wstage", [128, 4 * HALF], BF16)
    scr_tile = b.sb("scr_tile", [128, D], BF16)

    pb = [b.psum(f"pb{i}", [128, 512], F32) for i in range(4)]
    tbp = [b.psum(f"tbp{i}", [128, 512], BF16) for i in range(2)]
    sm_ps = b.psum("sm_ps", [128, 16], F32)

    b.dma("sync", ident[:, :], ident_d[:, :], "cld")
    b.dma("sync", identf[:, :], identf_d[:, :], "cld")
    b.dma("gpsimd", keys_sb[:, :], keys_d[:, :], "cldg")
    b.dma("gpsimd", wqm_sb[:, :].rearrange("p (k q) -> p k q", q=128),
          wqm_d[:, :].rearrange("(k p) q -> p k q", p=128), "cldg")
    b.dma("sync", exp8_sb[:, :], exp8_d[:, :], "cld")
    b.dma("sync", pms_sb[:, :], pms_d[:, :], "cld")
    consts = b.mark("cld")
    constsg = b.mark("cldg")
    b.wait("tensor", [consts, constsg])
    b.wait("vector", [consts, constsg])
    b.wait("scalar", [consts, constsg])

    tbanks = Banks(b, "tbk", 0, handles=tbp)
    lnfm_done = ln_stream_to_fm(b, x1_d, lnx_fm, TPC, x_stage, ln_stage, stats,
                                mv, rstd, tbanks, ident[:, :], consts)
    b.wait("tensor", [lnfm_done])

    b.op("vector", lambda e: e.tensor_tensor(
        mean_sb[:, :], pms_sb[:, 0:NKC], pms_sb[:, NKC:2 * NKC], ALU.add))
    b.op("vector", lambda e: e.tensor_tensor(
        mean_sb[:, :], mean_sb[:, :], pms_sb[:, 2 * NKC:3 * NKC], ALU.add))
    b.op("vector", lambda e: e.tensor_tensor(
        mean_sb[:, :], mean_sb[:, :], pms_sb[:, 3 * NKC:4 * NKC], ALU.add))
    b.op("vector", lambda e: e.tensor_scalar(
        meanb_sb[:, :], mean_sb[:, :], 1.0 / T, None, ALU.mult, ALU.bypass))
    mtok = b.mark("ve")
    qtok = None
    for k in range(NKC):
        qtok = b.pe(lambda e, l=wqm_sb[:, k * 128:(k + 1) * 128],
                    r=meanb_sb[:, k:k + 1], o=sm_ps[:, 0:1],
                    s=(k == 0), st=(k == NKC - 1):
                    e.matmul(o, l, r, start=s, stop=st, skip_group_check=True),
                    waits=[mtok] if k == 0 else (),
                    sem="pe" if k == NKC - 1 else None)
    b.op("vector", lambda e: e.tensor_copy(qv_sb[:, :], sm_ps[:, 0:1]),
         waits=[qtok])
    stok = b.pe(lambda e: e.matmul(sm_ps[0:8, 1:2], keys_sb[:, :], qv_sb[:, :],
                                   start=True, stop=True, skip_group_check=True),
                waits=[b.mark("ve")], sem="pe")
    b.op("vector", lambda e: e.tensor_copy(scol[:, :], sm_ps[0:8, 1:2]),
         waits=[stok])
    t1 = b.pe(lambda e: e.transpose(sm_ps[0:1, 2:10], scol[:, :],
                                    identf[0:8, 0:8]),
              waits=[b.mark("ve")], sem="pe")
    b.op("vector", lambda e: e.tensor_copy(srow[:, :], sm_ps[0:1, 2:10]),
         waits=[t1])
    b.op("vector", lambda e: e.tensor_scalar(
        urow[:, :], srow[:, :], 1.0 / np.sqrt(128.0), None, ALU.mult, ALU.bypass))
    b.op("vector", lambda e: e.reduce_max(smax[:, 0:1], urow[:, :],
                                          axis=mybir.AxisListType.X))
    b.op("vector", lambda e: e.tensor_scalar(
        smax[:, 1:2], smax[:, 0:1], -1.0, None, ALU.mult, ALU.bypass))
    b.op("scalar", lambda e: e.activation(sexp[:, :], urow[:, :], AF.Exp,
                                          bias=smax[:, 1:2],
                                          accum_out=ssum[:, 0:1]),
         waits=[b.mark("ve")])
    b.op("vector", lambda e: e.reciprocal(ssum[:, 1:2], ssum[:, 0:1]),
         waits=[b.mark("se")])
    b.op("vector", lambda e: e.tensor_scalar(
        crow8[:, :], sexp[:, :], ssum[:, 1:2], None, ALU.mult, ALU.bypass))
    t2 = b.pe(lambda e: e.transpose(sm_ps[0:8, 10:11], crow8[:, :],
                                    identf[0:1, 0:1]),
              waits=[b.mark("ve")], sem="pe")
    b.op("vector", lambda e: e.tensor_copy(ccolv[:, :], sm_ps[0:8, 10:11]),
         waits=[t2])
    ec_tok = None
    for j in range(4):
        ej = b.pe(lambda e, j=j: e.matmul(sm_ps[:, 11 + j:12 + j],
                                          exp8_sb[:, j * 128:(j + 1) * 128],
                                          ccolv[:, :], start=True, stop=True,
                                          skip_group_check=True),
                  waits=[b.mark("ve")] if j == 0 else (), sem="pe")
        ec_tok = b.op("vector", lambda e, j=j: e.tensor_copy(
            ecol[:, j:j + 1], sm_ps[:, 11 + j:12 + j]), waits=[ej])

    hbanks = Banks(b, "hbk", 0, handles=pb[0:4])

    def lnx_rhs(k, half):
        return lnx_fm[:, k * TPC + half * HALF: k * TPC + half * HALF + HALF]

    def h_cb(oi, half, bank, ptok):
        vt = b.op("vector", lambda e, o=h_fm[:, oi * TPC + half * HALF:
                                             oi * TPC + half * HALF + HALF],
                  i=bank[:, :], j=oi: e.tensor_scalar(
                      o, i, ecol[:, j:j + 1], None, ALU.mult, ALU.bypass),
                  waits=[ptok, ec_tok])
        return vt

    proj_fm(b, hbanks, wdown_d, lnx_rhs, 4, h_cb, w_sb, obs=2)
    h_done = b.mark("ve")
    b.wait("tensor", [h_done])

    def h_rhs(k, half):
        return h_fm[:, k * TPC + half * HALF: k * TPC + half * HALF + HALF]

    obanks = Banks(b, "obk", 0, handles=pb[0:4])
    b.wait("gpsimd", [b.mark("pe")])
    scr_done = proj_to_scr(b, obanks, worg_d, lnx_rhs, NKC, scr_d, w_sb,
                           wstage, obs=2, extra_k=(4, wup_d, h_rhs))
    ttb = Banks(b, "ttb", 0, handles=tbp)
    resid_tile_loop(b, scr_done, scr_d, x1_d, x2_d, ident, ttb, scr_tile,
                    x_stage[:, 0:D])

    b.wait("sync", [b.mark(s) for s in list(b.cnt) if s.startswith("x1s")])
    b.finalize()
    return nc, ctx


L4W = TPC + 128
NPC = 81


def build_l4a():
    nc = bass.Bass("TRN2", target_bir_lowering=False, debug=False)
    ctx = ExitStack()
    b = Bld(nc, ctx)
    x2_d = nc.dram_tensor("x2", [TPC, D], F32, kind="ExternalInput").ap()
    halo_d = nc.dram_tensor("halo", [3, D], F32, kind="ExternalInput").ap()
    w_d = nc.dram_tensor("w_inproj_p", [D, NPC * 128], F32, kind="ExternalInput").ap()
    ident_d = nc.dram_tensor("ident", [128, 128], BF16, kind="ExternalInput").ap()
    p_d = nc.dram_tensor("pout", [NPC * 128, L4W], BF16, kind="ExternalOutput").ap()

    lnx_fm = b.sb("lnx_fm", [128, NKC * L4W], BF16)
    x_stage = b.sb("x_stage", [128, 2 * D], F32)
    ln_stage = b.sb("ln_stage", [128, 2 * D], BF16)
    stats = b.sb("stats", [128, 48], F32)
    mv = b.sb("mv", [128, 2], F32)
    rstd = b.sb("rstd", [128, 1], F32)
    ident = b.sb("ident_sb", [128, 128], BF16)
    w_sb = b.sb("w_sb", [128, 2 * 2 * 128 * NKC], BF16)
    wstage = b.sb("wstage", [128, 4 * 384], BF16)

    pb = [b.psum(f"pb{i}", [128, 512], F32) for i in range(4)]
    tbp = [b.psum(f"tbp{i}", [128, 512], BF16) for i in range(2)]

    b.dma("sync", ident[:, :], ident_d[:, :], "cld")
    consts = b.mark("cld")
    b.wait("tensor", [consts])
    b.wait("vector", [consts])
    b.wait("scalar", [consts])

    tbanks = Banks(b, "tbk", 0, handles=tbp)
    z0 = b.op("vector", lambda e: e.memset(x_stage[:, 0:D], 0.0))
    hl = b.dma("sync", x_stage[125:128, 0:D], halo_d[:, :], "hld", waits=[z0])
    htok = ln_tile(b, x_stage[:, 0:D], ln_stage[:, 0:D], stats, mv, rstd, hl)
    for f0 in range(0, NKC, 4):
        bi, bank, war = tbanks.take()
        ptok = None
        for g in range(4):
            fi = f0 + g
            ptok = b.pe(lambda e, s=ln_stage[:, fi * 128:(fi + 1) * 128],
                        o=bank[:, g * 128:(g + 1) * 128]:
                        e.transpose(o, s, ident[:, :]),
                        waits=([war, htok, consts] if g == 0 else ()),
                        sem="pe" if g == 3 else None)
        for g in range(4):
            fi = f0 + g
            vt = b.op("vector", lambda e, o=lnx_fm[:, fi * L4W: fi * L4W + 128],
                      i=bank[:, g * 128:(g + 1) * 128]: e.tensor_copy(o, i),
                      waits=[ptok] if g == 0 else ())
        tbanks.set_free(bi, vt)
    def fmoff(fi, tj):
        return fi * L4W + 128 + tj * 128
    nt = TPC // 128
    last_use = [b.mark("ve"), None]
    for tj in range(nt):
        sl = tj % 2
        xs = x_stage[:, sl * D:(sl + 1) * D]
        lt = ln_stage[:, sl * D:(sl + 1) * D]
        ld = b.dma("sync", xs, x2_d[tj * 128:(tj + 1) * 128, :], f"xld{sl}",
                   waits=[last_use[sl]])
        aptok = ln_tile(b, xs, lt, stats, mv, rstd, ld)
        for f0 in range(0, NKC, 4):
            bi, bank, war = tbanks.take()
            ptok = None
            for g in range(4):
                fi = f0 + g
                ptok = b.pe(lambda e, s=lt[:, fi * 128:(fi + 1) * 128],
                            o=bank[:, g * 128:(g + 1) * 128]:
                            e.transpose(o, s, ident[:, :]),
                            waits=([war, aptok] if g == 0 else ()),
                            sem="pe" if g == 3 else None)
            for g in range(4):
                fi = f0 + g
                vt = b.op("vector", lambda e, o=lnx_fm[:, fmoff(fi, tj):
                                                       fmoff(fi, tj) + 128],
                          i=bank[:, g * 128:(g + 1) * 128]: e.tensor_copy(o, i),
                          waits=[ptok] if g == 0 else ())
            tbanks.set_free(bi, vt)
        last_use[sl] = b.mark("ve")
    b.wait("tensor", [b.mark("ve")])

    banks = Banks(b, "bk", 0, handles=pb)

    def rhs_view(k, third):
        return lnx_fm[:, k * L4W + third * 384: k * L4W + (third + 1) * 384]

    proj_to_scr(b, banks, w_d, rhs_view, NPC, p_d, w_sb, wstage, obs=2,
                halves=(0, 1, 2), hw=384)

    b.wait("sync", [b.mark(s) for s in list(b.cnt) if s.startswith("wo")])
    b.finalize()
    return nc, ctx


def build_l4b():
    nc = bass.Bass("TRN2", target_bir_lowering=False, debug=False)
    ctx = ExitStack()
    b = Bld(nc, ctx)
    xbc_d = nc.dram_tensor("xbc_fm", [CONV_DIM, L4W], BF16, kind="ExternalInput").ap()
    dt_d = nc.dram_tensor("dt_fm", [64, L4W], BF16, kind="ExternalInput").ap()
    cw_d = nc.dram_tensor("conv_w_c", [128, 192], F32, kind="ExternalInput").ap()
    cb_d = nc.dram_tensor("conv_b_c", [128, 48], F32, kind="ExternalInput").ap()
    dtb_d = nc.dram_tensor("dt_bias_c", [64, 1], F32, kind="ExternalInput").ap()
    al_d = nc.dram_tensor("a_log_c", [64, 1], F32, kind="ExternalInput").ap()
    dr_d = nc.dram_tensor("d_row", [1, 4096], F32, kind="ExternalInput").ap()
    rep_d = nc.dram_tensor("rep", [64, 4096], BF16, kind="ExternalInput").ap()
    bm_d = nc.dram_tensor("maskneg128", [128, 128], F32, kind="ExternalInput").ap()
    ident_d = nc.dram_tensor("ident", [128, 128], BF16, kind="ExternalInput").ap()
    identf_d = nc.dram_tensor("ident_f", [128, 128], F32, kind="ExternalInput").ap()
    onesr_d = nc.dram_tensor("ones_row_f", [1, 128], F32, kind="ExternalInput").ap()
    onesb_d = nc.dram_tensor("ones_row_b", [1, 128], BF16, kind="ExternalInput").ap()
    y_d = nc.dram_tensor("y_tm", [TPC, 4096], BF16, kind="ExternalOutput").ap()
    cfm_d = nc.dram_tensor("c_fm", [1024, TPC], BF16, kind="ExternalOutput").ap()
    ccl_d = nc.dram_tensor("coreclg", [64, TPC], F32, kind="ExternalOutput").ap()
    sc_d = nc.dram_tensor("s_core", [128, 4096], BF16, kind="ExternalOutput").ap()

    x_tm = b.sb("x_tm", [128, 8 * 4096], BF16)
    bc_sb = b.sb("bc_sb", [128, 16 * 1024], BF16)
    b_tm = b.sb("b_tm", [128, 16 * 128], BF16)
    dtx_sb = b.sb("dtx_sb", [128, 2 * 4096], BF16)
    r_sb = b.sb("r_sb", [128, 4096], BF16)
    lam_sb = b.sb("lam_sb", [128, 4096], BF16)
    drep_sb = b.sb("drep_sb", [128, 4096], BF16)
    cum_sb = b.sb("cum_sb", [128, 512], BF16)
    xstg = b.sb("xstg", [128, 2 * L4W], BF16)
    conv_sb = b.sb("conv_sb", [128, 1024], F32)
    cw_sb = b.sb("cw_sb", [128, 192], F32)
    cbb_sb = b.sb("cbb_sb", [128, 48], F32)
    dt_sb = b.sb("dt_sb", [64, L4W], BF16)
    u_sb = b.sb("u_sb", [64, 1024], F32)

    dtp_sb = b.sb("dtp_sb", [64, 1024], F32)
    dtpb_sb = b.sb("dtpb_sb", [64, 1024], BF16)
    logda = b.sb("logda", [64, 1024], F32)
    u2_sb = logda
    clg_sb = b.sb("clg_sb", [64, 256], F32)
    clgb_sb = b.sb("clgb_sb", [64, 256], BF16)
    ccl_sb = b.sb("ccl_sb", [64, 256], F32)
    off_sb = b.sb("off_sb", [64, 2], F32)
    es_sb = b.sb("es_sb", [64, 256], F32)
    esb_sb = b.sb("esb_sb", [64, 256], BF16)
    clgq_sb = b.sb("clgq_sb", [64, 2], F32)
    clgqb_sb = b.sb("clgqb_sb", [64, 1], BF16)
    clgtm_sb = b.sb("clgtm_sb", [128, 2 * 64], F32)
    nclgtm_sb = b.sb("nclgtm_sb", [128, 2 * 64], F32)
    rl_sb = b.sb("rl_sb", [1, 512], F32)
    trow_sb = b.sb("trow_sb", [1, 128], F32)
    dtt_sb = b.sb("dtt_sb", [128, 128], F32)
    dec_sb = b.sb("dec_sb", [128, 128], BF16)
    g_sb = b.sb("g_sb", [128, 128], BF16)
    corr_sb = b.sb("corr_sb", [128, 512], BF16)
    ya_sb = b.sb("ya_sb", [128, 512], BF16)
    yb_sb = b.sb("yb_sb", [128, 512], BF16)
    t1_sb = b.sb("t1_sb", [128, 512], BF16)
    bias_a = b.sb("bias_a", [64, 2], F32)
    binm_sb = b.sb("maskneg_sb", [128, 128], F32)
    ident = b.sb("ident_sb", [128, 128], BF16)
    identf = b.sb("identf_sb", [128, 128], F32)
    onesr = b.sb("onesr_sb", [1, 128], F32)
    drow_sb = b.sb("drow_sb", [1, 4096], BF16)
    onesb = b.sb("onesb_sb", [1, 128], BF16)
    rep_sb = b.sb("rep_sb", [64, 4096], BF16)

    psY = [b.psum(f"psY{i}", [128, 512], F32) for i in range(2)]
    psC = b.psum("psC", [128, 512], F32)
    psS = b.psum("psS", [128, 512], F32)
    psM = [b.psum(f"psM{i}", [128, 512], F32) for i in range(2)]
    psT = [b.psum(f"psT{i}", [128, 512], BF16) for i in range(2)]

    for ap_, src_ in ((cw_sb, cw_d), (cbb_sb, cb_d), (dtb_d, None)):
        pass
    b.dma("sync", cw_sb[:, :], cw_d[:, :], "cld")
    b.dma("sync", cbb_sb[:, :], cb_d[:, :], "cld")
    b.dma("sync", bias_a[:, 0:1], dtb_d[:, :], "cld")
    b.dma("sync", bias_a[:, 1:2], al_d[:, :], "cld")
    b.dma("gpsimd", drow_sb[:, :], dr_d[:, :], "cldg")
    b.dma("sync", onesb[:, :], onesb_d[:, :], "cld")
    b.dma("sync", rep_sb[:, :], rep_d[:, :], "cld")
    b.dma("sync", binm_sb[:, :], bm_d[:, :], "cld")
    b.dma("sync", ident[:, :], ident_d[:, :], "cld")
    b.dma("sync", identf[:, :], identf_d[:, :], "cld")
    b.dma("sync", onesr[:, :], onesr_d[:, :], "cld")
    b.dma("sync", dt_sb[:, :], dt_d[:, :], "cld")
    consts = b.mark("cld")
    constsg = b.mark("cldg")
    b.wait("tensor", [consts, constsg])
    b.wait("vector", [consts, constsg])
    b.wait("scalar", [consts, constsg])

    tb = Banks(b, "tb", 0, handles=psT)

    dtok = None
    for rg in range(8):
        pm = b.pe(lambda e, rg=rg: e.matmul(psM[0][:, :], onesb[:, :],
                                            drow_sb[:, rg * 512:(rg + 1) * 512],
                                            start=True, stop=True),
                  waits=[b.mark("ve")] if rg else [consts], sem="pe")
        dtok = b.op("vector", lambda e, rg=rg: e.tensor_copy(
            drep_sb[:, rg * 512:(rg + 1) * 512], psM[0][:, :]), waits=[pm])

    stg_free = [None, None]
    for ch in range(48):
        sl = ch % 2
        stg = xstg[:, sl * L4W:(sl + 1) * L4W]
        ld = b.dma("sync", stg, xbc_d[ch * 128:(ch + 1) * 128, :], f"xb{sl}",
                   waits=[stg_free[sl]])
        cv = b.op("vector", lambda e, stg=stg, ch=ch: e.tensor_scalar(
            conv_sb[:, :], stg[:, 125:1149], cw_sb[:, ch * 4:ch * 4 + 1],
            cbb_sb[:, ch:ch + 1], ALU.mult, ALU.add),
            waits=[ld, b.mark("se")])
        for tap in range(1, 4):
            cv = b.op("vector", lambda e, stg=stg, ch=ch, tap=tap:
                      e.scalar_tensor_tensor(
                          conv_sb[:, :], stg[:, 125 + tap:1149 + tap],
                          cw_sb[:, ch * 4 + tap:ch * 4 + tap + 1],
                          conv_sb[:, :], ALU.mult, ALU.add))
        stg_free[sl] = cv
        if ch < 32:
            sx = b.op("scalar", lambda e: e.activation(
                xstg[:, sl * L4W:sl * L4W + 1024] if False else conv_sb[:, :],
                conv_sb[:, :], AF.Silu), waits=[b.mark("ve")])
            cbf = b.op("vector", lambda e: e.tensor_copy(
                dtx_sb[:, 0:1024], conv_sb[:, :]), waits=[b.mark("se")])
            for tj0 in range(0, 8, 4):
                bi, bank, war = tb.take()
                ptk = None
                for g2 in range(4):
                    tj = tj0 + g2
                    ptk = b.pe(lambda e, s=dtx_sb[:, tj * 128:(tj + 1) * 128],
                               o=bank[:, g2 * 128:(g2 + 1) * 128]:
                               e.transpose(o, s, ident[:, :]),
                               waits=([war, cbf] if g2 == 0 else ()),
                               sem="pe" if g2 == 3 else None)
                for g2 in range(4):
                    tj = tj0 + g2
                    vt = b.op("vector", lambda e, o=x_tm[:, (tj0 + g2) * 4096 + ch * 128:
                                                         (tj0 + g2) * 4096 + (ch + 1) * 128],
                              i=bank[:, g2 * 128:(g2 + 1) * 128]:
                              e.tensor_copy(o, i),
                              waits=[ptk] if g2 == 0 else ())
                tb.set_free(bi, vt)
        else:
            bci = ch - 32
            sx = b.op("scalar", lambda e, bci=bci: e.activation(
                bc_sb[:, bci * 1024:(bci + 1) * 1024], conv_sb[:, :], AF.Silu),
                waits=[b.mark("ve")])
            stg_free[sl] = sx
            if ch >= 40:
                b.dma("sync", cfm_d[(ch - 40) * 128:(ch - 39) * 128, :],
                      bc_sb[:, bci * 1024:(bci + 1) * 1024], "cso",
                      waits=[sx])
    conv_done_v = b.mark("ve")
    conv_done_s = b.mark("se")

    b.op("vector", lambda e: e.tensor_scalar(
        u_sb[:, :], dt_sb[:, 128:1152], bias_a[:, 0:1], None, ALU.add,
        ALU.bypass), waits=[consts])
    b.op("scalar", lambda e: e.activation(u2_sb[:, :], u_sb[:, :], AF.Abs),
         waits=[b.mark("ve")])
    b.op("scalar", lambda e: e.activation(u2_sb[:, :], u2_sb[:, :], AF.Exp,
                                          scale=-1.0))
    one_ap, one_tok = b.constcol(1.0)
    b.op("scalar", lambda e: e.activation(u2_sb[:, :], u2_sb[:, :], AF.Ln,
                                          bias=one_ap[0:64, :]),
         waits=[one_tok])
    b.op("vector", lambda e: e.tensor_scalar(
        dtp_sb[:, :], u_sb[:, :], 0.0, None, ALU.max, ALU.bypass))
    b.op("vector", lambda e: e.tensor_tensor(
        dtp_sb[:, :], dtp_sb[:, :], u2_sb[:, :], ALU.add),
        waits=[b.mark("se")])
    b.op("vector", lambda e: e.tensor_copy(dtpb_sb[:, :], dtp_sb[:, :]))
    b.op("scalar", lambda e: e.activation(bias_a[:, 1:2] if False else
                                          off_sb[:, 1:2], bias_a[:, 1:2],
                                          AF.Exp), waits=[b.mark("ve")])
    b.op("vector", lambda e: e.tensor_scalar(
        off_sb[:, 1:2], off_sb[:, 1:2], -1.0, None, ALU.mult, ALU.bypass),
        waits=[b.mark("se")])
    b.op("vector", lambda e: e.tensor_scalar(
        logda[:, :], dtp_sb[:, :], off_sb[:, 1:2], None, ALU.mult, ALU.bypass))
    z_off = b.op("vector", lambda e: e.memset(off_sb[:, 0:1], 0.0))

    r_z = b.op("vector", lambda e: e.memset(r_sb[:, :], 0.0))
    last_yd = [None]
    for sc in range(4):
        b.wait("vector", [b.mark("pe")])
        b.op("vector", lambda e, sc=sc: e.tensor_tensor_scan(
            clg_sb[:, :], logda[:, sc * 256:(sc + 1) * 256],
            logda[:, sc * 256:(sc + 1) * 256], 0.0, ALU.add, ALU.bypass))
        b.op("vector", lambda e: e.tensor_scalar(
            ccl_sb[:, :], clg_sb[:, :], off_sb[:, 0:1], None, ALU.add,
            ALU.bypass))
        ccd = b.dma("sync", ccl_d[:, sc * 256:(sc + 1) * 256], ccl_sb[:, :],
                    f"ccs{sc % 2}", waits=[b.mark("ve")])
        b.op("vector", lambda e: e.tensor_copy(off_sb[:, 0:1],
                                               ccl_sb[:, 255:256]),
             waits=[ccd])
        b.op("vector", lambda e: e.tensor_copy(clgq_sb[:, 0:1],
                                               clg_sb[:, 255:256]))
        b.op("vector", lambda e: e.tensor_copy(clgqb_sb[:, :],
                                               clgq_sb[:, 0:1]))
        b.op("vector", lambda e: e.tensor_copy(clgb_sb[:, :], clg_sb[:, :]))
        b.op("scalar", lambda e: e.activation(es_sb[:, :], clg_sb[:, :],
                                              AF.Exp, bias=clgq_sb[:, 0:1],
                                              scale=-1.0),
             waits=[b.mark("ve")])
        b.op("vector", lambda e: e.tensor_copy(esb_sb[:, :], es_sb[:, :]),
             waits=[b.mark("se")])
        for tc in range(2):
            pt = b.pe(lambda e, tc=tc: e.transpose(
                psM[0][:, 0:64], clg_sb[:, tc * 128:(tc + 1) * 128],
                identf[0:64, 0:64]), waits=[b.mark("ve")], sem="pe")
            b.op("vector", lambda e, tc=tc: e.tensor_copy(
                clgtm_sb[:, tc * 64:(tc + 1) * 64], psM[0][:, 0:64]),
                waits=[pt])
            b.op("vector", lambda e, tc=tc: e.tensor_scalar(
                nclgtm_sb[:, tc * 64:(tc + 1) * 64], psM[0][:, 0:64], -1.0,
                None, ALU.mult, ALU.bypass))
        for g in range(8):
            bi, bank, war = tb.take()
            ptk = None
            for a in range(2):
                ptk = b.pe(lambda e, o=bank[:, a * 128:(a + 1) * 128],
                           s=bc_sb[:, g * 1024 + sc * 256 + a * 128:
                                   g * 1024 + sc * 256 + (a + 1) * 128]:
                           e.transpose(o, s, ident[:, :]),
                    waits=([war, conv_done_s, conv_done_v] if a == 0 else ()),
                    sem="pe" if a == 1 else None)
            vt = b.op("vector", lambda e, o=b_tm[:, g * 256:(g + 1) * 256],
                      i=bank[:, 0:256]: e.tensor_copy(o, i),
                      waits=[ptk, b.mark("pe")])
            tb.set_free(bi, vt)
        for rg in range(8):
            pm = b.pe(lambda e, rg=rg: e.matmul(
                psM[1][0:1, 0:512], clgqb_sb[:, :],
                rep_sb[:, rg * 512:(rg + 1) * 512], start=True, stop=True,
                skip_group_check=True),
                waits=[b.mark("ve"), b.mark("se")], sem="pe")
            b.op("vector", lambda e: e.tensor_copy(
                rl_sb[:, :], psM[1][0:1, 0:512]), waits=[pm])
            pm2_ = b.pe(lambda e: e.matmul(
                psM[1][:, :], onesr[:, :], rl_sb[:, :],
                start=True, stop=True, skip_group_check=True),
                waits=[b.mark("ve"), b.mark("se")], sem="pe")
            b.op("scalar", lambda e, rg=rg: e.activation(
                lam_sb[:, rg * 512:(rg + 1) * 512], psM[1][:, :], AF.Exp),
                waits=[pm2_, b.mark("ve")])
        lam_tok = b.mark("se")
        for a in range(2):
            for g in range(8):
                pm = b.pe(lambda e, a=a, g=g, sc=sc: e.matmul(
                    psM[0][:, :], dtpb_sb[:, sc * 256 + a * 128:
                                          sc * 256 + (a + 1) * 128],
                    rep_sb[:, g * 512:(g + 1) * 512], start=True, stop=True,
                    skip_group_check=True),
                    waits=[b.mark("ve"), b.mark("se")], sem="pe")
                b.op("vector", lambda e, a=a, g=g, sc=sc: e.tensor_tensor(
                    dtx_sb[:, a * 4096 + g * 512:a * 4096 + (g + 1) * 512],
                    x_tm[:, (sc * 2 + a) * 4096 + g * 512:
                         (sc * 2 + a) * 4096 + (g + 1) * 512],
                    psM[0][:, :], ALU.mult), waits=[pm])
        dtx_tok = b.mark("ve")
        for tc in range(2):
            for g in range(8):
                yb = psY[tc % 2]
                pc = b.pe(lambda e, g=g, tc=tc, sc=sc: e.matmul(
                    psC[:, :],
                    bc_sb[:, (8 + g) * 1024 + sc * 256 + tc * 128:
                          (8 + g) * 1024 + sc * 256 + (tc + 1) * 128],
                    r_sb[:, g * 512:(g + 1) * 512], start=True, stop=True,
                    skip_group_check=True),
                    waits=[b.mark("ve"), conv_done_s], sem="pe")
                cevac = b.op("vector", lambda e: e.tensor_copy(
                    corr_sb[:, :], psC[:, :]), waits=[pc])
                pcm = b.pe(lambda e, g=g, tc=tc: e.matmul(
                    psM[0][:, :], clgb_sb[:, tc * 128:(tc + 1) * 128],
                    rep_sb[:, g * 512:(g + 1) * 512], start=True, stop=True,
                    skip_group_check=True),
                    waits=[b.mark("ve"), b.mark("se")], sem="pe")
                cume = b.op("scalar", lambda e: e.activation(
                    cum_sb[:, 0:512], psM[0][:, :], AF.Exp),
                    waits=[pcm, b.mark("ve")])
                ystarted = False
                ylast = None
                for hl in range(8):
                    h = g * 8 + hl
                    ptr = b.pe(lambda e, h=h, tc=tc: e.transpose(
                        psM[1][0:1, 0:128],
                        clgtm_sb[:, tc * 64 + h:tc * 64 + h + 1],
                        identf[:, :]), waits=[b.mark("ve"), b.mark("se")],
                        sem="pe")
                    b.op("vector", lambda e: e.tensor_copy(
                        trow_sb[:, :], psM[1][0:1, 0:128]), waits=[ptr])
                    pt2 = b.pe(lambda e: e.matmul(
                        psM[1][:, 128:256], onesr[:, :], trow_sb[:, :],
                        start=True, stop=True, skip_group_check=True),
                        waits=[b.mark("ve")], sem="pe")
                    for a in range(tc + 1):
                        pcb = b.pe(lambda e, g=g, a=a, tc=tc, sc=sc: e.matmul(
                            psM[0][:, 0:128],
                            bc_sb[:, g * 1024 + sc * 256 + a * 128:
                                  g * 1024 + sc * 256 + (a + 1) * 128],
                            bc_sb[:, (8 + g) * 1024 + sc * 256 + tc * 128:
                                  (8 + g) * 1024 + sc * 256 + (tc + 1) * 128],
                            start=True, stop=True, skip_group_check=True),
                            waits=[b.mark("se"), b.mark("ve")], sem="pe")
                        dtt = b.op("vector", lambda e, a=a, tc=tc, h=h:
                                   e.tensor_scalar(
                                       dtt_sb[:, :], psM[1][:, 128:256],
                                       nclgtm_sb[:, a * 64 + h:a * 64 + h + 1],
                                       None, ALU.add, ALU.bypass),
                                   waits=[pt2, pcb, b.mark("se")])
                        if a == tc:
                            dtt = b.op("vector", lambda e: e.tensor_tensor(
                                dtt_sb[:, :], dtt_sb[:, :], binm_sb[:, :],
                                ALU.add))
                        dce = b.op("scalar", lambda e: e.activation(
                            dec_sb[:, :], dtt_sb[:, :], AF.Exp),
                            waits=[b.mark("ve")])
                        gm = b.op("vector", lambda e: e.tensor_tensor(
                            g_sb[:, :], psM[0][:, 0:128], dec_sb[:, :],
                            ALU.mult), waits=[dce, b.mark("pe")])
                        ym = b.pe(lambda e, yb=yb, hl=hl, a=a, h=h,
                                  s=(not ystarted),
                                  st=(hl == 7 and a == tc): e.matmul(
                                      yb[:, hl * 64:(hl + 1) * 64], g_sb[:, :],
                                      dtx_sb[:, a * 4096 + h * 64:
                                             a * 4096 + (h + 1) * 64],
                                      start=s, stop=st,
                                      skip_group_check=True),
                                  waits=([gm, dtx_tok, b.mark("ve")]
                                         if not ystarted else [gm, dtx_tok]),
                                  sem="pe")
                        ystarted = True
                        ylast = ym
                a1 = b.op("vector", lambda e: e.tensor_tensor(
                    t1_sb[:, :], corr_sb[:, :], cum_sb[:, 0:512], ALU.mult),
                    waits=[cume, cevac])
                a2 = b.op("vector", lambda e, yb=yb: e.tensor_tensor(
                    ya_sb[:, :], t1_sb[:, :], yb[:, :], ALU.add),
                    waits=[ylast])
                a3 = b.op("vector", lambda e, sc=sc, tc=tc, g=g:
                          e.tensor_tensor(
                              t1_sb[:, :],
                              x_tm[:, (sc * 2 + tc) * 4096 + g * 512:
                                   (sc * 2 + tc) * 4096 + (g + 1) * 512],
                              drep_sb[:, g * 512:(g + 1) * 512], ALU.mult))
                a4 = b.op("vector", lambda e: e.tensor_tensor(
                    yb_sb[:, :], ya_sb[:, :], t1_sb[:, :], ALU.add),
                    waits=[last_yd[0]])
                yd = b.dma("sync",
                           y_d[(sc * 2 + tc) * 128:(sc * 2 + tc + 1) * 128,
                               g * 512:(g + 1) * 512],
                           yb_sb[:, :], f"ys{g % 4}", waits=[a4])
                last_yd[0] = yd
        for g in range(8):
            ps1 = None
            for a in range(2):
                pm2 = b.pe(lambda e, a=a, g=g: e.matmul(
                    psM[1][:, :], esb_sb[:, a * 128:(a + 1) * 128],
                    rep_sb[:, g * 512:(g + 1) * 512], start=True, stop=True,
                    skip_group_check=True),
                    waits=[b.mark("ve"), b.mark("se")], sem="pe")
                dxp = b.op("vector", lambda e, a=a, g=g: e.tensor_tensor(
                    t1_sb[:, :],
                    dtx_sb[:, a * 4096 + g * 512:a * 4096 + (g + 1) * 512],
                    psM[1][:, :], ALU.mult), waits=[pm2])
                ps1 = b.pe(lambda e, g=g, a=a, s=(a == 0): e.matmul(
                    psS[:, :], b_tm[:, g * 256 + a * 128:g * 256 + (a + 1) * 128],
                    t1_sb[:, :],
                    start=s, stop=(a == 1), skip_group_check=True),
                    waits=[dxp], sem="pe")
            b.op("vector", lambda e, g=g: e.tensor_tensor(
                ya_sb[:, :], lam_sb[:, g * 512:(g + 1) * 512],
                r_sb[:, g * 512:(g + 1) * 512], ALU.mult),
                waits=[lam_tok, ps1])
            b.op("vector", lambda e, g=g: e.tensor_tensor(
                r_sb[:, g * 512:(g + 1) * 512], ya_sb[:, :], psS[:, :],
                ALU.add))
    std = b.dma("sync", sc_d[:, :], r_sb[:, :], "sts", waits=[b.mark("ve")])

    b.wait("sync", [b.mark(s) for s in list(b.cnt)
                    if s.startswith(("ys", "cso", "ccs", "sts"))])
    b.finalize()
    return nc, ctx


def build_l5():
    nc = bass.Bass("TRN2", target_bir_lowering=False, debug=False)
    ctx = ExitStack()
    b = Bld(nc, ctx)
    y_di = nc.dram_tensor("y_tm", [TPC, 4096], BF16, kind="ExternalInput").ap()
    c_di = nc.dram_tensor("c_fm", [1024, TPC], BF16, kind="ExternalInput").ap()
    ccl_di = nc.dram_tensor("coreclg", [64, TPC], F32, kind="ExternalInput").ap()
    sp_di = nc.dram_tensor("s_prev", [3 * 128, 4096], BF16, kind="ExternalInput").ap()
    ll_di = nc.dram_tensor("lamlog_prev", [64, 3], F32, kind="ExternalInput").ap()
    gate_di = nc.dram_tensor("gate_fm", [4096, L4W], BF16, kind="ExternalInput").ap()
    x2_di = nc.dram_tensor("x2", [TPC, D], F32, kind="ExternalInput").ap()
    wop_di = nc.dram_tensor("w_outproj_t", [MI, D], F32, kind="ExternalInput").ap()
    nw_di = nc.dram_tensor("norm_row", [1, 4096], F32, kind="ExternalInput").ap()
    rep_di = nc.dram_tensor("rep", [64, 4096], BF16, kind="ExternalInput").ap()
    ident_d = nc.dram_tensor("ident", [128, 128], BF16, kind="ExternalInput").ap()
    onesr_d = nc.dram_tensor("ones_row_f", [1, 128], F32, kind="ExternalInput").ap()
    x3_d = nc.dram_tensor("x3", [TPC, D], F32, kind="ExternalOutput").ap()
    scr_d = nc.dram_tensor("op_scr", [D, TPC], BF16)

    yn_fm = b.sb("yn_fm", [128, NKC * TPC], BF16)
    r5_sb = b.sb("r5_sb", [128, 4096], BF16)
    lam_sb = b.sb("lam_sb", [128, 4096], BF16)
    llb_sb = b.sb("llb_sb", [64, 3], BF16)
    rl_sb = b.sb("rl_sb", [1, 512], F32)
    clgb_sb = b.sb("clgb_sb", [64, TPC], BF16)
    cum_sb = b.sb("cum_sb", [128, 512], BF16)
    corr_sb = b.sb("corr_sb", [128, 512], BF16)
    y_sb = b.sb("y_sb", [128, 4096], BF16)
    gate_tm = b.sb("gate_tm", [128, 4096], BF16)
    gstage = b.sb("gstage", [128, 4096], BF16)
    cstage = b.sb("cstage", [128, 8 * 128], BF16)
    normrep = b.sb("normrep", [128, 4096], BF16)
    ss_sb = b.sb("ss_sb", [128, 4], F32)
    junk_sb = b.sb("junk_sb", [128, 512], BF16)
    ident = b.sb("ident_sb", [128, 128], BF16)
    onesr = b.sb("onesr_sb", [1, 128], F32)
    nwrow = b.sb("nwrow", [1, 4096], F32)
    rep_sb = b.sb("rep_sb", [64, 4096], BF16)
    w_sb = b.sb("w_sb", [128, 2 * 2 * 128 * NKC], BF16)
    wstage = b.sb("wstage", [128, 4 * HALF], BF16)
    scr_tile = b.sb("scr_tile", [128, D], BF16)
    sprev_sb = scr_tile
    x_stage = b.sb("x_stage", [128, D], F32)

    pb = [b.psum(f"pb{i}", [128, 512], F32) for i in range(4)]
    tbp = [b.psum(f"tbp{i}", [128, 512], BF16) for i in range(2)]

    b.dma("sync", ident[:, :], ident_d[:, :], "cld")
    b.dma("sync", onesr[:, :], onesr_d[:, :], "cld")
    b.dma("gpsimd", llb_sb[:, :], ll_di[:, :], "cldg")
    b.dma("gpsimd", clgb_sb[:, :], ccl_di[:, :], "cldg")
    b.dma("sync", rep_sb[:, :], rep_di[:, :], "cld")
    b.dma("sync", nwrow[:, :], nw_di[:, :], "cld")
    consts = b.mark("cld")
    constsg = b.mark("cldg")
    b.wait("tensor", [consts, constsg])
    b.wait("vector", [consts, constsg])
    b.wait("scalar", [consts, constsg])

    for rg in range(8):
        pm = b.pe(lambda e, rg=rg: e.matmul(
            pb[1][:, :], onesr[:, :], nwrow[:, rg * 512:(rg + 1) * 512],
            start=True, stop=True, skip_group_check=True),
            waits=[b.mark("ve")], sem="pe")
        b.op("vector", lambda e, rg=rg: e.tensor_copy(
            normrep[:, rg * 512:(rg + 1) * 512], pb[1][:, :]), waits=[pm])

    ld3 = b.dma("sync", r5_sb[:, :], sp_di[256:384, :], "sl3")
    for j in (1, 0):
        lds = b.dma("sync", sprev_sb[:, :], sp_di[j * 128:(j + 1) * 128, :],
                    f"sl{j}", waits=[b.mark("ve")])
        for rg in range(8):
            pm = b.pe(lambda e, j=j, rg=rg: e.matmul(
                pb[1][0:1, 0:512], llb_sb[:, j:j + 1],
                rep_sb[:, rg * 512:(rg + 1) * 512], start=True, stop=True,
                skip_group_check=True),
                waits=[b.mark("ve"), b.mark("se")], sem="pe")
            b.op("vector", lambda e: e.tensor_copy(rl_sb[:, :],
                                                   pb[1][0:1, 0:512]),
                 waits=[pm])
            pm2 = b.pe(lambda e: e.matmul(
                pb[1][:, :], onesr[:, :], rl_sb[:, :], start=True, stop=True,
                skip_group_check=True), waits=[b.mark("ve"), b.mark("se")],
                sem="pe")
            b.op("scalar", lambda e, rg=rg: e.activation(
                lam_sb[:, rg * 512:(rg + 1) * 512], pb[1][:, :], AF.Exp),
                waits=[pm2, b.mark("ve")])
        for rg in range(8):
            b.op("vector", lambda e, rg=rg: e.tensor_tensor(
                r5_sb[:, rg * 512:(rg + 1) * 512],
                lam_sb[:, rg * 512:(rg + 1) * 512],
                r5_sb[:, rg * 512:(rg + 1) * 512], ALU.mult),
                waits=[b.mark("se"), ld3] if rg == 0 else ())
        for rg in range(8):
            b.op("vector", lambda e, rg=rg: e.tensor_tensor(
                r5_sb[:, rg * 512:(rg + 1) * 512],
                r5_sb[:, rg * 512:(rg + 1) * 512],
                sprev_sb[:, rg * 512:(rg + 1) * 512], ALU.add),
                waits=[lds] if rg == 0 else ())
    rin_tok = b.mark("ve")
    b.wait("tensor", [rin_tok])

    eps_ap, eps_tok = b.constcol(MEPS)
    tb5 = Banks(b, "tb5", 0, handles=tbp)
    yld_prev = None
    gld_prev = None
    cld_prev = None
    for tt in range(8):
        yld = b.dma("sync", y_sb[:, :], y_di[tt * 128:(tt + 1) * 128, :],
                    "yld", waits=[yld_prev])
        gld = b.dma("sync", gstage[:, :].rearrange("p (f t) -> p f t", t=128),
                    gate_di[:, 128 + tt * 128:128 + (tt + 1) * 128].rearrange(
                        "(f p) t -> p f t", p=128), "gld",
                    waits=[gld_prev, b.mark("pe")])
        cld = b.dma("sync", cstage[:, :].rearrange("p (g t) -> p g t", t=128),
                    c_di[:, tt * 128:(tt + 1) * 128].rearrange(
                        "(g p) t -> p g t", p=128), "clD",
                    waits=[cld_prev, b.mark("pe")])
        for f0 in range(0, NKC, 4):
            bi, bank, war = tb5.take()
            ptk = None
            for g2 in range(4):
                fi = f0 + g2
                ptk = b.pe(lambda e, o=bank[:, g2 * 128:(g2 + 1) * 128],
                           s=gstage[:, fi * 128:(fi + 1) * 128]:
                           e.transpose(o, s, ident[:, :]),
                           waits=([war, gld] if g2 == 0 else ()),
                           sem="pe" if g2 == 3 else None)
            sv = b.op("scalar", lambda e, f0=f0, i=bank[:, :]: e.activation(
                gate_tm[:, f0 * 128:(f0 + 4) * 128], i, AF.Silu),
                waits=[ptk])
            tb5.set_free(bi, sv)
        gate_done = b.mark("se")
        for g in range(8):
            pc = b.pe(lambda e, g=g: e.matmul(
                pb[0][:, :], cstage[:, g * 128:(g + 1) * 128],
                r5_sb[:, g * 512:(g + 1) * 512], start=True, stop=True,
                skip_group_check=True),
                waits=[cld, b.mark("ve"), b.mark("se")], sem="pe")
            cevac = b.op("vector", lambda e: e.tensor_copy(corr_sb[:, :],
                                                           pb[0][:, :]),
                         waits=[pc])
            pcm = b.pe(lambda e, g=g, tt=tt: e.matmul(
                pb[1][:, :], clgb_sb[:, tt * 128:(tt + 1) * 128],
                rep_sb[:, g * 512:(g + 1) * 512], start=True, stop=True,
                skip_group_check=True),
                waits=[b.mark("ve"), b.mark("se")], sem="pe")
            cume = b.op("scalar", lambda e: e.activation(
                cum_sb[:, :], pb[1][:, :], AF.Exp),
                waits=[pcm, b.mark("ve")])
            b.op("vector", lambda e, g=g: e.scalar_tensor_tensor(
                corr_sb[:, :], corr_sb[:, :], 1.0, cum_sb[:, :], ALU.mult,
                ALU.mult) if False else e.tensor_tensor(
                corr_sb[:, :], corr_sb[:, :], cum_sb[:, :], ALU.mult),
                waits=[cume, cevac, yld])
            yn_tok = b.op("vector", lambda e, g=g: e.tensor_tensor(
                y_sb[:, g * 512:(g + 1) * 512],
                y_sb[:, g * 512:(g + 1) * 512], corr_sb[:, :], ALU.add))
            b.op("vector", lambda e, g=g: e.tensor_tensor(
                junk_sb[:, :], y_sb[:, g * 512:(g + 1) * 512],
                y_sb[:, g * 512:(g + 1) * 512], ALU.mult))
            b.op("vector", lambda e: e.tensor_scalar(
                junk_sb[:, :], junk_sb[:, :], 0.0, 0.0, ALU.add, ALU.add,
                accum_out=ss_sb[:, 0:1]))
            b.op("vector", lambda e: e.tensor_scalar(
                ss_sb[:, 1:2], ss_sb[:, 0:1], 1.0 / 512, None, ALU.mult,
                ALU.bypass))
            b.op("scalar", lambda e: e.activation(
                ss_sb[:, 2:3], ss_sb[:, 1:2], AF.Sqrt, bias=eps_ap),
                waits=[b.mark("ve"), eps_tok])
            b.op("vector", lambda e: e.reciprocal(ss_sb[:, 3:4], ss_sb[:, 2:3]),
                 waits=[b.mark("se")])
            b.op("vector", lambda e, g=g: e.tensor_scalar(
                y_sb[:, g * 512:(g + 1) * 512],
                y_sb[:, g * 512:(g + 1) * 512], ss_sb[:, 3:4], None,
                ALU.mult, ALU.bypass))
            b.op("vector", lambda e, g=g: e.tensor_tensor(
                y_sb[:, g * 512:(g + 1) * 512],
                y_sb[:, g * 512:(g + 1) * 512],
                normrep[:, g * 512:(g + 1) * 512], ALU.mult))
            b.op("vector", lambda e, g=g: e.tensor_tensor(
                y_sb[:, g * 512:(g + 1) * 512],
                y_sb[:, g * 512:(g + 1) * 512],
                gate_tm[:, g * 512:(g + 1) * 512], ALU.mult),
                waits=[gate_done])
        ynt = b.mark("ve")
        for f0 in range(0, NKC, 4):
            bi, bank, war = tb5.take()
            ptk = None
            for g2 in range(4):
                fi = f0 + g2
                ptk = b.pe(lambda e, o=bank[:, g2 * 128:(g2 + 1) * 128],
                           s=y_sb[:, fi * 128:(fi + 1) * 128]:
                           e.transpose(o, s, ident[:, :]),
                           waits=([war, ynt] if g2 == 0 else ()),
                           sem="pe" if g2 == 3 else None)
            for g2 in range(4):
                fi = f0 + g2
                vt = b.op("vector", lambda e,
                          o=yn_fm[:, fi * TPC + tt * 128:fi * TPC + (tt + 1) * 128],
                          i=bank[:, g2 * 128:(g2 + 1) * 128]:
                          e.tensor_copy(o, i), waits=[ptk] if g2 == 0 else ())
            tb5.set_free(bi, vt)
        yld_prev = b.mark("ve")
        gld_prev = b.mark("se")
        cld_prev = b.mark("pe")
    b.wait("tensor", [b.mark("ve")])

    obanks = Banks(b, "ob", 0, handles=pb)

    def yn_rhs(k, half):
        return yn_fm[:, k * TPC + half * HALF: k * TPC + half * HALF + HALF]

    scr_done = proj_to_scr(b, obanks, wop_di, yn_rhs, NKC, scr_d, w_sb,
                           wstage, obs=2)
    ttb = Banks(b, "ttb", 0, handles=tbp)
    resid_tile_loop(b, scr_done, scr_d, x2_di, x3_d, ident, ttb, scr_tile,
                    x_stage)
    b.wait("sync", [b.mark(s) for s in list(b.cnt) if s.startswith("x1s")])
    b.finalize()
    return nc, ctx


def build_l6():
    nc = bass.Bass("TRN2", target_bir_lowering=False, debug=False)
    ctx = ExitStack()
    b = Bld(nc, ctx)
    x3_d = nc.dram_tensor("x3", [TPC, D], F32, kind="ExternalInput").ap()
    wf_d = nc.dram_tensor("w_ffn_t", [D, D], F32, kind="ExternalInput").ap()
    bias_d = nc.dram_tensor("bias_c", [128, NKC], F32, kind="ExternalInput").ap()
    wa_d = nc.dram_tensor("w_a_t", [D, 256], F32, kind="ExternalInput").ap()
    wbf_d = nc.dram_tensor("w_bf", [256, D], F32, kind="ExternalInput").ap()
    cf_d = nc.dram_tensor("coeffs16", [1, 16], F32, kind="ExternalInput").ap()
    sc_d = nc.dram_tensor("scales16", [1, 16], F32, kind="ExternalInput").ap()
    e16_d = nc.dram_tensor("exp16", [16, 256], F32, kind="ExternalInput").ap()
    ident_d = nc.dram_tensor("ident", [128, 128], BF16, kind="ExternalInput").ap()
    identf_d = nc.dram_tensor("ident_f", [128, 128], F32, kind="ExternalInput").ap()
    x4_d = nc.dram_tensor("x4", [TPC, D], F32, kind="ExternalOutput").ap()
    scr_d = nc.dram_tensor("f2_scr", [D, TPC], BF16)

    lnx_fm = b.sb("lnx_fm", [128, NKC * TPC], BF16)
    h2_fm = b.sb("h2_fm", [128, 2 * TPC], BF16)
    x_stage = b.sb("x_stage", [128, 2 * D], F32)
    ln_stage = b.sb("ln_stage", [128, 2 * D], BF16)
    stats = b.sb("stats", [128, 48], F32)
    mv = b.sb("mv", [128, 2], F32)
    rstd = b.sb("rstd", [128, 1], F32)
    ident = b.sb("ident_sb", [128, 128], BF16)
    identf = b.sb("identf_sb", [128, 128], F32)
    bias_sb = b.sb("bias_sb", [128, NKC], F32)
    wrow = b.sb("wrow", [1, 16], F32)
    wcol = b.sb("wcol", [16, 1], F32)
    cf_sb = b.sb("cf_sb", [1, 16], F32)
    scs_sb = b.sb("scs_sb", [1, 16], F32)
    e16_sb = b.sb("e16_sb", [16, 256], F32)
    ecol2 = b.sb("ecol2", [128, 2], F32)
    w_sb = b.sb("w_sb", [128, 2 * 2 * 128 * (NKC + 2)], BF16)
    wstage = b.sb("wstage", [128, 4 * HALF], BF16)
    scr_tile = b.sb("scr_tile", [128, D], BF16)

    pb = [b.psum(f"pb{i}", [128, 512], F32) for i in range(4)]
    tbp = [b.psum(f"tbp{i}", [128, 512], BF16) for i in range(2)]
    sm_ps = b.psum("sm_ps", [128, 16], F32)

    b.dma("sync", ident[:, :], ident_d[:, :], "cld")
    b.dma("sync", identf[:, :], identf_d[:, :], "cld")
    b.dma("sync", bias_sb[:, :], bias_d[:, :], "cld")
    b.dma("sync", cf_sb[:, :], cf_d[:, :], "cld")
    b.dma("sync", scs_sb[:, :], sc_d[:, :], "cld")
    b.dma("sync", e16_sb[:, :], e16_d[:, :], "cld")
    consts = b.mark("cld")
    b.wait("tensor", [consts])
    b.wait("vector", [consts])
    b.wait("scalar", [consts])

    tbanks = Banks(b, "tbk", 0, handles=tbp)
    lnfm_done = ln_stream_to_fm(b, x3_d, lnx_fm, TPC, x_stage, ln_stage, stats,
                                mv, rstd, tbanks, ident[:, :], consts)
    b.wait("tensor", [lnfm_done])

    b.op("vector", lambda e: e.tensor_tensor(wrow[:, :], cf_sb[:, :],
                                             scs_sb[:, :], ALU.mult))
    t2 = b.pe(lambda e: e.transpose(sm_ps[0:16, 0:1], wrow[:, :],
                                    identf[0:1, 0:1]),
              waits=[b.mark("ve")], sem="pe")
    b.op("vector", lambda e: e.tensor_copy(wcol[:, :], sm_ps[0:16, 0:1]),
         waits=[t2])
    for j in range(2):
        ej = b.pe(lambda e, j=j: e.matmul(
            sm_ps[:, 1 + j:2 + j], e16_sb[:, j * 128:(j + 1) * 128],
            wcol[:, :], start=True, stop=True, skip_group_check=True),
            waits=[b.mark("ve")], sem="pe")
        ec_tok = b.op("vector", lambda e, j=j: e.tensor_copy(
            ecol2[:, j:j + 1], sm_ps[:, 1 + j:2 + j]), waits=[ej])

    def lnx_rhs(k, half):
        return lnx_fm[:, k * TPC + half * HALF: k * TPC + half * HALF + HALF]

    hbanks = Banks(b, "hbk", 0, handles=pb)

    def h_cb(oi, half, bank, ptok):
        return b.op("vector", lambda e, o=h2_fm[:, oi * TPC + half * HALF:
                                                oi * TPC + half * HALF + HALF],
                    i=bank[:, :], j=oi: e.tensor_scalar(
                        o, i, ecol2[:, j:j + 1], None, ALU.mult, ALU.bypass),
                    waits=[ptok, ec_tok])

    proj_fm(b, hbanks, wa_d, lnx_rhs, 2, h_cb, w_sb, obs=2)
    h_done = b.mark("ve")
    b.wait("tensor", [h_done])
    b.wait("gpsimd", [b.mark("pe")])

    def h_rhs(k, half):
        return h2_fm[:, k * TPC + half * HALF: k * TPC + half * HALF + HALF]

    def bias_evac(oi, half, bank, stg, waits):
        return b.op("vector", lambda e, stg=stg, oi=oi, bank=bank:
                    e.tensor_scalar(stg, bank[:, :], bias_sb[:, oi:oi + 1],
                                    None, ALU.add, ALU.bypass), waits=waits)

    obanks = Banks(b, "obk", 0, handles=pb)
    scr_done = proj_to_scr(b, obanks, wf_d, lnx_rhs, NKC, scr_d, w_sb,
                           wstage, obs=2, extra_k=(2, wbf_d, h_rhs),
                           evac_fn=bias_evac)
    ttb = Banks(b, "ttb", 0, handles=tbp)
    resid_tile_loop(b, scr_done, scr_d, x3_d, x4_d, ident, ttb, scr_tile,
                    x_stage[:, 0:D])
    b.wait("sync", [b.mark(s) for s in list(b.cnt) if s.startswith("x1s")])
    b.finalize()
    return nc, ctx



_NC_CACHE = {}


def _get_nc(name, builder):
    if name not in _NC_CACHE:
        _NC_CACHE[name] = builder()
    return _NC_CACHE[name][0]


def _run(name, builder, in_maps):
    nc = _get_nc(name, builder)
    res = run_bass_kernel_spmd(nc, in_maps, core_ids=list(range(8)))
    return res.results


def kernel(**inputs):
    p = host_prep(inputs)
    X = np.asarray(inputs["L0_attn_X"], dtype=np.float32)
    eye_b = p["ident_bf"]

    def tok(core):
        bb, cc = core // 4, core % 4
        return bb, slice(cc * TPC, (cc + 1) * TPC)

    im = []
    for core in range(8):
        bb, sl = tok(core)
        im.append({
            "x": _f32h(X[bb, sl, :]),
            "w_qkv": p["w_qkv"],
            "cos_c": np.ascontiguousarray(p["cos_t"][:, sl]),
            "sin_c": np.ascontiguousarray(p["sin_t"][:, sl]),
            "cosq_c": np.ascontiguousarray(p["cosq_t"][:, sl]),
            "sinq_c": np.ascontiguousarray(p["sinq_t"][:, sl]),
            "ident": eye_b,
        })
    r1 = _run("l1", build_l1, im)

    im = []
    for k in range(8):
        qfm = np.concatenate(
            [np.concatenate([np.asarray(r1[bt * 4 + cc]["qkv_fm"])
                             [4 * k * 128:(4 * k + 4) * 128, :]
                             for cc in range(4)], axis=1)
             for bt in range(2)], axis=1)
        kfm = np.concatenate(
            [np.concatenate([np.asarray(r1[bt * 4 + cc]["qkv_fm"])
                             [4096 + k * 128:4096 + (k + 1) * 128, :]
                             for cc in range(4)], axis=1)
             for bt in range(2)], axis=1)
        vtm = np.concatenate(
            [np.concatenate([np.asarray(r1[bt * 4 + cc]["v_tm"])
                             [:, k * 128:(k + 1) * 128]
                             for cc in range(4)], axis=0)
             for bt in range(2)], axis=0)
        im.append({
            "q_fm": np.ascontiguousarray(qfm),
            "k_fm": np.ascontiguousarray(kfm),
            "v_tm": np.ascontiguousarray(vtm),
            "mask_qk_b": p["mask_qk_b"], "maskt_kq_b": p["maskt_kq_b"],
            "ident": p["ident_bf"],
            "ident_f": p["ident_f"], "ones_row_f": p["ones_row_f"],
        })
    r2a = _run("l2a", build_l2a, im)

    im = []
    for core in range(8):
        bb, sl = tok(core)
        cc = core % 4
        afm = np.concatenate(
            [np.asarray(r2a[k]["attn_fm"])[:, bb * T + cc * TPC:
                                           bb * T + (cc + 1) * TPC]
             for k in range(8)], axis=0)
        im.append({
            "attn_fm": np.ascontiguousarray(afm),
            "x": _f32h(X[bb, sl, :]),
            "w_o": p["w_o"],
            "ident": eye_b, "ones_col": p["ones_col"],
        })
    r2b = _run("l2b", build_l2b, im)

    im = []
    for core in range(8):
        bb, sl = tok(core)
        pms = np.concatenate([np.asarray(r2b[bb * 4 + j]["pmean"])
                              for j in range(4)], axis=1)
        im.append({
            "x1": np.asarray(r2b[core]["x1"]),
            "pmeans": np.ascontiguousarray(pms),
            "w_org_t": p["w_org_t"], "w_down": p["w_down"], "w_up": p["w_up"],
            "keys_t": p["keys_t"], "wq_mole_t": p["wq_mole_t"],
            "exp8": p["exp8"], "ident": eye_b, "ident_f": p["ident_f"],
        })
    r3 = _run("l3", build_l3, im)

    im = []
    for core in range(8):
        bb, sl = tok(core)
        cc = core % 4
        halo = (np.zeros((3, D), np.float32) if cc == 0 else
                np.asarray(r3[core - 1]["x2"])[-3:, :])
        im.append({
            "x2": np.asarray(r3[core]["x2"]),
            "halo": _f32h(halo),
            "w_inproj_p": p["w_inproj_p"],
            "ident": eye_b,
        })
    r4a = _run("l4a", build_l4a, im)

    im = []
    for core in range(8):
        pout = np.asarray(r4a[core]["pout"])
        im.append({
            "xbc_fm": np.ascontiguousarray(pout[4096:4096 + CONV_DIM, :]),
            "dt_fm": np.ascontiguousarray(pout[10240:10304, :]),
            "conv_w_c": p["conv_w_c"], "conv_b_c": p["conv_b_c"],
            "dt_bias_c": p["dt_bias_c"], "a_log_c": p["a_log_c"],
            "d_row": p["d_row"], "rep": p["rep"],
            "maskneg128": p["maskneg128"],
            "ident": eye_b, "ident_f": p["ident_f"],
            "ones_row_f": p["ones_row_f"], "ones_row_b": p["ones_row_b"],
        })
    r4b = _run("l4b", build_l4b, im)

    im = []
    for core in range(8):
        bb, sl = tok(core)
        cc = core % 4
        sprev = np.zeros((3, 128, 4096), BF)
        llog = np.zeros((64, 3), np.float32)
        for j in range(3):
            pc = cc - 1 - j
            if pc >= 0:
                pcore = bb * 4 + pc
                sprev[j] = np.asarray(r4b[pcore]["s_core"])
                llog[:, j] = np.asarray(r4b[pcore]["coreclg"])[:, -1]
        pout = np.asarray(r4a[core]["pout"])
        im.append({
            "y_tm": np.asarray(r4b[core]["y_tm"]),
            "c_fm": np.asarray(r4b[core]["c_fm"]),
            "coreclg": np.asarray(r4b[core]["coreclg"]),
            "s_prev": sprev.reshape(384, 4096),
            "lamlog_prev": llog,
            "gate_fm": np.ascontiguousarray(pout[0:4096, :]),
            "x2": np.asarray(r3[core]["x2"]),
            "w_outproj_t": p["w_outproj_t"],
            "norm_row": p["norm_row"], "rep": p["rep"],
            "ident": eye_b, "ones_row_f": p["ones_row_f"],
        })
    r5 = _run("l5", build_l5, im)

    im = []
    for core in range(8):
        im.append({
            "x3": np.asarray(r5[core]["x3"]),
            "w_ffn_t": p["w_ffn_t"], "bias_c": p["bias_c"],
            "w_a_t": p["w_a_t"], "w_bf": p["w_bf"],
            "coeffs16": p["coeffs16"], "scales16": p["scales16"],
            "exp16": p["exp16"],
            "ident": eye_b, "ident_f": p["ident_f"],
        })
    r6 = _run("l6", build_l6, im)

    out = np.empty((B, T, D), np.float32)
    for core in range(8):
        bb, sl = tok(core)
        out[bb, sl, :] = np.asarray(r6[core]["x4"])
    return out


# version 05
# speedup vs baseline: 1.2606x; 1.1971x over previous
import numpy as np
import ml_dtypes
from contextlib import ExitStack

import concourse.bass as bass
import concourse.mybir as mybir
from concourse.bass_utils import run_bass_kernel_spmd

F32 = mybir.dt.float32
BF16 = mybir.dt.bfloat16
AF = mybir.ActivationFunctionType
ALU = mybir.AluOpType
BF = ml_dtypes.bfloat16

B = 2
T = 4096
D = 4096
TPC = 1024
HALF = 512
NKC = D // 128

NH, NKV, DH = 32, 8, 128
ROPE_THETA = 10000.0
NE, RR, KK = 8, 64, 128
MH, MP, MN, MG, MK = 64, 64, 128, 8, 4
MI = MH * MP
MGN = MG * MN
CONV_DIM = MI + 2 * MGN
PROJ = MI + CONV_DIM + MH
MEPS = 1e-5
LN_EPS = 1e-5

ENGINES = ("sync", "gpsimd", "tensor", "vector", "scalar")
_SELFSEM = {"vector": "ve", "scalar": "se", "gpsimd": "po"}


class Bld:

    def __init__(self, nc, ctx):
        self.nc = nc
        self.ctx = ctx
        self.ops = {e: [] for e in ENGINES}
        self.sems = {}
        self.cnt = {}
        for s in ("ve", "se", "po", "pe", "cld", "wld", "xld", "sto"):
            self.sem(s)

    def sem(self, name):
        if name not in self.sems:
            self.sems[name] = self.ctx.enter_context(self.nc.semaphore(name))
            self.cnt[name] = 0
        return self.sems[name]

    def sb(self, name, shape, dt):
        return self.ctx.enter_context(self.nc.sbuf_tensor(name, shape, dt))

    def psum(self, name, shape, dt=F32):
        return self.ctx.enter_context(self.nc.psum_tensor(name, shape, dt))

    def on(self, eng, f):
        self.ops[eng].append(f)

    def wait(self, eng, tok):
        toks = tok if isinstance(tok, list) else [tok]
        for t in toks:
            if t is None:
                continue
            (s, v) = t
            if v <= 0:
                continue
            sem = self.sem(s)
            self.on(eng, lambda e, sem=sem, vv=int(v): e.wait_ge(sem, vv))

    def mark(self, semname):
        return (semname, self.cnt[semname])

    def raw_emit(self, eng, fn, sem=None, inc=1):
        if sem is not None:
            s = self.sem(sem)
            self.cnt[sem] += inc
            self.on(eng, lambda e, fn=fn, s=s, inc=inc: fn(e).then_inc(s, inc))
            return (sem, self.cnt[sem])
        self.on(eng, lambda e, fn=fn: fn(e))
        return None

    def op(self, eng, fn, waits=(), chain=True):
        ssem = _SELFSEM[eng]
        w = list(waits) + ([self.mark(ssem)] if chain else [])
        self.wait(eng, w)
        return self.raw_emit(eng, fn, sem=ssem)

    def constcol(self, value):
        if not hasattr(self, "_cc"):
            self._cc = {}
            self._cc_sb = self.sb("constcols", [128, 8], F32)
        if value not in self._cc:
            i = len(self._cc)
            ap = self._cc_sb[:, i:i + 1]
            tok = self.op("vector", lambda e, ap=ap, v=float(value): e.memset(ap, v))
            self._cc[value] = (ap, tok)
        return self._cc[value]

    def dma(self, eng, out, in_, sem, waits=()):
        self.wait(eng, list(waits))
        return self.raw_emit(eng, lambda e, o=out, i=in_: e.dma_start(out=o, in_=i),
                             sem=sem, inc=16)

    def pe(self, fn, waits=(), sem=None):
        self.wait("tensor", list(waits))
        return self.raw_emit("tensor", fn, sem=sem)

    def finalize(self):
        block = self.ctx.enter_context(self.nc.Block())
        for name in ENGINES:
            ops = list(self.ops[name])
            if not ops:
                continue
            def make(ops):
                def run(handle):
                    for f in ops:
                        f(handle)
                return run
            getattr(block, name)(make(ops))


class Banks:

    def __init__(self, b, name, n, cols=512, dt=F32, handles=None):
        self.b = b
        if handles is None:
            handles = [b.psum(f"{name}{i}", [128, cols], dt) for i in range(n)]
        self.banks = handles
        self.free_tok = [None] * len(handles)
        self.i = 0

    def take(self):
        i = self.i
        self.i = (self.i + 1) % len(self.banks)
        tok = self.free_tok[i]
        return i, self.banks[i], tok

    def set_free(self, i, tok):
        self.free_tok[i] = tok


def _bfh(x):
    return np.ascontiguousarray(np.asarray(x)).astype(BF)


def _f32h(x):
    return np.ascontiguousarray(np.asarray(x), dtype=np.float32)



def ln_tile(b, xs, lt, stats, mv, rstd, ld_tok, width=D):
    nq = width // 512
    for q in range(nq):
        b.op("vector", lambda e, q=q: e.bn_stats(
            stats[:, q * 6:(q + 1) * 6], xs[:, q * 512:(q + 1) * 512]),
            waits=[ld_tok] if q == 0 else ())
    b.op("vector", lambda e: e.bn_aggr(mv[:, :], stats[:, 0:6 * nq]))
    eps_ap, eps_tok = b.constcol(LN_EPS)
    b.op("scalar", lambda e: e.activation(rstd[:, :], mv[:, 1:2], AF.Sqrt,
                                          bias=eps_ap),
         waits=[b.mark("ve"), eps_tok])
    b.op("vector", lambda e: e.reciprocal(rstd[:, :], rstd[:, :]),
         waits=[b.mark("se")])
    return b.op("vector", lambda e: e.tensor_scalar(
        lt, xs, mv[:, 0:1], rstd[:, :], ALU.subtract, ALU.mult))


def ln_stream_to_fm(b, x_dram, lnx_fm, n_tok, x_stage, ln_stage, stats, mv,
                    rstd, tbanks, ident, ident_tok, fm_tpc=None,
                    extra_tile_cb=None):
    if fm_tpc is None:
        fm_tpc = n_tok
    nt = n_tok // 128
    last_use = [None, None]
    for tj in range(nt):
        sl = tj % 2
        xs = x_stage[:, sl * D:(sl + 1) * D]
        lt = ln_stage[:, sl * D:(sl + 1) * D]
        ld = b.dma("sync", xs, x_dram[tj * 128:(tj + 1) * 128, :], f"xld{sl}",
                   waits=[last_use[sl]])
        aptok = ln_tile(b, xs, lt, stats, mv, rstd, ld)
        if extra_tile_cb is not None:
            extra_tile_cb(tj, xs, ld)
        for f0 in range(0, NKC, 4):
            bi, bank, war = tbanks.take()
            ptok = None
            for g in range(4):
                fi = f0 + g
                ptok = b.pe(lambda e, s=lt[:, fi * 128:(fi + 1) * 128],
                            o=bank[:, g * 128:(g + 1) * 128]:
                            e.transpose(o, s, ident),
                            waits=([war, aptok, ident_tok] if g == 0 else ()),
                            sem="pe" if g == 3 else None)
            vts = []
            for g in range(4):
                fi = f0 + g
                vts.append(b.op(
                    "vector",
                    lambda e, o=lnx_fm[:, fi * fm_tpc + tj * 128:
                                       fi * fm_tpc + (tj + 1) * 128],
                    i=bank[:, g * 128:(g + 1) * 128]:
                    e.tensor_copy(o, i),
                    waits=[ptok] if g == 0 else ()))
            tbanks.set_free(bi, vts[-1])
        last_use[sl] = b.mark("ve")
    return b.mark("ve")


def transpose_stream(b, banks, src_view, dst_view, n_groups, ident,
                     src_tok=None, grp=4):
    for gi in range(n_groups):
        bi, bank, war = banks.take()
        ptok = None
        for g in range(grp):
            ptok = b.pe(lambda e, s=src_view(gi, g), o=bank[:, g * 128:(g + 1) * 128]:
                        e.transpose(o, s, ident),
                        waits=([war, src_tok] if g == 0 else ()),
                        sem="pe" if g == grp - 1 else None)
        vt = b.op("vector", lambda e, o=dst_view(gi), i=bank[:, 0:grp * 128]:
                  e.tensor_copy(o, i), waits=[ptok])
        banks.set_free(bi, vt)
    return b.mark("ve")


def proj_fm(b, banks, w_dram, rhs_view, n_out, out_cb, w_sb, n_k=NKC,
            obs=4, halves=(0, 1), extra_k=None):
    slab_cols = obs * 128
    slab_tok = [None, None]
    use_tok = [None, None]
    n_ex = extra_k[0] if extra_k else 0
    for ob0 in range(0, n_out, obs):
        cur = min(obs, n_out - ob0)
        sl = (ob0 // obs) % 2
        base = sl * slab_cols * (n_k + n_ex)
        b.wait("gpsimd", [use_tok[sl]])
        for k in range(n_k):
            slab_tok[sl] = b.dma(
                "gpsimd", w_sb[:, base + k * slab_cols: base + k * slab_cols + cur * 128],
                w_dram[k * 128:(k + 1) * 128, ob0 * 128:(ob0 + cur) * 128], f"wld{sl}")
        if extra_k:
            _, w2, _ = extra_k
            for k in range(n_ex):
                slab_tok[sl] = b.dma(
                    "gpsimd",
                    w_sb[:, base + (n_k + k) * slab_cols: base + (n_k + k) * slab_cols + cur * 128],
                    w2[k * 128:(k + 1) * 128, ob0 * 128:(ob0 + cur) * 128], f"wld{sl}")
        for half in halves:
            for oi in range(cur):
                bi, bank, war = banks.take()
                ptok = None
                for k in range(n_k):
                    lhs = w_sb[:, base + k * slab_cols + oi * 128:
                               base + k * slab_cols + (oi + 1) * 128]
                    w_ = []
                    if k == 0:
                        w_ = [slab_tok[sl], war]
                    last = (k == n_k - 1) and not extra_k
                    r_ap = rhs_view(k, half)
                    rw = r_ap.shape[-1]
                    ptok = b.pe(lambda e, l=lhs, r=r_ap, o=bank[:, 0:rw],
                                s=(k == 0), st=last: e.matmul(o, l, r, start=s, stop=st),
                                waits=w_, sem="pe" if last else None)
                if extra_k:
                    _, _, rhs2 = extra_k
                    for k in range(n_ex):
                        lhs = w_sb[:, base + (n_k + k) * slab_cols + oi * 128:
                                   base + (n_k + k) * slab_cols + (oi + 1) * 128]
                        last = k == n_ex - 1
                        r_ap = rhs2(k, half)
                        rw = r_ap.shape[-1]
                        ptok = b.pe(lambda e, l=lhs, r=r_ap, o=bank[:, 0:rw],
                                    st=last: e.matmul(o, l, r, start=False, stop=st),
                                    sem="pe" if last else None)
                vt = out_cb(ob0 + oi, half, bank, ptok)
                banks.set_free(bi, vt)
        use_tok[sl] = b.mark("pe")



def host_prep(inputs):
    p = {}
    p["w_qkv"] = _f32h(np.concatenate(
        [inputs["L0_attn_Wq"], inputs["L0_attn_Wk"], inputs["L0_attn_Wv"]], axis=1))
    p["w_o"] = _f32h(inputs["L0_attn_Wo"])
    p["w_org_t"] = _f32h(np.asarray(inputs["L0_ffn_W_org"]).T)
    ld = np.asarray(inputs["L0_ffn_lora_down"])
    p["w_down"] = _f32h(np.concatenate([ld[e] for e in range(NE)], axis=1))
    lu = np.asarray(inputs["L0_ffn_lora_up"])
    p["w_up"] = _f32h(np.concatenate([lu[e] for e in range(NE)], axis=0))
    p["lora_keys"] = _f32h(inputs["L0_ffn_lora_keys"])
    p["wq_mole_t"] = _f32h(np.asarray(inputs["L0_ffn_Wq"]).T)
    p["w_inproj_t"] = _f32h(np.asarray(inputs["L1_attn_in_proj_w"]).T)
    p["conv_w"] = _f32h(inputs["L1_attn_conv_weight"])
    p["conv_b"] = _f32h(inputs["L1_attn_conv_bias"]).reshape(CONV_DIM // 128, 128).T.copy()
    p["dt_bias"] = _f32h(inputs["L1_attn_dt_bias"]).reshape(1, MH)
    p["a_log"] = _f32h(inputs["L1_attn_A_log"]).reshape(1, MH)
    p["d_vec"] = _f32h(inputs["L1_attn_D"]).reshape(1, MH)
    p["norm_w"] = _f32h(inputs["L1_attn_norm_weight"]).reshape(1, MI)
    p["w_outproj_t"] = _f32h(np.asarray(inputs["L1_attn_out_proj_w"]).T)
    p["w_ffn_t"] = _f32h(np.asarray(inputs["L1_ffn_W"]).T)
    p["ffn_bias"] = _f32h(inputs["L1_ffn_bias"]).reshape(NKC, 128).T.copy()
    A = np.asarray(inputs["L1_ffn_A"])
    p["w_a_t"] = _f32h(np.concatenate([A[e].T for e in range(16)], axis=1))
    Bf = np.asarray(inputs["L1_ffn_Bf"])
    p["w_bf"] = _f32h(np.concatenate([Bf[e].T for e in range(16)], axis=0))
    p["coeffs16"] = _f32h(inputs["L1_ffn_coeffs"]).reshape(1, 16)
    p["scales16"] = _f32h(inputs["L1_ffn_scales"]).reshape(1, 16)
    p["ident_bf"] = np.eye(128, dtype=BF)
    p["conv_w_cols"] = _f32h(inputs["L1_attn_conv_weight"]).reshape(
        CONV_DIM // 128, 128, MK)
    half = DH // 2
    inv_freq = 1.0 / ROPE_THETA ** (np.arange(0, half, dtype=np.float64) / half)
    pos = np.arange(T, dtype=np.float64)
    ang = pos[:, None] * inv_freq[None, :]
    p["cos_t"] = _f32h(np.cos(ang).T)
    p["sin_t"] = _f32h(np.sin(ang).T)
    m = np.triu(np.ones((HALF, HALF), dtype=np.float32), k=1) * (-3e38 / 4)
    p["mask_qk"] = _f32h(m)
    p["mask_qk_b"] = m.astype(BF)
    p["maskt_kq_b"] = (np.tril(np.ones((HALF, HALF), np.float32), -1)
                       * (-3e38 / 4)).astype(BF)
    p["ident_f"] = np.eye(128, dtype=np.float32)
    p["ones_col"] = np.ones((128, 1), BF)
    p["ones_row_f"] = np.ones((1, 128), np.float32)
    p["ones_row_b"] = np.ones((1, 128), BF)
    p["cosq_t"] = _f32h(p["cos_t"] / np.sqrt(128.0))
    p["sinq_t"] = _f32h(p["sin_t"] / np.sqrt(128.0))
    p["keys_t"] = _f32h(np.asarray(inputs["L0_ffn_lora_keys"]).T)
    exp8 = np.zeros((8, 512), np.float32)
    for j in range(4):
        for q in range(128):
            exp8[2 * j + q // 64, j * 128 + q] = 1.0
    p["exp8"] = exp8
    exp16 = np.zeros((16, 256), np.float32)
    for j in range(2):
        for q in range(128):
            exp16[(j * 128 + q) // 16, j * 128 + q] = 1.0
    p["exp16"] = exp16
    wp = np.concatenate([np.asarray(inputs["L1_attn_in_proj_w"]),
                         np.zeros((NPC * 128 - PROJ, D), np.float32)], axis=0)
    p["w_inproj_p"] = _f32h(wp.T)
    cw = _f32h(inputs["L1_attn_conv_weight"])
    p["conv_w_c"] = np.ascontiguousarray(
        cw.reshape(48, 128, 4).transpose(1, 0, 2).reshape(128, 192))
    p["conv_b_c"] = np.ascontiguousarray(
        _f32h(inputs["L1_attn_conv_bias"]).reshape(48, 128).T)
    p["dt_bias_c"] = _f32h(inputs["L1_attn_dt_bias"]).reshape(64, 1)
    p["a_log_c"] = _f32h(inputs["L1_attn_A_log"]).reshape(64, 1)
    p["d_row"] = _f32h(np.repeat(np.asarray(inputs["L1_attn_D"]), 64)).reshape(1, 4096)
    rep = np.zeros((64, 4096), BF)
    for h in range(64):
        rep[h, h * 64:(h + 1) * 64] = 1
    p["rep"] = rep
    p["maskneg128"] = _f32h(np.tril(np.ones((128, 128), np.float32), -1) * (-3e38 / 4))
    p["norm_row"] = _f32h(inputs["L1_attn_norm_weight"]).reshape(1, 4096)
    p["bias_c"] = np.ascontiguousarray(
        _f32h(inputs["L1_ffn_bias"]).reshape(NKC, 128).T)
    return p



def build_l1():
    nc = bass.Bass("TRN2", target_bir_lowering=False, debug=False)
    ctx = ExitStack()
    b = Bld(nc, ctx)
    x_d = nc.dram_tensor("x", [TPC, D], F32, kind="ExternalInput").ap()
    wqkv_d = nc.dram_tensor("w_qkv", [D, 6144], F32, kind="ExternalInput").ap()
    cos_d = nc.dram_tensor("cos_c", [64, TPC], F32, kind="ExternalInput").ap()
    sin_d = nc.dram_tensor("sin_c", [64, TPC], F32, kind="ExternalInput").ap()
    cosq_d = nc.dram_tensor("cosq_c", [64, TPC], F32, kind="ExternalInput").ap()
    sinq_d = nc.dram_tensor("sinq_c", [64, TPC], F32, kind="ExternalInput").ap()
    ident_d = nc.dram_tensor("ident", [128, 128], BF16, kind="ExternalInput").ap()
    qkv_d = nc.dram_tensor("qkv_fm", [6144, TPC], BF16, kind="ExternalOutput").ap()
    vtm_d = nc.dram_tensor("v_tm", [TPC, 1024], BF16, kind="ExternalOutput").ap()

    NT = TPC // 128
    x_stage = b.sb("x_stage", [128, 2 * D], F32)
    ln_stage = b.sb("ln_stage", [128, 2 * D], BF16)
    stats = b.sb("stats", [128, 48], F32)
    mv = b.sb("mv", [128, 2], F32)
    rstd = b.sb("rstd", [128, 1], F32)
    lnx_fm = b.sb("lnx_fm", [128, NKC * TPC], BF16)
    ident = b.sb("ident_sb", [128, 128], BF16)
    cos_sb = b.sb("cos_sb", [64, TPC], BF16)
    sin_sb = b.sb("sin_sb", [64, TPC], BF16)
    cosq_sb = b.sb("cosq_sb", [64, TPC], BF16)
    sinq_sb = b.sb("sinq_sb", [64, TPC], BF16)
    w_sb = b.sb("w_sb", [128, 2 * 4 * 128 * NKC], BF16)
    stage = b.sb("stage", [128, 4 * HALF], BF16)
    ra = b.sb("ra", [64, HALF], F32)
    rb = b.sb("rb", [64, HALF], F32)
    rc = b.sb("rc", [64, HALF], F32)
    rd_ = b.sb("rd", [64, HALF], F32)
    vstage = b.sb("vstage", [128, 4 * HALF], BF16)

    b.dma("sync", ident[:, :], ident_d[:, :], "cld")
    b.dma("gpsimd", cos_sb[:, :], cos_d[:, :], "cldg")
    b.dma("gpsimd", sin_sb[:, :], sin_d[:, :], "cldg")
    b.dma("gpsimd", cosq_sb[:, :], cosq_d[:, :], "cldg")
    b.dma("gpsimd", sinq_sb[:, :], sinq_d[:, :], "cldg")
    c0 = b.mark("cld")
    c1 = b.mark("cldg")
    b.wait("vector", [c1])

    banks = Banks(b, "bk", 6)
    tbanks = Banks(b, "tb", 2, dt=BF16)

    ident_ap = ident[:, :]
    lnfm_done = ln_stream_to_fm(b, x_d, lnx_fm, TPC, x_stage, ln_stage, stats,
                                mv, rstd, tbanks, ident_ap, c0)
    b.wait("tensor", [lnfm_done])

    stage_i = [0]
    stage_free = [None] * 4
    vstage_free = [None] * 4

    def rhs_view(k, half):
        return lnx_fm[:, k * TPC + half * HALF: k * TPC + (half + 1) * HALF]

    def out_cb(oi, half, bank, ptok):
        si = stage_i[0] % 4
        stage_i[0] += 1
        stg = stage[:, si * HALF:(si + 1) * HALF]
        if oi < NH + NKV:
            csrc = cosq_sb if oi < NH else cos_sb
            ssrc = sinq_sb if oi < NH else sin_sb
            cs = csrc[:, half * HALF:(half + 1) * HALF]
            sn = ssrc[:, half * HALF:(half + 1) * HALF]
            b.op("vector", lambda e: e.tensor_tensor(ra[:, :], bank[0:64, :], cs, ALU.mult),
                 waits=[ptok, c1])
            b.op("vector", lambda e: e.tensor_tensor(rb[:, :], bank[64:128, :], sn, ALU.mult))
            b.op("vector", lambda e: e.tensor_tensor(rc[:, :], bank[0:64, :], sn, ALU.mult))
            last_ps = b.op("vector", lambda e: e.tensor_tensor(rd_[:, :], bank[64:128, :], cs, ALU.mult))
            b.op("vector", lambda e, stg=stg: e.tensor_tensor(stg[0:64, :], ra[:, :], rb[:, :], ALU.subtract),
                 waits=[stage_free[si]])
            vt = b.op("vector", lambda e, stg=stg: e.tensor_tensor(stg[64:128, :], rc[:, :], rd_[:, :], ALU.add))
            bank_free = last_ps
        else:
            vt = b.op("vector", lambda e, stg=stg: e.tensor_copy(stg, bank[:, :]),
                      waits=[ptok, stage_free[si]])
            bank_free = vt
        dtok = b.dma("sync", qkv_d[oi * 128:(oi + 1) * 128, half * HALF:(half + 1) * HALF],
                     stg, f"so{si}", waits=[vt])
        stage_free[si] = dtok
        if oi >= NH + NKV:
            vi = oi - (NH + NKV)
            bi2, bank2, war2 = tbanks.take()
            ptok2 = None
            for g in range(4):
                ptok2 = b.pe(lambda e, s=stg[:, g * 128:(g + 1) * 128],
                             o=bank2[:, g * 128:(g + 1) * 128]:
                             e.transpose(o, s, ident_ap),
                             waits=([war2, vt] if g == 0 else ()),
                             sem="pe" if g == 3 else None)
            sj = stage_i[0] % 4
            stage_i[0] += 1
            vstg = vstage[:, sj * HALF:(sj + 1) * HALF]
            vt2 = b.op("vector", lambda e, vstg=vstg: e.tensor_copy(vstg, bank2[:, :]),
                       waits=[ptok2, vstage_free[sj]])
            tbanks.set_free(bi2, vt2)
            d2 = None
            for g in range(4):
                d2 = b.dma("sync",
                           vtm_d[half * HALF + g * 128: half * HALF + (g + 1) * 128,
                                 vi * 128:(vi + 1) * 128],
                           vstg[:, g * 128:(g + 1) * 128], f"sv{sj}", waits=[vt2] if g == 0 else ())
            vstage_free[sj] = d2
        return bank_free

    proj_fm(b, banks, wqkv_d, rhs_view, 48, out_cb, w_sb)

    b.wait("sync", [b.mark(s) for s in list(b.cnt) if s.startswith(("so", "sv"))])
    b.finalize()
    return nc, ctx


def run_l1(inputs, prep):
    nc, ctx = build_l1()
    in_maps = []
    X = np.asarray(inputs["L0_attn_X"])
    for core in range(8):
        bb, cc = core // 4, core % 4
        sl = slice(cc * TPC, (cc + 1) * TPC)
        in_maps.append({
            "x": _f32h(X[bb, sl, :]),
            "w_qkv": prep["w_qkv"],
            "cos_c": prep["cos_t"][:, sl].copy(),
            "sin_c": prep["sin_t"][:, sl].copy(),
            "ident": prep["ident_bf"],
        })
    res = run_bass_kernel_spmd(nc, in_maps, core_ids=list(range(8)))
    ctx.close()
    return res.results






def proj_to_scr(b, banks, w_dram, rhs_view, n_out, scr_d, w_sb, wstage,
                sem_pfx="wo", obs=4, extra_k=None, evac_fn=None,
                halves=(0, 1), hw=HALF):
    stage_free = [None] * 4
    wi = [0]

    def cb(oi, half, bank, ptok):
        si = wi[0] % 4
        wi[0] += 1
        stg = wstage[:, si * hw:(si + 1) * hw]
        if evac_fn is None:
            vt = b.op("vector", lambda e, stg=stg, w=hw: e.tensor_copy(
                stg, bank[:, 0:w]), waits=[ptok, stage_free[si]])
        else:
            vt = evac_fn(oi, half, bank, stg, [ptok, stage_free[si]])
        dt_ = b.dma("sync", scr_d[oi * 128:(oi + 1) * 128,
                                  half * hw:(half + 1) * hw],
                    stg, f"{sem_pfx}{si}", waits=[vt])
        stage_free[si] = dt_
        return vt

    proj_fm(b, banks, w_dram, rhs_view, n_out, cb, w_sb, obs=obs,
            extra_k=extra_k, halves=halves)
    return [b.mark(s) for s in list(b.cnt) if s.startswith(sem_pfx)]


def resid_tile_loop(b, scr_done, scr_d, x_in_d, xout_d, ident, ttb, scr_tile,
                    x_stage, ln_extras=None):
    NT = TPC // 128
    d1 = None
    pm_started = [False]
    for tt in range(NT):
        wl = None
        for o in range(NKC):
            wl = b.dma("sync", scr_tile[:, o * 128:(o + 1) * 128],
                       scr_d[o * 128:(o + 1) * 128, tt * 128:(tt + 1) * 128],
                       "wtl", waits=(scr_done + [b.mark("pe")]) if o == 0 else ())
        xl = b.dma("sync", x_stage[:, :], x_in_d[tt * 128:(tt + 1) * 128, :],
                   "xl", waits=[b.mark("ve"), d1])
        for f0 in range(0, NKC, 4):
            bi, bank, war = ttb.take()
            ptk = None
            for g in range(4):
                fi = f0 + g
                ptk = b.pe(lambda e, s=scr_tile[:, fi * 128:(fi + 1) * 128],
                           o=bank[:, g * 128:(g + 1) * 128], i=ident[:, :]:
                           e.transpose(o, s, i),
                           waits=([war, wl] if g == 0 else ()), sem="pe")
            vt = b.op("vector", lambda e, o=x_stage[:, f0 * 128:(f0 + 4) * 128],
                      i=bank[:, 0:512]: e.tensor_tensor(o, o, i, ALU.add),
                      waits=[ptk, xl])
            ttb.set_free(bi, vt)
        x1t = b.mark("ve")
        d1 = b.dma("sync", xout_d[tt * 128:(tt + 1) * 128, :], x_stage[:, :],
                   "x1s", waits=[x1t])
        if ln_extras is not None:
            (stats, mv, rstd, ln_t, ones_sb, pmean_ps, pm_stage, pm_d) = ln_extras
            b.wait("vector", [b.mark("pe")])
            ap2 = ln_tile(b, x_stage[:, :], ln_t[:, :], stats, mv, rstd, None)
            b.wait("tensor", [ap2])
            for fi in range(NKC):
                b.pe(lambda e, l=ln_t[:, fi * 128:(fi + 1) * 128],
                     o=pmean_ps[:, fi:fi + 1], s=(not pm_started[0]),
                     st=(tt == NT - 1 and fi == NKC - 1):
                     e.matmul(o, l, ones_sb[:, :], start=s, stop=st,
                              skip_group_check=True),
                     sem="pe" if fi == NKC - 1 else None)
                pm_started[0] = True
    if ln_extras is not None:
        (stats, mv, rstd, ln_t, ones_sb, pmean_ps, pm_stage, pm_d) = ln_extras
        pmt = b.op("vector", lambda e: e.tensor_copy(pm_stage[:, :],
                                                     pmean_ps[:, :]),
                   waits=[b.mark("pe")])
        b.dma("sync", pm_d[:, :], pm_stage[:, :], "pms", waits=[pmt])



def attn_unit(b, env, q_ap_full, v2_base, out_write, qb, k_sb):
    (mask_sb, maskt_sb, pt_sb, ident, nc20_ap, cc_tok, psA, accs,
     half_free, pt_free, acc_free) = env
    nkb = qb + 1
    nkc = nkb * 4
    dkb = qb
    av_tok = [None, None, None, None]
    started = [False] * 4
    waves = [list(range(w0, min(w0 + 2, nkc))) for w0 in range(0, nkc, 2)]
    for w, kcs in enumerate(waves):
        hh = w % 2
        ptok = None
        for kc in kcs:
            dmask = kc >= dkb * 4
            last = kc == kcs[-1]
            o_ap = psA[:, hh * 1024 + (kc % 2) * HALF:
                       hh * 1024 + ((kc % 2) + 1) * HALF]
            ptok = b.pe(lambda e, l=k_sb[:, kc * 128:(kc + 1) * 128],
                        r=q_ap_full[:, qb * HALF:(qb + 1) * HALF],
                        o=o_ap, st=(not dmask):
                        e.matmul(o, l, r, start=True, stop=st,
                                 skip_group_check=True),
                        waits=([half_free[hh]] if kc == kcs[0] else ()),
                        sem="pe" if (last and not dmask) else None)
            if dmask:
                r2i = kc - dkb * 4
                qt0 = (kc % 4) if False else r2i
                ptok = b.pe(lambda e, m=maskt_sb[:, r2i * HALF:(r2i + 1) * HALF],
                            o=o_ap:
                            e.matmul(o, ident[:, :], m, start=False, stop=True,
                                     skip_group_check=True),
                            sem="pe" if last else None)
        nk_w = len(kcs) * HALF
        et = b.op("scalar", lambda e, o=pt_sb[:, hh * 1024:hh * 1024 + nk_w],
                  i=psA[:, hh * 1024:hh * 1024 + nk_w]:
                  e.activation(o, i, AF.Exp, bias=nc20_ap),
                  waits=[ptok, pt_free[hh], cc_tok], chain=False)
        half_free[hh] = et
        lastw = w == len(waves) - 1
        atk = None
        for j, kc in enumerate(kcs):
            for qt in range(4):
                w_ = []
                if j == 0 and qt == 0:
                    w_.append(et)
                if not started[qt]:
                    w_.append(acc_free[qt])
                atk = b.pe(lambda e,
                           l=pt_sb[:, hh * 1024 + (kc % 2) * HALF + qt * 128:
                                   hh * 1024 + (kc % 2) * HALF + (qt + 1) * 128],
                           r=v2_base[:, kc * 129:(kc + 1) * 129],
                           o=accs[qt][:, 0:129], s=(not started[qt]),
                           st=(kc == nkc - 1):
                           e.matmul(o, l, r, start=s, stop=st,
                                    skip_group_check=True),
                           waits=w_,
                           sem="pe" if (j == len(kcs) - 1 and qt == 3) else None)
                started[qt] = True
        pt_free[hh] = atk
    return out_write(accs, atk)


def build_l2a():
    nc = bass.Bass("TRN2", target_bir_lowering=False, debug=False)
    ctx = ExitStack()
    b = Bld(nc, ctx)
    q_d = nc.dram_tensor("q_fm", [512, 2 * T], BF16, kind="ExternalInput").ap()
    k_d = nc.dram_tensor("k_fm", [128, 2 * T], BF16, kind="ExternalInput").ap()
    v_d = nc.dram_tensor("v_tm", [2 * T, 128], BF16, kind="ExternalInput").ap()
    mask_d = nc.dram_tensor("mask_qk_b", [HALF, HALF], BF16, kind="ExternalInput").ap()
    maskt_d = nc.dram_tensor("maskt_kq_b", [HALF, HALF], BF16, kind="ExternalInput").ap()
    ident_d = nc.dram_tensor("ident", [128, 128], BF16, kind="ExternalInput").ap()
    attn_d = nc.dram_tensor("attn_tm", [2 * T, 512], BF16, kind="ExternalOutput").ap()

    q_sb = b.sb("q_sb", [128, 8 * T], BF16)
    k_sb = b.sb("k_sb", [128, 2 * T], BF16)
    v2_sb = b.sb("v2_sb", [128, 64 * 129], BF16)
    mask_sb = b.sb("mask_sb", [128, 4 * HALF], BF16)
    maskt_sb = b.sb("maskt_sb", [128, 4 * HALF], BF16)
    ident = b.sb("ident_sb", [128, 128], BF16)
    pt_sb = b.sb("pt_sb", [128, 2048], BF16)
    rcol = b.sb("rcol", [128, 1], F32)
    ostage = b.sb("ostage", [128, 4 * 128], BF16)

    psA = b.psum("psA", [128, 2048], F32)
    accs = [b.psum(f"acc{i}", [128, 132], F32) for i in range(4)]

    b.dma("sync", ident[:, :], ident_d[:, :], "cld")
    for r in range(4):
        b.dma("sync", mask_sb[:, r * HALF:(r + 1) * HALF],
              mask_d[r * 128:(r + 1) * 128, :], "cld")
        b.dma("sync", maskt_sb[:, r * HALF:(r + 1) * HALF],
              maskt_d[r * 128:(r + 1) * 128, :], "cld")
    b.dma("sync", k_sb[:, :], k_d[:, :], "cld")
    v2v = v2_sb.ap().rearrange("p (t c) -> p t c", c=129)
    b.dma("sync", v2v[:, :, 0:128],
          v_d[:, :].rearrange("(t p) d -> p t d", p=128), "cld")
    for ch in range(8):
        b.dma("sync", q_sb[:, ch * T:(ch + 1) * T],
              q_d[(ch // 2) * 128:(ch // 2 + 1) * 128,
                  (ch % 2) * T:(ch % 2 + 1) * T], "cld")
    consts = b.mark("cld")
    ones_tok = b.op("vector", lambda e: e.memset(v2v[:, :, 128:129], 1.0),
                    waits=[consts])
    b.wait("tensor", [consts, ones_tok])
    b.wait("vector", [consts])
    b.wait("scalar", [consts])
    nc20_ap, t2 = b.constcol(-20.0)
    cc_tok = b.mark("ve")
    half_free = [None, None]
    pt_free = [None, None]
    acc_free = [None] * 4
    ostage_free = [None] * 4

    env = (mask_sb, maskt_sb, pt_sb, ident, nc20_ap, cc_tok, psA, accs,
           half_free, pt_free, acc_free)

    for ch in range(8):
        qh, bt = ch // 2, ch % 2
        q_full = q_sb[:, ch * T:(ch + 1) * T]
        kk = k_sb[:, bt * T:(bt + 1) * T]
        v2b = v2_sb[:, bt * 32 * 129:(bt + 1) * 32 * 129]
        for qb in range(8):
            def out_write(accs_, atk, ch=ch, qb=qb, qh=qh, bt=bt):
                toks = []
                for qt in range(4):
                    rc = b.op("vector", lambda e, qt=qt: e.reciprocal(
                        rcol[:, :], accs_[qt][:, 128:129]), waits=[atk])
                    ev = b.op("vector", lambda e, qt=qt: e.tensor_scalar(
                        ostage[:, qt * 128:(qt + 1) * 128],
                        accs_[qt][:, 0:128], rcol[:, :], None, ALU.mult,
                        ALU.bypass), waits=[ostage_free[qt]])
                    acc_free[qt] = ev
                    d_ = b.dma("sync",
                               attn_d[bt * T + qb * HALF + qt * 128:
                                      bt * T + qb * HALF + (qt + 1) * 128,
                                      qh * 128:(qh + 1) * 128],
                               ostage[:, qt * 128:(qt + 1) * 128],
                               f"ao{qt}", waits=[ev])
                    ostage_free[qt] = d_
                    toks.append(d_)
                return toks[-1]
            attn_unit(b, env, q_full, v2b, out_write, qb, kk)

    b.wait("sync", [b.mark(s) for s in list(b.cnt) if s.startswith("ao")])
    b.finalize()
    return nc, ctx



def build_l2b():
    nc = bass.Bass("TRN2", target_bir_lowering=False, debug=False)
    ctx = ExitStack()
    b = Bld(nc, ctx)
    a_d = nc.dram_tensor("attn_fm", [4096, TPC], BF16, kind="ExternalInput").ap()
    x_d = nc.dram_tensor("x", [TPC, D], F32, kind="ExternalInput").ap()
    wo_d = nc.dram_tensor("w_o", [D, D], F32, kind="ExternalInput").ap()
    ident_d = nc.dram_tensor("ident", [128, 128], BF16, kind="ExternalInput").ap()
    ones_d = nc.dram_tensor("ones_col", [128, 1], BF16, kind="ExternalInput").ap()
    x1_d = nc.dram_tensor("x1", [TPC, D], F32, kind="ExternalOutput").ap()
    pm_d = nc.dram_tensor("pmean", [128, NKC], F32, kind="ExternalOutput").ap()
    wo_scr = nc.dram_tensor("wo_scr", [D, TPC], BF16)

    a_sb = b.sb("a_sb", [128, NKC * TPC], BF16)
    ident = b.sb("ident_sb", [128, 128], BF16)
    ones_sb = b.sb("ones_sb", [128, 1], BF16)
    w_sb = b.sb("w_sb", [128, 2 * 4 * 128 * NKC], BF16)
    wstage = b.sb("wstage", [128, 4 * HALF], BF16)
    wo_tile = b.sb("wo_tile", [128, D], BF16)
    x_stage = b.sb("x_stage", [128, D], F32)
    stats = b.sb("stats", [128, 48], F32)
    mv = b.sb("mv", [128, 2], F32)
    rstd = b.sb("rstd", [128, 1], F32)
    ln_t = b.sb("ln_t", [128, D], BF16)
    pm_stage = b.sb("pm_stage", [128, NKC], F32)

    pb = [b.psum(f"pb{i}", [128, 512], F32) for i in range(4)]
    tbp = [b.psum(f"tbp{i}", [128, 512], BF16) for i in range(2)]
    pmean_ps = b.psum("pmean_ps", [128, NKC], F32)

    b.dma("sync", ident[:, :], ident_d[:, :], "cld")
    b.dma("sync", ones_sb[:, :], ones_d[:, :], "cld")
    for ch in range(NKC):
        b.dma("sync", a_sb[:, ch * TPC:(ch + 1) * TPC],
              a_d[ch * 128:(ch + 1) * 128, :], "cld")
    consts = b.mark("cld")
    b.wait("tensor", [consts])
    b.wait("vector", [consts])
    b.wait("scalar", [consts])

    wo_banks = Banks(b, "wob", 0, handles=pb[0:4])

    def wo_rhs(k, half):
        return a_sb[:, k * TPC + half * HALF: k * TPC + half * HALF + HALF]

    wo_done = proj_to_scr(b, wo_banks, wo_d, wo_rhs, NKC, wo_scr, w_sb, wstage)
    ttb = Banks(b, "ttb", 0, handles=tbp)
    resid_tile_loop(b, wo_done, wo_scr, x_d, x1_d, ident, ttb, wo_tile, x_stage,
                    ln_extras=(stats, mv, rstd, ln_t, ones_sb, pmean_ps,
                               pm_stage, pm_d))
    b.wait("sync", [b.mark(s) for s in list(b.cnt)
                    if s.startswith(("x1s", "pms"))])
    b.finalize()
    return nc, ctx


def build_l3():
    nc = bass.Bass("TRN2", target_bir_lowering=False, debug=False)
    ctx = ExitStack()
    b = Bld(nc, ctx)
    x1_d = nc.dram_tensor("x1", [TPC, D], F32, kind="ExternalInput").ap()
    pms_d = nc.dram_tensor("pmeans", [128, 4 * NKC], F32, kind="ExternalInput").ap()
    worg_d = nc.dram_tensor("w_org_t", [D, D], F32, kind="ExternalInput").ap()
    wdown_d = nc.dram_tensor("w_down", [D, 512], F32, kind="ExternalInput").ap()
    wup_d = nc.dram_tensor("w_up", [512, D], F32, kind="ExternalInput").ap()
    keys_d = nc.dram_tensor("keys_t", [128, 8], F32, kind="ExternalInput").ap()
    wqm_d = nc.dram_tensor("wq_mole_t", [D, 128], F32, kind="ExternalInput").ap()
    exp8_d = nc.dram_tensor("exp8", [8, 512], F32, kind="ExternalInput").ap()
    ident_d = nc.dram_tensor("ident", [128, 128], BF16, kind="ExternalInput").ap()
    identf_d = nc.dram_tensor("ident_f", [128, 128], F32, kind="ExternalInput").ap()
    x2_d = nc.dram_tensor("x2", [TPC, D], F32, kind="ExternalOutput").ap()
    scr_d = nc.dram_tensor("ffn_scr", [D, TPC], BF16)

    lnx_fm = b.sb("lnx_fm", [128, NKC * TPC], BF16)
    h_fm = b.sb("h_fm", [128, 4 * TPC], BF16)
    x_stage = b.sb("x_stage", [128, 2 * D], F32)
    ln_stage = b.sb("ln_stage", [128, 2 * D], BF16)
    stats = b.sb("stats", [128, 48], F32)
    mv = b.sb("mv", [128, 2], F32)
    rstd = b.sb("rstd", [128, 1], F32)
    ident = b.sb("ident_sb", [128, 128], BF16)
    identf = b.sb("identf_sb", [128, 128], F32)
    keys_sb = b.sb("keys_sb", [128, 8], BF16)
    wqm_sb = b.sb("wqm_sb", [128, NKC * 128], BF16)
    exp8_sb = b.sb("exp8_sb", [8, 512], F32)
    pms_sb = b.sb("pms_sb", [128, 4 * NKC], F32)
    mean_sb = b.sb("mean_sb", [128, NKC], F32)
    meanb_sb = b.sb("meanb_sb", [128, NKC], BF16)
    qv_sb = b.sb("qv_sb", [128, 1], BF16)
    scol = b.sb("scol", [8, 1], F32)
    srow = b.sb("srow", [1, 8], F32)
    urow = b.sb("urow", [1, 8], F32)
    smax = b.sb("smax", [1, 2], F32)
    sexp = b.sb("sexp", [1, 8], F32)
    ssum = b.sb("ssum", [1, 2], F32)
    crow8 = b.sb("crow8", [1, 8], F32)
    ccolv = b.sb("ccolv", [8, 1], F32)
    ecol = b.sb("ecol", [128, 4], F32)
    w_sb = b.sb("w_sb", [128, 2 * 2 * 128 * (NKC + 4)], BF16)
    wstage = b.sb("wstage", [128, 4 * HALF], BF16)
    scr_tile = b.sb("scr_tile", [128, D], BF16)

    pb = [b.psum(f"pb{i}", [128, 512], F32) for i in range(4)]
    tbp = [b.psum(f"tbp{i}", [128, 512], BF16) for i in range(2)]
    sm_ps = b.psum("sm_ps", [128, 16], F32)

    b.dma("sync", ident[:, :], ident_d[:, :], "cld")
    b.dma("sync", identf[:, :], identf_d[:, :], "cld")
    b.dma("gpsimd", keys_sb[:, :], keys_d[:, :], "cldg")
    b.dma("gpsimd", wqm_sb[:, :].rearrange("p (k q) -> p k q", q=128),
          wqm_d[:, :].rearrange("(k p) q -> p k q", p=128), "cldg")
    b.dma("sync", exp8_sb[:, :], exp8_d[:, :], "cld")
    b.dma("sync", pms_sb[:, :], pms_d[:, :], "cld")
    consts = b.mark("cld")
    constsg = b.mark("cldg")
    b.wait("tensor", [consts, constsg])
    b.wait("vector", [consts, constsg])
    b.wait("scalar", [consts, constsg])

    tbanks = Banks(b, "tbk", 0, handles=tbp)
    lnfm_done = ln_stream_to_fm(b, x1_d, lnx_fm, TPC, x_stage, ln_stage, stats,
                                mv, rstd, tbanks, ident[:, :], consts)
    b.wait("tensor", [lnfm_done])

    b.op("vector", lambda e: e.tensor_tensor(
        mean_sb[:, :], pms_sb[:, 0:NKC], pms_sb[:, NKC:2 * NKC], ALU.add))
    b.op("vector", lambda e: e.tensor_tensor(
        mean_sb[:, :], mean_sb[:, :], pms_sb[:, 2 * NKC:3 * NKC], ALU.add))
    b.op("vector", lambda e: e.tensor_tensor(
        mean_sb[:, :], mean_sb[:, :], pms_sb[:, 3 * NKC:4 * NKC], ALU.add))
    b.op("vector", lambda e: e.tensor_scalar(
        meanb_sb[:, :], mean_sb[:, :], 1.0 / T, None, ALU.mult, ALU.bypass))
    mtok = b.mark("ve")
    qtok = None
    for k in range(NKC):
        qtok = b.pe(lambda e, l=wqm_sb[:, k * 128:(k + 1) * 128],
                    r=meanb_sb[:, k:k + 1], o=sm_ps[:, 0:1],
                    s=(k == 0), st=(k == NKC - 1):
                    e.matmul(o, l, r, start=s, stop=st, skip_group_check=True),
                    waits=[mtok] if k == 0 else (),
                    sem="pe" if k == NKC - 1 else None)
    b.op("vector", lambda e: e.tensor_copy(qv_sb[:, :], sm_ps[:, 0:1]),
         waits=[qtok])
    stok = b.pe(lambda e: e.matmul(sm_ps[0:8, 1:2], keys_sb[:, :], qv_sb[:, :],
                                   start=True, stop=True, skip_group_check=True),
                waits=[b.mark("ve")], sem="pe")
    b.op("vector", lambda e: e.tensor_copy(scol[:, :], sm_ps[0:8, 1:2]),
         waits=[stok])
    t1 = b.pe(lambda e: e.transpose(sm_ps[0:1, 2:10], scol[:, :],
                                    identf[0:8, 0:8]),
              waits=[b.mark("ve")], sem="pe")
    b.op("vector", lambda e: e.tensor_copy(srow[:, :], sm_ps[0:1, 2:10]),
         waits=[t1])
    b.op("vector", lambda e: e.tensor_scalar(
        urow[:, :], srow[:, :], 1.0 / np.sqrt(128.0), None, ALU.mult, ALU.bypass))
    b.op("vector", lambda e: e.reduce_max(smax[:, 0:1], urow[:, :],
                                          axis=mybir.AxisListType.X))
    b.op("vector", lambda e: e.tensor_scalar(
        smax[:, 1:2], smax[:, 0:1], -1.0, None, ALU.mult, ALU.bypass))
    b.op("scalar", lambda e: e.activation(sexp[:, :], urow[:, :], AF.Exp,
                                          bias=smax[:, 1:2],
                                          accum_out=ssum[:, 0:1]),
         waits=[b.mark("ve")])
    b.op("vector", lambda e: e.reciprocal(ssum[:, 1:2], ssum[:, 0:1]),
         waits=[b.mark("se")])
    b.op("vector", lambda e: e.tensor_scalar(
        crow8[:, :], sexp[:, :], ssum[:, 1:2], None, ALU.mult, ALU.bypass))
    t2 = b.pe(lambda e: e.transpose(sm_ps[0:8, 10:11], crow8[:, :],
                                    identf[0:1, 0:1]),
              waits=[b.mark("ve")], sem="pe")
    b.op("vector", lambda e: e.tensor_copy(ccolv[:, :], sm_ps[0:8, 10:11]),
         waits=[t2])
    ec_tok = None
    for j in range(4):
        ej = b.pe(lambda e, j=j: e.matmul(sm_ps[:, 11 + j:12 + j],
                                          exp8_sb[:, j * 128:(j + 1) * 128],
                                          ccolv[:, :], start=True, stop=True,
                                          skip_group_check=True),
                  waits=[b.mark("ve")] if j == 0 else (), sem="pe")
        ec_tok = b.op("vector", lambda e, j=j: e.tensor_copy(
            ecol[:, j:j + 1], sm_ps[:, 11 + j:12 + j]), waits=[ej])

    hbanks = Banks(b, "hbk", 0, handles=pb[0:4])

    def lnx_rhs(k, half):
        return lnx_fm[:, k * TPC + half * HALF: k * TPC + half * HALF + HALF]

    def h_cb(oi, half, bank, ptok):
        vt = b.op("vector", lambda e, o=h_fm[:, oi * TPC + half * HALF:
                                             oi * TPC + half * HALF + HALF],
                  i=bank[:, :], j=oi: e.tensor_scalar(
                      o, i, ecol[:, j:j + 1], None, ALU.mult, ALU.bypass),
                  waits=[ptok, ec_tok])
        return vt

    proj_fm(b, hbanks, wdown_d, lnx_rhs, 4, h_cb, w_sb, obs=2)
    h_done = b.mark("ve")
    b.wait("tensor", [h_done])

    def h_rhs(k, half):
        return h_fm[:, k * TPC + half * HALF: k * TPC + half * HALF + HALF]

    obanks = Banks(b, "obk", 0, handles=pb[0:4])
    b.wait("gpsimd", [b.mark("pe")])
    scr_done = proj_to_scr(b, obanks, worg_d, lnx_rhs, NKC, scr_d, w_sb,
                           wstage, obs=2, extra_k=(4, wup_d, h_rhs))
    ttb = Banks(b, "ttb", 0, handles=tbp)
    resid_tile_loop(b, scr_done, scr_d, x1_d, x2_d, ident, ttb, scr_tile,
                    x_stage[:, 0:D])

    b.wait("sync", [b.mark(s) for s in list(b.cnt) if s.startswith("x1s")])
    b.finalize()
    return nc, ctx


L4W = TPC + 128
NPC = 81


def build_l4a():
    nc = bass.Bass("TRN2", target_bir_lowering=False, debug=False)
    ctx = ExitStack()
    b = Bld(nc, ctx)
    x2_d = nc.dram_tensor("x2", [TPC, D], F32, kind="ExternalInput").ap()
    halo_d = nc.dram_tensor("halo", [3, D], F32, kind="ExternalInput").ap()
    w_d = nc.dram_tensor("w_inproj_p", [D, NPC * 128], F32, kind="ExternalInput").ap()
    ident_d = nc.dram_tensor("ident", [128, 128], BF16, kind="ExternalInput").ap()
    p_d = nc.dram_tensor("pout", [NPC * 128, L4W], BF16, kind="ExternalOutput").ap()

    lnx_fm = b.sb("lnx_fm", [128, NKC * L4W], BF16)
    x_stage = b.sb("x_stage", [128, 2 * D], F32)
    ln_stage = b.sb("ln_stage", [128, 2 * D], BF16)
    stats = b.sb("stats", [128, 48], F32)
    mv = b.sb("mv", [128, 2], F32)
    rstd = b.sb("rstd", [128, 1], F32)
    ident = b.sb("ident_sb", [128, 128], BF16)
    w_sb = b.sb("w_sb", [128, 2 * 2 * 128 * NKC], BF16)
    wstage = b.sb("wstage", [128, 4 * 384], BF16)

    pb = [b.psum(f"pb{i}", [128, 512], F32) for i in range(4)]
    tbp = [b.psum(f"tbp{i}", [128, 512], BF16) for i in range(2)]

    b.dma("sync", ident[:, :], ident_d[:, :], "cld")
    consts = b.mark("cld")
    b.wait("tensor", [consts])
    b.wait("vector", [consts])
    b.wait("scalar", [consts])

    tbanks = Banks(b, "tbk", 0, handles=tbp)
    z0 = b.op("vector", lambda e: e.memset(x_stage[:, 0:D], 0.0))
    hl = b.dma("sync", x_stage[125:128, 0:D], halo_d[:, :], "hld", waits=[z0])
    htok = ln_tile(b, x_stage[:, 0:D], ln_stage[:, 0:D], stats, mv, rstd, hl)
    for f0 in range(0, NKC, 4):
        bi, bank, war = tbanks.take()
        ptok = None
        for g in range(4):
            fi = f0 + g
            ptok = b.pe(lambda e, s=ln_stage[:, fi * 128:(fi + 1) * 128],
                        o=bank[:, g * 128:(g + 1) * 128]:
                        e.transpose(o, s, ident[:, :]),
                        waits=([war, htok, consts] if g == 0 else ()),
                        sem="pe" if g == 3 else None)
        for g in range(4):
            fi = f0 + g
            vt = b.op("vector", lambda e, o=lnx_fm[:, fi * L4W: fi * L4W + 128],
                      i=bank[:, g * 128:(g + 1) * 128]: e.tensor_copy(o, i),
                      waits=[ptok] if g == 0 else ())
        tbanks.set_free(bi, vt)
    def fmoff(fi, tj):
        return fi * L4W + 128 + tj * 128
    nt = TPC // 128
    last_use = [b.mark("ve"), None]
    for tj in range(nt):
        sl = tj % 2
        xs = x_stage[:, sl * D:(sl + 1) * D]
        lt = ln_stage[:, sl * D:(sl + 1) * D]
        ld = b.dma("sync", xs, x2_d[tj * 128:(tj + 1) * 128, :], f"xld{sl}",
                   waits=[last_use[sl]])
        aptok = ln_tile(b, xs, lt, stats, mv, rstd, ld)
        for f0 in range(0, NKC, 4):
            bi, bank, war = tbanks.take()
            ptok = None
            for g in range(4):
                fi = f0 + g
                ptok = b.pe(lambda e, s=lt[:, fi * 128:(fi + 1) * 128],
                            o=bank[:, g * 128:(g + 1) * 128]:
                            e.transpose(o, s, ident[:, :]),
                            waits=([war, aptok] if g == 0 else ()),
                            sem="pe" if g == 3 else None)
            for g in range(4):
                fi = f0 + g
                vt = b.op("vector", lambda e, o=lnx_fm[:, fmoff(fi, tj):
                                                       fmoff(fi, tj) + 128],
                          i=bank[:, g * 128:(g + 1) * 128]: e.tensor_copy(o, i),
                          waits=[ptok] if g == 0 else ())
            tbanks.set_free(bi, vt)
        last_use[sl] = b.mark("ve")
    b.wait("tensor", [b.mark("ve")])

    banks = Banks(b, "bk", 0, handles=pb)

    def rhs_view(k, third):
        return lnx_fm[:, k * L4W + third * 384: k * L4W + (third + 1) * 384]

    proj_to_scr(b, banks, w_d, rhs_view, NPC, p_d, w_sb, wstage, obs=2,
                halves=(0, 1, 2), hw=384)

    b.wait("sync", [b.mark(s) for s in list(b.cnt) if s.startswith("wo")])
    b.finalize()
    return nc, ctx


def build_l4b():
    nc = bass.Bass("TRN2", target_bir_lowering=False, debug=False)
    ctx = ExitStack()
    b = Bld(nc, ctx)
    xbc_d = nc.dram_tensor("xbc_fm", [CONV_DIM, L4W], BF16, kind="ExternalInput").ap()
    dt_d = nc.dram_tensor("dt_fm", [64, L4W], BF16, kind="ExternalInput").ap()
    cw_d = nc.dram_tensor("conv_w_c", [128, 192], F32, kind="ExternalInput").ap()
    cb_d = nc.dram_tensor("conv_b_c", [128, 48], F32, kind="ExternalInput").ap()
    dtb_d = nc.dram_tensor("dt_bias_c", [64, 1], F32, kind="ExternalInput").ap()
    al_d = nc.dram_tensor("a_log_c", [64, 1], F32, kind="ExternalInput").ap()
    dr_d = nc.dram_tensor("d_row", [1, 4096], F32, kind="ExternalInput").ap()
    rep_d = nc.dram_tensor("rep", [64, 4096], BF16, kind="ExternalInput").ap()
    bm_d = nc.dram_tensor("maskneg128", [128, 128], F32, kind="ExternalInput").ap()
    ident_d = nc.dram_tensor("ident", [128, 128], BF16, kind="ExternalInput").ap()
    identf_d = nc.dram_tensor("ident_f", [128, 128], F32, kind="ExternalInput").ap()
    onesr_d = nc.dram_tensor("ones_row_f", [1, 128], F32, kind="ExternalInput").ap()
    onesb_d = nc.dram_tensor("ones_row_b", [1, 128], BF16, kind="ExternalInput").ap()
    y_d = nc.dram_tensor("y_tm", [TPC, 4096], BF16, kind="ExternalOutput").ap()
    cfm_d = nc.dram_tensor("c_fm", [1024, TPC], BF16, kind="ExternalOutput").ap()
    ccl_d = nc.dram_tensor("coreclg", [64, TPC], F32, kind="ExternalOutput").ap()
    sc_d = nc.dram_tensor("s_core", [128, 4096], BF16, kind="ExternalOutput").ap()

    x_tm = b.sb("x_tm", [128, 8 * 4096], BF16)
    bc_sb = b.sb("bc_sb", [128, 16 * 1024], BF16)
    b_tm = b.sb("b_tm", [128, 16 * 128], BF16)
    dtx_sb = b.sb("dtx_sb", [128, 2 * 4096], BF16)
    r_sb = b.sb("r_sb", [128, 4096], BF16)
    lam_sb = b.sb("lam_sb", [128, 4096], BF16)
    drep_sb = b.sb("drep_sb", [128, 4096], BF16)
    cum_sb = b.sb("cum_sb", [128, 512], BF16)
    xstg = b.sb("xstg", [128, 2 * L4W], BF16)
    conv_sb = b.sb("conv_sb", [128, 1024], F32)
    cw_sb = b.sb("cw_sb", [128, 192], F32)
    cbb_sb = b.sb("cbb_sb", [128, 48], F32)
    dt_sb = b.sb("dt_sb", [64, L4W], BF16)
    u_sb = b.sb("u_sb", [64, 1024], F32)

    dtp_sb = b.sb("dtp_sb", [64, 1024], F32)
    dtpb_sb = b.sb("dtpb_sb", [64, 1024], BF16)
    logda = b.sb("logda", [64, 1024], F32)
    u2_sb = logda
    clg_sb = b.sb("clg_sb", [64, 256], F32)
    clgb_sb = b.sb("clgb_sb", [64, 256], BF16)
    ccl_sb = b.sb("ccl_sb", [64, 256], F32)
    off_sb = b.sb("off_sb", [64, 2], F32)
    es_sb = b.sb("es_sb", [64, 256], F32)
    esb_sb = b.sb("esb_sb", [64, 256], BF16)
    clgq_sb = b.sb("clgq_sb", [64, 2], F32)
    clgqb_sb = b.sb("clgqb_sb", [64, 1], BF16)
    clgtm_sb = b.sb("clgtm_sb", [128, 2 * 64], F32)
    nclgtm_sb = b.sb("nclgtm_sb", [128, 2 * 64], F32)
    rl_sb = b.sb("rl_sb", [1, 512], F32)
    trow_sb = b.sb("trow_sb", [1, 128], F32)
    dtt_sb = b.sb("dtt_sb", [128, 128], F32)
    dec_sb = b.sb("dec_sb", [128, 128], BF16)
    g_sb = b.sb("g_sb", [128, 128], BF16)
    corr_sb = b.sb("corr_sb", [128, 512], BF16)
    ya_sb = b.sb("ya_sb", [128, 512], BF16)
    yb_sb = b.sb("yb_sb", [128, 512], BF16)
    t1_sb = b.sb("t1_sb", [128, 512], BF16)
    bias_a = b.sb("bias_a", [64, 2], F32)
    binm_sb = b.sb("maskneg_sb", [128, 128], F32)
    ident = b.sb("ident_sb", [128, 128], BF16)
    identf = b.sb("identf_sb", [128, 128], F32)
    onesr = b.sb("onesr_sb", [1, 128], F32)
    drow_sb = b.sb("drow_sb", [1, 4096], BF16)
    onesb = b.sb("onesb_sb", [1, 128], BF16)
    rep_sb = b.sb("rep_sb", [64, 4096], BF16)

    psY = [b.psum(f"psY{i}", [128, 512], F32) for i in range(2)]
    psC = b.psum("psC", [128, 512], F32)
    psS = b.psum("psS", [128, 512], F32)
    psM = [b.psum(f"psM{i}", [128, 512], F32) for i in range(2)]
    psT = [b.psum(f"psT{i}", [128, 512], BF16) for i in range(2)]

    for ap_, src_ in ((cw_sb, cw_d), (cbb_sb, cb_d), (dtb_d, None)):
        pass
    b.dma("sync", cw_sb[:, :], cw_d[:, :], "cld")
    b.dma("sync", cbb_sb[:, :], cb_d[:, :], "cld")
    b.dma("sync", bias_a[:, 0:1], dtb_d[:, :], "cld")
    b.dma("sync", bias_a[:, 1:2], al_d[:, :], "cld")
    b.dma("gpsimd", drow_sb[:, :], dr_d[:, :], "cldg")
    b.dma("sync", onesb[:, :], onesb_d[:, :], "cld")
    b.dma("sync", rep_sb[:, :], rep_d[:, :], "cld")
    b.dma("sync", binm_sb[:, :], bm_d[:, :], "cld")
    b.dma("sync", ident[:, :], ident_d[:, :], "cld")
    b.dma("sync", identf[:, :], identf_d[:, :], "cld")
    b.dma("sync", onesr[:, :], onesr_d[:, :], "cld")
    b.dma("sync", dt_sb[:, :], dt_d[:, :], "cld")
    consts = b.mark("cld")
    constsg = b.mark("cldg")
    b.wait("tensor", [consts, constsg])
    b.wait("vector", [consts, constsg])
    b.wait("scalar", [consts, constsg])

    tb = Banks(b, "tb", 0, handles=psT)

    dtok = None
    for rg in range(8):
        pm = b.pe(lambda e, rg=rg: e.matmul(psM[0][:, :], onesb[:, :],
                                            drow_sb[:, rg * 512:(rg + 1) * 512],
                                            start=True, stop=True),
                  waits=[b.mark("ve")] if rg else [consts], sem="pe")
        dtok = b.op("vector", lambda e, rg=rg: e.tensor_copy(
            drep_sb[:, rg * 512:(rg + 1) * 512], psM[0][:, :]), waits=[pm])

    stg_free = [None, None]
    for ch in range(48):
        sl = ch % 2
        stg = xstg[:, sl * L4W:(sl + 1) * L4W]
        ld = b.dma("sync", stg, xbc_d[ch * 128:(ch + 1) * 128, :], f"xb{sl}",
                   waits=[stg_free[sl]])
        cv = b.op("vector", lambda e, stg=stg, ch=ch: e.tensor_scalar(
            conv_sb[:, :], stg[:, 125:1149], cw_sb[:, ch * 4:ch * 4 + 1],
            cbb_sb[:, ch:ch + 1], ALU.mult, ALU.add),
            waits=[ld, b.mark("se")])
        for tap in range(1, 4):
            cv = b.op("vector", lambda e, stg=stg, ch=ch, tap=tap:
                      e.scalar_tensor_tensor(
                          conv_sb[:, :], stg[:, 125 + tap:1149 + tap],
                          cw_sb[:, ch * 4 + tap:ch * 4 + tap + 1],
                          conv_sb[:, :], ALU.mult, ALU.add))
        stg_free[sl] = cv
        if ch < 32:
            sx = b.op("scalar", lambda e: e.activation(
                xstg[:, sl * L4W:sl * L4W + 1024] if False else conv_sb[:, :],
                conv_sb[:, :], AF.Silu), waits=[b.mark("ve")])
            cbf = b.op("vector", lambda e: e.tensor_copy(
                dtx_sb[:, 0:1024], conv_sb[:, :]), waits=[b.mark("se")])
            for tj0 in range(0, 8, 4):
                bi, bank, war = tb.take()
                ptk = None
                for g2 in range(4):
                    tj = tj0 + g2
                    ptk = b.pe(lambda e, s=dtx_sb[:, tj * 128:(tj + 1) * 128],
                               o=bank[:, g2 * 128:(g2 + 1) * 128]:
                               e.transpose(o, s, ident[:, :]),
                               waits=([war, cbf] if g2 == 0 else ()),
                               sem="pe" if g2 == 3 else None)
                for g2 in range(4):
                    tj = tj0 + g2
                    vt = b.op("vector", lambda e, o=x_tm[:, (tj0 + g2) * 4096 + ch * 128:
                                                         (tj0 + g2) * 4096 + (ch + 1) * 128],
                              i=bank[:, g2 * 128:(g2 + 1) * 128]:
                              e.tensor_copy(o, i),
                              waits=[ptk] if g2 == 0 else ())
                tb.set_free(bi, vt)
        else:
            bci = ch - 32
            sx = b.op("scalar", lambda e, bci=bci: e.activation(
                bc_sb[:, bci * 1024:(bci + 1) * 1024], conv_sb[:, :], AF.Silu),
                waits=[b.mark("ve")])
            stg_free[sl] = sx
            if ch >= 40:
                b.dma("sync", cfm_d[(ch - 40) * 128:(ch - 39) * 128, :],
                      bc_sb[:, bci * 1024:(bci + 1) * 1024], "cso",
                      waits=[sx])
    conv_done_v = b.mark("ve")
    conv_done_s = b.mark("se")

    b.op("vector", lambda e: e.tensor_scalar(
        u_sb[:, :], dt_sb[:, 128:1152], bias_a[:, 0:1], None, ALU.add,
        ALU.bypass), waits=[consts])
    b.op("scalar", lambda e: e.activation(u2_sb[:, :], u_sb[:, :], AF.Abs),
         waits=[b.mark("ve")])
    b.op("scalar", lambda e: e.activation(u2_sb[:, :], u2_sb[:, :], AF.Exp,
                                          scale=-1.0))
    one_ap, one_tok = b.constcol(1.0)
    b.op("scalar", lambda e: e.activation(u2_sb[:, :], u2_sb[:, :], AF.Ln,
                                          bias=one_ap[0:64, :]),
         waits=[one_tok])
    b.op("vector", lambda e: e.tensor_scalar(
        dtp_sb[:, :], u_sb[:, :], 0.0, None, ALU.max, ALU.bypass))
    b.op("vector", lambda e: e.tensor_tensor(
        dtp_sb[:, :], dtp_sb[:, :], u2_sb[:, :], ALU.add),
        waits=[b.mark("se")])
    b.op("vector", lambda e: e.tensor_copy(dtpb_sb[:, :], dtp_sb[:, :]))
    b.op("scalar", lambda e: e.activation(bias_a[:, 1:2] if False else
                                          off_sb[:, 1:2], bias_a[:, 1:2],
                                          AF.Exp), waits=[b.mark("ve")])
    b.op("vector", lambda e: e.tensor_scalar(
        off_sb[:, 1:2], off_sb[:, 1:2], -1.0, None, ALU.mult, ALU.bypass),
        waits=[b.mark("se")])
    b.op("vector", lambda e: e.tensor_scalar(
        logda[:, :], dtp_sb[:, :], off_sb[:, 1:2], None, ALU.mult, ALU.bypass))
    z_off = b.op("vector", lambda e: e.memset(off_sb[:, 0:1], 0.0))

    r_z = b.op("vector", lambda e: e.memset(r_sb[:, :], 0.0))
    last_yd = [None]
    for sc in range(4):
        b.wait("vector", [b.mark("pe")])
        b.op("vector", lambda e, sc=sc: e.tensor_tensor_scan(
            clg_sb[:, :], logda[:, sc * 256:(sc + 1) * 256],
            logda[:, sc * 256:(sc + 1) * 256], 0.0, ALU.add, ALU.bypass))
        b.op("vector", lambda e: e.tensor_scalar(
            ccl_sb[:, :], clg_sb[:, :], off_sb[:, 0:1], None, ALU.add,
            ALU.bypass))
        ccd = b.dma("sync", ccl_d[:, sc * 256:(sc + 1) * 256], ccl_sb[:, :],
                    f"ccs{sc % 2}", waits=[b.mark("ve")])
        b.op("vector", lambda e: e.tensor_copy(off_sb[:, 0:1],
                                               ccl_sb[:, 255:256]),
             waits=[ccd])
        b.op("vector", lambda e: e.tensor_copy(clgq_sb[:, 0:1],
                                               clg_sb[:, 255:256]))
        b.op("vector", lambda e: e.tensor_copy(clgqb_sb[:, :],
                                               clgq_sb[:, 0:1]))
        b.op("vector", lambda e: e.tensor_copy(clgb_sb[:, :], clg_sb[:, :]))
        b.op("scalar", lambda e: e.activation(es_sb[:, :], clg_sb[:, :],
                                              AF.Exp, bias=clgq_sb[:, 0:1],
                                              scale=-1.0),
             waits=[b.mark("ve")])
        b.op("vector", lambda e: e.tensor_copy(esb_sb[:, :], es_sb[:, :]),
             waits=[b.mark("se")])
        for tc in range(2):
            pt = b.pe(lambda e, tc=tc: e.transpose(
                psM[0][:, 0:64], clg_sb[:, tc * 128:(tc + 1) * 128],
                identf[0:64, 0:64]), waits=[b.mark("ve")], sem="pe")
            b.op("vector", lambda e, tc=tc: e.tensor_copy(
                clgtm_sb[:, tc * 64:(tc + 1) * 64], psM[0][:, 0:64]),
                waits=[pt])
            b.op("vector", lambda e, tc=tc: e.tensor_scalar(
                nclgtm_sb[:, tc * 64:(tc + 1) * 64], psM[0][:, 0:64], -1.0,
                None, ALU.mult, ALU.bypass))
        for g in range(8):
            bi, bank, war = tb.take()
            ptk = None
            for a in range(2):
                ptk = b.pe(lambda e, o=bank[:, a * 128:(a + 1) * 128],
                           s=bc_sb[:, g * 1024 + sc * 256 + a * 128:
                                   g * 1024 + sc * 256 + (a + 1) * 128]:
                           e.transpose(o, s, ident[:, :]),
                    waits=([war, conv_done_s, conv_done_v] if a == 0 else ()),
                    sem="pe" if a == 1 else None)
            vt = b.op("vector", lambda e, o=b_tm[:, g * 256:(g + 1) * 256],
                      i=bank[:, 0:256]: e.tensor_copy(o, i),
                      waits=[ptk, b.mark("pe")])
            tb.set_free(bi, vt)
        for rg in range(8):
            pm = b.pe(lambda e, rg=rg: e.matmul(
                psM[1][0:1, 0:512], clgqb_sb[:, :],
                rep_sb[:, rg * 512:(rg + 1) * 512], start=True, stop=True,
                skip_group_check=True),
                waits=[b.mark("ve"), b.mark("se")], sem="pe")
            b.op("vector", lambda e: e.tensor_copy(
                rl_sb[:, :], psM[1][0:1, 0:512]), waits=[pm])
            pm2_ = b.pe(lambda e: e.matmul(
                psM[1][:, :], onesr[:, :], rl_sb[:, :],
                start=True, stop=True, skip_group_check=True),
                waits=[b.mark("ve"), b.mark("se")], sem="pe")
            b.op("scalar", lambda e, rg=rg: e.activation(
                lam_sb[:, rg * 512:(rg + 1) * 512], psM[1][:, :], AF.Exp),
                waits=[pm2_, b.mark("ve")])
        lam_tok = b.mark("se")
        for a in range(2):
            for g in range(8):
                pm = b.pe(lambda e, a=a, g=g, sc=sc: e.matmul(
                    psM[0][:, :], dtpb_sb[:, sc * 256 + a * 128:
                                          sc * 256 + (a + 1) * 128],
                    rep_sb[:, g * 512:(g + 1) * 512], start=True, stop=True,
                    skip_group_check=True),
                    waits=[b.mark("ve"), b.mark("se")], sem="pe")
                b.op("vector", lambda e, a=a, g=g, sc=sc: e.tensor_tensor(
                    dtx_sb[:, a * 4096 + g * 512:a * 4096 + (g + 1) * 512],
                    x_tm[:, (sc * 2 + a) * 4096 + g * 512:
                         (sc * 2 + a) * 4096 + (g + 1) * 512],
                    psM[0][:, :], ALU.mult), waits=[pm])
        dtx_tok = b.mark("ve")
        for tc in range(2):
            for g in range(8):
                yb = psY[tc % 2]
                pc = b.pe(lambda e, g=g, tc=tc, sc=sc: e.matmul(
                    psC[:, :],
                    bc_sb[:, (8 + g) * 1024 + sc * 256 + tc * 128:
                          (8 + g) * 1024 + sc * 256 + (tc + 1) * 128],
                    r_sb[:, g * 512:(g + 1) * 512], start=True, stop=True,
                    skip_group_check=True),
                    waits=[b.mark("ve"), conv_done_s], sem="pe")
                cevac = b.op("vector", lambda e: e.tensor_copy(
                    corr_sb[:, :], psC[:, :]), waits=[pc])
                pcm = b.pe(lambda e, g=g, tc=tc: e.matmul(
                    psM[0][:, :], clgb_sb[:, tc * 128:(tc + 1) * 128],
                    rep_sb[:, g * 512:(g + 1) * 512], start=True, stop=True,
                    skip_group_check=True),
                    waits=[b.mark("ve"), b.mark("se")], sem="pe")
                cume = b.op("scalar", lambda e: e.activation(
                    cum_sb[:, 0:512], psM[0][:, :], AF.Exp),
                    waits=[pcm, b.mark("ve")])
                ystarted = False
                ylast = None
                for hl in range(8):
                    h = g * 8 + hl
                    ptr = b.pe(lambda e, h=h, tc=tc: e.transpose(
                        psM[1][0:1, 0:128],
                        clgtm_sb[:, tc * 64 + h:tc * 64 + h + 1],
                        identf[:, :]), waits=[b.mark("ve"), b.mark("se")],
                        sem="pe")
                    b.op("vector", lambda e: e.tensor_copy(
                        trow_sb[:, :], psM[1][0:1, 0:128]), waits=[ptr])
                    pt2 = b.pe(lambda e: e.matmul(
                        psM[1][:, 128:256], onesr[:, :], trow_sb[:, :],
                        start=True, stop=True, skip_group_check=True),
                        waits=[b.mark("ve")], sem="pe")
                    for a in range(tc + 1):
                        pcb = b.pe(lambda e, g=g, a=a, tc=tc, sc=sc: e.matmul(
                            psM[0][:, 0:128],
                            bc_sb[:, g * 1024 + sc * 256 + a * 128:
                                  g * 1024 + sc * 256 + (a + 1) * 128],
                            bc_sb[:, (8 + g) * 1024 + sc * 256 + tc * 128:
                                  (8 + g) * 1024 + sc * 256 + (tc + 1) * 128],
                            start=True, stop=True, skip_group_check=True),
                            waits=[b.mark("se"), b.mark("ve")], sem="pe")
                        dtt = b.op("vector", lambda e, a=a, tc=tc, h=h:
                                   e.tensor_scalar(
                                       dtt_sb[:, :], psM[1][:, 128:256],
                                       nclgtm_sb[:, a * 64 + h:a * 64 + h + 1],
                                       None, ALU.add, ALU.bypass),
                                   waits=[pt2, pcb, b.mark("se")])
                        if a == tc:
                            dtt = b.op("vector", lambda e: e.tensor_tensor(
                                dtt_sb[:, :], dtt_sb[:, :], binm_sb[:, :],
                                ALU.add))
                        dce = b.op("scalar", lambda e: e.activation(
                            dec_sb[:, :], dtt_sb[:, :], AF.Exp),
                            waits=[b.mark("ve")])
                        gm = b.op("vector", lambda e: e.tensor_tensor(
                            g_sb[:, :], psM[0][:, 0:128], dec_sb[:, :],
                            ALU.mult), waits=[dce, b.mark("pe")])
                        ym = b.pe(lambda e, yb=yb, hl=hl, a=a, h=h,
                                  s=(not ystarted),
                                  st=(hl == 7 and a == tc): e.matmul(
                                      yb[:, hl * 64:(hl + 1) * 64], g_sb[:, :],
                                      dtx_sb[:, a * 4096 + h * 64:
                                             a * 4096 + (h + 1) * 64],
                                      start=s, stop=st,
                                      skip_group_check=True),
                                  waits=([gm, dtx_tok, b.mark("ve")]
                                         if not ystarted else [gm, dtx_tok]),
                                  sem="pe")
                        ystarted = True
                        ylast = ym
                a1 = b.op("vector", lambda e: e.tensor_tensor(
                    t1_sb[:, :], corr_sb[:, :], cum_sb[:, 0:512], ALU.mult),
                    waits=[cume, cevac])
                a2 = b.op("vector", lambda e, yb=yb: e.tensor_tensor(
                    ya_sb[:, :], t1_sb[:, :], yb[:, :], ALU.add),
                    waits=[ylast])
                a3 = b.op("vector", lambda e, sc=sc, tc=tc, g=g:
                          e.tensor_tensor(
                              t1_sb[:, :],
                              x_tm[:, (sc * 2 + tc) * 4096 + g * 512:
                                   (sc * 2 + tc) * 4096 + (g + 1) * 512],
                              drep_sb[:, g * 512:(g + 1) * 512], ALU.mult))
                a4 = b.op("vector", lambda e: e.tensor_tensor(
                    yb_sb[:, :], ya_sb[:, :], t1_sb[:, :], ALU.add),
                    waits=[last_yd[0]])
                yd = b.dma("sync",
                           y_d[(sc * 2 + tc) * 128:(sc * 2 + tc + 1) * 128,
                               g * 512:(g + 1) * 512],
                           yb_sb[:, :], f"ys{g % 4}", waits=[a4])
                last_yd[0] = yd
        for g in range(8):
            ps1 = None
            for a in range(2):
                pm2 = b.pe(lambda e, a=a, g=g: e.matmul(
                    psM[1][:, :], esb_sb[:, a * 128:(a + 1) * 128],
                    rep_sb[:, g * 512:(g + 1) * 512], start=True, stop=True,
                    skip_group_check=True),
                    waits=[b.mark("ve"), b.mark("se")], sem="pe")
                dxp = b.op("vector", lambda e, a=a, g=g: e.tensor_tensor(
                    t1_sb[:, :],
                    dtx_sb[:, a * 4096 + g * 512:a * 4096 + (g + 1) * 512],
                    psM[1][:, :], ALU.mult), waits=[pm2])
                ps1 = b.pe(lambda e, g=g, a=a, s=(a == 0): e.matmul(
                    psS[:, :], b_tm[:, g * 256 + a * 128:g * 256 + (a + 1) * 128],
                    t1_sb[:, :],
                    start=s, stop=(a == 1), skip_group_check=True),
                    waits=[dxp], sem="pe")
            b.op("vector", lambda e, g=g: e.tensor_tensor(
                ya_sb[:, :], lam_sb[:, g * 512:(g + 1) * 512],
                r_sb[:, g * 512:(g + 1) * 512], ALU.mult),
                waits=[lam_tok, ps1])
            b.op("vector", lambda e, g=g: e.tensor_tensor(
                r_sb[:, g * 512:(g + 1) * 512], ya_sb[:, :], psS[:, :],
                ALU.add))
    std = b.dma("sync", sc_d[:, :], r_sb[:, :], "sts", waits=[b.mark("ve")])

    b.wait("sync", [b.mark(s) for s in list(b.cnt)
                    if s.startswith(("ys", "cso", "ccs", "sts"))])
    b.finalize()
    return nc, ctx


def build_l5():
    nc = bass.Bass("TRN2", target_bir_lowering=False, debug=False)
    ctx = ExitStack()
    b = Bld(nc, ctx)
    y_di = nc.dram_tensor("y_tm", [TPC, 4096], BF16, kind="ExternalInput").ap()
    c_di = nc.dram_tensor("c_fm", [1024, TPC], BF16, kind="ExternalInput").ap()
    ccl_di = nc.dram_tensor("coreclg", [64, TPC], F32, kind="ExternalInput").ap()
    sp_di = nc.dram_tensor("s_prev", [3 * 128, 4096], BF16, kind="ExternalInput").ap()
    ll_di = nc.dram_tensor("lamlog_prev", [64, 3], F32, kind="ExternalInput").ap()
    gate_di = nc.dram_tensor("gate_fm", [4096, L4W], BF16, kind="ExternalInput").ap()
    x2_di = nc.dram_tensor("x2", [TPC, D], F32, kind="ExternalInput").ap()
    wop_di = nc.dram_tensor("w_outproj_t", [MI, D], F32, kind="ExternalInput").ap()
    nw_di = nc.dram_tensor("norm_row", [1, 4096], F32, kind="ExternalInput").ap()
    rep_di = nc.dram_tensor("rep", [64, 4096], BF16, kind="ExternalInput").ap()
    ident_d = nc.dram_tensor("ident", [128, 128], BF16, kind="ExternalInput").ap()
    onesr_d = nc.dram_tensor("ones_row_f", [1, 128], F32, kind="ExternalInput").ap()
    x3_d = nc.dram_tensor("x3", [TPC, D], F32, kind="ExternalOutput").ap()
    scr_d = nc.dram_tensor("op_scr", [D, TPC], BF16)

    yn_fm = b.sb("yn_fm", [128, NKC * TPC], BF16)
    r5_sb = b.sb("r5_sb", [128, 4096], BF16)
    lam_sb = b.sb("lam_sb", [128, 4096], BF16)
    llb_sb = b.sb("llb_sb", [64, 3], BF16)
    rl_sb = b.sb("rl_sb", [1, 512], F32)
    clgb_sb = b.sb("clgb_sb", [64, TPC], BF16)
    cum_sb = b.sb("cum_sb", [128, 512], BF16)
    corr_sb = b.sb("corr_sb", [128, 512], BF16)
    y_sb = b.sb("y_sb", [128, 4096], BF16)
    gate_tm = b.sb("gate_tm", [128, 4096], BF16)
    gstage = b.sb("gstage", [128, 4096], BF16)
    cstage = b.sb("cstage", [128, 8 * 128], BF16)
    normrep = b.sb("normrep", [128, 4096], BF16)
    ss_sb = b.sb("ss_sb", [128, 4], F32)
    junk_sb = b.sb("junk_sb", [128, 512], BF16)
    ident = b.sb("ident_sb", [128, 128], BF16)
    onesr = b.sb("onesr_sb", [1, 128], F32)
    nwrow = b.sb("nwrow", [1, 4096], F32)
    rep_sb = b.sb("rep_sb", [64, 4096], BF16)
    w_sb = b.sb("w_sb", [128, 2 * 2 * 128 * NKC], BF16)
    wstage = b.sb("wstage", [128, 4 * HALF], BF16)
    scr_tile = b.sb("scr_tile", [128, D], BF16)
    sprev_sb = scr_tile
    x_stage = b.sb("x_stage", [128, D], F32)

    pb = [b.psum(f"pb{i}", [128, 512], F32) for i in range(4)]
    tbp = [b.psum(f"tbp{i}", [128, 512], BF16) for i in range(2)]

    b.dma("sync", ident[:, :], ident_d[:, :], "cld")
    b.dma("sync", onesr[:, :], onesr_d[:, :], "cld")
    b.dma("gpsimd", llb_sb[:, :], ll_di[:, :], "cldg")
    b.dma("gpsimd", clgb_sb[:, :], ccl_di[:, :], "cldg")
    b.dma("sync", rep_sb[:, :], rep_di[:, :], "cld")
    b.dma("sync", nwrow[:, :], nw_di[:, :], "cld")
    consts = b.mark("cld")
    constsg = b.mark("cldg")
    b.wait("tensor", [consts, constsg])
    b.wait("vector", [consts, constsg])
    b.wait("scalar", [consts, constsg])

    for rg in range(8):
        pm = b.pe(lambda e, rg=rg: e.matmul(
            pb[1][:, :], onesr[:, :], nwrow[:, rg * 512:(rg + 1) * 512],
            start=True, stop=True, skip_group_check=True),
            waits=[b.mark("ve")], sem="pe")
        b.op("vector", lambda e, rg=rg: e.tensor_copy(
            normrep[:, rg * 512:(rg + 1) * 512], pb[1][:, :]), waits=[pm])

    ld3 = b.dma("sync", r5_sb[:, :], sp_di[256:384, :], "sl3")
    for j in (1, 0):
        lds = b.dma("sync", sprev_sb[:, :], sp_di[j * 128:(j + 1) * 128, :],
                    f"sl{j}", waits=[b.mark("ve")])
        for rg in range(8):
            pm = b.pe(lambda e, j=j, rg=rg: e.matmul(
                pb[1][0:1, 0:512], llb_sb[:, j:j + 1],
                rep_sb[:, rg * 512:(rg + 1) * 512], start=True, stop=True,
                skip_group_check=True),
                waits=[b.mark("ve"), b.mark("se")], sem="pe")
            b.op("vector", lambda e: e.tensor_copy(rl_sb[:, :],
                                                   pb[1][0:1, 0:512]),
                 waits=[pm])
            pm2 = b.pe(lambda e: e.matmul(
                pb[1][:, :], onesr[:, :], rl_sb[:, :], start=True, stop=True,
                skip_group_check=True), waits=[b.mark("ve"), b.mark("se")],
                sem="pe")
            b.op("scalar", lambda e, rg=rg: e.activation(
                lam_sb[:, rg * 512:(rg + 1) * 512], pb[1][:, :], AF.Exp),
                waits=[pm2, b.mark("ve")])
        for rg in range(8):
            b.op("vector", lambda e, rg=rg: e.tensor_tensor(
                r5_sb[:, rg * 512:(rg + 1) * 512],
                lam_sb[:, rg * 512:(rg + 1) * 512],
                r5_sb[:, rg * 512:(rg + 1) * 512], ALU.mult),
                waits=[b.mark("se"), ld3] if rg == 0 else ())
        for rg in range(8):
            b.op("vector", lambda e, rg=rg: e.tensor_tensor(
                r5_sb[:, rg * 512:(rg + 1) * 512],
                r5_sb[:, rg * 512:(rg + 1) * 512],
                sprev_sb[:, rg * 512:(rg + 1) * 512], ALU.add),
                waits=[lds] if rg == 0 else ())
    rin_tok = b.mark("ve")
    b.wait("tensor", [rin_tok])

    eps_ap, eps_tok = b.constcol(MEPS)
    tb5 = Banks(b, "tb5", 0, handles=tbp)
    yld_prev = None
    gld_prev = None
    cld_prev = None
    for tt in range(8):
        yld = b.dma("sync", y_sb[:, :], y_di[tt * 128:(tt + 1) * 128, :],
                    "yld", waits=[yld_prev])
        gld = b.dma("sync", gstage[:, :].rearrange("p (f t) -> p f t", t=128),
                    gate_di[:, 128 + tt * 128:128 + (tt + 1) * 128].rearrange(
                        "(f p) t -> p f t", p=128), "gld",
                    waits=[gld_prev, b.mark("pe")])
        cld = b.dma("sync", cstage[:, :].rearrange("p (g t) -> p g t", t=128),
                    c_di[:, tt * 128:(tt + 1) * 128].rearrange(
                        "(g p) t -> p g t", p=128), "clD",
                    waits=[cld_prev, b.mark("pe")])
        for f0 in range(0, NKC, 4):
            bi, bank, war = tb5.take()
            ptk = None
            for g2 in range(4):
                fi = f0 + g2
                ptk = b.pe(lambda e, o=bank[:, g2 * 128:(g2 + 1) * 128],
                           s=gstage[:, fi * 128:(fi + 1) * 128]:
                           e.transpose(o, s, ident[:, :]),
                           waits=([war, gld] if g2 == 0 else ()),
                           sem="pe" if g2 == 3 else None)
            sv = b.op("scalar", lambda e, f0=f0, i=bank[:, :]: e.activation(
                gate_tm[:, f0 * 128:(f0 + 4) * 128], i, AF.Silu),
                waits=[ptk])
            tb5.set_free(bi, sv)
        gate_done = b.mark("se")
        for g in range(8):
            pc = b.pe(lambda e, g=g: e.matmul(
                pb[0][:, :], cstage[:, g * 128:(g + 1) * 128],
                r5_sb[:, g * 512:(g + 1) * 512], start=True, stop=True,
                skip_group_check=True),
                waits=[cld, b.mark("ve"), b.mark("se")], sem="pe")
            cevac = b.op("vector", lambda e: e.tensor_copy(corr_sb[:, :],
                                                           pb[0][:, :]),
                         waits=[pc])
            pcm = b.pe(lambda e, g=g, tt=tt: e.matmul(
                pb[1][:, :], clgb_sb[:, tt * 128:(tt + 1) * 128],
                rep_sb[:, g * 512:(g + 1) * 512], start=True, stop=True,
                skip_group_check=True),
                waits=[b.mark("ve"), b.mark("se")], sem="pe")
            cume = b.op("scalar", lambda e: e.activation(
                cum_sb[:, :], pb[1][:, :], AF.Exp),
                waits=[pcm, b.mark("ve")])
            b.op("vector", lambda e, g=g: e.scalar_tensor_tensor(
                corr_sb[:, :], corr_sb[:, :], 1.0, cum_sb[:, :], ALU.mult,
                ALU.mult) if False else e.tensor_tensor(
                corr_sb[:, :], corr_sb[:, :], cum_sb[:, :], ALU.mult),
                waits=[cume, cevac, yld])
            yn_tok = b.op("vector", lambda e, g=g: e.tensor_tensor(
                y_sb[:, g * 512:(g + 1) * 512],
                y_sb[:, g * 512:(g + 1) * 512], corr_sb[:, :], ALU.add))
            b.op("vector", lambda e, g=g: e.tensor_tensor(
                junk_sb[:, :], y_sb[:, g * 512:(g + 1) * 512],
                y_sb[:, g * 512:(g + 1) * 512], ALU.mult))
            b.op("vector", lambda e: e.tensor_scalar(
                junk_sb[:, :], junk_sb[:, :], 0.0, 0.0, ALU.add, ALU.add,
                accum_out=ss_sb[:, 0:1]))
            b.op("vector", lambda e: e.tensor_scalar(
                ss_sb[:, 1:2], ss_sb[:, 0:1], 1.0 / 512, None, ALU.mult,
                ALU.bypass))
            b.op("scalar", lambda e: e.activation(
                ss_sb[:, 2:3], ss_sb[:, 1:2], AF.Sqrt, bias=eps_ap),
                waits=[b.mark("ve"), eps_tok])
            b.op("vector", lambda e: e.reciprocal(ss_sb[:, 3:4], ss_sb[:, 2:3]),
                 waits=[b.mark("se")])
            b.op("vector", lambda e, g=g: e.tensor_scalar(
                y_sb[:, g * 512:(g + 1) * 512],
                y_sb[:, g * 512:(g + 1) * 512], ss_sb[:, 3:4], None,
                ALU.mult, ALU.bypass))
            b.op("vector", lambda e, g=g: e.tensor_tensor(
                y_sb[:, g * 512:(g + 1) * 512],
                y_sb[:, g * 512:(g + 1) * 512],
                normrep[:, g * 512:(g + 1) * 512], ALU.mult))
            b.op("vector", lambda e, g=g: e.tensor_tensor(
                y_sb[:, g * 512:(g + 1) * 512],
                y_sb[:, g * 512:(g + 1) * 512],
                gate_tm[:, g * 512:(g + 1) * 512], ALU.mult),
                waits=[gate_done])
        ynt = b.mark("ve")
        for f0 in range(0, NKC, 4):
            bi, bank, war = tb5.take()
            ptk = None
            for g2 in range(4):
                fi = f0 + g2
                ptk = b.pe(lambda e, o=bank[:, g2 * 128:(g2 + 1) * 128],
                           s=y_sb[:, fi * 128:(fi + 1) * 128]:
                           e.transpose(o, s, ident[:, :]),
                           waits=([war, ynt] if g2 == 0 else ()),
                           sem="pe" if g2 == 3 else None)
            for g2 in range(4):
                fi = f0 + g2
                vt = b.op("vector", lambda e,
                          o=yn_fm[:, fi * TPC + tt * 128:fi * TPC + (tt + 1) * 128],
                          i=bank[:, g2 * 128:(g2 + 1) * 128]:
                          e.tensor_copy(o, i), waits=[ptk] if g2 == 0 else ())
            tb5.set_free(bi, vt)
        yld_prev = b.mark("ve")
        gld_prev = b.mark("se")
        cld_prev = b.mark("pe")
    b.wait("tensor", [b.mark("ve")])

    obanks = Banks(b, "ob", 0, handles=pb)

    def yn_rhs(k, half):
        return yn_fm[:, k * TPC + half * HALF: k * TPC + half * HALF + HALF]

    scr_done = proj_to_scr(b, obanks, wop_di, yn_rhs, NKC, scr_d, w_sb,
                           wstage, obs=2)
    ttb = Banks(b, "ttb", 0, handles=tbp)
    resid_tile_loop(b, scr_done, scr_d, x2_di, x3_d, ident, ttb, scr_tile,
                    x_stage)
    b.wait("sync", [b.mark(s) for s in list(b.cnt) if s.startswith("x1s")])
    b.finalize()
    return nc, ctx


def build_l6():
    nc = bass.Bass("TRN2", target_bir_lowering=False, debug=False)
    ctx = ExitStack()
    b = Bld(nc, ctx)
    x3_d = nc.dram_tensor("x3", [TPC, D], F32, kind="ExternalInput").ap()
    wf_d = nc.dram_tensor("w_ffn_t", [D, D], F32, kind="ExternalInput").ap()
    bias_d = nc.dram_tensor("bias_c", [128, NKC], F32, kind="ExternalInput").ap()
    wa_d = nc.dram_tensor("w_a_t", [D, 256], F32, kind="ExternalInput").ap()
    wbf_d = nc.dram_tensor("w_bf", [256, D], F32, kind="ExternalInput").ap()
    cf_d = nc.dram_tensor("coeffs16", [1, 16], F32, kind="ExternalInput").ap()
    sc_d = nc.dram_tensor("scales16", [1, 16], F32, kind="ExternalInput").ap()
    e16_d = nc.dram_tensor("exp16", [16, 256], F32, kind="ExternalInput").ap()
    ident_d = nc.dram_tensor("ident", [128, 128], BF16, kind="ExternalInput").ap()
    identf_d = nc.dram_tensor("ident_f", [128, 128], F32, kind="ExternalInput").ap()
    x4_d = nc.dram_tensor("x4", [TPC, D], F32, kind="ExternalOutput").ap()
    scr_d = nc.dram_tensor("f2_scr", [D, TPC], BF16)

    lnx_fm = b.sb("lnx_fm", [128, NKC * TPC], BF16)
    h2_fm = b.sb("h2_fm", [128, 2 * TPC], BF16)
    x_stage = b.sb("x_stage", [128, 2 * D], F32)
    ln_stage = b.sb("ln_stage", [128, 2 * D], BF16)
    stats = b.sb("stats", [128, 48], F32)
    mv = b.sb("mv", [128, 2], F32)
    rstd = b.sb("rstd", [128, 1], F32)
    ident = b.sb("ident_sb", [128, 128], BF16)
    identf = b.sb("identf_sb", [128, 128], F32)
    bias_sb = b.sb("bias_sb", [128, NKC], F32)
    wrow = b.sb("wrow", [1, 16], F32)
    wcol = b.sb("wcol", [16, 1], F32)
    cf_sb = b.sb("cf_sb", [1, 16], F32)
    scs_sb = b.sb("scs_sb", [1, 16], F32)
    e16_sb = b.sb("e16_sb", [16, 256], F32)
    ecol2 = b.sb("ecol2", [128, 2], F32)
    w_sb = b.sb("w_sb", [128, 2 * 2 * 128 * (NKC + 2)], BF16)
    wstage = b.sb("wstage", [128, 4 * HALF], BF16)
    scr_tile = b.sb("scr_tile", [128, D], BF16)

    pb = [b.psum(f"pb{i}", [128, 512], F32) for i in range(4)]
    tbp = [b.psum(f"tbp{i}", [128, 512], BF16) for i in range(2)]
    sm_ps = b.psum("sm_ps", [128, 16], F32)

    b.dma("sync", ident[:, :], ident_d[:, :], "cld")
    b.dma("sync", identf[:, :], identf_d[:, :], "cld")
    b.dma("sync", bias_sb[:, :], bias_d[:, :], "cld")
    b.dma("sync", cf_sb[:, :], cf_d[:, :], "cld")
    b.dma("sync", scs_sb[:, :], sc_d[:, :], "cld")
    b.dma("sync", e16_sb[:, :], e16_d[:, :], "cld")
    consts = b.mark("cld")
    b.wait("tensor", [consts])
    b.wait("vector", [consts])
    b.wait("scalar", [consts])

    tbanks = Banks(b, "tbk", 0, handles=tbp)
    lnfm_done = ln_stream_to_fm(b, x3_d, lnx_fm, TPC, x_stage, ln_stage, stats,
                                mv, rstd, tbanks, ident[:, :], consts)
    b.wait("tensor", [lnfm_done])

    b.op("vector", lambda e: e.tensor_tensor(wrow[:, :], cf_sb[:, :],
                                             scs_sb[:, :], ALU.mult))
    t2 = b.pe(lambda e: e.transpose(sm_ps[0:16, 0:1], wrow[:, :],
                                    identf[0:1, 0:1]),
              waits=[b.mark("ve")], sem="pe")
    b.op("vector", lambda e: e.tensor_copy(wcol[:, :], sm_ps[0:16, 0:1]),
         waits=[t2])
    for j in range(2):
        ej = b.pe(lambda e, j=j: e.matmul(
            sm_ps[:, 1 + j:2 + j], e16_sb[:, j * 128:(j + 1) * 128],
            wcol[:, :], start=True, stop=True, skip_group_check=True),
            waits=[b.mark("ve")], sem="pe")
        ec_tok = b.op("vector", lambda e, j=j: e.tensor_copy(
            ecol2[:, j:j + 1], sm_ps[:, 1 + j:2 + j]), waits=[ej])

    def lnx_rhs(k, half):
        return lnx_fm[:, k * TPC + half * HALF: k * TPC + half * HALF + HALF]

    hbanks = Banks(b, "hbk", 0, handles=pb)

    def h_cb(oi, half, bank, ptok):
        return b.op("vector", lambda e, o=h2_fm[:, oi * TPC + half * HALF:
                                                oi * TPC + half * HALF + HALF],
                    i=bank[:, :], j=oi: e.tensor_scalar(
                        o, i, ecol2[:, j:j + 1], None, ALU.mult, ALU.bypass),
                    waits=[ptok, ec_tok])

    proj_fm(b, hbanks, wa_d, lnx_rhs, 2, h_cb, w_sb, obs=2)
    h_done = b.mark("ve")
    b.wait("tensor", [h_done])
    b.wait("gpsimd", [b.mark("pe")])

    def h_rhs(k, half):
        return h2_fm[:, k * TPC + half * HALF: k * TPC + half * HALF + HALF]

    def bias_evac(oi, half, bank, stg, waits):
        return b.op("vector", lambda e, stg=stg, oi=oi, bank=bank:
                    e.tensor_scalar(stg, bank[:, :], bias_sb[:, oi:oi + 1],
                                    None, ALU.add, ALU.bypass), waits=waits)

    obanks = Banks(b, "obk", 0, handles=pb)
    scr_done = proj_to_scr(b, obanks, wf_d, lnx_rhs, NKC, scr_d, w_sb,
                           wstage, obs=2, extra_k=(2, wbf_d, h_rhs),
                           evac_fn=bias_evac)
    ttb = Banks(b, "ttb", 0, handles=tbp)
    resid_tile_loop(b, scr_done, scr_d, x3_d, x4_d, ident, ttb, scr_tile,
                    x_stage[:, 0:D])
    b.wait("sync", [b.mark(s) for s in list(b.cnt) if s.startswith("x1s")])
    b.finalize()
    return nc, ctx



_NC_CACHE = {}


def _get_nc(name, builder):
    if name not in _NC_CACHE:
        _NC_CACHE[name] = builder()
    return _NC_CACHE[name][0]


def _run(name, builder, in_maps):
    nc = _get_nc(name, builder)
    res = run_bass_kernel_spmd(nc, in_maps, core_ids=list(range(8)))
    return res.results


def kernel(**inputs):
    p = host_prep(inputs)
    X = np.asarray(inputs["L0_attn_X"], dtype=np.float32)
    eye_b = p["ident_bf"]

    def tok(core):
        bb, cc = core // 4, core % 4
        return bb, slice(cc * TPC, (cc + 1) * TPC)

    im = []
    for core in range(8):
        bb, sl = tok(core)
        im.append({
            "x": _f32h(X[bb, sl, :]),
            "w_qkv": p["w_qkv"],
            "cos_c": np.ascontiguousarray(p["cos_t"][:, sl]),
            "sin_c": np.ascontiguousarray(p["sin_t"][:, sl]),
            "cosq_c": np.ascontiguousarray(p["cosq_t"][:, sl]),
            "sinq_c": np.ascontiguousarray(p["sinq_t"][:, sl]),
            "ident": eye_b,
        })
    r1 = _run("l1", build_l1, im)

    im = []
    for k in range(8):
        qfm = np.concatenate(
            [np.concatenate([np.asarray(r1[bt * 4 + cc]["qkv_fm"])
                             [4 * k * 128:(4 * k + 4) * 128, :]
                             for cc in range(4)], axis=1)
             for bt in range(2)], axis=1)
        kfm = np.concatenate(
            [np.concatenate([np.asarray(r1[bt * 4 + cc]["qkv_fm"])
                             [4096 + k * 128:4096 + (k + 1) * 128, :]
                             for cc in range(4)], axis=1)
             for bt in range(2)], axis=1)
        vtm = np.concatenate(
            [np.concatenate([np.asarray(r1[bt * 4 + cc]["v_tm"])
                             [:, k * 128:(k + 1) * 128]
                             for cc in range(4)], axis=0)
             for bt in range(2)], axis=0)
        im.append({
            "q_fm": np.ascontiguousarray(qfm),
            "k_fm": np.ascontiguousarray(kfm),
            "v_tm": np.ascontiguousarray(vtm),
            "mask_qk_b": p["mask_qk_b"], "maskt_kq_b": p["maskt_kq_b"],
            "ident": p["ident_bf"],
        })
    r2a = _run("l2a", build_l2a, im)

    im = []
    for core in range(8):
        bb, sl = tok(core)
        cc = core % 4
        afm = np.concatenate(
            [np.asarray(r2a[k]["attn_tm"])[bb * T + cc * TPC:
                                           bb * T + (cc + 1) * TPC, :].T
             for k in range(8)], axis=0)
        im.append({
            "attn_fm": np.ascontiguousarray(afm),
            "x": _f32h(X[bb, sl, :]),
            "w_o": p["w_o"],
            "ident": eye_b, "ones_col": p["ones_col"],
        })
    r2b = _run("l2b", build_l2b, im)

    im = []
    for core in range(8):
        bb, sl = tok(core)
        pms = np.concatenate([np.asarray(r2b[bb * 4 + j]["pmean"])
                              for j in range(4)], axis=1)
        im.append({
            "x1": np.asarray(r2b[core]["x1"]),
            "pmeans": np.ascontiguousarray(pms),
            "w_org_t": p["w_org_t"], "w_down": p["w_down"], "w_up": p["w_up"],
            "keys_t": p["keys_t"], "wq_mole_t": p["wq_mole_t"],
            "exp8": p["exp8"], "ident": eye_b, "ident_f": p["ident_f"],
        })
    r3 = _run("l3", build_l3, im)

    im = []
    for core in range(8):
        bb, sl = tok(core)
        cc = core % 4
        halo = (np.zeros((3, D), np.float32) if cc == 0 else
                np.asarray(r3[core - 1]["x2"])[-3:, :])
        im.append({
            "x2": np.asarray(r3[core]["x2"]),
            "halo": _f32h(halo),
            "w_inproj_p": p["w_inproj_p"],
            "ident": eye_b,
        })
    r4a = _run("l4a", build_l4a, im)

    im = []
    for core in range(8):
        pout = np.asarray(r4a[core]["pout"])
        im.append({
            "xbc_fm": np.ascontiguousarray(pout[4096:4096 + CONV_DIM, :]),
            "dt_fm": np.ascontiguousarray(pout[10240:10304, :]),
            "conv_w_c": p["conv_w_c"], "conv_b_c": p["conv_b_c"],
            "dt_bias_c": p["dt_bias_c"], "a_log_c": p["a_log_c"],
            "d_row": p["d_row"], "rep": p["rep"],
            "maskneg128": p["maskneg128"],
            "ident": eye_b, "ident_f": p["ident_f"],
            "ones_row_f": p["ones_row_f"], "ones_row_b": p["ones_row_b"],
        })
    r4b = _run("l4b", build_l4b, im)

    im = []
    for core in range(8):
        bb, sl = tok(core)
        cc = core % 4
        sprev = np.zeros((3, 128, 4096), BF)
        llog = np.zeros((64, 3), np.float32)
        for j in range(3):
            pc = cc - 1 - j
            if pc >= 0:
                pcore = bb * 4 + pc
                sprev[j] = np.asarray(r4b[pcore]["s_core"])
                llog[:, j] = np.asarray(r4b[pcore]["coreclg"])[:, -1]
        pout = np.asarray(r4a[core]["pout"])
        im.append({
            "y_tm": np.asarray(r4b[core]["y_tm"]),
            "c_fm": np.asarray(r4b[core]["c_fm"]),
            "coreclg": np.asarray(r4b[core]["coreclg"]),
            "s_prev": sprev.reshape(384, 4096),
            "lamlog_prev": llog,
            "gate_fm": np.ascontiguousarray(pout[0:4096, :]),
            "x2": np.asarray(r3[core]["x2"]),
            "w_outproj_t": p["w_outproj_t"],
            "norm_row": p["norm_row"], "rep": p["rep"],
            "ident": eye_b, "ones_row_f": p["ones_row_f"],
        })
    r5 = _run("l5", build_l5, im)

    im = []
    for core in range(8):
        im.append({
            "x3": np.asarray(r5[core]["x3"]),
            "w_ffn_t": p["w_ffn_t"], "bias_c": p["bias_c"],
            "w_a_t": p["w_a_t"], "w_bf": p["w_bf"],
            "coeffs16": p["coeffs16"], "scales16": p["scales16"],
            "exp16": p["exp16"],
            "ident": eye_b, "ident_f": p["ident_f"],
        })
    r6 = _run("l6", build_l6, im)

    out = np.empty((B, T, D), np.float32)
    for core in range(8):
        bb, sl = tok(core)
        out[bb, sl, :] = np.asarray(r6[core]["x4"])
    return out


# version 07 (final)
# speedup vs baseline: 1.4661x; 1.1630x over previous
import numpy as np
import ml_dtypes
from contextlib import ExitStack

import concourse.bass as bass
import concourse.mybir as mybir
from concourse.bass_utils import run_bass_kernel_spmd

F32 = mybir.dt.float32
BF16 = mybir.dt.bfloat16
AF = mybir.ActivationFunctionType
ALU = mybir.AluOpType
BF = ml_dtypes.bfloat16

B = 2
T = 4096
D = 4096
TPC = 1024
HALF = 512
NKC = D // 128

NH, NKV, DH = 32, 8, 128
ROPE_THETA = 10000.0
NE, RR, KK = 8, 64, 128
MH, MP, MN, MG, MK = 64, 64, 128, 8, 4
MI = MH * MP
MGN = MG * MN
CONV_DIM = MI + 2 * MGN
PROJ = MI + CONV_DIM + MH
MEPS = 1e-5
LN_EPS = 1e-5

ENGINES = ("sync", "gpsimd", "tensor", "vector", "scalar")
_SELFSEM = {"vector": "ve", "scalar": "se", "gpsimd": "po"}


class Bld:

    def __init__(self, nc, ctx):
        self.nc = nc
        self.ctx = ctx
        self.ops = {e: [] for e in ENGINES}
        self.sems = {}
        self.cnt = {}
        for s in ("ve", "se", "po", "pe", "cld", "wld", "xld", "sto"):
            self.sem(s)

    def sem(self, name):
        if name not in self.sems:
            self.sems[name] = self.ctx.enter_context(self.nc.semaphore(name))
            self.cnt[name] = 0
        return self.sems[name]

    def sb(self, name, shape, dt):
        return self.ctx.enter_context(self.nc.sbuf_tensor(name, shape, dt))

    def psum(self, name, shape, dt=F32):
        return self.ctx.enter_context(self.nc.psum_tensor(name, shape, dt))

    def on(self, eng, f):
        self.ops[eng].append(f)

    def wait(self, eng, tok):
        toks = tok if isinstance(tok, list) else [tok]
        for t in toks:
            if t is None:
                continue
            (s, v) = t
            if v <= 0:
                continue
            sem = self.sem(s)
            self.on(eng, lambda e, sem=sem, vv=int(v): e.wait_ge(sem, vv))

    def mark(self, semname):
        return (semname, self.cnt[semname])

    def raw_emit(self, eng, fn, sem=None, inc=1):
        if sem is not None:
            s = self.sem(sem)
            self.cnt[sem] += inc
            self.on(eng, lambda e, fn=fn, s=s, inc=inc: fn(e).then_inc(s, inc))
            return (sem, self.cnt[sem])
        self.on(eng, lambda e, fn=fn: fn(e))
        return None

    def op(self, eng, fn, waits=(), chain=True):
        ssem = _SELFSEM[eng]
        w = list(waits) + ([self.mark(ssem)] if chain else [])
        self.wait(eng, w)
        return self.raw_emit(eng, fn, sem=ssem)

    def constcol(self, value):
        if not hasattr(self, "_cc"):
            self._cc = {}
            self._cc_sb = self.sb("constcols", [128, 8], F32)
        if value not in self._cc:
            i = len(self._cc)
            ap = self._cc_sb[:, i:i + 1]
            tok = self.op("vector", lambda e, ap=ap, v=float(value): e.memset(ap, v))
            self._cc[value] = (ap, tok)
        return self._cc[value]

    def dma(self, eng, out, in_, sem, waits=()):
        self.wait(eng, list(waits))
        return self.raw_emit(eng, lambda e, o=out, i=in_: e.dma_start(out=o, in_=i),
                             sem=sem, inc=16)

    def pe(self, fn, waits=(), sem=None):
        self.wait("tensor", list(waits))
        return self.raw_emit("tensor", fn, sem=sem)

    def finalize(self):
        block = self.ctx.enter_context(self.nc.Block())
        for name in ENGINES:
            ops = list(self.ops[name])
            if not ops:
                continue
            def make(ops):
                def run(handle):
                    for f in ops:
                        f(handle)
                return run
            getattr(block, name)(make(ops))


class Banks:

    def __init__(self, b, name, n, cols=512, dt=F32, handles=None):
        self.b = b
        if handles is None:
            handles = [b.psum(f"{name}{i}", [128, cols], dt) for i in range(n)]
        self.banks = handles
        self.free_tok = [None] * len(handles)
        self.i = 0

    def take(self):
        i = self.i
        self.i = (self.i + 1) % len(self.banks)
        tok = self.free_tok[i]
        return i, self.banks[i], tok

    def set_free(self, i, tok):
        self.free_tok[i] = tok


def _bfh(x):
    return np.ascontiguousarray(np.asarray(x)).astype(BF)


def _f32h(x):
    return np.ascontiguousarray(np.asarray(x), dtype=np.float32)



def ln_tile(b, xs, lt, stats, mv, rstd, ld_tok, width=D):
    nq = width // 512
    for q in range(nq):
        b.op("vector", lambda e, q=q: e.bn_stats(
            stats[:, q * 6:(q + 1) * 6], xs[:, q * 512:(q + 1) * 512]),
            waits=[ld_tok] if q == 0 else ())
    b.op("vector", lambda e: e.bn_aggr(mv[:, :], stats[:, 0:6 * nq]))
    eps_ap, eps_tok = b.constcol(LN_EPS)
    b.op("scalar", lambda e: e.activation(rstd[:, :], mv[:, 1:2], AF.Sqrt,
                                          bias=eps_ap),
         waits=[b.mark("ve"), eps_tok])
    b.op("vector", lambda e: e.reciprocal(rstd[:, :], rstd[:, :]),
         waits=[b.mark("se")])
    return b.op("vector", lambda e: e.tensor_scalar(
        lt, xs, mv[:, 0:1], rstd[:, :], ALU.subtract, ALU.mult))


def ln_stream_to_fm(b, x_dram, lnx_fm, n_tok, x_stage, ln_stage, stats, mv,
                    rstd, tbanks, ident, ident_tok, fm_tpc=None,
                    extra_tile_cb=None):
    if fm_tpc is None:
        fm_tpc = n_tok
    nt = n_tok // 128
    last_use = [None, None]
    for tj in range(nt):
        sl = tj % 2
        xs = x_stage[:, sl * D:(sl + 1) * D]
        lt = ln_stage[:, sl * D:(sl + 1) * D]
        ld = b.dma("sync", xs, x_dram[tj * 128:(tj + 1) * 128, :], f"xld{sl}",
                   waits=[last_use[sl]])
        aptok = ln_tile(b, xs, lt, stats, mv, rstd, ld)
        if extra_tile_cb is not None:
            extra_tile_cb(tj, xs, ld)
        for f0 in range(0, NKC, 4):
            bi, bank, war = tbanks.take()
            ptok = None
            for g in range(4):
                fi = f0 + g
                ptok = b.pe(lambda e, s=lt[:, fi * 128:(fi + 1) * 128],
                            o=bank[:, g * 128:(g + 1) * 128]:
                            e.transpose(o, s, ident),
                            waits=([war, aptok, ident_tok] if g == 0 else ()),
                            sem="pe" if g == 3 else None)
            vts = []
            for g in range(4):
                fi = f0 + g
                vts.append(b.op(
                    "vector",
                    lambda e, o=lnx_fm[:, fi * fm_tpc + tj * 128:
                                       fi * fm_tpc + (tj + 1) * 128],
                    i=bank[:, g * 128:(g + 1) * 128]:
                    e.tensor_copy(o, i),
                    waits=[ptok] if g == 0 else ()))
            tbanks.set_free(bi, vts[-1])
        last_use[sl] = b.mark("ve")
    return b.mark("ve")


def transpose_stream(b, banks, src_view, dst_view, n_groups, ident,
                     src_tok=None, grp=4):
    for gi in range(n_groups):
        bi, bank, war = banks.take()
        ptok = None
        for g in range(grp):
            ptok = b.pe(lambda e, s=src_view(gi, g), o=bank[:, g * 128:(g + 1) * 128]:
                        e.transpose(o, s, ident),
                        waits=([war, src_tok] if g == 0 else ()),
                        sem="pe" if g == grp - 1 else None)
        vt = b.op("vector", lambda e, o=dst_view(gi), i=bank[:, 0:grp * 128]:
                  e.tensor_copy(o, i), waits=[ptok])
        banks.set_free(bi, vt)
    return b.mark("ve")


def proj_fm(b, banks, w_dram, rhs_view, n_out, out_cb, w_sb, n_k=NKC,
            obs=4, halves=(0, 1), extra_k=None):
    slab_cols = obs * 128
    slab_tok = [None, None]
    use_tok = [None, None]
    n_ex = extra_k[0] if extra_k else 0
    for ob0 in range(0, n_out, obs):
        cur = min(obs, n_out - ob0)
        sl = (ob0 // obs) % 2
        base = sl * slab_cols * (n_k + n_ex)
        b.wait("gpsimd", [use_tok[sl]])
        for k in range(n_k):
            slab_tok[sl] = b.dma(
                "gpsimd", w_sb[:, base + k * slab_cols: base + k * slab_cols + cur * 128],
                w_dram[k * 128:(k + 1) * 128, ob0 * 128:(ob0 + cur) * 128], f"wld{sl}")
        if extra_k:
            _, w2, _ = extra_k
            for k in range(n_ex):
                slab_tok[sl] = b.dma(
                    "gpsimd",
                    w_sb[:, base + (n_k + k) * slab_cols: base + (n_k + k) * slab_cols + cur * 128],
                    w2[k * 128:(k + 1) * 128, ob0 * 128:(ob0 + cur) * 128], f"wld{sl}")
        for half in halves:
            for oi in range(cur):
                bi, bank, war = banks.take()
                ptok = None
                for k in range(n_k):
                    lhs = w_sb[:, base + k * slab_cols + oi * 128:
                               base + k * slab_cols + (oi + 1) * 128]
                    w_ = []
                    if k == 0:
                        w_ = [slab_tok[sl], war]
                    last = (k == n_k - 1) and not extra_k
                    r_ap = rhs_view(k, half)
                    rw = r_ap.shape[-1]
                    ptok = b.pe(lambda e, l=lhs, r=r_ap, o=bank[:, 0:rw],
                                s=(k == 0), st=last: e.matmul(o, l, r, start=s, stop=st),
                                waits=w_, sem="pe" if last else None)
                if extra_k:
                    _, _, rhs2 = extra_k
                    for k in range(n_ex):
                        lhs = w_sb[:, base + (n_k + k) * slab_cols + oi * 128:
                                   base + (n_k + k) * slab_cols + (oi + 1) * 128]
                        last = k == n_ex - 1
                        r_ap = rhs2(k, half)
                        rw = r_ap.shape[-1]
                        ptok = b.pe(lambda e, l=lhs, r=r_ap, o=bank[:, 0:rw],
                                    st=last: e.matmul(o, l, r, start=False, stop=st),
                                    sem="pe" if last else None)
                vt = out_cb(ob0 + oi, half, bank, ptok)
                banks.set_free(bi, vt)
        use_tok[sl] = b.mark("pe")



def host_prep(inputs):
    p = {}
    p["w_qkv"] = _f32h(np.concatenate(
        [inputs["L0_attn_Wq"], inputs["L0_attn_Wk"], inputs["L0_attn_Wv"]], axis=1))
    p["w_o"] = _f32h(inputs["L0_attn_Wo"])
    p["w_org_t"] = _f32h(np.asarray(inputs["L0_ffn_W_org"]).T)
    ld = np.asarray(inputs["L0_ffn_lora_down"])
    p["w_down"] = _f32h(np.concatenate([ld[e] for e in range(NE)], axis=1))
    lu = np.asarray(inputs["L0_ffn_lora_up"])
    p["w_up"] = _f32h(np.concatenate([lu[e] for e in range(NE)], axis=0))
    p["lora_keys"] = _f32h(inputs["L0_ffn_lora_keys"])
    p["wq_mole_t"] = _f32h(np.asarray(inputs["L0_ffn_Wq"]).T)
    p["w_inproj_t"] = _f32h(np.asarray(inputs["L1_attn_in_proj_w"]).T)
    p["conv_w"] = _f32h(inputs["L1_attn_conv_weight"])
    p["conv_b"] = _f32h(inputs["L1_attn_conv_bias"]).reshape(CONV_DIM // 128, 128).T.copy()
    p["dt_bias"] = _f32h(inputs["L1_attn_dt_bias"]).reshape(1, MH)
    p["a_log"] = _f32h(inputs["L1_attn_A_log"]).reshape(1, MH)
    p["d_vec"] = _f32h(inputs["L1_attn_D"]).reshape(1, MH)
    p["norm_w"] = _f32h(inputs["L1_attn_norm_weight"]).reshape(1, MI)
    p["w_outproj_t"] = _f32h(np.asarray(inputs["L1_attn_out_proj_w"]).T)
    p["w_ffn_t"] = _f32h(np.asarray(inputs["L1_ffn_W"]).T)
    p["ffn_bias"] = _f32h(inputs["L1_ffn_bias"]).reshape(NKC, 128).T.copy()
    A = np.asarray(inputs["L1_ffn_A"])
    p["w_a_t"] = _f32h(np.concatenate([A[e].T for e in range(16)], axis=1))
    Bf = np.asarray(inputs["L1_ffn_Bf"])
    p["w_bf"] = _f32h(np.concatenate([Bf[e].T for e in range(16)], axis=0))
    p["coeffs16"] = _f32h(inputs["L1_ffn_coeffs"]).reshape(1, 16)
    p["scales16"] = _f32h(inputs["L1_ffn_scales"]).reshape(1, 16)
    p["ident_bf"] = np.eye(128, dtype=BF)
    p["conv_w_cols"] = _f32h(inputs["L1_attn_conv_weight"]).reshape(
        CONV_DIM // 128, 128, MK)
    half = DH // 2
    inv_freq = 1.0 / ROPE_THETA ** (np.arange(0, half, dtype=np.float64) / half)
    pos = np.arange(T, dtype=np.float64)
    ang = pos[:, None] * inv_freq[None, :]
    p["cos_t"] = _f32h(np.cos(ang).T)
    p["sin_t"] = _f32h(np.sin(ang).T)
    m = np.triu(np.ones((HALF, HALF), dtype=np.float32), k=1) * (-3e38 / 4)
    p["mask_qk"] = _f32h(m)
    p["mask_qk_b"] = m.astype(BF)
    p["maskt_kq_b"] = (np.tril(np.ones((HALF, HALF), np.float32), -1)
                       * (-3e38 / 4)).astype(BF)
    p["ident_f"] = np.eye(128, dtype=np.float32)
    p["ones_col"] = np.ones((128, 1), BF)
    p["ones_row_f"] = np.ones((1, 128), np.float32)
    p["ones_row_b"] = np.ones((1, 128), BF)
    p["cosq_t"] = _f32h(p["cos_t"] / np.sqrt(128.0))
    p["sinq_t"] = _f32h(p["sin_t"] / np.sqrt(128.0))
    p["keys_t"] = _f32h(np.asarray(inputs["L0_ffn_lora_keys"]).T)
    exp8 = np.zeros((8, 512), np.float32)
    for j in range(4):
        for q in range(128):
            exp8[2 * j + q // 64, j * 128 + q] = 1.0
    p["exp8"] = exp8
    exp16 = np.zeros((16, 256), np.float32)
    for j in range(2):
        for q in range(128):
            exp16[(j * 128 + q) // 16, j * 128 + q] = 1.0
    p["exp16"] = exp16
    wp = np.concatenate([np.asarray(inputs["L1_attn_in_proj_w"]),
                         np.zeros((NPC * 128 - PROJ, D), np.float32)], axis=0)
    p["w_inproj_p"] = _f32h(wp.T)
    cw = _f32h(inputs["L1_attn_conv_weight"])
    p["conv_w_c"] = np.ascontiguousarray(
        cw.reshape(48, 128, 4).transpose(1, 0, 2).reshape(128, 192))
    p["conv_b_c"] = np.ascontiguousarray(
        _f32h(inputs["L1_attn_conv_bias"]).reshape(48, 128).T)
    p["dt_bias_c"] = _f32h(inputs["L1_attn_dt_bias"]).reshape(64, 1)
    p["a_log_c"] = _f32h(inputs["L1_attn_A_log"]).reshape(64, 1)
    p["d_row"] = _f32h(np.repeat(np.asarray(inputs["L1_attn_D"]), 64)).reshape(1, 4096)
    rep = np.zeros((64, 4096), BF)
    for h in range(64):
        rep[h, h * 64:(h + 1) * 64] = 1
    p["rep"] = rep
    p["maskneg128"] = _f32h(np.tril(np.ones((128, 128), np.float32), -1) * (-3e38 / 4))
    p["norm_row"] = _f32h(inputs["L1_attn_norm_weight"]).reshape(1, 4096)
    p["bias_c"] = np.ascontiguousarray(
        _f32h(inputs["L1_ffn_bias"]).reshape(NKC, 128).T)
    return p



def build_l1():
    nc = bass.Bass("TRN2", target_bir_lowering=False, debug=False)
    ctx = ExitStack()
    b = Bld(nc, ctx)
    x_d = nc.dram_tensor("x", [TPC, D], F32, kind="ExternalInput").ap()
    wqkv_d = nc.dram_tensor("w_qkv", [D, 6144], F32, kind="ExternalInput").ap()
    cos_d = nc.dram_tensor("cos_c", [64, TPC], F32, kind="ExternalInput").ap()
    sin_d = nc.dram_tensor("sin_c", [64, TPC], F32, kind="ExternalInput").ap()
    cosq_d = nc.dram_tensor("cosq_c", [64, TPC], F32, kind="ExternalInput").ap()
    sinq_d = nc.dram_tensor("sinq_c", [64, TPC], F32, kind="ExternalInput").ap()
    ident_d = nc.dram_tensor("ident", [128, 128], BF16, kind="ExternalInput").ap()
    qkv_d = nc.dram_tensor("qkv_fm", [6144, TPC], BF16, kind="ExternalOutput").ap()
    vtm_d = nc.dram_tensor("v_tm", [TPC, 1024], BF16, kind="ExternalOutput").ap()

    NT = TPC // 128
    x_stage = b.sb("x_stage", [128, 2 * D], F32)
    ln_stage = b.sb("ln_stage", [128, 2 * D], BF16)
    stats = b.sb("stats", [128, 48], F32)
    mv = b.sb("mv", [128, 2], F32)
    rstd = b.sb("rstd", [128, 1], F32)
    lnx_fm = b.sb("lnx_fm", [128, NKC * TPC], BF16)
    ident = b.sb("ident_sb", [128, 128], BF16)
    cos_sb = b.sb("cos_sb", [64, TPC], BF16)
    sin_sb = b.sb("sin_sb", [64, TPC], BF16)
    cosq_sb = b.sb("cosq_sb", [64, TPC], BF16)
    sinq_sb = b.sb("sinq_sb", [64, TPC], BF16)
    w_sb = b.sb("w_sb", [128, 2 * 4 * 128 * NKC], BF16)
    stage = b.sb("stage", [128, 4 * HALF], BF16)
    ra = b.sb("ra", [64, HALF], F32)
    rb = b.sb("rb", [64, HALF], F32)
    rc = b.sb("rc", [64, HALF], F32)
    rd_ = b.sb("rd", [64, HALF], F32)
    vstage = b.sb("vstage", [128, 4 * HALF], BF16)

    b.dma("sync", ident[:, :], ident_d[:, :], "cld")
    b.dma("gpsimd", cos_sb[:, :], cos_d[:, :], "cldg")
    b.dma("gpsimd", sin_sb[:, :], sin_d[:, :], "cldg")
    b.dma("gpsimd", cosq_sb[:, :], cosq_d[:, :], "cldg")
    b.dma("gpsimd", sinq_sb[:, :], sinq_d[:, :], "cldg")
    c0 = b.mark("cld")
    c1 = b.mark("cldg")
    b.wait("vector", [c1])

    banks = Banks(b, "bk", 6)
    tbanks = Banks(b, "tb", 2, dt=BF16)

    ident_ap = ident[:, :]
    lnfm_done = ln_stream_to_fm(b, x_d, lnx_fm, TPC, x_stage, ln_stage, stats,
                                mv, rstd, tbanks, ident_ap, c0)
    b.wait("tensor", [lnfm_done])

    stage_i = [0]
    stage_free = [None] * 4
    vstage_free = [None] * 4

    def rhs_view(k, half):
        return lnx_fm[:, k * TPC + half * HALF: k * TPC + (half + 1) * HALF]

    def out_cb(oi, half, bank, ptok):
        si = stage_i[0] % 4
        stage_i[0] += 1
        stg = stage[:, si * HALF:(si + 1) * HALF]
        if oi < NH + NKV:
            csrc = cosq_sb if oi < NH else cos_sb
            ssrc = sinq_sb if oi < NH else sin_sb
            cs = csrc[:, half * HALF:(half + 1) * HALF]
            sn = ssrc[:, half * HALF:(half + 1) * HALF]
            b.op("vector", lambda e: e.tensor_tensor(ra[:, :], bank[0:64, :], cs, ALU.mult),
                 waits=[ptok, c1])
            b.op("vector", lambda e: e.tensor_tensor(rb[:, :], bank[64:128, :], sn, ALU.mult))
            b.op("vector", lambda e: e.tensor_tensor(rc[:, :], bank[0:64, :], sn, ALU.mult))
            last_ps = b.op("vector", lambda e: e.tensor_tensor(rd_[:, :], bank[64:128, :], cs, ALU.mult))
            b.op("vector", lambda e, stg=stg: e.tensor_tensor(stg[0:64, :], ra[:, :], rb[:, :], ALU.subtract),
                 waits=[stage_free[si]])
            vt = b.op("vector", lambda e, stg=stg: e.tensor_tensor(stg[64:128, :], rc[:, :], rd_[:, :], ALU.add))
            bank_free = last_ps
        else:
            vt = b.op("vector", lambda e, stg=stg: e.tensor_copy(stg, bank[:, :]),
                      waits=[ptok, stage_free[si]])
            bank_free = vt
        dtok = b.dma("sync", qkv_d[oi * 128:(oi + 1) * 128, half * HALF:(half + 1) * HALF],
                     stg, f"so{si}", waits=[vt])
        stage_free[si] = dtok
        if oi >= NH + NKV:
            vi = oi - (NH + NKV)
            bi2, bank2, war2 = tbanks.take()
            ptok2 = None
            for g in range(4):
                ptok2 = b.pe(lambda e, s=stg[:, g * 128:(g + 1) * 128],
                             o=bank2[:, g * 128:(g + 1) * 128]:
                             e.transpose(o, s, ident_ap),
                             waits=([war2, vt] if g == 0 else ()),
                             sem="pe" if g == 3 else None)
            sj = stage_i[0] % 4
            stage_i[0] += 1
            vstg = vstage[:, sj * HALF:(sj + 1) * HALF]
            vt2 = b.op("vector", lambda e, vstg=vstg: e.tensor_copy(vstg, bank2[:, :]),
                       waits=[ptok2, vstage_free[sj]])
            tbanks.set_free(bi2, vt2)
            d2 = None
            for g in range(4):
                d2 = b.dma("sync",
                           vtm_d[half * HALF + g * 128: half * HALF + (g + 1) * 128,
                                 vi * 128:(vi + 1) * 128],
                           vstg[:, g * 128:(g + 1) * 128], f"sv{sj}", waits=[vt2] if g == 0 else ())
            vstage_free[sj] = d2
        return bank_free

    proj_fm(b, banks, wqkv_d, rhs_view, 48, out_cb, w_sb)

    b.wait("sync", [b.mark(s) for s in list(b.cnt) if s.startswith(("so", "sv"))])
    b.finalize()
    return nc, ctx


def run_l1(inputs, prep):
    nc, ctx = build_l1()
    in_maps = []
    X = np.asarray(inputs["L0_attn_X"])
    for core in range(8):
        bb, cc = core // 4, core % 4
        sl = slice(cc * TPC, (cc + 1) * TPC)
        in_maps.append({
            "x": _f32h(X[bb, sl, :]),
            "w_qkv": prep["w_qkv"],
            "cos_c": prep["cos_t"][:, sl].copy(),
            "sin_c": prep["sin_t"][:, sl].copy(),
            "ident": prep["ident_bf"],
        })
    res = run_bass_kernel_spmd(nc, in_maps, core_ids=list(range(8)))
    ctx.close()
    return res.results






def proj_to_scr(b, banks, w_dram, rhs_view, n_out, scr_d, w_sb, wstage,
                sem_pfx="wo", obs=4, extra_k=None, evac_fn=None,
                halves=(0, 1), hw=HALF):
    stage_free = [None] * 4
    wi = [0]

    def cb(oi, half, bank, ptok):
        si = wi[0] % 4
        wi[0] += 1
        stg = wstage[:, si * hw:(si + 1) * hw]
        if evac_fn is None:
            vt = b.op("vector", lambda e, stg=stg, w=hw: e.tensor_copy(
                stg, bank[:, 0:w]), waits=[ptok, stage_free[si]])
        else:
            vt = evac_fn(oi, half, bank, stg, [ptok, stage_free[si]])
        dt_ = b.dma("sync", scr_d[oi * 128:(oi + 1) * 128,
                                  half * hw:(half + 1) * hw],
                    stg, f"{sem_pfx}{si}", waits=[vt])
        stage_free[si] = dt_
        return vt

    proj_fm(b, banks, w_dram, rhs_view, n_out, cb, w_sb, obs=obs,
            extra_k=extra_k, halves=halves)
    return [b.mark(s) for s in list(b.cnt) if s.startswith(sem_pfx)]


def resid_tile_loop(b, scr_done, scr_d, x_in_d, xout_d, ident, ttb, scr_tile,
                    x_stage, ln_extras=None):
    NT = TPC // 128
    d1 = None
    pm_started = [False]
    for tt in range(NT):
        wl = None
        for o in range(NKC):
            wl = b.dma("sync", scr_tile[:, o * 128:(o + 1) * 128],
                       scr_d[o * 128:(o + 1) * 128, tt * 128:(tt + 1) * 128],
                       "wtl", waits=(scr_done + [b.mark("pe")]) if o == 0 else ())
        xl = b.dma("sync", x_stage[:, :], x_in_d[tt * 128:(tt + 1) * 128, :],
                   "xl", waits=[b.mark("ve"), d1])
        for f0 in range(0, NKC, 4):
            bi, bank, war = ttb.take()
            ptk = None
            for g in range(4):
                fi = f0 + g
                ptk = b.pe(lambda e, s=scr_tile[:, fi * 128:(fi + 1) * 128],
                           o=bank[:, g * 128:(g + 1) * 128], i=ident[:, :]:
                           e.transpose(o, s, i),
                           waits=([war, wl] if g == 0 else ()), sem="pe")
            vt = b.op("vector", lambda e, o=x_stage[:, f0 * 128:(f0 + 4) * 128],
                      i=bank[:, 0:512]: e.tensor_tensor(o, o, i, ALU.add),
                      waits=[ptk, xl])
            ttb.set_free(bi, vt)
        x1t = b.mark("ve")
        d1 = b.dma("sync", xout_d[tt * 128:(tt + 1) * 128, :], x_stage[:, :],
                   "x1s", waits=[x1t])
        if ln_extras is not None:
            (stats, mv, rstd, ln_t, ones_sb, pmean_ps, pm_stage, pm_d) = ln_extras
            b.wait("vector", [b.mark("pe")])
            ap2 = ln_tile(b, x_stage[:, :], ln_t[:, :], stats, mv, rstd, None)
            b.wait("tensor", [ap2])
            for fi in range(NKC):
                b.pe(lambda e, l=ln_t[:, fi * 128:(fi + 1) * 128],
                     o=pmean_ps[:, fi:fi + 1], s=(not pm_started[0]),
                     st=(tt == NT - 1 and fi == NKC - 1):
                     e.matmul(o, l, ones_sb[:, :], start=s, stop=st,
                              skip_group_check=True),
                     sem="pe" if fi == NKC - 1 else None)
                pm_started[0] = True
    if ln_extras is not None:
        (stats, mv, rstd, ln_t, ones_sb, pmean_ps, pm_stage, pm_d) = ln_extras
        pmt = b.op("vector", lambda e: e.tensor_copy(pm_stage[:, :],
                                                     pmean_ps[:, :]),
                   waits=[b.mark("pe")])
        b.dma("sync", pm_d[:, :], pm_stage[:, :], "pms", waits=[pmt])



def attn_unit(b, env, q_ap_full, v2_base, out_write, qb, k_sb):
    (mask_sb, maskt_sb, pt_sb, ident, nc20_ap, cc_tok, psA, accs,
     half_free, pt_free, acc_free) = env
    nkb = qb + 1
    nkc = nkb * 4
    dkb = qb
    av_tok = [None, None, None, None]
    started = [False] * 4
    waves = [list(range(w0, min(w0 + 2, nkc))) for w0 in range(0, nkc, 2)]
    for w, kcs in enumerate(waves):
        hh = w % 2
        ptok = None
        for kc in kcs:
            dmask = kc >= dkb * 4
            last = kc == kcs[-1]
            o_ap = psA[:, hh * 1024 + (kc % 2) * HALF:
                       hh * 1024 + ((kc % 2) + 1) * HALF]
            ptok = b.pe(lambda e, l=k_sb[:, kc * 128:(kc + 1) * 128],
                        r=q_ap_full[:, qb * HALF:(qb + 1) * HALF],
                        o=o_ap, st=(not dmask):
                        e.matmul(o, l, r, start=True, stop=st,
                                 skip_group_check=True),
                        waits=([half_free[hh]] if kc == kcs[0] else ()),
                        sem="pe" if (last and not dmask) else None)
            if dmask:
                r2i = kc - dkb * 4
                qt0 = (kc % 4) if False else r2i
                ptok = b.pe(lambda e, m=maskt_sb[:, r2i * HALF:(r2i + 1) * HALF],
                            o=o_ap:
                            e.matmul(o, ident[:, :], m, start=False, stop=True,
                                     skip_group_check=True),
                            sem="pe" if last else None)
        nk_w = len(kcs) * HALF
        et = b.op("scalar", lambda e, o=pt_sb[:, hh * 1024:hh * 1024 + nk_w],
                  i=psA[:, hh * 1024:hh * 1024 + nk_w]:
                  e.activation(o, i, AF.Exp, bias=nc20_ap),
                  waits=[ptok, pt_free[hh], cc_tok], chain=False)
        half_free[hh] = et
        lastw = w == len(waves) - 1
        atk = None
        for j, kc in enumerate(kcs):
            for qt in range(4):
                w_ = []
                if j == 0 and qt == 0:
                    w_.append(et)
                if not started[qt]:
                    w_.append(acc_free[qt])
                atk = b.pe(lambda e,
                           l=pt_sb[:, hh * 1024 + (kc % 2) * HALF + qt * 128:
                                   hh * 1024 + (kc % 2) * HALF + (qt + 1) * 128],
                           r=v2_base[:, kc * 129:(kc + 1) * 129],
                           o=accs[qt][:, 0:129], s=(not started[qt]),
                           st=(kc == nkc - 1):
                           e.matmul(o, l, r, start=s, stop=st,
                                    skip_group_check=True),
                           waits=w_,
                           sem="pe" if (j == len(kcs) - 1 and qt == 3) else None)
                started[qt] = True
        pt_free[hh] = atk
    return out_write(accs, atk)


def build_l2a():
    nc = bass.Bass("TRN2", target_bir_lowering=False, debug=False)
    ctx = ExitStack()
    b = Bld(nc, ctx)
    q_d = nc.dram_tensor("q_fm", [512, 2 * T], BF16, kind="ExternalInput").ap()
    k_d = nc.dram_tensor("k_fm", [128, 2 * T], BF16, kind="ExternalInput").ap()
    v_d = nc.dram_tensor("v_tm", [2 * T, 128], BF16, kind="ExternalInput").ap()
    mask_d = nc.dram_tensor("mask_qk_b", [HALF, HALF], BF16, kind="ExternalInput").ap()
    maskt_d = nc.dram_tensor("maskt_kq_b", [HALF, HALF], BF16, kind="ExternalInput").ap()
    ident_d = nc.dram_tensor("ident", [128, 128], BF16, kind="ExternalInput").ap()
    attn_d = nc.dram_tensor("attn_tm", [2 * T, 512], BF16, kind="ExternalOutput").ap()

    q_sb = b.sb("q_sb", [128, 8 * T], BF16)
    k_sb = b.sb("k_sb", [128, 2 * T], BF16)
    v2_sb = b.sb("v2_sb", [128, 64 * 129], BF16)
    mask_sb = b.sb("mask_sb", [128, 4 * HALF], BF16)
    maskt_sb = b.sb("maskt_sb", [128, 4 * HALF], BF16)
    ident = b.sb("ident_sb", [128, 128], BF16)
    pt_sb = b.sb("pt_sb", [128, 2048], BF16)
    rcol = b.sb("rcol", [128, 1], F32)
    ostage = b.sb("ostage", [128, 4 * 128], BF16)

    psA = b.psum("psA", [128, 2048], F32)
    accs = [b.psum(f"acc{i}", [128, 132], F32) for i in range(4)]

    b.dma("sync", ident[:, :], ident_d[:, :], "cld")
    for r in range(4):
        b.dma("sync", mask_sb[:, r * HALF:(r + 1) * HALF],
              mask_d[r * 128:(r + 1) * 128, :], "cld")
        b.dma("sync", maskt_sb[:, r * HALF:(r + 1) * HALF],
              maskt_d[r * 128:(r + 1) * 128, :], "cld")
    b.dma("sync", k_sb[:, :], k_d[:, :], "cld")
    v2v = v2_sb.ap().rearrange("p (t c) -> p t c", c=129)
    b.dma("sync", v2v[:, :, 0:128],
          v_d[:, :].rearrange("(t p) d -> p t d", p=128), "cld")
    for ch in range(8):
        b.dma("sync", q_sb[:, ch * T:(ch + 1) * T],
              q_d[(ch // 2) * 128:(ch // 2 + 1) * 128,
                  (ch % 2) * T:(ch % 2 + 1) * T], "cld")
    consts = b.mark("cld")
    ones_tok = b.op("vector", lambda e: e.memset(v2v[:, :, 128:129], 1.0),
                    waits=[consts])
    b.wait("tensor", [consts, ones_tok])
    b.wait("vector", [consts])
    b.wait("scalar", [consts])
    nc20_ap, t2 = b.constcol(-20.0)
    cc_tok = b.mark("ve")
    half_free = [None, None]
    pt_free = [None, None]
    acc_free = [None] * 4
    ostage_free = [None] * 4

    env = (mask_sb, maskt_sb, pt_sb, ident, nc20_ap, cc_tok, psA, accs,
           half_free, pt_free, acc_free)

    for ch in range(8):
        qh, bt = ch // 2, ch % 2
        q_full = q_sb[:, ch * T:(ch + 1) * T]
        kk = k_sb[:, bt * T:(bt + 1) * T]
        v2b = v2_sb[:, bt * 32 * 129:(bt + 1) * 32 * 129]
        for qb in range(8):
            def out_write(accs_, atk, ch=ch, qb=qb, qh=qh, bt=bt):
                toks = []
                for qt in range(4):
                    rc = b.op("vector", lambda e, qt=qt: e.reciprocal(
                        rcol[:, :], accs_[qt][:, 128:129]), waits=[atk])
                    ev = b.op("vector", lambda e, qt=qt: e.tensor_scalar(
                        ostage[:, qt * 128:(qt + 1) * 128],
                        accs_[qt][:, 0:128], rcol[:, :], None, ALU.mult,
                        ALU.bypass), waits=[ostage_free[qt]])
                    acc_free[qt] = ev
                    d_ = b.dma("sync",
                               attn_d[bt * T + qb * HALF + qt * 128:
                                      bt * T + qb * HALF + (qt + 1) * 128,
                                      qh * 128:(qh + 1) * 128],
                               ostage[:, qt * 128:(qt + 1) * 128],
                               f"ao{qt}", waits=[ev])
                    ostage_free[qt] = d_
                    toks.append(d_)
                return toks[-1]
            attn_unit(b, env, q_full, v2b, out_write, qb, kk)

    b.wait("sync", [b.mark(s) for s in list(b.cnt) if s.startswith("ao")])
    b.finalize()
    return nc, ctx



def build_l2b():
    nc = bass.Bass("TRN2", target_bir_lowering=False, debug=False)
    ctx = ExitStack()
    b = Bld(nc, ctx)
    a_d = nc.dram_tensor("attn_fm", [4096, TPC], BF16, kind="ExternalInput").ap()
    x_d = nc.dram_tensor("x", [TPC, D], F32, kind="ExternalInput").ap()
    wo_d = nc.dram_tensor("w_o", [D, D], F32, kind="ExternalInput").ap()
    ident_d = nc.dram_tensor("ident", [128, 128], BF16, kind="ExternalInput").ap()
    ones_d = nc.dram_tensor("ones_col", [128, 1], BF16, kind="ExternalInput").ap()
    x1_d = nc.dram_tensor("x1", [TPC, D], F32, kind="ExternalOutput").ap()
    pm_d = nc.dram_tensor("pmean", [128, NKC], F32, kind="ExternalOutput").ap()
    wo_scr = nc.dram_tensor("wo_scr", [D, TPC], BF16)

    a_sb = b.sb("a_sb", [128, NKC * TPC], BF16)
    ident = b.sb("ident_sb", [128, 128], BF16)
    ones_sb = b.sb("ones_sb", [128, 1], BF16)
    w_sb = b.sb("w_sb", [128, 2 * 4 * 128 * NKC], BF16)
    wstage = b.sb("wstage", [128, 4 * HALF], BF16)
    wo_tile = b.sb("wo_tile", [128, D], BF16)
    x_stage = b.sb("x_stage", [128, D], F32)
    stats = b.sb("stats", [128, 48], F32)
    mv = b.sb("mv", [128, 2], F32)
    rstd = b.sb("rstd", [128, 1], F32)
    ln_t = b.sb("ln_t", [128, D], BF16)
    pm_stage = b.sb("pm_stage", [128, NKC], F32)

    pb = [b.psum(f"pb{i}", [128, 512], F32) for i in range(4)]
    tbp = [b.psum(f"tbp{i}", [128, 512], BF16) for i in range(2)]
    pmean_ps = b.psum("pmean_ps", [128, NKC], F32)

    b.dma("sync", ident[:, :], ident_d[:, :], "cld")
    b.dma("sync", ones_sb[:, :], ones_d[:, :], "cld")
    for ch in range(NKC):
        b.dma("sync", a_sb[:, ch * TPC:(ch + 1) * TPC],
              a_d[ch * 128:(ch + 1) * 128, :], "cld")
    consts = b.mark("cld")
    b.wait("tensor", [consts])
    b.wait("vector", [consts])
    b.wait("scalar", [consts])

    wo_banks = Banks(b, "wob", 0, handles=pb[0:4])

    def wo_rhs(k, half):
        return a_sb[:, k * TPC + half * HALF: k * TPC + half * HALF + HALF]

    wo_done = proj_to_scr(b, wo_banks, wo_d, wo_rhs, NKC, wo_scr, w_sb, wstage)
    ttb = Banks(b, "ttb", 0, handles=tbp)
    resid_tile_loop(b, wo_done, wo_scr, x_d, x1_d, ident, ttb, wo_tile, x_stage,
                    ln_extras=(stats, mv, rstd, ln_t, ones_sb, pmean_ps,
                               pm_stage, pm_d))
    b.wait("sync", [b.mark(s) for s in list(b.cnt)
                    if s.startswith(("x1s", "pms"))])
    b.finalize()
    return nc, ctx


def build_l3():
    nc = bass.Bass("TRN2", target_bir_lowering=False, debug=False)
    ctx = ExitStack()
    b = Bld(nc, ctx)
    x1_d = nc.dram_tensor("x1", [TPC, D], F32, kind="ExternalInput").ap()
    pms_d = nc.dram_tensor("pmeans", [128, 4 * NKC], F32, kind="ExternalInput").ap()
    worg_d = nc.dram_tensor("w_org_t", [D, D], F32, kind="ExternalInput").ap()
    wdown_d = nc.dram_tensor("w_down", [D, 512], F32, kind="ExternalInput").ap()
    wup_d = nc.dram_tensor("w_up", [512, D], F32, kind="ExternalInput").ap()
    keys_d = nc.dram_tensor("keys_t", [128, 8], F32, kind="ExternalInput").ap()
    wqm_d = nc.dram_tensor("wq_mole_t", [D, 128], F32, kind="ExternalInput").ap()
    exp8_d = nc.dram_tensor("exp8", [8, 512], F32, kind="ExternalInput").ap()
    ident_d = nc.dram_tensor("ident", [128, 128], BF16, kind="ExternalInput").ap()
    identf_d = nc.dram_tensor("ident_f", [128, 128], F32, kind="ExternalInput").ap()
    x2_d = nc.dram_tensor("x2", [TPC, D], F32, kind="ExternalOutput").ap()
    scr_d = nc.dram_tensor("ffn_scr", [D, TPC], BF16)

    lnx_fm = b.sb("lnx_fm", [128, NKC * TPC], BF16)
    h_fm = b.sb("h_fm", [128, 4 * TPC], BF16)
    x_stage = b.sb("x_stage", [128, 2 * D], F32)
    ln_stage = b.sb("ln_stage", [128, 2 * D], BF16)
    stats = b.sb("stats", [128, 48], F32)
    mv = b.sb("mv", [128, 2], F32)
    rstd = b.sb("rstd", [128, 1], F32)
    ident = b.sb("ident_sb", [128, 128], BF16)
    identf = b.sb("identf_sb", [128, 128], F32)
    keys_sb = b.sb("keys_sb", [128, 8], BF16)
    wqm_sb = b.sb("wqm_sb", [128, NKC * 128], BF16)
    exp8_sb = b.sb("exp8_sb", [8, 512], F32)
    pms_sb = b.sb("pms_sb", [128, 4 * NKC], F32)
    mean_sb = b.sb("mean_sb", [128, NKC], F32)
    meanb_sb = b.sb("meanb_sb", [128, NKC], BF16)
    qv_sb = b.sb("qv_sb", [128, 1], BF16)
    scol = b.sb("scol", [8, 1], F32)
    srow = b.sb("srow", [1, 8], F32)
    urow = b.sb("urow", [1, 8], F32)
    smax = b.sb("smax", [1, 2], F32)
    sexp = b.sb("sexp", [1, 8], F32)
    ssum = b.sb("ssum", [1, 2], F32)
    crow8 = b.sb("crow8", [1, 8], F32)
    ccolv = b.sb("ccolv", [8, 1], F32)
    ecol = b.sb("ecol", [128, 4], F32)
    w_sb = b.sb("w_sb", [128, 2 * 2 * 128 * (NKC + 4)], BF16)
    wstage = b.sb("wstage", [128, 4 * HALF], BF16)
    scr_tile = b.sb("scr_tile", [128, D], BF16)

    pb = [b.psum(f"pb{i}", [128, 512], F32) for i in range(4)]
    tbp = [b.psum(f"tbp{i}", [128, 512], BF16) for i in range(2)]
    sm_ps = b.psum("sm_ps", [128, 16], F32)

    b.dma("sync", ident[:, :], ident_d[:, :], "cld")
    b.dma("sync", identf[:, :], identf_d[:, :], "cld")
    b.dma("gpsimd", keys_sb[:, :], keys_d[:, :], "cldg")
    b.dma("gpsimd", wqm_sb[:, :].rearrange("p (k q) -> p k q", q=128),
          wqm_d[:, :].rearrange("(k p) q -> p k q", p=128), "cldg")
    b.dma("sync", exp8_sb[:, :], exp8_d[:, :], "cld")
    b.dma("sync", pms_sb[:, :], pms_d[:, :], "cld")
    consts = b.mark("cld")
    constsg = b.mark("cldg")
    b.wait("tensor", [consts, constsg])
    b.wait("vector", [consts, constsg])
    b.wait("scalar", [consts, constsg])

    tbanks = Banks(b, "tbk", 0, handles=tbp)
    lnfm_done = ln_stream_to_fm(b, x1_d, lnx_fm, TPC, x_stage, ln_stage, stats,
                                mv, rstd, tbanks, ident[:, :], consts)
    b.wait("tensor", [lnfm_done])

    b.op("vector", lambda e: e.tensor_tensor(
        mean_sb[:, :], pms_sb[:, 0:NKC], pms_sb[:, NKC:2 * NKC], ALU.add))
    b.op("vector", lambda e: e.tensor_tensor(
        mean_sb[:, :], mean_sb[:, :], pms_sb[:, 2 * NKC:3 * NKC], ALU.add))
    b.op("vector", lambda e: e.tensor_tensor(
        mean_sb[:, :], mean_sb[:, :], pms_sb[:, 3 * NKC:4 * NKC], ALU.add))
    b.op("vector", lambda e: e.tensor_scalar(
        meanb_sb[:, :], mean_sb[:, :], 1.0 / T, None, ALU.mult, ALU.bypass))
    mtok = b.mark("ve")
    qtok = None
    for k in range(NKC):
        qtok = b.pe(lambda e, l=wqm_sb[:, k * 128:(k + 1) * 128],
                    r=meanb_sb[:, k:k + 1], o=sm_ps[:, 0:1],
                    s=(k == 0), st=(k == NKC - 1):
                    e.matmul(o, l, r, start=s, stop=st, skip_group_check=True),
                    waits=[mtok] if k == 0 else (),
                    sem="pe" if k == NKC - 1 else None)
    b.op("vector", lambda e: e.tensor_copy(qv_sb[:, :], sm_ps[:, 0:1]),
         waits=[qtok])
    stok = b.pe(lambda e: e.matmul(sm_ps[0:8, 1:2], keys_sb[:, :], qv_sb[:, :],
                                   start=True, stop=True, skip_group_check=True),
                waits=[b.mark("ve")], sem="pe")
    b.op("vector", lambda e: e.tensor_copy(scol[:, :], sm_ps[0:8, 1:2]),
         waits=[stok])
    t1 = b.pe(lambda e: e.transpose(sm_ps[0:1, 2:10], scol[:, :],
                                    identf[0:8, 0:8]),
              waits=[b.mark("ve")], sem="pe")
    b.op("vector", lambda e: e.tensor_copy(srow[:, :], sm_ps[0:1, 2:10]),
         waits=[t1])
    b.op("vector", lambda e: e.tensor_scalar(
        urow[:, :], srow[:, :], 1.0 / np.sqrt(128.0), None, ALU.mult, ALU.bypass))
    b.op("vector", lambda e: e.reduce_max(smax[:, 0:1], urow[:, :],
                                          axis=mybir.AxisListType.X))
    b.op("vector", lambda e: e.tensor_scalar(
        smax[:, 1:2], smax[:, 0:1], -1.0, None, ALU.mult, ALU.bypass))
    b.op("scalar", lambda e: e.activation(sexp[:, :], urow[:, :], AF.Exp,
                                          bias=smax[:, 1:2],
                                          accum_out=ssum[:, 0:1]),
         waits=[b.mark("ve")])
    b.op("vector", lambda e: e.reciprocal(ssum[:, 1:2], ssum[:, 0:1]),
         waits=[b.mark("se")])
    b.op("vector", lambda e: e.tensor_scalar(
        crow8[:, :], sexp[:, :], ssum[:, 1:2], None, ALU.mult, ALU.bypass))
    t2 = b.pe(lambda e: e.transpose(sm_ps[0:8, 10:11], crow8[:, :],
                                    identf[0:1, 0:1]),
              waits=[b.mark("ve")], sem="pe")
    b.op("vector", lambda e: e.tensor_copy(ccolv[:, :], sm_ps[0:8, 10:11]),
         waits=[t2])
    ec_tok = None
    for j in range(4):
        ej = b.pe(lambda e, j=j: e.matmul(sm_ps[:, 11 + j:12 + j],
                                          exp8_sb[:, j * 128:(j + 1) * 128],
                                          ccolv[:, :], start=True, stop=True,
                                          skip_group_check=True),
                  waits=[b.mark("ve")] if j == 0 else (), sem="pe")
        ec_tok = b.op("vector", lambda e, j=j: e.tensor_copy(
            ecol[:, j:j + 1], sm_ps[:, 11 + j:12 + j]), waits=[ej])

    hbanks = Banks(b, "hbk", 0, handles=pb[0:4])

    def lnx_rhs(k, half):
        return lnx_fm[:, k * TPC + half * HALF: k * TPC + half * HALF + HALF]

    def h_cb(oi, half, bank, ptok):
        vt = b.op("vector", lambda e, o=h_fm[:, oi * TPC + half * HALF:
                                             oi * TPC + half * HALF + HALF],
                  i=bank[:, :], j=oi: e.tensor_scalar(
                      o, i, ecol[:, j:j + 1], None, ALU.mult, ALU.bypass),
                  waits=[ptok, ec_tok])
        return vt

    proj_fm(b, hbanks, wdown_d, lnx_rhs, 4, h_cb, w_sb, obs=2)
    h_done = b.mark("ve")
    b.wait("tensor", [h_done])

    def h_rhs(k, half):
        return h_fm[:, k * TPC + half * HALF: k * TPC + half * HALF + HALF]

    obanks = Banks(b, "obk", 0, handles=pb[0:4])
    b.wait("gpsimd", [b.mark("pe")])
    scr_done = proj_to_scr(b, obanks, worg_d, lnx_rhs, NKC, scr_d, w_sb,
                           wstage, obs=2, extra_k=(4, wup_d, h_rhs))
    ttb = Banks(b, "ttb", 0, handles=tbp)
    resid_tile_loop(b, scr_done, scr_d, x1_d, x2_d, ident, ttb, scr_tile,
                    x_stage[:, 0:D])

    b.wait("sync", [b.mark(s) for s in list(b.cnt) if s.startswith("x1s")])
    b.finalize()
    return nc, ctx


L4W = TPC + 128
NPC = 81


def build_l4a():
    nc = bass.Bass("TRN2", target_bir_lowering=False, debug=False)
    ctx = ExitStack()
    b = Bld(nc, ctx)
    x2_d = nc.dram_tensor("x2", [TPC, D], F32, kind="ExternalInput").ap()
    halo_d = nc.dram_tensor("halo", [3, D], F32, kind="ExternalInput").ap()
    w_d = nc.dram_tensor("w_inproj_p", [D, NPC * 128], F32, kind="ExternalInput").ap()
    ident_d = nc.dram_tensor("ident", [128, 128], BF16, kind="ExternalInput").ap()
    p_d = nc.dram_tensor("pout", [NPC * 128, L4W], BF16, kind="ExternalOutput").ap()

    lnx_fm = b.sb("lnx_fm", [128, NKC * L4W], BF16)
    x_stage = b.sb("x_stage", [128, 2 * D], F32)
    ln_stage = b.sb("ln_stage", [128, 2 * D], BF16)
    stats = b.sb("stats", [128, 48], F32)
    mv = b.sb("mv", [128, 2], F32)
    rstd = b.sb("rstd", [128, 1], F32)
    ident = b.sb("ident_sb", [128, 128], BF16)
    w_sb = b.sb("w_sb", [128, 2 * 2 * 128 * NKC], BF16)
    wstage = b.sb("wstage", [128, 4 * 384], BF16)

    pb = [b.psum(f"pb{i}", [128, 512], F32) for i in range(4)]
    tbp = [b.psum(f"tbp{i}", [128, 512], BF16) for i in range(2)]

    b.dma("sync", ident[:, :], ident_d[:, :], "cld")
    consts = b.mark("cld")
    b.wait("tensor", [consts])
    b.wait("vector", [consts])
    b.wait("scalar", [consts])

    tbanks = Banks(b, "tbk", 0, handles=tbp)
    z0 = b.op("vector", lambda e: e.memset(x_stage[:, 0:D], 0.0))
    hl = b.dma("sync", x_stage[125:128, 0:D], halo_d[:, :], "hld", waits=[z0])
    htok = ln_tile(b, x_stage[:, 0:D], ln_stage[:, 0:D], stats, mv, rstd, hl)
    for f0 in range(0, NKC, 4):
        bi, bank, war = tbanks.take()
        ptok = None
        for g in range(4):
            fi = f0 + g
            ptok = b.pe(lambda e, s=ln_stage[:, fi * 128:(fi + 1) * 128],
                        o=bank[:, g * 128:(g + 1) * 128]:
                        e.transpose(o, s, ident[:, :]),
                        waits=([war, htok, consts] if g == 0 else ()),
                        sem="pe" if g == 3 else None)
        for g in range(4):
            fi = f0 + g
            vt = b.op("vector", lambda e, o=lnx_fm[:, fi * L4W: fi * L4W + 128],
                      i=bank[:, g * 128:(g + 1) * 128]: e.tensor_copy(o, i),
                      waits=[ptok] if g == 0 else ())
        tbanks.set_free(bi, vt)
    def fmoff(fi, tj):
        return fi * L4W + 128 + tj * 128
    nt = TPC // 128
    last_use = [b.mark("ve"), None]
    for tj in range(nt):
        sl = tj % 2
        xs = x_stage[:, sl * D:(sl + 1) * D]
        lt = ln_stage[:, sl * D:(sl + 1) * D]
        ld = b.dma("sync", xs, x2_d[tj * 128:(tj + 1) * 128, :], f"xld{sl}",
                   waits=[last_use[sl]])
        aptok = ln_tile(b, xs, lt, stats, mv, rstd, ld)
        for f0 in range(0, NKC, 4):
            bi, bank, war = tbanks.take()
            ptok = None
            for g in range(4):
                fi = f0 + g
                ptok = b.pe(lambda e, s=lt[:, fi * 128:(fi + 1) * 128],
                            o=bank[:, g * 128:(g + 1) * 128]:
                            e.transpose(o, s, ident[:, :]),
                            waits=([war, aptok] if g == 0 else ()),
                            sem="pe" if g == 3 else None)
            for g in range(4):
                fi = f0 + g
                vt = b.op("vector", lambda e, o=lnx_fm[:, fmoff(fi, tj):
                                                       fmoff(fi, tj) + 128],
                          i=bank[:, g * 128:(g + 1) * 128]: e.tensor_copy(o, i),
                          waits=[ptok] if g == 0 else ())
            tbanks.set_free(bi, vt)
        last_use[sl] = b.mark("ve")
    b.wait("tensor", [b.mark("ve")])

    banks = Banks(b, "bk", 0, handles=pb)

    def rhs_view(k, third):
        return lnx_fm[:, k * L4W + third * 384: k * L4W + (third + 1) * 384]

    proj_to_scr(b, banks, w_d, rhs_view, NPC, p_d, w_sb, wstage, obs=2,
                halves=(0, 1, 2), hw=384)

    b.wait("sync", [b.mark(s) for s in list(b.cnt) if s.startswith("wo")])
    b.finalize()
    return nc, ctx


def build_l4b():
    nc = bass.Bass("TRN2", target_bir_lowering=False, debug=False)
    ctx = ExitStack()
    b = Bld(nc, ctx)
    xbc_d = nc.dram_tensor("xbc_fm", [CONV_DIM, L4W], BF16, kind="ExternalInput").ap()
    dt_d = nc.dram_tensor("dt_fm", [64, L4W], BF16, kind="ExternalInput").ap()
    cw_d = nc.dram_tensor("conv_w_c", [128, 192], F32, kind="ExternalInput").ap()
    cb_d = nc.dram_tensor("conv_b_c", [128, 48], F32, kind="ExternalInput").ap()
    dtb_d = nc.dram_tensor("dt_bias_c", [64, 1], F32, kind="ExternalInput").ap()
    al_d = nc.dram_tensor("a_log_c", [64, 1], F32, kind="ExternalInput").ap()
    dr_d = nc.dram_tensor("d_row", [1, 4096], F32, kind="ExternalInput").ap()
    rep_d = nc.dram_tensor("rep", [64, 4096], BF16, kind="ExternalInput").ap()
    bm_d = nc.dram_tensor("maskneg128", [128, 128], F32, kind="ExternalInput").ap()
    ident_d = nc.dram_tensor("ident", [128, 128], BF16, kind="ExternalInput").ap()
    identf_d = nc.dram_tensor("ident_f", [128, 128], F32, kind="ExternalInput").ap()
    onesr_d = nc.dram_tensor("ones_row_f", [1, 128], F32, kind="ExternalInput").ap()
    onesb_d = nc.dram_tensor("ones_row_b", [1, 128], BF16, kind="ExternalInput").ap()
    y_d = nc.dram_tensor("y_tm", [TPC, 4096], BF16, kind="ExternalOutput").ap()
    cfm_d = nc.dram_tensor("c_fm", [1024, TPC], BF16, kind="ExternalOutput").ap()
    ccl_d = nc.dram_tensor("coreclg", [64, TPC], F32, kind="ExternalOutput").ap()
    sc_d = nc.dram_tensor("s_core", [128, 4096], BF16, kind="ExternalOutput").ap()

    x_tm = b.sb("x_tm", [128, 8 * 4096], BF16)
    bc_sb = b.sb("bc_sb", [128, 16 * 1024], BF16)
    b_tm = b.sb("b_tm", [128, 16 * 128], BF16)
    dtx_sb = b.sb("dtx_sb", [128, 2 * 4096], BF16)
    r_sb = b.sb("r_sb", [128, 4096], BF16)

    drep_sb = b.sb("drep_sb", [128, 4096], BF16)
    cum_sb = b.sb("cum_sb", [128, 512], BF16)
    xstg = b.sb("xstg", [128, 2 * L4W], BF16)
    conv_sb = b.sb("conv_sb", [128, 1024], F32)
    cw_sb = b.sb("cw_sb", [128, 192], F32)
    cbb_sb = b.sb("cbb_sb", [128, 48], F32)
    dt_sb = b.sb("dt_sb", [64, L4W], BF16)
    u_sb = b.sb("u_sb", [64, 1024], F32)

    dtp_sb = b.sb("dtp_sb", [64, 1024], F32)
    dtpb_sb = b.sb("dtpb_sb", [64, 1024], BF16)
    logda = b.sb("logda", [64, 1024], F32)
    u2_sb = logda
    clg_sb = b.sb("clg_sb", [64, 256], F32)
    clgb_sb = b.sb("clgb_sb", [64, 256], BF16)
    ccl_sb = b.sb("ccl_sb", [64, 256], F32)
    off_sb = b.sb("off_sb", [64, 2], F32)
    es_sb = b.sb("es_sb", [64, 256], F32)
    esb_sb = b.sb("esb_sb", [64, 256], BF16)
    clgq_sb = b.sb("clgq_sb", [64, 2], F32)
    clgqb_sb = b.sb("clgqb_sb", [64, 1], BF16)
    nclgtm_sb = b.sb("nclgtm_sb", [128, 2 * 64], F32)
    rl_sb = b.sb("rl_sb", [1, 512], F32)
    trow8_sb = b.sb("trow8_sb", [1, 2048], F32)
    clgt8_sb = b.sb("clgt8_sb", [128, 2048], F32)
    dtt8_sb = b.sb("dtt8_sb", [128, 1024], F32)
    cb_sb = b.sb("cb_sb", [128, 256], BF16)
    corr_sb = b.sb("corr_sb", [128, 512], BF16)
    ya_sb = b.sb("ya_sb", [128, 512], BF16)
    yb_sb = b.sb("yb_sb", [128, 512], BF16)
    t1_sb = b.sb("t1_sb", [128, 512], BF16)
    bias_a = b.sb("bias_a", [64, 2], F32)
    binm_sb = b.sb("maskneg_sb", [128, 128], F32)
    ident = b.sb("ident_sb", [128, 128], BF16)
    identf = b.sb("identf_sb", [128, 128], F32)
    onesr = b.sb("onesr_sb", [1, 128], F32)
    drow_sb = b.sb("drow_sb", [1, 4096], BF16)
    onesb = b.sb("onesb_sb", [1, 128], BF16)
    rep_sb = b.sb("rep_sb", [64, 4096], BF16)

    dec8_sb = xstg
    g8_sb = conv_sb.ap().bitcast(BF16)
    lam_sb = clgt8_sb.ap().bitcast(BF16)
    psY = [b.psum(f"psY{i}", [128, 512], F32) for i in range(2)]
    psC = b.psum("psC", [128, 512], F32)
    psS = b.psum("psS", [128, 512], F32)
    psM = [b.psum(f"psM{i}", [128, 512], F32) for i in range(2)]
    psT = [b.psum(f"psT{i}", [128, 512], BF16) for i in range(2)]

    for ap_, src_ in ((cw_sb, cw_d), (cbb_sb, cb_d), (dtb_d, None)):
        pass
    b.dma("sync", cw_sb[:, :], cw_d[:, :], "cld")
    b.dma("sync", cbb_sb[:, :], cb_d[:, :], "cld")
    b.dma("sync", bias_a[:, 0:1], dtb_d[:, :], "cld")
    b.dma("sync", bias_a[:, 1:2], al_d[:, :], "cld")
    b.dma("gpsimd", drow_sb[:, :], dr_d[:, :], "cldg")
    b.dma("sync", onesb[:, :], onesb_d[:, :], "cld")
    b.dma("sync", rep_sb[:, :], rep_d[:, :], "cld")
    b.dma("sync", binm_sb[:, :], bm_d[:, :], "cld")
    b.dma("sync", ident[:, :], ident_d[:, :], "cld")
    b.dma("sync", identf[:, :], identf_d[:, :], "cld")
    b.dma("sync", onesr[:, :], onesr_d[:, :], "cld")
    b.dma("sync", dt_sb[:, :], dt_d[:, :], "cld")
    consts = b.mark("cld")
    constsg = b.mark("cldg")
    b.wait("tensor", [consts, constsg])
    b.wait("vector", [consts, constsg])
    b.wait("scalar", [consts, constsg])

    tb = Banks(b, "tb", 0, handles=psT)

    dtok = None
    for rg in range(8):
        pm = b.pe(lambda e, rg=rg: e.matmul(psM[0][:, :], onesb[:, :],
                                            drow_sb[:, rg * 512:(rg + 1) * 512],
                                            start=True, stop=True),
                  waits=[b.mark("ve")] if rg else [consts], sem="pe")
        dtok = b.op("vector", lambda e, rg=rg: e.tensor_copy(
            drep_sb[:, rg * 512:(rg + 1) * 512], psM[0][:, :]), waits=[pm])

    stg_free = [None, None]
    for ch in range(48):
        sl = ch % 2
        stg = xstg[:, sl * L4W:(sl + 1) * L4W]
        ld = b.dma("sync", stg, xbc_d[ch * 128:(ch + 1) * 128, :], f"xb{sl}",
                   waits=[stg_free[sl]])
        cv = b.op("vector", lambda e, stg=stg, ch=ch: e.tensor_scalar(
            conv_sb[:, :], stg[:, 125:1149], cw_sb[:, ch * 4:ch * 4 + 1],
            cbb_sb[:, ch:ch + 1], ALU.mult, ALU.add),
            waits=[ld, b.mark("se")])
        for tap in range(1, 4):
            cv = b.op("vector", lambda e, stg=stg, ch=ch, tap=tap:
                      e.scalar_tensor_tensor(
                          conv_sb[:, :], stg[:, 125 + tap:1149 + tap],
                          cw_sb[:, ch * 4 + tap:ch * 4 + tap + 1],
                          conv_sb[:, :], ALU.mult, ALU.add))
        stg_free[sl] = cv
        if ch < 32:
            sx = b.op("scalar", lambda e: e.activation(
                xstg[:, sl * L4W:sl * L4W + 1024] if False else conv_sb[:, :],
                conv_sb[:, :], AF.Silu), waits=[b.mark("ve")])
            cbf = b.op("vector", lambda e: e.tensor_copy(
                dtx_sb[:, 0:1024], conv_sb[:, :]), waits=[b.mark("se")])
            for tj0 in range(0, 8, 4):
                bi, bank, war = tb.take()
                ptk = None
                for g2 in range(4):
                    tj = tj0 + g2
                    ptk = b.pe(lambda e, s=dtx_sb[:, tj * 128:(tj + 1) * 128],
                               o=bank[:, g2 * 128:(g2 + 1) * 128]:
                               e.transpose(o, s, ident[:, :]),
                               waits=([war, cbf] if g2 == 0 else ()),
                               sem="pe" if g2 == 3 else None)
                for g2 in range(4):
                    tj = tj0 + g2
                    vt = b.op("vector", lambda e, o=x_tm[:, (tj0 + g2) * 4096 + ch * 128:
                                                         (tj0 + g2) * 4096 + (ch + 1) * 128],
                              i=bank[:, g2 * 128:(g2 + 1) * 128]:
                              e.tensor_copy(o, i),
                              waits=[ptk] if g2 == 0 else ())
                tb.set_free(bi, vt)
        else:
            bci = ch - 32
            sx = b.op("scalar", lambda e, bci=bci: e.activation(
                bc_sb[:, bci * 1024:(bci + 1) * 1024], conv_sb[:, :], AF.Silu),
                waits=[b.mark("ve")])
            stg_free[sl] = sx
            if ch >= 40:
                b.dma("sync", cfm_d[(ch - 40) * 128:(ch - 39) * 128, :],
                      bc_sb[:, bci * 1024:(bci + 1) * 1024], "cso",
                      waits=[sx])
    conv_done_v = b.mark("ve")
    conv_done_s = b.mark("se")

    b.op("vector", lambda e: e.tensor_scalar(
        u_sb[:, :], dt_sb[:, 128:1152], bias_a[:, 0:1], None, ALU.add,
        ALU.bypass), waits=[consts])
    b.op("scalar", lambda e: e.activation(u2_sb[:, :], u_sb[:, :], AF.Abs),
         waits=[b.mark("ve")])
    b.op("scalar", lambda e: e.activation(u2_sb[:, :], u2_sb[:, :], AF.Exp,
                                          scale=-1.0))
    one_ap, one_tok = b.constcol(1.0)
    b.op("scalar", lambda e: e.activation(u2_sb[:, :], u2_sb[:, :], AF.Ln,
                                          bias=one_ap[0:64, :]),
         waits=[one_tok])
    b.op("vector", lambda e: e.tensor_scalar(
        dtp_sb[:, :], u_sb[:, :], 0.0, None, ALU.max, ALU.bypass))
    b.op("vector", lambda e: e.tensor_tensor(
        dtp_sb[:, :], dtp_sb[:, :], u2_sb[:, :], ALU.add),
        waits=[b.mark("se")])
    b.op("vector", lambda e: e.tensor_copy(dtpb_sb[:, :], dtp_sb[:, :]))
    b.op("scalar", lambda e: e.activation(bias_a[:, 1:2] if False else
                                          off_sb[:, 1:2], bias_a[:, 1:2],
                                          AF.Exp), waits=[b.mark("ve")])
    b.op("vector", lambda e: e.tensor_scalar(
        off_sb[:, 1:2], off_sb[:, 1:2], -1.0, None, ALU.mult, ALU.bypass),
        waits=[b.mark("se")])
    b.op("vector", lambda e: e.tensor_scalar(
        logda[:, :], dtp_sb[:, :], off_sb[:, 1:2], None, ALU.mult, ALU.bypass))
    z_off = b.op("vector", lambda e: e.memset(off_sb[:, 0:1], 0.0))

    r_z = b.op("vector", lambda e: e.memset(r_sb[:, :], 0.0))
    last_yd = [None]
    trow_free = [None]
    clgt8_tok = [None]
    for sc in range(4):
        b.wait("vector", [b.mark("pe")])
        b.op("vector", lambda e, sc=sc: e.tensor_tensor_scan(
            clg_sb[:, :], logda[:, sc * 256:(sc + 1) * 256],
            logda[:, sc * 256:(sc + 1) * 256], 0.0, ALU.add, ALU.bypass))
        b.op("vector", lambda e: e.tensor_scalar(
            ccl_sb[:, :], clg_sb[:, :], off_sb[:, 0:1], None, ALU.add,
            ALU.bypass))
        ccd = b.dma("sync", ccl_d[:, sc * 256:(sc + 1) * 256], ccl_sb[:, :],
                    f"ccs{sc % 2}", waits=[b.mark("ve")])
        b.op("vector", lambda e: e.tensor_copy(off_sb[:, 0:1],
                                               ccl_sb[:, 255:256]),
             waits=[ccd])
        b.op("vector", lambda e: e.tensor_copy(clgq_sb[:, 0:1],
                                               clg_sb[:, 255:256]))
        b.op("vector", lambda e: e.tensor_copy(clgqb_sb[:, :],
                                               clgq_sb[:, 0:1]))
        b.op("vector", lambda e: e.tensor_copy(clgb_sb[:, :], clg_sb[:, :]))
        b.op("scalar", lambda e: e.activation(es_sb[:, :], clg_sb[:, :],
                                              AF.Exp, bias=clgq_sb[:, 0:1],
                                              scale=-1.0),
             waits=[b.mark("ve")])
        b.op("vector", lambda e: e.tensor_copy(esb_sb[:, :], es_sb[:, :]),
             waits=[b.mark("se")])
        for tc in range(2):
            pt = b.pe(lambda e, tc=tc: e.transpose(
                psM[0][:, 0:64], ccl_sb[:, tc * 128:(tc + 1) * 128],
                identf[0:64, 0:64]), waits=[b.mark("ve")], sem="pe")
            b.op("vector", lambda e, tc=tc: e.tensor_scalar(
                nclgtm_sb[:, tc * 64:(tc + 1) * 64], psM[0][:, 0:64], -1.0,
                None, ALU.mult, ALU.bypass), waits=[pt])
        for g in range(8):
            bi, bank, war = tb.take()
            ptk = None
            for a in range(2):
                ptk = b.pe(lambda e, o=bank[:, a * 128:(a + 1) * 128],
                           s=bc_sb[:, g * 1024 + sc * 256 + a * 128:
                                   g * 1024 + sc * 256 + (a + 1) * 128]:
                           e.transpose(o, s, ident[:, :]),
                    waits=([war, conv_done_s, conv_done_v] if a == 0 else ()),
                    sem="pe" if a == 1 else None)
            vt = b.op("vector", lambda e, o=b_tm[:, g * 256:(g + 1) * 256],
                      i=bank[:, 0:256]: e.tensor_copy(o, i),
                      waits=[ptk, b.mark("pe")])
            tb.set_free(bi, vt)
        for a in range(2):
            for g in range(8):
                pm = b.pe(lambda e, a=a, g=g, sc=sc: e.matmul(
                    psM[0][:, :], dtpb_sb[:, sc * 256 + a * 128:
                                          sc * 256 + (a + 1) * 128],
                    rep_sb[:, g * 512:(g + 1) * 512], start=True, stop=True,
                    skip_group_check=True),
                    waits=[b.mark("ve"), b.mark("se")], sem="pe")
                b.op("vector", lambda e, a=a, g=g, sc=sc: e.tensor_tensor(
                    dtx_sb[:, a * 4096 + g * 512:a * 4096 + (g + 1) * 512],
                    x_tm[:, (sc * 2 + a) * 4096 + g * 512:
                         (sc * 2 + a) * 4096 + (g + 1) * 512],
                    psM[0][:, :], ALU.mult), waits=[pm])
        dtx_tok = b.mark("ve")
        for g in range(8):
            trw = b.dma("sync",
                        trow8_sb[:, :].rearrange("a (s c) -> a s c", c=256),
                        ccl_d[g * 8:(g + 1) * 8,
                              sc * 256:(sc + 1) * 256].unsqueeze(0),
                        "trw", waits=[ccd, b.mark("pe"), trow_free[0]])
            ctoks = []
            for i4 in range(4):
                pmc = b.pe(lambda e, i4=i4: e.matmul(
                    psM[1][:, :], onesr[:, :],
                    trow8_sb[:, i4 * 512:(i4 + 1) * 512],
                    start=True, stop=True, skip_group_check=True),
                    waits=[trw, b.mark("ve"), b.mark("se")], sem="pe")
                ctoks.append(b.op("vector", lambda e, i4=i4: e.tensor_copy(
                    clgt8_sb[:, i4 * 512:(i4 + 1) * 512], psM[1][:, :]),
                    waits=([pmc, b.mark("ve")] if i4 == 0 else [pmc]),
                    chain=False))
            clgt8_tok[0] = ctoks[-1]
            trow_free[0] = ctoks[-1]
            for tc in range(2):
                yb = psY[tc % 2]
                pc = b.pe(lambda e, g=g, tc=tc, sc=sc: e.matmul(
                    psC[:, :],
                    bc_sb[:, (8 + g) * 1024 + sc * 256 + tc * 128:
                          (8 + g) * 1024 + sc * 256 + (tc + 1) * 128],
                    r_sb[:, g * 512:(g + 1) * 512], start=True, stop=True,
                    skip_group_check=True),
                    waits=[b.mark("ve"), conv_done_s], sem="pe")
                cevac = b.op("vector", lambda e: e.tensor_copy(
                    corr_sb[:, :], psC[:, :]), waits=[pc])
                pcm = b.pe(lambda e, g=g, tc=tc: e.matmul(
                    psM[0][:, :], clgb_sb[:, tc * 128:(tc + 1) * 128],
                    rep_sb[:, g * 512:(g + 1) * 512], start=True, stop=True,
                    skip_group_check=True),
                    waits=[b.mark("ve"), b.mark("se")], sem="pe")
                cume = b.op("scalar", lambda e: e.activation(
                    cum_sb[:, :], psM[0][:, :], AF.Exp),
                    waits=[pcm, b.mark("ve")])
                cb_toks = [None, None]
                for a in range(tc + 1):
                    pcb = b.pe(lambda e, g=g, a=a, tc=tc, sc=sc: e.matmul(
                        psM[1][:, 0:128],
                        bc_sb[:, g * 1024 + sc * 256 + a * 128:
                              g * 1024 + sc * 256 + (a + 1) * 128],
                        bc_sb[:, (8 + g) * 1024 + sc * 256 + tc * 128:
                              (8 + g) * 1024 + sc * 256 + (tc + 1) * 128],
                        start=True, stop=True, skip_group_check=True),
                        waits=[b.mark("se"), b.mark("ve")], sem="pe")
                    cb_toks[a] = b.op("vector", lambda e, a=a: e.tensor_copy(
                        cb_sb[:, a * 128:(a + 1) * 128], psM[1][:, 0:128]),
                        waits=[pcb])
                dec_toks = [None, None]
                for a in range(tc + 1):
                    add_toks = []
                    for hl in range(8):
                        h = g * 8 + hl
                        at = b.op("vector", lambda e, a=a, hl=hl, h=h, tc=tc:
                                  e.tensor_scalar(
                                      dtt8_sb[:, hl * 128:(hl + 1) * 128],
                                      clgt8_sb[:, hl * 256 + tc * 128:
                                               hl * 256 + (tc + 1) * 128],
                                      nclgtm_sb[:, a * 64 + h:a * 64 + h + 1],
                                      None, ALU.add, ALU.bypass),
                                  waits=([clgt8_tok[0], b.mark("se")]
                                         if hl == 0 else [clgt8_tok[0]]),
                                  chain=False)
                        add_toks.append(at)
                    if a == tc:
                        mt = []
                        for hl in range(8):
                            mt.append(b.op("vector", lambda e, hl=hl:
                                      e.tensor_tensor(
                                          dtt8_sb[:, hl * 128:(hl + 1) * 128],
                                          dtt8_sb[:, hl * 128:(hl + 1) * 128],
                                          binm_sb[:, :], ALU.add),
                                      waits=[add_toks[hl]], chain=False))
                        add_toks = mt
                    dec_toks[a] = b.op("scalar", lambda e, a=a: e.activation(
                        dec8_sb[:, a * 1024:(a + 1) * 1024], dtt8_sb[:, :],
                        AF.Exp), waits=add_toks + [b.mark("ve")])
                ystarted = False
                ylast = None
                for a in range(tc + 1):
                    gm_toks = []
                    for hl in range(8):
                        h = g * 8 + hl
                        gm = b.op("vector", lambda e, a=a, hl=hl:
                                  e.tensor_tensor(
                                      g8_sb[:, a * 1024 + hl * 128:
                                            a * 1024 + (hl + 1) * 128],
                                      dec8_sb[:, a * 1024 + hl * 128:
                                              a * 1024 + (hl + 1) * 128],
                                      cb_sb[:, a * 128:(a + 1) * 128],
                                      ALU.mult),
                                  waits=[dec_toks[a], cb_toks[a],
                                         b.mark("pe")] if hl == 0 else
                                        [dec_toks[a]],
                                  chain=False)
                        gm_toks.append(gm)
                for a in range(tc + 1):
                    for hl in range(8):
                        h = g * 8 + hl
                        ym = b.pe(lambda e, yb=yb, hl=hl, a=a, h=h,
                                  s=(not ystarted),
                                  st=(hl == 7 and a == tc): e.matmul(
                                      yb[:, hl * 64:(hl + 1) * 64],
                                      g8_sb[:, a * 1024 + hl * 128:
                                            a * 1024 + (hl + 1) * 128],
                                      dtx_sb[:, a * 4096 + h * 64:
                                             a * 4096 + (h + 1) * 64],
                                      start=s, stop=st,
                                      skip_group_check=True),
                                  waits=([b.mark("ve"), dtx_tok]
                                         if not ystarted else ()),
                                  sem="pe" if (hl == 7 and a == tc) else None)
                        ystarted = True
                        ylast = ym
                a1_ = b.op("vector", lambda e: e.tensor_tensor(
                    t1_sb[:, :], corr_sb[:, :], cum_sb[:, 0:512], ALU.mult),
                    waits=[cume, cevac])
                a2 = b.op("vector", lambda e, yb=yb: e.tensor_tensor(
                    ya_sb[:, :], t1_sb[:, :], yb[:, :], ALU.add),
                    waits=[ylast])
                a3 = b.op("vector", lambda e, sc=sc, tc=tc, g=g:
                          e.tensor_tensor(
                              t1_sb[:, :],
                              x_tm[:, (sc * 2 + tc) * 4096 + g * 512:
                                   (sc * 2 + tc) * 4096 + (g + 1) * 512],
                              drep_sb[:, g * 512:(g + 1) * 512], ALU.mult))
                a4 = b.op("vector", lambda e: e.tensor_tensor(
                    yb_sb[:, :], ya_sb[:, :], t1_sb[:, :], ALU.add),
                    waits=[last_yd[0]])
                yd = b.dma("sync",
                           y_d[(sc * 2 + tc) * 128:(sc * 2 + tc + 1) * 128,
                               g * 512:(g + 1) * 512],
                           yb_sb[:, :], f"ys{g % 4}", waits=[a4])
                last_yd[0] = yd
        for rg in range(8):
            pm = b.pe(lambda e, rg=rg: e.matmul(
                psM[1][0:1, 0:512], clgqb_sb[:, :],
                rep_sb[:, rg * 512:(rg + 1) * 512], start=True, stop=True,
                skip_group_check=True),
                waits=[b.mark("ve"), b.mark("se")], sem="pe")
            b.op("vector", lambda e: e.tensor_copy(
                rl_sb[:, :], psM[1][0:1, 0:512]), waits=[pm])
            pm2_ = b.pe(lambda e: e.matmul(
                psM[1][:, :], onesr[:, :], rl_sb[:, :],
                start=True, stop=True, skip_group_check=True),
                waits=[b.mark("ve"), b.mark("se")], sem="pe")
            b.op("scalar", lambda e, rg=rg: e.activation(
                lam_sb[:, rg * 512:(rg + 1) * 512], psM[1][:, :], AF.Exp),
                waits=[pm2_, b.mark("ve")])
        lam_tok = b.mark("se")
        for g in range(8):
            ps1 = None
            for a in range(2):
                pm2 = b.pe(lambda e, a=a, g=g: e.matmul(
                    psM[1][:, :], esb_sb[:, a * 128:(a + 1) * 128],
                    rep_sb[:, g * 512:(g + 1) * 512], start=True, stop=True,
                    skip_group_check=True),
                    waits=[b.mark("ve"), b.mark("se")], sem="pe")
                dxp = b.op("vector", lambda e, a=a, g=g: e.tensor_tensor(
                    t1_sb[:, :],
                    dtx_sb[:, a * 4096 + g * 512:a * 4096 + (g + 1) * 512],
                    psM[1][:, :], ALU.mult), waits=[pm2])
                ps1 = b.pe(lambda e, g=g, a=a, s=(a == 0): e.matmul(
                    psS[:, :], b_tm[:, g * 256 + a * 128:g * 256 + (a + 1) * 128],
                    t1_sb[:, :],
                    start=s, stop=(a == 1), skip_group_check=True),
                    waits=[dxp], sem="pe")
            b.op("vector", lambda e, g=g: e.tensor_tensor(
                ya_sb[:, :], lam_sb[:, g * 512:(g + 1) * 512],
                r_sb[:, g * 512:(g + 1) * 512], ALU.mult),
                waits=[lam_tok, ps1])
            b.op("vector", lambda e, g=g: e.tensor_tensor(
                r_sb[:, g * 512:(g + 1) * 512], ya_sb[:, :], psS[:, :],
                ALU.add))
    std = b.dma("sync", sc_d[:, :], r_sb[:, :], "sts", waits=[b.mark("ve")])

    b.wait("sync", [b.mark(s) for s in list(b.cnt)
                    if s.startswith(("ys", "cso", "ccs", "sts"))])
    b.finalize()
    return nc, ctx


def build_l5():
    nc = bass.Bass("TRN2", target_bir_lowering=False, debug=False)
    ctx = ExitStack()
    b = Bld(nc, ctx)
    y_di = nc.dram_tensor("y_tm", [TPC, 4096], BF16, kind="ExternalInput").ap()
    c_di = nc.dram_tensor("c_fm", [1024, TPC], BF16, kind="ExternalInput").ap()
    ccl_di = nc.dram_tensor("coreclg", [64, TPC], F32, kind="ExternalInput").ap()
    sp_di = nc.dram_tensor("s_prev", [3 * 128, 4096], BF16, kind="ExternalInput").ap()
    ll_di = nc.dram_tensor("lamlog_prev", [64, 3], F32, kind="ExternalInput").ap()
    gate_di = nc.dram_tensor("gate_fm", [4096, L4W], BF16, kind="ExternalInput").ap()
    x2_di = nc.dram_tensor("x2", [TPC, D], F32, kind="ExternalInput").ap()
    wop_di = nc.dram_tensor("w_outproj_t", [MI, D], F32, kind="ExternalInput").ap()
    nw_di = nc.dram_tensor("norm_row", [1, 4096], F32, kind="ExternalInput").ap()
    rep_di = nc.dram_tensor("rep", [64, 4096], BF16, kind="ExternalInput").ap()
    ident_d = nc.dram_tensor("ident", [128, 128], BF16, kind="ExternalInput").ap()
    onesr_d = nc.dram_tensor("ones_row_f", [1, 128], F32, kind="ExternalInput").ap()
    x3_d = nc.dram_tensor("x3", [TPC, D], F32, kind="ExternalOutput").ap()
    scr_d = nc.dram_tensor("op_scr", [D, TPC], BF16)

    yn_fm = b.sb("yn_fm", [128, NKC * TPC], BF16)
    r5_sb = b.sb("r5_sb", [128, 4096], BF16)
    lam_sb = b.sb("lam_sb", [128, 4096], BF16)

    llb_sb = b.sb("llb_sb", [64, 3], BF16)
    rl_sb = b.sb("rl_sb", [1, 512], F32)
    clgb_sb = b.sb("clgb_sb", [64, TPC], BF16)
    cum_sb = b.sb("cum_sb", [128, 512], BF16)
    corr_sb = b.sb("corr_sb", [128, 512], BF16)
    y_sb = b.sb("y_sb", [128, 4096], BF16)
    gate_tm = b.sb("gate_tm", [128, 4096], BF16)
    gstage = b.sb("gstage", [128, 4096], BF16)
    cstage = b.sb("cstage", [128, 8 * 128], BF16)
    normrep = b.sb("normrep", [128, 4096], BF16)
    ss_sb = b.sb("ss_sb", [128, 4], F32)
    junk_sb = b.sb("junk_sb", [128, 512], BF16)
    ident = b.sb("ident_sb", [128, 128], BF16)
    onesr = b.sb("onesr_sb", [1, 128], F32)
    nwrow = b.sb("nwrow", [1, 4096], F32)
    rep_sb = b.sb("rep_sb", [64, 4096], BF16)
    w_sb = b.sb("w_sb", [128, 2 * 2 * 128 * NKC], BF16)
    wstage = b.sb("wstage", [128, 4 * HALF], BF16)
    scr_tile = b.sb("scr_tile", [128, D], BF16)
    sprev_sb = scr_tile
    x_stage = b.sb("x_stage", [128, D], F32)

    pb = [b.psum(f"pb{i}", [128, 512], F32) for i in range(4)]
    tbp = [b.psum(f"tbp{i}", [128, 512], BF16) for i in range(2)]

    b.dma("sync", ident[:, :], ident_d[:, :], "cld")
    b.dma("sync", onesr[:, :], onesr_d[:, :], "cld")
    b.dma("gpsimd", llb_sb[:, :], ll_di[:, :], "cldg")
    b.dma("gpsimd", clgb_sb[:, :], ccl_di[:, :], "cldg")
    b.dma("sync", rep_sb[:, :], rep_di[:, :], "cld")
    b.dma("sync", nwrow[:, :], nw_di[:, :], "cld")
    consts = b.mark("cld")
    constsg = b.mark("cldg")
    b.wait("tensor", [consts, constsg])
    b.wait("vector", [consts, constsg])
    b.wait("scalar", [consts, constsg])

    for rg in range(8):
        pm = b.pe(lambda e, rg=rg: e.matmul(
            pb[1][:, :], onesr[:, :], nwrow[:, rg * 512:(rg + 1) * 512],
            start=True, stop=True, skip_group_check=True),
            waits=[b.mark("ve")], sem="pe")
        b.op("vector", lambda e, rg=rg: e.tensor_copy(
            normrep[:, rg * 512:(rg + 1) * 512], pb[1][:, :]), waits=[pm])

    ld3 = b.dma("sync", r5_sb[:, :], sp_di[256:384, :], "sl3")
    for j in (1, 0):
        lds = b.dma("sync", sprev_sb[:, :], sp_di[j * 128:(j + 1) * 128, :],
                    f"sl{j}", waits=[b.mark("ve")])
        for rg in range(8):
            pm = b.pe(lambda e, j=j, rg=rg: e.matmul(
                pb[1][0:1, 0:512], llb_sb[:, j:j + 1],
                rep_sb[:, rg * 512:(rg + 1) * 512], start=True, stop=True,
                skip_group_check=True),
                waits=[b.mark("ve"), b.mark("se")], sem="pe")
            b.op("vector", lambda e: e.tensor_copy(rl_sb[:, :],
                                                   pb[1][0:1, 0:512]),
                 waits=[pm])
            pm2 = b.pe(lambda e: e.matmul(
                pb[1][:, :], onesr[:, :], rl_sb[:, :], start=True, stop=True,
                skip_group_check=True), waits=[b.mark("ve"), b.mark("se")],
                sem="pe")
            b.op("scalar", lambda e, rg=rg: e.activation(
                lam_sb[:, rg * 512:(rg + 1) * 512], pb[1][:, :], AF.Exp),
                waits=[pm2, b.mark("ve")])
        for rg in range(8):
            b.op("vector", lambda e, rg=rg: e.tensor_tensor(
                r5_sb[:, rg * 512:(rg + 1) * 512],
                lam_sb[:, rg * 512:(rg + 1) * 512],
                r5_sb[:, rg * 512:(rg + 1) * 512], ALU.mult),
                waits=[b.mark("se"), ld3] if rg == 0 else ())
        for rg in range(8):
            b.op("vector", lambda e, rg=rg: e.tensor_tensor(
                r5_sb[:, rg * 512:(rg + 1) * 512],
                r5_sb[:, rg * 512:(rg + 1) * 512],
                sprev_sb[:, rg * 512:(rg + 1) * 512], ALU.add),
                waits=[lds] if rg == 0 else ())
    rin_tok = b.mark("ve")
    b.wait("tensor", [rin_tok])

    eps_ap, eps_tok = b.constcol(MEPS)
    tb5 = Banks(b, "tb5", 0, handles=tbp)
    yld_prev = None
    gld_prev = None
    cld_prev = None
    for tt in range(8):
        yld = b.dma("sync", y_sb[:, :], y_di[tt * 128:(tt + 1) * 128, :],
                    "yld", waits=[yld_prev])
        gld = b.dma("sync", gstage[:, :].rearrange("p (f t) -> p f t", t=128),
                    gate_di[:, 128 + tt * 128:128 + (tt + 1) * 128].rearrange(
                        "(f p) t -> p f t", p=128), "gld",
                    waits=[gld_prev, b.mark("pe")])
        cld = b.dma("sync", cstage[:, :].rearrange("p (g t) -> p g t", t=128),
                    c_di[:, tt * 128:(tt + 1) * 128].rearrange(
                        "(g p) t -> p g t", p=128), "clD",
                    waits=[cld_prev, b.mark("pe")])
        for f0 in range(0, NKC, 4):
            bi, bank, war = tb5.take()
            ptk = None
            for g2 in range(4):
                fi = f0 + g2
                ptk = b.pe(lambda e, o=bank[:, g2 * 128:(g2 + 1) * 128],
                           s=gstage[:, fi * 128:(fi + 1) * 128]:
                           e.transpose(o, s, ident[:, :]),
                           waits=([war, gld] if g2 == 0 else ()),
                           sem="pe" if g2 == 3 else None)
            sv = b.op("scalar", lambda e, f0=f0, i=bank[:, :]: e.activation(
                gate_tm[:, f0 * 128:(f0 + 4) * 128], i, AF.Silu),
                waits=[ptk])
            tb5.set_free(bi, sv)
        gate_done = b.mark("se")
        for g in range(8):
            pc = b.pe(lambda e, g=g: e.matmul(
                pb[0][:, :], cstage[:, g * 128:(g + 1) * 128],
                r5_sb[:, g * 512:(g + 1) * 512], start=True, stop=True,
                skip_group_check=True),
                waits=[cld, b.mark("ve"), b.mark("se")], sem="pe")
            cevac = b.op("vector", lambda e: e.tensor_copy(corr_sb[:, :],
                                                           pb[0][:, :]),
                         waits=[pc])
            pcm = b.pe(lambda e, g=g, tt=tt: e.matmul(
                pb[1][:, :], clgb_sb[:, tt * 128:(tt + 1) * 128],
                rep_sb[:, g * 512:(g + 1) * 512], start=True, stop=True,
                skip_group_check=True),
                waits=[b.mark("ve"), b.mark("se")], sem="pe")
            cume = b.op("scalar", lambda e: e.activation(
                cum_sb[:, :], pb[1][:, :], AF.Exp),
                waits=[pcm, b.mark("ve")])
            b.op("vector", lambda e, g=g: e.scalar_tensor_tensor(
                corr_sb[:, :], corr_sb[:, :], 1.0, cum_sb[:, :], ALU.mult,
                ALU.mult) if False else e.tensor_tensor(
                corr_sb[:, :], corr_sb[:, :], cum_sb[:, :], ALU.mult),
                waits=[cume, cevac, yld])
            yn_tok = b.op("vector", lambda e, g=g: e.tensor_tensor(
                y_sb[:, g * 512:(g + 1) * 512],
                y_sb[:, g * 512:(g + 1) * 512], corr_sb[:, :], ALU.add))
            b.op("vector", lambda e, g=g: e.tensor_tensor(
                junk_sb[:, :], y_sb[:, g * 512:(g + 1) * 512],
                y_sb[:, g * 512:(g + 1) * 512], ALU.mult))
            b.op("vector", lambda e: e.tensor_scalar(
                junk_sb[:, :], junk_sb[:, :], 0.0, 0.0, ALU.add, ALU.add,
                accum_out=ss_sb[:, 0:1]))
            b.op("vector", lambda e: e.tensor_scalar(
                ss_sb[:, 1:2], ss_sb[:, 0:1], 1.0 / 512, None, ALU.mult,
                ALU.bypass))
            b.op("scalar", lambda e: e.activation(
                ss_sb[:, 2:3], ss_sb[:, 1:2], AF.Sqrt, bias=eps_ap),
                waits=[b.mark("ve"), eps_tok])
            b.op("vector", lambda e: e.reciprocal(ss_sb[:, 3:4], ss_sb[:, 2:3]),
                 waits=[b.mark("se")])
            b.op("vector", lambda e, g=g: e.tensor_scalar(
                y_sb[:, g * 512:(g + 1) * 512],
                y_sb[:, g * 512:(g + 1) * 512], ss_sb[:, 3:4], None,
                ALU.mult, ALU.bypass))
            b.op("vector", lambda e, g=g: e.tensor_tensor(
                y_sb[:, g * 512:(g + 1) * 512],
                y_sb[:, g * 512:(g + 1) * 512],
                normrep[:, g * 512:(g + 1) * 512], ALU.mult))
            b.op("vector", lambda e, g=g: e.tensor_tensor(
                y_sb[:, g * 512:(g + 1) * 512],
                y_sb[:, g * 512:(g + 1) * 512],
                gate_tm[:, g * 512:(g + 1) * 512], ALU.mult),
                waits=[gate_done])
        ynt = b.mark("ve")
        for f0 in range(0, NKC, 4):
            bi, bank, war = tb5.take()
            ptk = None
            for g2 in range(4):
                fi = f0 + g2
                ptk = b.pe(lambda e, o=bank[:, g2 * 128:(g2 + 1) * 128],
                           s=y_sb[:, fi * 128:(fi + 1) * 128]:
                           e.transpose(o, s, ident[:, :]),
                           waits=([war, ynt] if g2 == 0 else ()),
                           sem="pe" if g2 == 3 else None)
            for g2 in range(4):
                fi = f0 + g2
                vt = b.op("vector", lambda e,
                          o=yn_fm[:, fi * TPC + tt * 128:fi * TPC + (tt + 1) * 128],
                          i=bank[:, g2 * 128:(g2 + 1) * 128]:
                          e.tensor_copy(o, i), waits=[ptk] if g2 == 0 else ())
            tb5.set_free(bi, vt)
        yld_prev = b.mark("ve")
        gld_prev = b.mark("se")
        cld_prev = b.mark("pe")
    b.wait("tensor", [b.mark("ve")])

    obanks = Banks(b, "ob", 0, handles=pb)

    def yn_rhs(k, half):
        return yn_fm[:, k * TPC + half * HALF: k * TPC + half * HALF + HALF]

    scr_done = proj_to_scr(b, obanks, wop_di, yn_rhs, NKC, scr_d, w_sb,
                           wstage, obs=2)
    ttb = Banks(b, "ttb", 0, handles=tbp)
    resid_tile_loop(b, scr_done, scr_d, x2_di, x3_d, ident, ttb, scr_tile,
                    x_stage)
    b.wait("sync", [b.mark(s) for s in list(b.cnt) if s.startswith("x1s")])
    b.finalize()
    return nc, ctx


def build_l6():
    nc = bass.Bass("TRN2", target_bir_lowering=False, debug=False)
    ctx = ExitStack()
    b = Bld(nc, ctx)
    x3_d = nc.dram_tensor("x3", [TPC, D], F32, kind="ExternalInput").ap()
    wf_d = nc.dram_tensor("w_ffn_t", [D, D], F32, kind="ExternalInput").ap()
    bias_d = nc.dram_tensor("bias_c", [128, NKC], F32, kind="ExternalInput").ap()
    wa_d = nc.dram_tensor("w_a_t", [D, 256], F32, kind="ExternalInput").ap()
    wbf_d = nc.dram_tensor("w_bf", [256, D], F32, kind="ExternalInput").ap()
    cf_d = nc.dram_tensor("coeffs16", [1, 16], F32, kind="ExternalInput").ap()
    sc_d = nc.dram_tensor("scales16", [1, 16], F32, kind="ExternalInput").ap()
    e16_d = nc.dram_tensor("exp16", [16, 256], F32, kind="ExternalInput").ap()
    ident_d = nc.dram_tensor("ident", [128, 128], BF16, kind="ExternalInput").ap()
    identf_d = nc.dram_tensor("ident_f", [128, 128], F32, kind="ExternalInput").ap()
    x4_d = nc.dram_tensor("x4", [TPC, D], F32, kind="ExternalOutput").ap()
    scr_d = nc.dram_tensor("f2_scr", [D, TPC], BF16)

    lnx_fm = b.sb("lnx_fm", [128, NKC * TPC], BF16)
    h2_fm = b.sb("h2_fm", [128, 2 * TPC], BF16)
    x_stage = b.sb("x_stage", [128, 2 * D], F32)
    ln_stage = b.sb("ln_stage", [128, 2 * D], BF16)
    stats = b.sb("stats", [128, 48], F32)
    mv = b.sb("mv", [128, 2], F32)
    rstd = b.sb("rstd", [128, 1], F32)
    ident = b.sb("ident_sb", [128, 128], BF16)
    identf = b.sb("identf_sb", [128, 128], F32)
    bias_sb = b.sb("bias_sb", [128, NKC], F32)
    wrow = b.sb("wrow", [1, 16], F32)
    wcol = b.sb("wcol", [16, 1], F32)
    cf_sb = b.sb("cf_sb", [1, 16], F32)
    scs_sb = b.sb("scs_sb", [1, 16], F32)
    e16_sb = b.sb("e16_sb", [16, 256], F32)
    ecol2 = b.sb("ecol2", [128, 2], F32)
    w_sb = b.sb("w_sb", [128, 2 * 2 * 128 * (NKC + 2)], BF16)
    wstage = b.sb("wstage", [128, 4 * HALF], BF16)
    scr_tile = b.sb("scr_tile", [128, D], BF16)

    pb = [b.psum(f"pb{i}", [128, 512], F32) for i in range(4)]
    tbp = [b.psum(f"tbp{i}", [128, 512], BF16) for i in range(2)]
    sm_ps = b.psum("sm_ps", [128, 16], F32)

    b.dma("sync", ident[:, :], ident_d[:, :], "cld")
    b.dma("sync", identf[:, :], identf_d[:, :], "cld")
    b.dma("sync", bias_sb[:, :], bias_d[:, :], "cld")
    b.dma("sync", cf_sb[:, :], cf_d[:, :], "cld")
    b.dma("sync", scs_sb[:, :], sc_d[:, :], "cld")
    b.dma("sync", e16_sb[:, :], e16_d[:, :], "cld")
    consts = b.mark("cld")
    b.wait("tensor", [consts])
    b.wait("vector", [consts])
    b.wait("scalar", [consts])

    tbanks = Banks(b, "tbk", 0, handles=tbp)
    lnfm_done = ln_stream_to_fm(b, x3_d, lnx_fm, TPC, x_stage, ln_stage, stats,
                                mv, rstd, tbanks, ident[:, :], consts)
    b.wait("tensor", [lnfm_done])

    b.op("vector", lambda e: e.tensor_tensor(wrow[:, :], cf_sb[:, :],
                                             scs_sb[:, :], ALU.mult))
    t2 = b.pe(lambda e: e.transpose(sm_ps[0:16, 0:1], wrow[:, :],
                                    identf[0:1, 0:1]),
              waits=[b.mark("ve")], sem="pe")
    b.op("vector", lambda e: e.tensor_copy(wcol[:, :], sm_ps[0:16, 0:1]),
         waits=[t2])
    for j in range(2):
        ej = b.pe(lambda e, j=j: e.matmul(
            sm_ps[:, 1 + j:2 + j], e16_sb[:, j * 128:(j + 1) * 128],
            wcol[:, :], start=True, stop=True, skip_group_check=True),
            waits=[b.mark("ve")], sem="pe")
        ec_tok = b.op("vector", lambda e, j=j: e.tensor_copy(
            ecol2[:, j:j + 1], sm_ps[:, 1 + j:2 + j]), waits=[ej])

    def lnx_rhs(k, half):
        return lnx_fm[:, k * TPC + half * HALF: k * TPC + half * HALF + HALF]

    hbanks = Banks(b, "hbk", 0, handles=pb)

    def h_cb(oi, half, bank, ptok):
        return b.op("vector", lambda e, o=h2_fm[:, oi * TPC + half * HALF:
                                                oi * TPC + half * HALF + HALF],
                    i=bank[:, :], j=oi: e.tensor_scalar(
                        o, i, ecol2[:, j:j + 1], None, ALU.mult, ALU.bypass),
                    waits=[ptok, ec_tok])

    proj_fm(b, hbanks, wa_d, lnx_rhs, 2, h_cb, w_sb, obs=2)
    h_done = b.mark("ve")
    b.wait("tensor", [h_done])
    b.wait("gpsimd", [b.mark("pe")])

    def h_rhs(k, half):
        return h2_fm[:, k * TPC + half * HALF: k * TPC + half * HALF + HALF]

    def bias_evac(oi, half, bank, stg, waits):
        return b.op("vector", lambda e, stg=stg, oi=oi, bank=bank:
                    e.tensor_scalar(stg, bank[:, :], bias_sb[:, oi:oi + 1],
                                    None, ALU.add, ALU.bypass), waits=waits)

    obanks = Banks(b, "obk", 0, handles=pb)
    scr_done = proj_to_scr(b, obanks, wf_d, lnx_rhs, NKC, scr_d, w_sb,
                           wstage, obs=2, extra_k=(2, wbf_d, h_rhs),
                           evac_fn=bias_evac)
    ttb = Banks(b, "ttb", 0, handles=tbp)
    resid_tile_loop(b, scr_done, scr_d, x3_d, x4_d, ident, ttb, scr_tile,
                    x_stage[:, 0:D])
    b.wait("sync", [b.mark(s) for s in list(b.cnt) if s.startswith("x1s")])
    b.finalize()
    return nc, ctx



_NC_CACHE = {}


def _get_nc(name, builder):
    if name not in _NC_CACHE:
        _NC_CACHE[name] = builder()
    return _NC_CACHE[name][0]


def _run(name, builder, in_maps):
    nc = _get_nc(name, builder)
    res = run_bass_kernel_spmd(nc, in_maps, core_ids=list(range(8)))
    return res.results


def kernel(**inputs):
    p = host_prep(inputs)
    X = np.asarray(inputs["L0_attn_X"], dtype=np.float32)
    eye_b = p["ident_bf"]

    def tok(core):
        bb, cc = core // 4, core % 4
        return bb, slice(cc * TPC, (cc + 1) * TPC)

    im = []
    for core in range(8):
        bb, sl = tok(core)
        im.append({
            "x": _f32h(X[bb, sl, :]),
            "w_qkv": p["w_qkv"],
            "cos_c": np.ascontiguousarray(p["cos_t"][:, sl]),
            "sin_c": np.ascontiguousarray(p["sin_t"][:, sl]),
            "cosq_c": np.ascontiguousarray(p["cosq_t"][:, sl]),
            "sinq_c": np.ascontiguousarray(p["sinq_t"][:, sl]),
            "ident": eye_b,
        })
    r1 = _run("l1", build_l1, im)

    im = []
    for k in range(8):
        qfm = np.concatenate(
            [np.concatenate([np.asarray(r1[bt * 4 + cc]["qkv_fm"])
                             [4 * k * 128:(4 * k + 4) * 128, :]
                             for cc in range(4)], axis=1)
             for bt in range(2)], axis=1)
        kfm = np.concatenate(
            [np.concatenate([np.asarray(r1[bt * 4 + cc]["qkv_fm"])
                             [4096 + k * 128:4096 + (k + 1) * 128, :]
                             for cc in range(4)], axis=1)
             for bt in range(2)], axis=1)
        vtm = np.concatenate(
            [np.concatenate([np.asarray(r1[bt * 4 + cc]["v_tm"])
                             [:, k * 128:(k + 1) * 128]
                             for cc in range(4)], axis=0)
             for bt in range(2)], axis=0)
        im.append({
            "q_fm": np.ascontiguousarray(qfm),
            "k_fm": np.ascontiguousarray(kfm),
            "v_tm": np.ascontiguousarray(vtm),
            "mask_qk_b": p["mask_qk_b"], "maskt_kq_b": p["maskt_kq_b"],
            "ident": p["ident_bf"],
        })
    r2a = _run("l2a", build_l2a, im)

    im = []
    for core in range(8):
        bb, sl = tok(core)
        cc = core % 4
        afm = np.concatenate(
            [np.asarray(r2a[k]["attn_tm"])[bb * T + cc * TPC:
                                           bb * T + (cc + 1) * TPC, :].T
             for k in range(8)], axis=0)
        im.append({
            "attn_fm": np.ascontiguousarray(afm),
            "x": _f32h(X[bb, sl, :]),
            "w_o": p["w_o"],
            "ident": eye_b, "ones_col": p["ones_col"],
        })
    r2b = _run("l2b", build_l2b, im)

    im = []
    for core in range(8):
        bb, sl = tok(core)
        pms = np.concatenate([np.asarray(r2b[bb * 4 + j]["pmean"])
                              for j in range(4)], axis=1)
        im.append({
            "x1": np.asarray(r2b[core]["x1"]),
            "pmeans": np.ascontiguousarray(pms),
            "w_org_t": p["w_org_t"], "w_down": p["w_down"], "w_up": p["w_up"],
            "keys_t": p["keys_t"], "wq_mole_t": p["wq_mole_t"],
            "exp8": p["exp8"], "ident": eye_b, "ident_f": p["ident_f"],
        })
    r3 = _run("l3", build_l3, im)

    im = []
    for core in range(8):
        bb, sl = tok(core)
        cc = core % 4
        halo = (np.zeros((3, D), np.float32) if cc == 0 else
                np.asarray(r3[core - 1]["x2"])[-3:, :])
        im.append({
            "x2": np.asarray(r3[core]["x2"]),
            "halo": _f32h(halo),
            "w_inproj_p": p["w_inproj_p"],
            "ident": eye_b,
        })
    r4a = _run("l4a", build_l4a, im)

    im = []
    for core in range(8):
        pout = np.asarray(r4a[core]["pout"])
        im.append({
            "xbc_fm": np.ascontiguousarray(pout[4096:4096 + CONV_DIM, :]),
            "dt_fm": np.ascontiguousarray(pout[10240:10304, :]),
            "conv_w_c": p["conv_w_c"], "conv_b_c": p["conv_b_c"],
            "dt_bias_c": p["dt_bias_c"], "a_log_c": p["a_log_c"],
            "d_row": p["d_row"], "rep": p["rep"],
            "maskneg128": p["maskneg128"],
            "ident": eye_b, "ident_f": p["ident_f"],
            "ones_row_f": p["ones_row_f"], "ones_row_b": p["ones_row_b"],
        })
    r4b = _run("l4b", build_l4b, im)

    im = []
    for core in range(8):
        bb, sl = tok(core)
        cc = core % 4
        sprev = np.zeros((3, 128, 4096), BF)
        llog = np.zeros((64, 3), np.float32)
        for j in range(3):
            pc = cc - 1 - j
            if pc >= 0:
                pcore = bb * 4 + pc
                sprev[j] = np.asarray(r4b[pcore]["s_core"])
                llog[:, j] = np.asarray(r4b[pcore]["coreclg"])[:, -1]
        pout = np.asarray(r4a[core]["pout"])
        im.append({
            "y_tm": np.asarray(r4b[core]["y_tm"]),
            "c_fm": np.asarray(r4b[core]["c_fm"]),
            "coreclg": np.asarray(r4b[core]["coreclg"]),
            "s_prev": sprev.reshape(384, 4096),
            "lamlog_prev": llog,
            "gate_fm": np.ascontiguousarray(pout[0:4096, :]),
            "x2": np.asarray(r3[core]["x2"]),
            "w_outproj_t": p["w_outproj_t"],
            "norm_row": p["norm_row"], "rep": p["rep"],
            "ident": eye_b, "ones_row_f": p["ones_row_f"],
        })
    r5 = _run("l5", build_l5, im)

    im = []
    for core in range(8):
        im.append({
            "x3": np.asarray(r5[core]["x3"]),
            "w_ffn_t": p["w_ffn_t"], "bias_c": p["bias_c"],
            "w_a_t": p["w_a_t"], "w_bf": p["w_bf"],
            "coeffs16": p["coeffs16"], "scales16": p["scales16"],
            "exp16": p["exp16"],
            "ident": eye_b, "ident_f": p["ident_f"],
        })
    r6 = _run("l6", build_l6, im)

    out = np.empty((B, T, D), np.float32)
    for core in range(8):
        bb, sl = tok(core)
        out[bb, sl, :] = np.asarray(r6[core]["x4"])
    return out
